# version 06
# baseline (speedup 1.0000x reference)
import math
import os
from contextlib import ExitStack

DEV_STAGE = int(os.environ.get("DEV_STAGE", "9"))
DEV_NBLK = int(os.environ.get("DEV_NBLK", "0"))
DEV_S3 = int(os.environ.get("DEV_S3", "9"))

import numpy as np
import ml_dtypes

import concourse.bass as bass
import concourse.mybir as mybir
from concourse.bass_utils import run_bass_kernel_spmd

F32 = mybir.dt.float32
BF16 = mybir.dt.bfloat16
AF = mybir.ActivationFunctionType
ALU = mybir.AluOpType
BFD = ml_dtypes.bfloat16

D = 1024
DC = 8
TOWN = 4096
BLK = 512
NSUB = 4
NBLK = TOWN // BLK
H, HD, NF = 16, 64, 128
EPS_FAVOR = 1e-6
LN_EPS = 1e-5
LNINV = -0.5 * math.log(NF)
LOG_S = math.log(8192.0)

CONST_VALS = sorted(set([2.0, -2.0, 0.5, -0.5, -1.0, 1.0 / D, LN_EPS,
                         EPS_FAVOR, LNINV, -LOG_S, 1.0 / 32.0, 1e-20]))


class Ctx:
    def __init__(self, nc, stack):
        self.nc = nc
        self.stack = stack
        self.q = {"sync": [], "gpsimd": [], "vector": [], "scalar": [],
                  "tensor": []}
        self.sems = {}
        self.t = {}
        self.d = {}
        self._l0f_wfree = {}
        self._ln_dv = 0

    def sem(self, name):
        if name not in self.sems:
            h = self.stack.enter_context(self.nc.semaphore(name))
            self.sems[name] = [h, 0]
        return self.sems[name]

    def tick(self, name, k):
        s = self.sem(name)
        s[1] += k
        return s[1]

    def cur(self, name):
        return self.sem(name)[1]

    def emit(self, eng, fn):
        self.q[eng].append(fn)

    def wait(self, eng, semname, tgt=None):
        s = self.sem(semname)
        n = s[1] if tgt is None else tgt
        if n <= 0:
            return
        h = s[0]
        self.emit(eng, lambda e, h=h, n=n: e.wait_ge(h, n))

    def dma(self, eng, out, in_, semname, **kw):
        n = self.tick(semname, 16)
        h = self.sem(semname)[0]
        self.emit(eng, lambda e, out=out, in_=in_, h=h, kw=kw:
                  e.dma_start(out=out, in_=in_, **kw).then_inc(h, 16))
        return n

    def op(self, eng, fn, semname=None, k=1):
        if semname is None:
            self.emit(eng, fn)
            return None
        n = self.tick(semname, k)
        h = self.sem(semname)[0]
        self.emit(eng, lambda e, fn=fn, h=h, k=k: fn(e).then_inc(h, k))
        return n

    def finalize(self, block):
        for name in ("sync", "gpsimd", "vector", "scalar", "tensor"):
            fns = self.q[name]
            if not fns:
                continue
            getattr(block, name)(lambda e, fns=fns: [f(e) for f in fns])


def sbuf(c, name, shape, dtype):
    t = c.stack.enter_context(c.nc.sbuf_tensor(name, shape, dtype))
    c.t[name] = t
    return t


def psum_t(c, name, shape, dtype=F32):
    t = c.stack.enter_context(c.nc.psum_tensor(name, shape, dtype))
    c.t[name] = t
    return t


def load_w_dmajor(c, dst, w_ap, E):
    c.dma("gpsimd", dst.ap().rearrange("p (dc e) -> p dc e", e=E),
          w_ap.rearrange("(dc p) e -> p dc e", p=128), "s_wt")


def emit_ln_chunk(c, xsrc, xn_dst, lo_dst, prev_dv=0, sq_scr=None):
    st = c.t["r_stats"]
    f32s = c.t["r_f32"]
    sq = sq_scr if sq_scr is not None else c.t["r_xs"][:, 0:D]
    c.wait("gpsimd", "s_dv", c._ln_dv)
    n = c.op("gpsimd", lambda e: e.tensor_tensor(sq[:], xsrc, xsrc, ALU.mult),
             "s_gp")
    c.wait("vector", "s_gp")
    nr = c.op("vector", lambda e: e.reduce_sum(st[:, 0:1], xsrc,
                                               axis=mybir.AxisListType.X),
              "s_dv")
    c.wait("vector", "s_gp", n)
    nr = c.op("vector", lambda e: e.reduce_sum(st[:, 1:2], sq[:],
                                               axis=mybir.AxisListType.X),
              "s_dv")
    c.wait("gpsimd", "s_dv", nr)
    n = c.op("gpsimd", lambda e: e.tensor_scalar(st[:, 2:3], st[:, 0:1],
                                                 1.0 / D, None, ALU.mult),
             "s_gp")
    c.wait("gpsimd", "s_gp", n)
    n = c.op("gpsimd", lambda e: e.tensor_scalar(st[:, 3:4], st[:, 1:2],
                                                 1.0 / D, None, ALU.mult),
             "s_gp")
    c.wait("gpsimd", "s_gp", n)
    n = c.op("gpsimd", lambda e: e.tensor_tensor(st[:, 5:6], st[:, 2:3],
                                                 st[:, 2:3], ALU.mult), "s_gp")
    c.wait("gpsimd", "s_gp", n)
    n = c.op("gpsimd", lambda e: e.tensor_tensor(st[:, 3:4], st[:, 3:4],
                                                 st[:, 5:6], ALU.subtract),
             "s_gp")
    c.wait("gpsimd", "s_gp", n)
    n = c.op("gpsimd", lambda e: e.tensor_scalar(st[:, 3:4], st[:, 3:4],
                                                 LN_EPS, None, ALU.add),
             "s_gp")
    c.wait("vector", "s_gp", n)
    c.wait("vector", "s_ac")
    n = c.op("vector", lambda e: e.reciprocal(st[:, 6:7], st[:, 3:4]), "s_dv")
    c.wait("scalar", "s_dv", n)
    c.wait("scalar", "s_gp")
    n = c.op("scalar", lambda e: e.activation(st[:, 4:5], st[:, 6:7],
                                              AF.Sqrt), "s_ac")
    c.wait("gpsimd", "s_ac", n)
    if lo_dst is None:
        n = c.op("gpsimd", lambda e: e.tensor_scalar(
            xn_dst, xsrc, st[:, 2:3], st[:, 4:5], ALU.subtract, ALU.mult),
            "s_gp")
        c._ln_dv = c.cur("s_dv")
        return n
    n = c.op("gpsimd", lambda e: e.tensor_scalar(
        f32s[:], xsrc, st[:, 2:3], st[:, 4:5], ALU.subtract, ALU.mult), "s_gp")
    c.wait("gpsimd", "s_gp", n)
    n = c.op("gpsimd", lambda e: e.tensor_scalar(xn_dst, f32s[:], 0.0, None,
                                                 ALU.add), "s_gp")
    c.wait("gpsimd", "s_gp", n)
    n = c.op("gpsimd", lambda e: e.tensor_tensor(lo_dst, f32s[:], xn_dst,
                                                 ALU.subtract), "s_gp")
    c._ln_dv = c.cur("s_dv")
    return n


def emit_res(c, x_in, y_fm, x_out, xn_out, lo_out):
    rx, xs = c.t["r_x"], c.t["r_xs"]
    ry = c.t["r_ybuf"]
    lob = c.t["r_lobuf"]
    xnb = c.t["r_xn"]
    done = {}
    prev_dv = 0
    for ch in range(TOWN // 128):
        p = ch % 2
        t0 = ch * 128
        if ch >= 2:
            c.wait("sync", "s_rst%d" % p, done[ch - 2])
        c.dma("sync", rx[:, p * D:(p + 1) * D], x_in[t0:t0 + 128, :],
              "s_rin%d" % p)
        if y_fm is not None:
            c.dma("sync", ry[:, p * D:(p + 1) * D], y_fm[0:D, t0:t0 + 128],
                  "s_rin%d" % p, transpose=True)
        c.wait("vector", "s_rin%d" % p)
        c.op("vector", lambda e, p=p, has_y=y_fm is not None: (
            e.tensor_tensor(xs[:, p * D:(p + 1) * D], rx[:, p * D:(p + 1) * D],
                            ry[:, p * D:(p + 1) * D], ALU.add) if has_y else
            e.tensor_scalar(xs[:, p * D:(p + 1) * D], rx[:, p * D:(p + 1) * D],
                            0.0, None, ALU.add)), "s_dv")
        c.wait("scalar", "s_dv")
        c.dma("scalar", x_out[t0:t0 + 128, :], xs[:, p * D:(p + 1) * D],
              "s_rst%d" % p)
        if xn_out is not None:
            xnd = xnb[:, p * D:(p + 1) * D]
            lod = lob[:, p * D:(p + 1) * D] if lo_out is not None else None
            c.wait("gpsimd", "s_dv")
            prev_dv = emit_ln_chunk(c, xs[:, p * D:(p + 1) * D], xnd, lod,
                                    prev_dv, sq_scr=rx[:, p * D:(p + 1) * D])
            c.wait("scalar", "s_gp", prev_dv)
            c.dma("scalar", xn_out[t0:t0 + 128, :], xnd, "s_rst%d" % p)
            if lo_out is not None:
                c.dma("scalar", lo_out[t0:t0 + 128, :], lod, "s_rst%d" % p)
        done[ch] = c.cur("s_rst%d" % p)


def emit_l0a(c):
    wq, wk, wv, wo, wfe = (c.t["wq"], c.t["wk"], c.t["wv"], c.t["wo"],
                           c.t["wfeat"])
    P0, P1, P2, P3, P4 = (c.t["pp0"], c.t["pp1"], c.t["pp2"], c.t["pp3"],
                          c.t["pp4"])
    ST = [c.t["st0"], c.t["st1"], c.t["st2"]]
    STW = [455, 455, 130]
    PP = [P0, P1]
    PN = [P2, P3]

    xnf, khh, qhh = c.t["a_xnf"], c.t["a_khh"], c.t["a_qhh"]
    sqh, vaug = c.t["a_sqh"], c.t["a_vaug"]
    kpf, qpf, kptm = c.t["a_kpf"], c.t["a_qpf"], c.t["a_kptm"]
    sqm, am, numb = c.t["a_sqm"], c.t["a_am"], c.t["a_numb"]
    yall, yo = c.t["a_yall"], c.t["a_yo"]
    sf, sb = c.t["a_sf32"], c.t["a_sbf"]
    dnr, recb = c.t["a_dn"], c.t["a_rec"]
    sx, xnc = c.t["r_x"], c.t["r_xn"]
    tril, shead, ucol, ones1 = (c.t["c_tril"], c.t["c_shead"], c.t["c_ucol"],
                                c.t["c_ones1"])
    flag = c.t["c_flag"]
    xn0, x_pre, x_own, y0 = (c.d["xn0"].ap(), c.d["x_pre"].ap(),
                             c.d["x_own"].ap(), c.d["y0"].ap())

    c.op("vector", lambda e: e.memset(sf[:], 0.0))
    c.op("vector", lambda e: e.memset(sb[:], 0.0))
    c.op("vector", lambda e: e.memset(
        vaug.ap().rearrange("p (s hh ee) -> p s hh ee", s=4, ee=65)
        [:, :, :, 64:65], 1.0), "s_dv")
    init_dv = c.cur("s_dv")

    state_done = init_dv
    ln_store = {}
    prev_ln_dv = 0
    xnf_free = 0

    def sqrows(src, slot):
        c.wait("scalar", "s_ac")
        pe_g = {}
        for g in range(8):
            p = g % 2
            if g >= 2:
                c.wait("scalar", "s_pe", pe_g[g - 2])
            c.op("scalar", lambda e, g=g, p=p: e.activation(
                sqh[0:64, p * 2 * BLK:(p + 1) * 2 * BLK],
                src[0:64, 2 * g * BLK:2 * (g + 1) * BLK], AF.Square), "s_ac")
            c.wait("tensor", "s_ac")
            for hh in range(2):
                h = 2 * g + hh
                pe_g[g] = c.op("tensor", lambda e, h=h, p=p, hh=hh: e.matmul(
                    P4[0:16, 0:BLK], ucol[0:64, h * 16:h * 16 + 16],
                    sqh[0:64, (p * 2 + hh) * BLK:(p * 2 + hh + 1) * BLK],
                    start=(h == 0), stop=(h == H - 1)), "s_pe")
        c.wait("scalar", "s_pe")
        return c.op("scalar", lambda e, slot=slot: e.activation(
            sqm[:, slot * BLK:(slot + 1) * BLK], P4[0:16, 0:BLK],
            AF.Identity, scale=-0.5), "s_ac")

    def emit_s1(blk):
        nonlocal prev_ln_dv
        own = blk >= NBLK
        xsrc_d = x_own if own else x_pre
        t0 = (blk - NBLK) * BLK if own else blk * BLK
        g0 = blk * BLK
        for j in range(NSUB):
            p = j % 2
            c.wait("sync", "s_gp", prev_ln_dv)
            c.dma("sync", sx[:, p * D:(p + 1) * D],
                  xsrc_d[t0 + j * 128:t0 + j * 128 + 128, :], "s_in%d" % p)
            c.wait("gpsimd", "s_in%d" % p)
            c.wait("gpsimd", "s_xn%d" % p)
            prev_ln_dv = emit_ln_chunk(c, sx[:, p * D:(p + 1) * D],
                                       xnc[:, p * D:(p + 1) * D], None,
                                       prev_ln_dv,
                                       sq_scr=c.t["r_xs"][:, p * D:(p + 1) * D])
            c.wait("scalar", "s_gp", prev_ln_dv)
            c.dma("scalar", xn0[g0 + j * 128:g0 + j * 128 + 128, :],
                  xnc[:, p * D:(p + 1) * D], "s_xn%d" % p)

    def emit_s1_load(blk):
        g0 = blk * BLK
        c.wait("sync", "s_xn0")
        c.wait("sync", "s_xn1")
        c.wait("sync", "s_pe", xnf_free)
        for dc in range(DC):
            c.dma("sync", xnf[:, dc * BLK:(dc + 1) * BLK],
                  xn0[g0:g0 + BLK, dc * 128:(dc + 1) * 128], "s_tr",
                  transpose=True)

    blk_list = list(range(2 * NBLK))
    if DEV_NBLK:
        blk_list = list(range(NBLK, NBLK + DEV_NBLK))
    emit_s1(blk_list[0])
    pending_load = blk_list[0]
    for bi, blk in enumerate(blk_list):
        own = blk >= NBLK
        t0 = (blk - NBLK) * BLK if own else blk * BLK
        g0 = blk * BLK

        if DEV_STAGE < 2:
            emit_s1_load(blk)
            if bi + 1 < len(blk_list):
                emit_s1(blk_list[bi + 1])
            continue
        emit_s1_load(blk)
        c.wait("tensor", "s_tr")
        if blk == 0:
            c.wait("tensor", "s_wt")
            c.wait("tensor", "s_dv", init_dv)

        def projh(dst, w, pp):
            ev = {}
            c.wait("tensor", "s_ac")
            for h in range(H):
                pb = pp[h % 2]
                if h >= 2:
                    c.wait("tensor", "s_ac", ev[h - 2])
                for dc in range(DC):
                    c.op("tensor", lambda e, pb=pb, w=w, dc=dc, h=h: e.matmul(
                        pb[0:64, 0:BLK],
                        w[:, dc * D + h * 64:dc * D + h * 64 + 64],
                        xnf[:, dc * BLK:(dc + 1) * BLK],
                        start=(dc == 0), stop=(dc == DC - 1)),
                        "s_pe" if dc == DC - 1 else None)
                c.wait("scalar", "s_pe")
                ev[h] = c.op("scalar", lambda e, pb=pb, dst=dst, h=h: e.copy(
                    dst[0:64, h * BLK:(h + 1) * BLK], pb[0:64, 0:BLK]), "s_ac")

        projh(khh, wk, PP)
        if own:
            projh(qhh, wq, PP)

        vev = {}
        c.wait("tensor", "s_ac")
        for g in range(8):
            half, j = g // 4, g % 4
            pb = PN[g % 2]
            if g >= 2:
                c.wait("tensor", "s_ac", vev[g - 2])
            for dc in range(DC):
                c.op("tensor", lambda e, pb=pb, dc=dc, half=half, j=j: e.matmul(
                    pb[:, 0:512],
                    xnf[:, dc * BLK + j * 128:dc * BLK + j * 128 + 128],
                    wv[:, dc * D + half * 512:dc * D + half * 512 + 512],
                    start=(dc == 0), stop=(dc == DC - 1)),
                    "s_pe" if dc == DC - 1 else None)
            c.wait("scalar", "s_pe")
            vdst = vaug.ap().rearrange("p (s hh ee) -> p s hh ee", s=4, ee=65)[
                :, j, half * 8:half * 8 + 8, 0:64]
            vev[g] = c.op("scalar", lambda e, pb=pb, vdst=vdst: e.copy(
                vdst, pb[:, 0:512].rearrange("p (hh ee) -> p hh ee", ee=64)),
                "s_ac")
        xnf_free = c.cur("s_pe")
        if bi + 1 < len(blk_list):
            emit_s1(blk_list[bi + 1])

        if DEV_STAGE < 3:
            continue
        sqrows(khh, 0)
        c.wait("tensor", "s_ac")
        if DEV_S3 < 2:
            continue

        def featmaps(srcf, dstf, slot):
            ev = {}
            c.wait("tensor", "s_ac")
            for h in range(H):
                pb = PP[h % 2]
                if h >= 2:
                    c.wait("tensor", "s_ac", ev[h - 2])
                c.op("tensor", lambda e, pb=pb, h=h, slot=slot: e.matmul(
                    pb[:, 0:BLK], shead[:, h * 128:h * 128 + 128],
                    sqm[:, slot * BLK:(slot + 1) * BLK],
                    start=True, stop=False))
                c.op("tensor", lambda e, pb=pb, h=h, srcf=srcf: e.matmul(
                    pb[:, 0:BLK], wfe[0:64, 0:128],
                    srcf[0:64, h * BLK:(h + 1) * BLK],
                    start=False, stop=True), "s_pe")
                c.wait("scalar", "s_pe")
                ev[h] = c.op("scalar", lambda e, pb=pb, h=h, dstf=dstf:
                             e.activation(dstf[:, h * BLK:(h + 1) * BLK],
                                          pb[:, 0:BLK], AF.Exp, bias=LNINV),
                             "s_ac")

        if own:
            featmaps(khh, kpf, 0)
        if DEV_S3 < 4:
            continue
        if own:
            sqrows(qhh, 1)
            c.wait("tensor", "s_ac")
            featmaps(qhh, qpf, 1)

        if DEV_STAGE < 4:
            continue
        c.wait("tensor", "s_ac")
        last_ymult = 0
        for j in range(NSUB):
            c.wait("tensor", "s_dv", last_ymult)
            for qd in range(4):
                c.wait("tensor", "s_ac")
                for hh in range(4):
                    h = 4 * qd + hh
                    c.op("tensor", lambda e, h=h, hh=hh, j=j: e.matmul(
                        P4[:, hh * 128:(hh + 1) * 128],
                        sqm[:, j * 128:(j + 1) * 128],
                        shead[:, h * 128:h * 128 + 128],
                        start=True, stop=False))
                    c.op("tensor", lambda e, h=h, hh=hh, j=j: e.matmul(
                        P4[:, hh * 128:(hh + 1) * 128],
                        khh[0:64, h * BLK + j * 128:h * BLK + j * 128 + 128],
                        wfe[0:64, 0:128], start=False, stop=True),
                        "s_pe" if hh == 3 else None)
                c.wait("scalar", "s_pe")
                c.op("scalar", lambda e, qd=qd: e.activation(
                    kptm[:, qd * 512:(qd + 1) * 512], P4[:, 0:512], AF.Exp,
                    bias=LNINV), "s_ac")
            c.wait("tensor", "s_ac")
            if own:
                am_done, den_done, numb_done, y_done, pe_n = {}, {}, {}, {}, {}
                for h in range(H):
                    pa, pn = PP[h % 2], PN[h % 2]
                    if h >= 2:
                        c.wait("tensor", "s_dv", am_done[h - 2])
                    pe_a = c.op("tensor", lambda e, pa=pa, h=h, j=j: e.matmul(
                        pa[:, 0:128],
                        kpf[:, h * BLK + j * 128:h * BLK + j * 128 + 128],
                        qpf[:, h * BLK + j * 128:h * BLK + j * 128 + 128],
                        start=True, stop=True), "s_pe")
                    c.wait("vector", "s_pe", pe_a)
                    am_done[h] = c.op("vector", lambda e, pa=pa, h=h:
                                      e.tensor_tensor(
                                          am[:, (h % 2) * 128:(h % 2) * 128 + 128],
                                          pa[:, 0:128], tril[:], ALU.mult),
                                      "s_dv")
                    if h >= 2:
                        c.wait("tensor", "s_ac", numb_done[h - 2])
                        c.wait("tensor", "s_dv", den_done[h - 2])
                    c.wait("tensor", "s_dv", am_done[h])
                    if h == 0:
                        c.wait("tensor", "s_dv", state_done)
                    vsl = vaug[:, j * 1040 + h * 65:j * 1040 + (h + 1) * 65]
                    c.op("tensor", lambda e, pn=pn, h=h, j=j: e.matmul(
                        pn[0:65, 0:128], sb[:, h * 65:(h + 1) * 65],
                        qpf[:, h * BLK + j * 128:h * BLK + j * 128 + 128],
                        start=True, stop=False))
                    pe_n[h] = c.op("tensor", lambda e, pn=pn, h=h, vsl=vsl:
                                   e.matmul(pn[0:65, 0:128], vsl,
                                            am[:, (h % 2) * 128:(h % 2) * 128 + 128],
                                            start=False, stop=True), "s_pe")
                    sth, sto = ST[h // 7], (h % 7) * 65
                    c.op("tensor", lambda e, sth=sth, sto=sto, h=h, j=j,
                         vsl=vsl: e.matmul(
                             sth[:, sto:sto + 65],
                             kptm[:, h * 128:(h + 1) * 128],
                             vsl, start=True, stop=True), "s_pe")
                    c.wait("vector", "s_pe", pe_n[h])
                    nd = c.op("vector", lambda e, pn=pn, h=h: e.tensor_scalar(
                        dnr[0:1, (h % 2) * 128:(h % 2) * 128 + 128],
                        pn[64:65, 0:128], EPS_FAVOR, None, ALU.add), "s_dv")
                    c.wait("vector", "s_dv", nd)
                    nd = c.op("vector", lambda e, h=h: e.reciprocal(
                        dnr[0:1, 256 + (h % 2) * 128:256 + (h % 2) * 128 + 128],
                        dnr[0:1, (h % 2) * 128:(h % 2) * 128 + 128]), "s_dv")
                    c.wait("vector", "s_dv", nd)
                    den_done[h] = c.op("vector", lambda e, h=h: e.tensor_scalar(
                        recb[0:1, (h % 2) * 128:(h % 2) * 128 + 128],
                        dnr[0:1, 256 + (h % 2) * 128:256 + (h % 2) * 128 + 128],
                        0.0, None, ALU.add), "s_dv")
                    c.wait("tensor", "s_dv", den_done[h])
                    if h >= 1:
                        c.wait("tensor", "s_dv", y_done[h - 1])
                    pe_y = c.op("tensor", lambda e, h=h: e.matmul(
                        P4[0:64, 0:128], ones1[0:1, 0:64],
                        recb[0:1, (h % 2) * 128:(h % 2) * 128 + 128],
                        start=True, stop=True), "s_pe")
                    c.wait("scalar", "s_pe", pe_n[h])
                    if h >= 2:
                        c.wait("scalar", "s_dv", y_done[h - 2])
                    numb_done[h] = c.op("scalar", lambda e, pn=pn, h=h: e.copy(
                        numb[:, (h % 2) * 128:(h % 2) * 128 + 128],
                        pn[0:64, 0:128]), "s_ac")
                    c.wait("vector", "s_ac", numb_done[h])
                    c.wait("vector", "s_pe", pe_y)
                    ydst = yall[(h % 2) * 64:(h % 2) * 64 + 64,
                                (h // 2) * BLK + j * 128:
                                (h // 2) * BLK + j * 128 + 128]
                    y_done[h] = c.op("vector", lambda e, h=h, ydst=ydst:
                                     e.tensor_tensor(
                                         ydst,
                                         numb[:, (h % 2) * 128:(h % 2) * 128 + 128],
                                         P4[0:64, 0:128], ALU.mult), "s_dv")
                    last_ymult = y_done[h]
            else:
                if j == 0:
                    c.wait("tensor", "s_dv", state_done)
                else:
                    c.wait("tensor", "s_dv", state_done)
                for h in range(H):
                    sth, sto = ST[h // 7], (h % 7) * 65
                    vsl = vaug[:, j * 1040 + h * 65:j * 1040 + (h + 1) * 65]
                    c.op("tensor", lambda e, sth=sth, sto=sto, h=h, j=j,
                         vsl=vsl: e.matmul(
                             sth[:, sto:sto + 65],
                             kptm[:, h * 128:(h + 1) * 128],
                             vsl, start=True, stop=True), "s_pe")
            c.wait("vector", "s_pe")
            na = 0
            for si in range(3):
                na = c.op("vector", lambda e, si=si: e.tensor_tensor(
                    sf[:, si * 455:si * 455 + STW[si]],
                    sf[:, si * 455:si * 455 + STW[si]],
                    ST[si][:, 0:STW[si]], ALU.add), "s_dv")
            c.wait("vector", "s_dv", na)
            state_done = c.op("vector", lambda e: e.tensor_scalar(
                sb[:], sf[:], 0.0, None, ALU.add), "s_dv")

        if blk == NBLK - 1:
            c.wait("vector", "s_dv", state_done)
            na = c.op("vector", lambda e: e.tensor_scalar(
                sf[:], sf[:], flag[:, 0:1], None, ALU.mult), "s_dv")
            c.wait("vector", "s_dv", na)
            state_done = c.op("vector", lambda e: e.tensor_scalar(
                sb[:], sf[:], 0.0, None, ALU.add), "s_dv")

        if DEV_STAGE < 5:
            continue
        if own:
            c.wait("tensor", "s_dv")
            c.wait("scalar", "s_yst0")
            c.wait("scalar", "s_yst1")
            yst = {}
            for oc in range(DC):
                pb = PP[oc % 2]
                if oc >= 2:
                    c.wait("tensor", "s_ac")
                for ec in range(DC):
                    c.op("tensor", lambda e, pb=pb, oc=oc, ec=ec: e.matmul(
                        pb[:, 0:BLK],
                        wo[:, ec * D + oc * 128:ec * D + oc * 128 + 128],
                        yall[:, ec * BLK:(ec + 1) * BLK],
                        start=(ec == 0), stop=(ec == DC - 1)),
                        "s_pe" if ec == DC - 1 else None)
                c.wait("scalar", "s_pe")
                if oc >= 2:
                    c.wait("scalar", "s_yst%d" % (oc % 2), yst[oc - 2])
                nev = c.op("scalar", lambda e, pb=pb, oc=oc: e.copy(
                    yo[:, (oc % 2) * BLK:(oc % 2) * BLK + BLK], pb[:, 0:BLK]),
                    "s_ac")
                c.wait("scalar", "s_ac", nev)
                yst[oc] = c.dma("scalar",
                                y0[oc * 128:(oc + 1) * 128, t0:t0 + BLK],
                                yo[:, (oc % 2) * BLK:(oc % 2) * BLK + BLK],
                                "s_yst%d" % (oc % 2))




TT = 2048
NTT = TOWN // TT


def emit_l0f(c):
    P0, P1, P2, P3, P4 = (c.t["pp0"], c.t["pp1"], c.t["pp2"], c.t["pp3"],
                          c.t["pp4"])
    ST0, ST1, ST2 = c.t["st0"], c.t["st1"], c.t["st2"]
    STL = [ST0, ST2]
    PP, PN = [P0, P1], [P2, P3]
    fx = c.t["f_x2"]
    acc = c.t["f_acc"]
    w1s, w2s, b1r = c.t["f_w1"], c.t["f_w2"], c.t["f_b1"]
    hsb, hms, ybf = c.t["f_h"], c.t["f_hm"], c.t["f_y"]
    mkf = c.t["f_mk"]
    lg, mx, mk32 = c.t["f_lg"], c.t["f_mx"], c.t["f_mk32"]
    wgh, wgl, wgf, bgb = c.t["f_wgh"], c.t["f_wgl"], c.t["f_wgf"], c.t["f_bgb"]
    b2b, b2f, sel8 = c.t["f_b2b"], c.t["f_b2f"], c.t["c_sel8"]
    identf = c.t["c_identf"]
    xn1, xn1lo = c.d["xn1"].ap(), c.d["xn1lo"].ap()
    w1d, w2d = c.d["f_w1d"].ap(), c.d["f_w2d"].ap()
    y1 = c.d["y1"].ap()

    c.dma("sync", wgf[:].rearrange("p (dc e) -> p dc e", e=8),
          c.d["f_wgd"].ap().rearrange("(dc p) e -> p dc e", p=128), "s_ld")
    c.dma("sync", c.t["f_bgf"][:], c.d["f_bgd"].ap(), "s_ld")
    c.dma("sync", b2f[:], c.d["f_b2d"].ap(), "s_ld")
    c.dma("sync", b1r[:], c.d["f_b1d"].ap(), "s_ld")
    c.dma("gpsimd", sel8[:], c.d["c_sel8d"].ap(), "s_wt")
    c.dma("sync", identf[:], c.d["c_identfd"].ap(), "s_ld")
    c.wait("vector", "s_ld")
    c.wait("scalar", "s_ld")
    c.wait("vector", "s_wt")
    n = c.op("vector", lambda e: e.tensor_scalar(wgh[:], wgf[:], 0.0, None,
                                                 ALU.add), "s_dv")
    c.wait("vector", "s_dv", n)
    c.op("vector", lambda e: e.tensor_tensor(wgl[:], wgf[:], wgh[:],
                                             ALU.subtract), "s_dv")
    n = c.op("vector", lambda e: e.tensor_scalar(b2b[:], b2f[:], 0.0, None,
                                                 ALU.add), "s_dv")
    c.op("vector", lambda e: e.memset(c.t["c_ones1f"][:], 1.0), "s_dv")
    c.wait("tensor", "s_dv")
    c.wait("tensor", "s_ld")
    n = c.op("tensor", lambda e: e.matmul(
        ST1[0:128, 0:8], c.t["c_ones1f"][0:1, :], c.t["f_bgf"][0:1, :],
        start=True, stop=True), "s_pe")
    c.wait("vector", "s_pe", n)
    nbg = c.op("vector", lambda e: e.tensor_scalar(
        bgb[:], ST1[0:128, 0:8], 0.0, None, ALU.add), "s_dv")

    fxh = fx[:, 0:DC * 512]
    fxl = fx[:, DC * 512:2 * DC * 512]
    c.wait("tensor", "s_dv", nbg)
    tr_evac = 0
    pe_hist = {}
    rd_done = {}

    def gate_logits(ch):
        half = ch % 2
        ch4 = ch % 4
        c.wait("tensor", "s_dv", rd_done.get(ch - 2, 0))
        mms = []
        for src, wgx in ((fxh, wgh), (fxh, wgl), (fxl, wgh)):
            for dc in range(DC):
                mms.append((src, wgx, dc))
        for i, (src, wgx, dc) in enumerate(mms):
            pe_hist[ch] = c.op(
                "tensor", lambda e, src=src, wgx=wgx, dc=dc, ch4=ch4,
                half=half, i=i, last=(i == len(mms) - 1): e.matmul(
                    STL[half][0:128, 0:8],
                    src[:, dc * 512 + ch4 * 128:dc * 512 + ch4 * 128 + 128],
                    wgx[:, dc * 8:(dc + 1) * 8],
                    start=(i == 0), stop=last),
                "s_pe" if i == len(mms) - 1 else None) or pe_hist.get(ch)

    def gate_chain(ch):
        nonlocal tr_evac
        half = ch % 2
        c.wait("vector", "s_pe", pe_hist[ch])
        n = c.op("vector", lambda e, half=half: e.tensor_tensor(
            lg[:, half * 8:half * 8 + 8], STL[half][0:128, 0:8],
            bgb[:], ALU.add), "s_dv")
        c.wait("vector", "s_dv", n)
        n = c.op("vector", lambda e, half=half: e.reduce_max(
            mx[:, half:half + 1], lg[:, half * 8:half * 8 + 8],
            axis=mybir.AxisListType.X), "s_dv")
        c.wait("vector", "s_dv", n)
        n = c.op("vector", lambda e, half=half: e.tensor_scalar(
            mk32[:, half * 8:half * 8 + 8], lg[:, half * 8:half * 8 + 8],
            mx[:, half:half + 1], None, ALU.is_ge), "s_dv")
        rd_done[ch] = n
        c.wait("tensor", "s_dv", n)
        c.wait("tensor", "s_ac", tr_evac)
        n = c.op("tensor", lambda e, half=half: e.transpose(
            ST1[0:8, 128:256], mk32[:, half * 8:half * 8 + 8], identf[:]),
            "s_pe")
        c.wait("scalar", "s_pe", n)
        tr_evac = c.op("scalar", lambda e, ch=ch: e.copy(
            mkf[0:8, ch * 128:(ch + 1) * 128], ST1[0:8, 128:256]), "s_ac")

    prev_ch = None
    for mt in range(TOWN // 512):
        m0 = mt * 512
        c.wait("sync", "s_pe")
        for dc in range(DC):
            c.dma("sync", fxh[:, dc * 512:(dc + 1) * 512],
                  xn1[m0:m0 + 512, dc * 128:(dc + 1) * 128], "s_tr",
                  transpose=True)
            c.dma("sync", fxl[:, dc * 512:(dc + 1) * 512],
                  xn1lo[m0:m0 + 512, dc * 128:(dc + 1) * 128], "s_tr",
                  transpose=True)
        c.wait("tensor", "s_tr")
        for ch4 in range(4):
            ch = mt * 4 + ch4
            gate_logits(ch)
            if prev_ch is not None:
                gate_chain(prev_ch)
            prev_ch = ch
    gate_chain(prev_ch)

    last_hm = 0
    conv_dv = 0
    last_store = {0: 0, 1: 0}
    wfree = {}
    pe_fx_free = c.cur("s_pe")
    for tt in range(TOWN // TT):
        t0 = tt * TT
        c.wait("sync", "s_pe", pe_fx_free)
        c.wait("sync", "s_ac", tr_evac)
        for dc in range(DC):
            c.dma("sync", fx[:, dc * TT:(dc + 1) * TT],
                  xn1[t0:t0 + TT, dc * 128:(dc + 1) * 128], "s_tr",
                  transpose=True)
        c.wait("tensor", "s_tr")
        if tt == 0:
            c.wait("tensor", "s_ac", tr_evac)
        for eix in range(8):
            for pu in range(2):
                u = 2 * eix + pu
                c.wait("gpsimd", "s_pe", wfree.get("gu%d" % pu, 0))
                c.dma("gpsimd",
                      w1s.ap().rearrange("p (b dc i) -> p b dc i", b=2, i=512)
                      [:, pu], w1d[eix].rearrange("(dc p) i -> p dc i", p=128)
                      [:, :, pu * 512:(pu + 1) * 512], "s_w%d" % pu)
                c.wait("gpsimd", "s_pe", wfree.get("d%d" % pu, 0))
                c.dma("gpsimd",
                      w2s.ap().rearrange("p (b ic o) -> p b ic o", b=2, o=1024)
                      [:, pu],
                      w2d[eix, pu * 512:(pu + 1) * 512, :]
                      .rearrange("(ic p) o -> p ic o", p=128), "s_w%d" % pu)
            c.wait("tensor", "s_w0")
            c.wait("tensor", "s_w1")
            last_mm = 0
            for sub in range(TT // 512):
                for pu in range(2):
                    u = 2 * eix + pu
                    hev = {}
                    c.wait("tensor", "s_ac")
                    c.wait("tensor", "s_dv")
                    if pu == 1:
                        c.wait("scalar", "s_dv", last_hm)
                    for ic in range(4):
                        pb = PP[ic % 2]
                        if ic >= 2:
                            c.wait("tensor", "s_ac", hev[ic - 2])
                        for dc in range(DC):
                            c.op("tensor", lambda e, pb=pb, ic=ic, dc=dc,
                                 pu=pu, sub=sub: e.matmul(
                                     pb[:, 0:512],
                                     w1s[:, (pu * 8 + dc) * 512 + ic * 128:
                                         (pu * 8 + dc) * 512 + ic * 128 + 128],
                                     fx[:, dc * TT + sub * 512:
                                        dc * TT + sub * 512 + 512],
                                     start=(dc == 0), stop=(dc == DC - 1)),
                                 "s_pe" if dc == DC - 1 else None)
                        c.wait("scalar", "s_pe")
                        hev[ic] = c.op("scalar", lambda e, pb=pb, ic=ic, u=u:
                                       e.activation(
                                           hsb[:, ic * 512:(ic + 1) * 512],
                                           pb[:, 0:512], AF.Gelu,
                                           bias=b1r[:, u * 4 + ic:
                                                    u * 4 + ic + 1]), "s_ac")
                    if pu == 0:
                        c.wait("tensor", "s_dv", last_hm)
                        pe_mb = c.op("tensor", lambda e, eix=eix, sub=sub,
                                     t0=t0: e.matmul(
                                         P4[:, 0:512],
                                         sel8[:, eix * 128:(eix + 1) * 128],
                                         mkf[0:8, t0 + sub * 512:
                                             t0 + sub * 512 + 512],
                                         start=True, stop=True), "s_pe")
                        c.wait("vector", "s_pe", pe_mb)
                    c.wait("vector", "s_ac")
                    for ic in range(4):
                        last_hm = c.op("vector", lambda e, ic=ic, pu=pu:
                                       e.tensor_tensor(
                                           hms[:, pu * 2048 + ic * 512:
                                               pu * 2048 + ic * 512 + 512],
                                           hsb[:, ic * 512:(ic + 1) * 512],
                                           P4[:, 0:512], ALU.mult), "s_dv")
                h_last = c.cur("s_pe")
                c.wait("tensor", "s_dv", last_hm)
                aev = {}
                for oc in range(DC):
                    pd = PN[oc % 2]
                    if oc >= 2:
                        c.wait("tensor", "s_dv", aev[oc - 2])
                    for k8 in range(8):
                        pu, ic = k8 // 4, k8 % 4
                        last_mm = c.op(
                            "tensor", lambda e, pd=pd, oc=oc, ic=ic, pu=pu,
                            k8=k8: e.matmul(
                                pd[:, 0:512],
                                w2s[:, (pu * 4 + ic) * 1024 + oc * 128:
                                    (pu * 4 + ic) * 1024 + oc * 128 + 128],
                                hms[:, pu * 2048 + ic * 512:
                                    pu * 2048 + ic * 512 + 512],
                                start=(k8 == 0), stop=(k8 == 7)),
                            "s_pe" if k8 == 7 else None)
                    c.wait("vector", "s_pe")
                    if eix == 0:
                        c.wait("vector", "s_dv", conv_dv)
                    dst = acc[:, oc * TT + sub * 512:oc * TT + sub * 512 + 512]
                    aev[oc] = c.op("vector", lambda e, pd=pd, dst=dst,
                                   first=(eix == 0): (
                                       e.tensor_scalar(dst, pd[:, 0:512], 0.0,
                                                       None, ALU.add)
                                       if first else
                                       e.tensor_tensor(dst, dst, pd[:, 0:512],
                                                       ALU.add)), "s_dv")
            wfree["gu0"] = h_last
            wfree["gu1"] = h_last
            wfree["d0"] = last_mm
            wfree["d1"] = last_mm

        for sub in range(TT // 512):
            for oc in range(DC):
                pd = PN[oc % 2]
                c.wait("tensor", "s_dv")
                n = c.op("tensor", lambda e, pd=pd, oc=oc, sub=sub, t0=t0:
                         e.matmul(
                    pd[:, 0:512], b2b[0:8, oc * 128:(oc + 1) * 128],
                    mkf[0:8, t0 + sub * 512:t0 + sub * 512 + 512],
                    start=True, stop=True), "s_pe")
                c.wait("vector", "s_pe", n)
                dst = acc[:, oc * TT + sub * 512:oc * TT + sub * 512 + 512]
                c.op("vector", lambda e, pd=pd, dst=dst: e.tensor_tensor(
                    dst, dst, pd[:, 0:512], ALU.add), "s_dv")
        pe_fx_free = c.cur("s_pe")
        c.wait("vector", "s_dv")
        for oc in range(DC):
            p = oc % 2
            c.wait("vector", "s_yst%d" % p, last_store[p])
            conv_dv = c.op("vector", lambda e, oc=oc, p=p: e.tensor_scalar(
                ybf[:, p * TT:(p + 1) * TT], acc[:, oc * TT:(oc + 1) * TT],
                0.0, None, ALU.add), "s_dv")
            c.wait("scalar", "s_dv", conv_dv)
            last_store[p] = c.dma(
                "scalar", y1[oc * 128:(oc + 1) * 128, t0:t0 + TT],
                ybf[:, p * TT:(p + 1) * TT], "s_yst%d" % p)


def emit_l1a(c):
    P0, P1, P2, P3, P4 = (c.t["pp0"], c.t["pp1"], c.t["pp2"], c.t["pp3"],
                          c.t["pp4"])
    ST0 = c.t["st0"]
    PP = [P0, P1]
    xnf = c.t["g_xnf"]
    kraw = c.t["g_kraw"]
    ksq = c.t["g_ksq"]
    knb = c.t["g_kn"]
    vsb = c.t["g_v"]
    rows = c.t["g_rows"]
    arow, brow = c.t["g_ab"][0:1, 0:512], c.t["g_ab"][0:1, 512:1024]
    bqk = c.t["g_bqk"]
    bvr = c.t["g_bvr"]
    bols = c.t["g_bols"]
    wq, wk, wv, wo = c.t["gw_q"], c.t["gw_k"], c.t["gw_v"], c.t["gw_o"]
    ones1, ones128 = c.t["c_ones1"], c.t["c_ones128"]
    xn2 = c.d["xn2i"].ap()
    xn2f = c.d["xn2f"].ap()
    kn_d, v_d, qn_d = c.d["kn_d"].ap(), c.d["v_d"].ap(), c.d["qn_d"].ap()
    ao_tm, y2 = c.d["ao_tm"].ap(), c.d["y2"].ap()

    c.op("vector", lambda e: e.memset(ones128[:], 1.0), "s_dv")
    c.wait("vector", "s_ld")
    n = c.op("vector", lambda e: e.tensor_tensor(
        bols[:, 8:16], bols[:, 8:16], bols[:, 0:8], ALU.mult), "s_dv")

    pe_xnf_free = 0
    st_par = {0: 0, 1: 0}
    for til in range(24):
        qpass = til >= 16
        src = xn2 if qpass else xn2f
        t0 = (til - 16) * BLK if qpass else til * BLK
        g0 = til * BLK
        c.wait("sync", "s_pe", pe_xnf_free)
        for dc in range(DC):
            c.dma("sync", xnf[:, dc * BLK:(dc + 1) * BLK],
                  src[t0:t0 + BLK, dc * 128:(dc + 1) * 128], "s_tr",
                  transpose=True)
        c.wait("tensor", "s_tr")
        if til == 0:
            c.wait("tensor", "s_wt")
            c.wait("tensor", "s_dv", n)

        def qkproj(dst, w, bcol):
            ev = {}
            c.wait("tensor", "s_ac")
            c.wait("scalar", "s_dv")
            for ec in range(DC):
                pb = PP[ec % 2]
                if ec >= 2:
                    c.wait("tensor", "s_ac", ev[ec - 2])
                for dc in range(DC):
                    c.op("tensor", lambda e, pb=pb, w=w, dc=dc, ec=ec: e.matmul(
                        pb[:, 0:BLK],
                        w[:, dc * D + ec * 128:dc * D + ec * 128 + 128],
                        xnf[:, dc * BLK:(dc + 1) * BLK],
                        start=(dc == 0), stop=(dc == DC - 1)),
                        "s_pe" if dc == DC - 1 else None)
                c.wait("scalar", "s_pe")
                ev[ec] = c.op("scalar", lambda e, pb=pb, dst=dst, ec=ec,
                              bcol=bcol: e.activation(
                                  dst[:, ec * BLK:(ec + 1) * BLK],
                                  pb[:, 0:BLK], AF.Identity,
                                  bias=bqk[:, bcol + ec:bcol + ec + 1]),
                              "s_ac")
            return ev

        def lnrows(src_sb, dst_sb, dst_d, grow):
            c.wait("vector", "s_st0")
            c.wait("vector", "s_st1")
            c.wait("scalar", "s_ac")
            nsq = c.op("scalar", lambda e: e.activation(ksq[:], src_sb[:],
                                                        AF.Square), "s_ac")
            c.wait("tensor", "s_ac", nsq)
            c.wait("tensor", "s_dv")
            for ec in range(DC):
                c.op("tensor", lambda e, ec=ec: e.matmul(
                    P2[0:1, 0:BLK], ones128[:, 0:1],
                    src_sb[:, ec * BLK:(ec + 1) * BLK],
                    start=(ec == 0), stop=(ec == DC - 1)),
                    "s_pe" if ec == DC - 1 else None)
            for ec in range(DC):
                c.op("tensor", lambda e, ec=ec: e.matmul(
                    P3[0:1, 0:BLK], ones128[:, 0:1],
                    ksq[:, ec * BLK:(ec + 1) * BLK],
                    start=(ec == 0), stop=(ec == DC - 1)),
                    "s_pe" if ec == DC - 1 else None)
            c.wait("vector", "s_pe")
            nn = c.op("vector", lambda e: e.tensor_scalar(
                rows[0:1, 0:512], P2[0:1, 0:BLK], 1.0 / D, None, ALU.mult),
                "s_dv")
            nn2 = c.op("vector", lambda e: e.tensor_scalar(
                rows[0:1, 512:1024], P3[0:1, 0:BLK], 1.0 / D, None, ALU.mult),
                "s_dv")
            c.wait("vector", "s_dv", nn)
            nn3 = c.op("vector", lambda e: e.tensor_tensor(
                c.t["g_r2"][0:1, 0:512], rows[0:1, 0:512], rows[0:1, 0:512],
                ALU.mult), "s_dv")
            c.wait("vector", "s_dv", max(nn2, nn3))
            nn = c.op("vector", lambda e: e.tensor_tensor(
                c.t["g_r2"][0:1, 512:1024], rows[0:1, 512:1024],
                c.t["g_r2"][0:1, 0:512], ALU.subtract), "s_dv")
            c.wait("vector", "s_dv", nn)
            nn = c.op("vector", lambda e: e.tensor_scalar(
                c.t["g_r2"][0:1, 512:1024], c.t["g_r2"][0:1, 512:1024],
                LN_EPS, None, ALU.add), "s_dv")
            c.wait("vector", "s_dv", nn)
            nn = c.op("vector", lambda e: e.reciprocal(
                c.t["g_r3"][0:1, 0:512], c.t["g_r2"][0:1, 512:1024]), "s_dv")
            c.wait("scalar", "s_dv", nn)
            nn = c.op("scalar", lambda e: e.activation(
                c.t["g_r3"][0:1, 512:1024], c.t["g_r3"][0:1, 0:512], AF.Sqrt),
                "s_ac")
            c.wait("vector", "s_ac", nn)
            nn = c.op("vector", lambda e: e.tensor_scalar(
                arow[:], c.t["g_r3"][0:1, 512:1024], 0.0, None, ALU.add),
                "s_dv")
            c.wait("vector", "s_dv", nn)
            nn = c.op("vector", lambda e: e.tensor_tensor(
                brow[:], rows[0:1, 0:512], arow[:], ALU.mult), "s_dv")
            c.wait("vector", "s_dv", nn)
            nb = c.op("vector", lambda e: e.tensor_scalar(
                brow[:], brow[:], -1.0, None, ALU.mult), "s_dv")
            c.wait("tensor", "s_dv", nb)
            c.op("tensor", lambda e: e.matmul(
                P2[:, 0:BLK], ones1[0:1, :], arow[:], start=True, stop=True))
            nbc = c.op("tensor", lambda e: e.matmul(
                P3[:, 0:BLK], ones1[0:1, :], brow[:], start=True, stop=True),
                "s_pe")
            c.wait("vector", "s_pe", nbc)
            last = 0
            for ec in range(DC):
                n1 = c.op("vector", lambda e, ec=ec: e.tensor_tensor(
                    ksq[:, ec * BLK:(ec + 1) * BLK],
                    src_sb[:, ec * BLK:(ec + 1) * BLK], P2[:, 0:BLK],
                    ALU.mult), "s_dv")
                c.wait("vector", "s_dv", n1)
                last = c.op("vector", lambda e, ec=ec: e.tensor_tensor(
                    dst_sb[:, ec * BLK:(ec + 1) * BLK],
                    ksq[:, ec * BLK:(ec + 1) * BLK], P3[:, 0:BLK],
                    ALU.add), "s_dv")
            c.wait("scalar", "s_dv", last)
            par = til % 2
            c.wait("scalar", "s_st%d" % par, st_par[par])
            st_par[par] = c.dma(
                "scalar",
                dst_d.rearrange("(dc p) t -> p dc t", p=128)[:, :, grow:grow + BLK],
                dst_sb[:].rearrange("p (dc t) -> p dc t", t=BLK),
                "s_st%d" % par)
            return last

        if qpass:
            qkproj(kraw, wq, 0)
            lnrows(kraw, knb, qn_d, t0)
            pe_xnf_free = c.cur("s_pe")
            continue
        qkproj(kraw, wk, 8)
        lnrows(kraw, knb, kn_d, g0)

        vev = {}
        c.wait("tensor", "s_ac")
        for g in range(8):
            half, j = g // 4, g % 4
            pb = PP[g % 2]
            if g >= 2:
                c.wait("tensor", "s_ac", vev[g - 2])
            c.op("tensor", lambda e, pb=pb, half=half: e.matmul(
                pb[:, 0:512], ones1[0:1, 0:128],
                bvr[0:1, half * 512:(half + 1) * 512],
                start=True, stop=False))
            for dc in range(DC):
                c.op("tensor", lambda e, pb=pb, dc=dc, half=half, j=j: e.matmul(
                    pb[:, 0:512],
                    xnf[:, dc * BLK + j * 128:dc * BLK + j * 128 + 128],
                    wv[:, dc * D + half * 512:dc * D + half * 512 + 512],
                    start=False, stop=(dc == DC - 1)),
                    "s_pe" if dc == DC - 1 else None)
            c.wait("scalar", "s_pe")
            vev[g] = c.op("scalar", lambda e, pb=pb, j=j, half=half: e.copy(
                vsb[:, j * 1024 + half * 512:j * 1024 + half * 512 + 512],
                pb[:, 0:512]), "s_ac")
        pe_xnf_free = c.cur("s_pe")
        par = til % 2
        c.wait("scalar", "s_ac")
        c.wait("scalar", "s_st%d" % par, st_par[par])
        st_par[par] = c.dma(
            "scalar",
            v_d[g0:g0 + BLK, :].rearrange("(s p) e -> p s e", p=128),
            vsb[:].rearrange("p (s e) -> p s e", e=1024),
            "s_st%d" % par)
    c.wait("sync", "s_st0")
    c.wait("sync", "s_st1")

    qnt = c.t["g_qnt"]
    att = c.t["g_att"]
    outb = c.t["g_outb"]
    knl = c.t["g_knl"]
    vbl = c.t["g_vbl"]
    OUTB = [P0, P1, P2, P3]
    ao_par = {0: 0, 1: 0}
    kv_free = {0: 0, 1: 0}
    out_ev = 0
    qn_free = 0
    for qt in range(16):
        s0 = qt * 256
        c.wait("sync", "s_pe", qn_free)
        c.dma("sync", qnt[:].rearrange("p (dc s) -> p dc s", s=256),
              qn_d.rearrange("(dc p) t -> p dc t", p=128)[:, :, s0:s0 + 256],
              "s_q%d" % (qt % 2))
        c.wait("tensor", "s_q%d" % (qt % 2))
        c.wait("tensor", "s_ac", out_ev)
        def emit_scores(gsub, spb_free):
            kb, sub = gsub // 4, gsub % 4
            kpar = kb % 2
            if sub == 0:
                c.wait("tensor", "s_kv%d" % kpar)
            c.wait("tensor", "s_ac", spb_free)
            spb = P4 if gsub % 2 == 0 else ST0
            for ec in range(DC):
                c.op("tensor", lambda e, spb=spb, ec=ec, sub=sub,
                     kpar=kpar: e.matmul(
                         spb[0:128, 0:256],
                         knl[:, kpar * 4096 + ec * 512 + sub * 128:
                             kpar * 4096 + ec * 512 + sub * 128 + 128],
                         qnt[:, ec * 256:(ec + 1) * 256],
                         start=(ec == 0), stop=(ec == DC - 1)),
                     "s_pe" if ec == DC - 1 else None)
            nsc = c.cur("s_pe")
            nse = c.op("scalar", lambda e, spb=spb, sub=gsub, nsc=nsc: (
                e.wait_ge(c.sem("s_pe")[0], nsc),
                e.activation(att[:, (sub % 4) * 256:(sub % 4) * 256 + 256],
                             spb[0:128, 0:256], AF.Sigmoid, bias=-LOG_S,
                             scale=1.0 / 32.0))[-1], "s_ac")
            return nse

        def emit_out(gsub, nse):
            kb, sub = gsub // 4, gsub % 4
            kpar = kb % 2
            c.wait("tensor", "s_ac", nse)
            for g in range(4):
                sc, eh = g // 2, g % 2
                c.op("tensor", lambda e, g=g, sc=sc, eh=eh, sub=sub,
                     gsub=gsub, kpar=kpar, first=(gsub == 0),
                     lastm=(gsub == 63): e.matmul(
                         OUTB[g][0:128, 0:512],
                         att[:, (gsub % 4) * 256 + sc * 128:
                             (gsub % 4) * 256 + sc * 128 + 128],
                         vbl[:, kpar * 4096 + sub * 1024 + eh * 512:
                             kpar * 4096 + sub * 1024 + eh * 512 + 512],
                         start=first, stop=lastm),
                     "s_pe" if g == 3 else None)

        def emit_kv_load(kb):
            kpar = kb % 2
            c.wait("sync", "s_pe", kv_free[kpar])
            c.dma("sync",
                  knl[:, kpar * 4096:(kpar + 1) * 4096]
                  .rearrange("p (dc t) -> p dc t", t=512),
                  kn_d.rearrange("(dc p) t -> p dc t", p=128)
                  [:, :, kb * 512:(kb + 1) * 512], "s_kv%d" % kpar)
            c.dma("sync",
                  vbl[:, kpar * 4096:(kpar + 1) * 4096]
                  .rearrange("p (s e) -> p s e", e=1024),
                  v_d[kb * 512:(kb + 1) * 512, :]
                  .rearrange("(s p) e -> p s e", p=128), "s_kv%d" % kpar)

        if qt == 0:
            emit_kv_load(0)
            emit_kv_load(1)
        pend = []
        nse_hist = {}
        for gsub in range(64):
            nse = emit_scores(gsub, nse_hist.get(gsub - 2, 0))
            nse_hist[gsub] = nse
            pend.append((gsub, nse))
            if len(pend) > 2:
                pg, pn = pend.pop(0)
                emit_out(pg, pn)
                pkb, psub = pg // 4, pg % 4
                if psub == 3:
                    kv_free[pkb % 2] = c.cur("s_pe")
                    if pkb + 2 <= 15:
                        emit_kv_load(pkb + 2)
                    elif qt < 15:
                        emit_kv_load(pkb + 2 - 16)
        for pg, pn in pend:
            emit_out(pg, pn)
        kv_free[0] = c.cur("s_pe")
        kv_free[1] = c.cur("s_pe")
        if qt < 15:
            emit_kv_load(0 if False else 1)
        qn_free = c.cur("s_pe")
        c.wait("scalar", "s_pe")
        par = qt % 2
        c.wait("scalar", "s_ao%d" % par, ao_par[par])
        for g in range(4):
            sc, eh = g // 2, g % 2
            out_ev = c.op("scalar", lambda e, g=g, sc=sc, eh=eh, par=par:
                          e.copy(
                              outb[:, par * 2048 + sc * 1024 + eh * 512:
                                   par * 2048 + sc * 1024 + eh * 512 + 512],
                              OUTB[g][0:128, 0:512]), "s_ac")
        c.wait("scalar", "s_ac", out_ev)
        ao_par[par] = c.dma(
            "scalar",
            ao_tm[s0:s0 + 256, :].rearrange("(s p) e -> p s e", p=128),
            outb[:, par * 2048:(par + 1) * 2048]
            .rearrange("p (s e) -> p s e", e=1024), "s_ao%d" % par)
    c.wait("sync", "s_ao0")
    c.wait("sync", "s_ao1")

    aof = c.t["g_xnf"]
    ybf = c.t["g_kraw"]
    yst = {0: 0, 1: 0}
    pe_aof_free = c.cur("s_pe")
    for til in range(8):
        t0 = til * BLK
        c.wait("sync", "s_pe", pe_aof_free)
        for ec in range(DC):
            c.dma("sync", aof[:, ec * BLK:(ec + 1) * BLK],
                  ao_tm[t0:t0 + BLK, ec * 128:(ec + 1) * 128], "s_tr",
                  transpose=True)
        c.wait("tensor", "s_tr")
        c.wait("tensor", "s_ac")
        ev = {}
        for oc in range(DC):
            pb = PP[oc % 2]
            if oc >= 2:
                c.wait("tensor", "s_ac", ev[oc - 2])
            for ec in range(DC):
                c.op("tensor", lambda e, pb=pb, oc=oc, ec=ec: e.matmul(
                    pb[:, 0:BLK],
                    wo[:, ec * D + oc * 128:ec * D + oc * 128 + 128],
                    aof[:, ec * BLK:(ec + 1) * BLK],
                    start=(ec == 0), stop=(ec == DC - 1)),
                    "s_pe" if ec == DC - 1 else None)
            c.wait("scalar", "s_pe")
            par = oc % 2
            c.wait("scalar", "s_y%d" % par, yst[par])
            ev[oc] = c.op("scalar", lambda e, pb=pb, oc=oc, par=par:
                          e.activation(
                              ybf[:, par * BLK:(par + 1) * BLK], pb[:, 0:BLK],
                              AF.Identity,
                              bias=bols[:, 8 + oc:8 + oc + 1],
                              scale=bols[:, oc:oc + 1]), "s_ac")
            c.wait("scalar", "s_ac", ev[oc])
            yst[par] = c.dma("scalar",
                             y2[oc * 128:(oc + 1) * 128, t0:t0 + BLK],
                             ybf[:, par * BLK:(par + 1) * BLK], "s_y%d" % par)
        pe_aof_free = c.cur("s_pe")
    c.wait("sync", "s_y0")
    c.wait("sync", "s_y1")


def alloc_common(c):
    nc = c.nc
    dt = nc.dram_tensor
    c.d["c_pool"] = dt("c_pool", [128, len(CONST_VALS)], F32,
                       kind="ExternalInput")
    sbuf(c, "c_cpool", [128, len(CONST_VALS)], F32)
    for i, v in enumerate(CONST_VALS):
        nc.const_aps.register(F32, v, c.t["c_cpool"][:, i:i + 1])
    sbuf(c, "c_ones1", [1, 128], BF16)
    sbuf(c, "c_ones1f", [1, 128], F32)
    sbuf(c, "r_x", [128, 2 * D], F32)
    sbuf(c, "r_xs", [128, 2 * D], F32)
    sbuf(c, "r_stats", [128, 8], F32)
    sbuf(c, "r_sq", [128, D], BF16)
    sbuf(c, "r_xb", [128, D], BF16)
    sbuf(c, "r_f32", [128, D], F32)
    sbuf(c, "r_xn", [128, 2 * D], BF16)
    psum_t(c, "pp0", [128, 512])
    psum_t(c, "pp1", [128, 512])
    psum_t(c, "pp2", [128, 512])
    psum_t(c, "pp3", [128, 512])
    psum_t(c, "pp4", [128, 512])
    psum_t(c, "st0", [128, 455])
    psum_t(c, "st1", [128, 455])
    psum_t(c, "st2", [128, 130])
    c.dma("sync", c.t["c_cpool"][:], c.d["c_pool"].ap(), "s_ld")
    c.op("vector", lambda e: e.memset(c.t["c_ones1"][:], 1.0))


def finish(c, store_sems):
    for sname in store_sems:
        c.wait("sync", sname)
    c.finalize(c.block)


def build_program_a():
    nc = bass.Bass("TRN2", target_bir_lowering=False)
    stack = ExitStack()
    c = Ctx(nc, stack)
    with stack:
        dt = nc.dram_tensor
        c.d["x_pre"] = dt("x_pre", [TOWN, D], F32, kind="ExternalInput")
        c.d["x_own"] = dt("x_own", [TOWN, D], F32, kind="ExternalInput")
        c.d["w_q"] = dt("w_q", [D, D], F32, kind="ExternalInput")
        c.d["w_k"] = dt("w_k", [D, D], F32, kind="ExternalInput")
        c.d["w_v"] = dt("w_v", [D, D], F32, kind="ExternalInput")
        c.d["w_o"] = dt("w_o", [D, D], F32, kind="ExternalInput")
        c.d["w_f"] = dt("w_f", [HD, NF], F32, kind="ExternalInput")
        c.d["c_trild"] = dt("c_trild", [128, 128], F32, kind="ExternalInput")
        c.d["c_sheadd"] = dt("c_sheadd", [16, 16 * 128], F32,
                             kind="ExternalInput")
        c.d["c_ucold"] = dt("c_ucold", [64, 16 * 16], F32,
                            kind="ExternalInput")
        c.d["c_flagd"] = dt("c_flagd", [128, 1], F32, kind="ExternalInput")
        c.d["xn0"] = dt("xn0", [2 * TOWN, D], BF16)
        c.d["y0"] = dt("y0", [D, TOWN], BF16)
        c.d["x1"] = dt("x1", [TOWN, D], F32, kind="ExternalOutput")
        c.d["xn1"] = dt("xn1", [TOWN, D], BF16, kind="ExternalOutput")
        c.d["xn1lo"] = dt("xn1lo", [TOWN, D], BF16, kind="ExternalOutput")

        for n in ("wq", "wk", "wv", "wo"):
            sbuf(c, n, [128, DC * D], BF16)
        sbuf(c, "wfeat", [64, 128], BF16)
        sbuf(c, "c_tril", [128, 128], BF16)
        sbuf(c, "c_shead", [16, 16 * 128], BF16)
        sbuf(c, "c_ucol", [64, 16 * 16], BF16)
        sbuf(c, "c_flag", [128, 1], F32)
        sbuf(c, "a_xnf", [128, DC * BLK], BF16)
        sbuf(c, "a_khh", [64, H * BLK], BF16)
        sbuf(c, "a_qhh", [64, H * BLK], BF16)
        sbuf(c, "a_sqh", [64, 4 * BLK], BF16)
        sbuf(c, "a_vaug", [128, 4 * 1040], BF16)
        sbuf(c, "a_kpf", [128, H * BLK], BF16)
        sbuf(c, "a_qpf", [128, H * BLK], BF16)
        sbuf(c, "a_kptm", [128, H * 128], BF16)
        sbuf(c, "a_sqm", [16, 2 * BLK], BF16)
        sbuf(c, "a_am", [128, 2 * 128], BF16)
        sbuf(c, "a_numb", [64, 2 * 128], BF16)
        sbuf(c, "a_yall", [128, DC * BLK], BF16)
        sbuf(c, "a_yo", [128, 2 * BLK], BF16)
        sbuf(c, "a_sf32", [128, H * 65], F32)
        sbuf(c, "a_sbf", [128, H * 65], BF16)
        sbuf(c, "a_dn", [1, 512], F32)
        sbuf(c, "a_rec", [1, 256], BF16)
        c.t["r_ybuf"] = c.t["a_yall"][:, 0:2 * D]
        c.t["r_lobuf"] = c.t["a_yall"][:, 2 * D:4 * D]

        c.block = stack.enter_context(nc.Block())
        alloc_common(c)
        c.dma("gpsimd", c.t["c_tril"][:], c.d["c_trild"].ap(), "s_wt")
        c.dma("gpsimd", c.t["c_shead"][:], c.d["c_sheadd"].ap(), "s_wt")
        c.dma("gpsimd", c.t["c_ucol"][:], c.d["c_ucold"].ap(), "s_wt")
        c.dma("sync", c.t["c_flag"][:], c.d["c_flagd"].ap(), "s_ld")
        load_w_dmajor(c, c.t["wq"], c.d["w_q"].ap(), D)
        load_w_dmajor(c, c.t["wk"], c.d["w_k"].ap(), D)
        load_w_dmajor(c, c.t["wv"], c.d["w_v"].ap(), D)
        load_w_dmajor(c, c.t["wo"], c.d["w_o"].ap(), D)
        c.dma("gpsimd", c.t["wfeat"][:], c.d["w_f"].ap(), "s_wt")
        for eng in ("scalar", "vector"):
            c.wait(eng, "s_wt")
            c.wait(eng, "s_ld")
        c.wait("tensor", "s_ld")

        emit_l0a(c)
        c.wait("sync", "s_yst0")
        c.wait("sync", "s_yst1")
        c.wait("vector", "s_xn0")
        c.wait("vector", "s_xn1")
        c.wait("scalar", "s_xn0")
        c.wait("scalar", "s_xn1")
        c.wait("sync", "s_pe")
        emit_res(c, c.d["x_own"].ap(), c.d["y0"].ap(), c.d["x1"].ap(),
                 c.d["xn1"].ap(), c.d["xn1lo"].ap())
        finish(c, ["s_rst0", "s_rst1"])
    return nc


def build_program_b():
    nc = bass.Bass("TRN2", target_bir_lowering=False)
    stack = ExitStack()
    c = Ctx(nc, stack)
    with stack:
        dt = nc.dram_tensor
        c.d["x1"] = dt("x1", [TOWN, D], F32, kind="ExternalInput")
        c.d["xn1"] = dt("xn1", [TOWN, D], BF16, kind="ExternalInput")
        c.d["xn1lo"] = dt("xn1lo", [TOWN, D], BF16, kind="ExternalInput")
        c.d["f_wgd"] = dt("f_wgd", [D, 8], F32, kind="ExternalInput")
        c.d["f_bgd"] = dt("f_bgd", [1, 8], F32, kind="ExternalInput")
        c.d["f_w1d"] = dt("f_w1d", [8, D, D], F32, kind="ExternalInput")
        c.d["f_b1d"] = dt("f_b1d", [128, 64], F32, kind="ExternalInput")
        c.d["f_w2d"] = dt("f_w2d", [8, D, D], F32, kind="ExternalInput")
        c.d["f_b2d"] = dt("f_b2d", [8, D], F32, kind="ExternalInput")
        c.d["c_sel8d"] = dt("c_sel8d", [8, 8 * 128], F32,
                            kind="ExternalInput")
        c.d["c_identfd"] = dt("c_identfd", [128, 128], F32,
                              kind="ExternalInput")
        c.d["y1"] = dt("y1", [D, TOWN], BF16)
        c.d["x2"] = dt("x2", [TOWN, D], F32, kind="ExternalOutput")
        c.d["xn2"] = dt("xn2", [TOWN, D], BF16, kind="ExternalOutput")
        c.d["dbg_mk"] = dt("dbg_mk", [8, TOWN], BF16, kind="ExternalOutput")

        sbuf(c, "f_x2", [128, DC * TT], BF16)
        sbuf(c, "f_acc", [128, DC * TT], F32)
        sbuf(c, "f_w1", [128, 2 * DC * 512], BF16)
        sbuf(c, "f_w2", [128, 2 * 4 * D], BF16)
        sbuf(c, "f_b1", [128, 64], F32)
        sbuf(c, "f_h", [128, 4 * 512], BF16)
        sbuf(c, "f_hm", [128, 8 * 512], BF16)
        sbuf(c, "f_y", [128, 2 * TT], BF16)
        sbuf(c, "f_mk", [8, TOWN], BF16)
        sbuf(c, "f_lg", [128, 16], F32)
        sbuf(c, "f_mx", [128, 2], F32)
        sbuf(c, "f_mk32", [128, 16], F32)
        sbuf(c, "f_wgh", [128, 64], BF16)
        sbuf(c, "f_wgl", [128, 64], BF16)
        sbuf(c, "f_wgf", [128, 64], F32)
        sbuf(c, "f_bgf", [1, 8], F32)
        sbuf(c, "f_bgb", [128, 8], F32)
        sbuf(c, "f_b2b", [8, D], BF16)
        sbuf(c, "f_b2f", [8, D], F32)
        sbuf(c, "c_sel8", [8, 8 * 128], BF16)
        sbuf(c, "c_identf", [128, 128], F32)
        c.t["r_ybuf"] = c.t["f_y"][:, 0:2 * D]
        c.t["r_lobuf"] = c.t["f_y"][:, 2 * D:4 * D]

        c.block = stack.enter_context(nc.Block())
        alloc_common(c)
        emit_l0f(c)
        c.dma("sync", c.d["dbg_mk"].ap(), c.t["f_mk"][:], "s_rin0")
        c.wait("sync", "s_yst0")
        c.wait("sync", "s_yst1")
        c.wait("sync", "s_pe")
        emit_res(c, c.d["x1"].ap(), c.d["y1"].ap(), c.d["x2"].ap(),
                 c.d["xn2"].ap(), None)
        finish(c, ["s_rst0", "s_rst1"])
    return nc


def build_program_c():
    nc = bass.Bass("TRN2", target_bir_lowering=False)
    stack = ExitStack()
    c = Ctx(nc, stack)
    with stack:
        dt = nc.dram_tensor
        c.d["x2"] = dt("x2", [TOWN, D], F32, kind="ExternalInput")
        c.d["xn2i"] = dt("xn2i", [TOWN, D], BF16, kind="ExternalInput")
        c.d["xn2f"] = dt("xn2f", [2 * TOWN, D], BF16, kind="ExternalInput")
        c.d["gw_qd"] = dt("gw_qd", [D, D], F32, kind="ExternalInput")
        c.d["gw_kd"] = dt("gw_kd", [D, D], F32, kind="ExternalInput")
        c.d["gw_vd"] = dt("gw_vd", [D, D], F32, kind="ExternalInput")
        c.d["gw_od"] = dt("gw_od", [D, D], F32, kind="ExternalInput")
        c.d["g_bqkd"] = dt("g_bqkd", [128, 16], F32, kind="ExternalInput")
        c.d["g_bvrd"] = dt("g_bvrd", [1, D], F32, kind="ExternalInput")
        c.d["g_bolsd"] = dt("g_bolsd", [128, 16], F32, kind="ExternalInput")
        c.d["kn_d"] = dt("kn_d", [D, 2 * TOWN], BF16)
        c.d["v_d"] = dt("v_d", [2 * TOWN, D], BF16)
        c.d["qn_d"] = dt("qn_d", [D, TOWN], BF16)
        c.d["ao_tm"] = dt("ao_tm", [TOWN, D], BF16)
        c.d["y2"] = dt("y2", [D, TOWN], BF16)
        c.d["x3"] = dt("x3", [TOWN, D], F32, kind="ExternalOutput")
        c.d["xn3"] = dt("xn3", [TOWN, D], BF16, kind="ExternalOutput")

        for n in ("gw_q", "gw_k", "gw_v", "gw_o"):
            sbuf(c, n, [128, DC * D], BF16)
        sbuf(c, "g_xnf", [128, DC * BLK], BF16)
        sbuf(c, "g_kraw", [128, DC * BLK], BF16)
        sbuf(c, "g_ksq", [128, DC * BLK], BF16)
        sbuf(c, "g_kn", [128, DC * BLK], BF16)
        sbuf(c, "g_v", [128, 4 * D], BF16)
        sbuf(c, "g_rows", [1, 1024], F32)
        sbuf(c, "g_r2", [1, 1024], F32)
        sbuf(c, "g_r3", [1, 1024], F32)
        sbuf(c, "g_ab", [1, 1024], BF16)
        sbuf(c, "g_bqk", [128, 16], F32)
        sbuf(c, "g_bvr", [1, D], BF16)
        sbuf(c, "g_bols", [128, 16], F32)
        sbuf(c, "c_ones128", [128, 1], BF16)
        sbuf(c, "g_qnt", [128, DC * 256], BF16)
        sbuf(c, "g_att", [128, 4 * 256], BF16)
        sbuf(c, "g_outb", [128, 2 * 2048], BF16)
        sbuf(c, "g_knl", [128, 2 * 4096], BF16)
        sbuf(c, "g_vbl", [128, 2 * 4096], BF16)
        rb = sbuf(c, "r_ybuf_t", [128, 2 * D], BF16)
        c.t["r_ybuf"] = rb[:, :]
        c.t["r_lobuf"] = rb[:, :]

        c.block = stack.enter_context(nc.Block())
        alloc_common(c)
        load_w_dmajor(c, c.t["gw_q"], c.d["gw_qd"].ap(), D)
        load_w_dmajor(c, c.t["gw_k"], c.d["gw_kd"].ap(), D)
        load_w_dmajor(c, c.t["gw_v"], c.d["gw_vd"].ap(), D)
        load_w_dmajor(c, c.t["gw_o"], c.d["gw_od"].ap(), D)
        c.dma("sync", c.t["g_bqk"][:], c.d["g_bqkd"].ap(), "s_ld")
        c.dma("gpsimd", c.t["g_bvr"][:], c.d["g_bvrd"].ap(), "s_wt")
        c.dma("sync", c.t["g_bols"][:], c.d["g_bolsd"].ap(), "s_ld")
        for eng in ("scalar", "vector"):
            c.wait(eng, "s_ld")
            c.wait(eng, "s_wt")
        c.wait("tensor", "s_ld")
        emit_l1a(c)
        c.wait("sync", "s_pe")
        emit_res(c, c.d["x2"].ap(), c.d["y2"].ap(), c.d["x3"].ap(),
                 c.d["xn3"].ap(), None)
        finish(c, ["s_rst0", "s_rst1"])
    return nc




def emit_l1f(c):
    P0, P1, P2, P3, P4 = (c.t["pp0"], c.t["pp1"], c.t["pp2"], c.t["pp3"],
                          c.t["pp4"])
    ST0, ST1 = c.t["st0"], c.t["st1"]
    PP, PN = [P0, P1], [P2, P3]
    fx = c.t["m_x"]
    acc = c.t["m_acc"]
    wgx, wux, wdx = c.t["m_wg"], c.t["m_wu"], c.t["m_wd"]
    gsb, gub, hms = c.t["m_g"], c.t["m_gu"], c.t["m_hm"]
    ybf = c.t["m_hm"]
    cmf = c.t["m_cm"]
    scr, w16, t16, mcol = c.t["m_scr"], c.t["m_w16"], c.t["m_t16"], c.t["m_mc"]
    gwt, sel16, identf = c.t["m_gwt"], c.t["c_sel16"], c.t["c_identf"]
    xn3 = c.d["xn3i"].ap()
    egd, eud, edd = c.d["m_egd"].ap(), c.d["m_eud"].ap(), c.d["m_edd"].ap()
    sgd, sud, sdd = c.d["m_sgd"].ap(), c.d["m_sud"].ap(), c.d["m_sdd"].ap()
    y3 = c.d["y3"].ap()

    c.dma("gpsimd", gwt[:].rearrange("p (dc e) -> p dc e", e=16),
          c.d["m_gwtd"].ap().rearrange("(dc p) e -> p dc e", p=128), "s_wt")
    c.dma("gpsimd", sel16[:], c.d["c_sel16d"].ap(), "s_wt")
    c.dma("sync", identf[:], c.d["c_identfd"].ap(), "s_ld")
    c.wait("vector", "s_ld")
    c.wait("vector", "s_wt")
    c.wait("scalar", "s_ld")
    c.wait("scalar", "s_wt")

    c.wait("tensor", "s_ld")
    c.wait("tensor", "s_wt")
    tr_evac = 0
    fxm = fx[:, 0:DC * 512]
    STL = [ST0, c.t["st2"]]
    pe_hist = {}
    st_free = {}
    scr_rd = {}

    def gate_logits(ch):
        half = ch % 2
        ch4 = ch % 4
        c.wait("tensor", "s_ac", st_free.get(ch - 2, 0))
        for dc in range(DC):
            pe_hist[ch] = c.op("tensor", lambda e, dc=dc, ch4=ch4, half=half:
                               e.matmul(
                                   STL[half][0:128, 0:16],
                                   fxm[:, dc * 512 + ch4 * 128:
                                       dc * 512 + ch4 * 128 + 128],
                                   gwt[:, dc * 16:(dc + 1) * 16],
                                   start=(dc == 0), stop=(dc == DC - 1)),
                               "s_pe" if dc == DC - 1 else None) \
                or pe_hist.get(ch)

    def gate_chain(ch):
        nonlocal tr_evac
        half = ch % 2
        c.wait("scalar", "s_pe", pe_hist[ch])
        c.wait("scalar", "s_dv", scr_rd.get(ch - 2, 0))
        nsg = c.op("scalar", lambda e, half=half: e.activation(
            scr[:, half * 16:half * 16 + 16], STL[half][0:128, 0:16],
            AF.Sigmoid), "s_ac")
        st_free[ch] = nsg
        c.wait("vector", "s_ac", nsg)
        sl = slice(half * 16, half * 16 + 16)
        mc = half * 8
        n = c.op("vector", lambda e, sl=sl: e.tensor_scalar(
            w16[:, sl], scr[:, sl], 0.0, None, ALU.add), "s_dv")
        for r in range(4):
            c.wait("vector", "s_dv", n)
            n = c.op("vector", lambda e, r=r, sl=sl, mc=mc: e.reduce_max(
                mcol[:, mc + r:mc + r + 1], w16[:, sl],
                axis=mybir.AxisListType.X), "s_dv")
            if r < 3:
                c.wait("vector", "s_dv", n)
                n = c.op("vector", lambda e, r=r, sl=sl, mc=mc: e.tensor_scalar(
                    t16[:, sl], w16[:, sl], mcol[:, mc + r:mc + r + 1], None,
                    ALU.is_ge), "s_dv")
                c.wait("vector", "s_dv", n)
                n = c.op("vector", lambda e, sl=sl: e.scalar_tensor_tensor(
                    w16[:, sl], t16[:, sl], -2.0, w16[:, sl], ALU.mult,
                    ALU.add), "s_dv")
        c.wait("vector", "s_dv", n)
        n = c.op("vector", lambda e, mc=mc: e.reduce_sum(
            mcol[:, mc + 4:mc + 5], mcol[:, mc:mc + 4],
            axis=mybir.AxisListType.X), "s_dv")
        c.wait("vector", "s_dv", n)
        n = c.op("vector", lambda e, mc=mc: e.tensor_scalar(
            mcol[:, mc + 4:mc + 5], mcol[:, mc + 4:mc + 5], 1e-20, None,
            ALU.add), "s_dv")
        c.wait("vector", "s_dv", n)
        n = c.op("vector", lambda e, mc=mc: e.reciprocal(
            mcol[:, mc + 5:mc + 6], mcol[:, mc + 4:mc + 5]), "s_dv")
        c.wait("vector", "s_dv", n)
        n = c.op("vector", lambda e, sl=sl, mc=mc: e.tensor_scalar(
            t16[:, sl], scr[:, sl], mcol[:, mc + 3:mc + 4], None, ALU.is_ge),
            "s_dv")
        c.wait("vector", "s_dv", n)
        n = c.op("vector", lambda e, sl=sl: e.tensor_tensor(
            t16[:, sl], t16[:, sl], scr[:, sl], ALU.mult), "s_dv")
        c.wait("vector", "s_dv", n)
        n = c.op("vector", lambda e, sl=sl, mc=mc: e.tensor_scalar(
            t16[:, sl], t16[:, sl], mcol[:, mc + 5:mc + 6], None, ALU.mult),
            "s_dv")
        scr_rd[ch] = n
        c.wait("tensor", "s_dv", n)
        c.wait("tensor", "s_ac", tr_evac)
        n = c.op("tensor", lambda e, sl=sl: e.transpose(
            ST1[0:16, 128:256], t16[:, sl], identf[:]), "s_pe")
        c.wait("scalar", "s_pe", n)
        if ch % 16 == 0 and ch > 0:
            c.wait("scalar", "s_cm%d" % ((ch // 16 - 1) % 2))
        tr_evac = c.op("scalar", lambda e, ch=ch: e.copy(
            cmf[0:16, (ch % 16) * 128:(ch % 16) * 128 + 128],
            ST1[0:16, 128:256]), "s_ac")
        if ch % 16 == 15:
            grp = ch // 16
            c.wait("scalar", "s_ac", tr_evac)
            c.dma("scalar", c.d["cm_d"].ap()[0:16, grp * TT:(grp + 1) * TT],
                  cmf[0:16, :], "s_cm%d" % (grp % 2))

    prev_ch = None
    for mt in range(TOWN // 512):
        m0 = mt * 512
        c.wait("sync", "s_pe")
        for dc in range(DC):
            c.dma("sync", fxm[:, dc * 512:(dc + 1) * 512],
                  xn3[m0:m0 + 512, dc * 128:(dc + 1) * 128], "s_tr",
                  transpose=True)
        c.wait("tensor", "s_tr")
        for ch4 in range(4):
            ch = mt * 4 + ch4
            gate_logits(ch)
            if prev_ch is not None:
                gate_chain(prev_ch)
            prev_ch = ch
    gate_chain(prev_ch)

    last_hm = 0
    conv_dv = 0
    last_store = {0: 0, 1: 0}
    wfree = {}
    pe_fx_free = c.cur("s_pe")
    cb_free = 0
    for tt in range(TOWN // TT):
        t0 = tt * TT
        c.wait("sync", "s_pe", pe_fx_free)
        c.wait("sync", "s_ac", tr_evac)
        for dc in range(DC):
            c.dma("sync", fx[:, dc * TT:(dc + 1) * TT],
                  xn3[t0:t0 + TT, dc * 128:(dc + 1) * 128], "s_tr",
                  transpose=True)
        c.wait("sync", "s_cm0")
        c.wait("sync", "s_cm1")
        c.wait("sync", "s_pe", cb_free)
        c.dma("sync", cmf[0:16, :], c.d["cm_d"].ap()[0:16, t0:t0 + TT],
              "s_cml")
        c.wait("tensor", "s_tr")
        c.wait("tensor", "s_cml")
        c.wait("vector", "s_yst0", last_store[0])
        c.wait("vector", "s_yst1", last_store[1])
        for pr in range(9):
            for pu in range(2):
                u = 2 * pr + pu
                c.wait("gpsimd", "s_pe", wfree.get("gu%d" % pu, 0))
                if u < 16:
                    c.dma("gpsimd",
                          wgx.ap().rearrange("p (b dc i) -> p b dc i", b=2,
                                             i=512)[:, pu],
                          egd[u].rearrange("(dc p) i -> p dc i", p=128),
                          "s_w%d" % pu)
                    c.dma("gpsimd",
                          wux.ap().rearrange("p (b dc i) -> p b dc i", b=2,
                                             i=512)[:, pu],
                          eud[u].rearrange("(dc p) i -> p dc i", p=128),
                          "s_w%d" % pu)
                    c.wait("gpsimd", "s_pe", wfree.get("d%d" % pu, 0))
                    c.dma("gpsimd",
                          wdx.ap().rearrange("p (b ic o) -> p b ic o", b=2,
                                             o=1024)[:, pu],
                          edd[u].rearrange("(ic p) o -> p ic o", p=128),
                          "s_w%d" % pu)
                else:
                    hs = u - 16
                    c.dma("gpsimd",
                          wgx.ap().rearrange("p (b dc i) -> p b dc i", b=2,
                                             i=512)[:, pu],
                          sgd.rearrange("(dc p) i -> p dc i", p=128)
                          [:, :, hs * 512:(hs + 1) * 512], "s_w%d" % pu)
                    c.dma("gpsimd",
                          wux.ap().rearrange("p (b dc i) -> p b dc i", b=2,
                                             i=512)[:, pu],
                          sud.rearrange("(dc p) i -> p dc i", p=128)
                          [:, :, hs * 512:(hs + 1) * 512], "s_w%d" % pu)
                    c.wait("gpsimd", "s_pe", wfree.get("d%d" % pu, 0))
                    c.dma("gpsimd",
                          wdx.ap().rearrange("p (b ic o) -> p b ic o", b=2,
                                             o=1024)[:, pu],
                          sdd[hs * 512:(hs + 1) * 512, :]
                          .rearrange("(ic p) o -> p ic o", p=128),
                          "s_w%d" % pu)
            c.wait("tensor", "s_w0")
            c.wait("tensor", "s_w1")
            last_mm = 0
            for sub in range(TT // 512):
                for pu in range(2):
                    u = 2 * pr + pu
                    gev, uev, gu_done = {}, {}, 0
                    c.wait("tensor", "s_ac")
                    c.wait("tensor", "s_dv")
                    if pu == 1:
                        c.wait("scalar", "s_dv", last_hm)
                        c.wait("vector", "s_dv", last_hm)
                    for ic in range(4):
                        pb = PP[ic % 2]
                        if ic >= 2:
                            c.wait("tensor", "s_ac", gev[ic - 2])
                        for dc in range(DC):
                            c.op("tensor", lambda e, pb=pb, ic=ic, dc=dc,
                                 pu=pu, sub=sub: e.matmul(
                                     pb[:, 0:512],
                                     wgx[:, (pu * 8 + dc) * 512 + ic * 128:
                                         (pu * 8 + dc) * 512 + ic * 128 + 128],
                                     fx[:, dc * TT + sub * 512:
                                        dc * TT + sub * 512 + 512],
                                     start=(dc == 0), stop=(dc == DC - 1)),
                                 "s_pe" if dc == DC - 1 else None)
                        c.wait("scalar", "s_pe")
                        gev[ic] = c.op("scalar", lambda e, pb=pb, ic=ic:
                                       e.activation(
                                           gsb[:, ic * 512:(ic + 1) * 512],
                                           pb[:, 0:512], AF.Silu), "s_ac")
                    for ic in range(4):
                        pb = PN[ic % 2]
                        if ic >= 2:
                            c.wait("tensor", "s_dv", uev[ic - 2])
                        for dc in range(DC):
                            c.op("tensor", lambda e, pb=pb, ic=ic, dc=dc,
                                 pu=pu, sub=sub: e.matmul(
                                     pb[:, 0:512],
                                     wux[:, (pu * 8 + dc) * 512 + ic * 128:
                                         (pu * 8 + dc) * 512 + ic * 128 + 128],
                                     fx[:, dc * TT + sub * 512:
                                        dc * TT + sub * 512 + 512],
                                     start=(dc == 0), stop=(dc == DC - 1)),
                                 "s_pe" if dc == DC - 1 else None)
                        c.wait("vector", "s_pe")
                        c.wait("vector", "s_ac", gev[ic])
                        uev[ic] = c.op("vector", lambda e, pb=pb, ic=ic:
                                       e.tensor_tensor(
                                           gub[:, ic * 512:(ic + 1) * 512],
                                           gsb[:, ic * 512:(ic + 1) * 512],
                                           pb[:, 0:512], ALU.mult), "s_dv")
                        gu_done = uev[ic]
                    if u < 16:
                        c.wait("tensor", "s_dv", last_hm)
                        pe_mb = c.op("tensor", lambda e, u=u, sub=sub:
                                     e.matmul(
                                         P4[:, 0:512],
                                         sel16[:, u * 128:(u + 1) * 128],
                                         cmf[0:16, sub * 512:
                                             sub * 512 + 512],
                                         start=True, stop=True), "s_pe")
                        c.wait("vector", "s_pe", pe_mb)
                        c.wait("vector", "s_dv", gu_done)
                        for ic in range(4):
                            last_hm = c.op("vector", lambda e, ic=ic, pu=pu:
                                           e.tensor_tensor(
                                               hms[:, pu * 2048 + ic * 512:
                                                   pu * 2048 + ic * 512 + 512],
                                               gub[:, ic * 512:(ic + 1) * 512],
                                               P4[:, 0:512], ALU.mult), "s_dv")
                    else:
                        c.wait("vector", "s_dv", gu_done)
                        for ic in range(4):
                            last_hm = c.op("vector", lambda e, ic=ic, pu=pu:
                                           e.tensor_scalar(
                                               hms[:, pu * 2048 + ic * 512:
                                                   pu * 2048 + ic * 512 + 512],
                                               gub[:, ic * 512:(ic + 1) * 512],
                                               0.0, None, ALU.add), "s_dv")
                h_last = c.cur("s_pe")
                c.wait("tensor", "s_dv", last_hm)
                aev = {}
                for oc in range(DC):
                    pd = PN[oc % 2]
                    if oc >= 2:
                        c.wait("tensor", "s_dv", aev[oc - 2])
                    for k8 in range(8):
                        pu, ic = k8 // 4, k8 % 4
                        last_mm = c.op(
                            "tensor", lambda e, pd=pd, oc=oc, ic=ic, pu=pu,
                            k8=k8: e.matmul(
                                pd[:, 0:512],
                                wdx[:, (pu * 4 + ic) * 1024 + oc * 128:
                                    (pu * 4 + ic) * 1024 + oc * 128 + 128],
                                hms[:, pu * 2048 + ic * 512:
                                    pu * 2048 + ic * 512 + 512],
                                start=(k8 == 0), stop=(k8 == 7)),
                            "s_pe" if k8 == 7 else None)
                    c.wait("vector", "s_pe")
                    if pr == 0:
                        c.wait("vector", "s_dv", conv_dv)
                    dst = acc[:, oc * TT + sub * 512:oc * TT + sub * 512 + 512]
                    aev[oc] = c.op("vector", lambda e, pd=pd, dst=dst,
                                   first=(pr == 0): (
                                       e.tensor_scalar(dst, pd[:, 0:512], 0.0,
                                                       None, ALU.add)
                                       if first else
                                       e.tensor_tensor(dst, dst, pd[:, 0:512],
                                                       ALU.add)), "s_dv")
            wfree["gu0"] = h_last
            wfree["gu1"] = h_last
            wfree["d0"] = last_mm
            wfree["d1"] = last_mm
        pe_fx_free = c.cur("s_pe")
        cb_free = c.cur("s_pe")
        c.wait("vector", "s_dv")
        for oc in range(DC):
            p = oc % 2
            c.wait("vector", "s_yst%d" % p, last_store[p])
            conv_dv = c.op("vector", lambda e, oc=oc, p=p: e.tensor_scalar(
                ybf[:, p * TT:(p + 1) * TT], acc[:, oc * TT:(oc + 1) * TT],
                0.0, None, ALU.add), "s_dv")
            c.wait("scalar", "s_dv", conv_dv)
            last_store[p] = c.dma(
                "scalar", y3[oc * 128:(oc + 1) * 128, t0:t0 + TT],
                ybf[:, p * TT:(p + 1) * TT], "s_yst%d" % p)


def build_program_d():
    nc = bass.Bass("TRN2", target_bir_lowering=False)
    stack = ExitStack()
    c = Ctx(nc, stack)
    with stack:
        dt = nc.dram_tensor
        c.d["x3"] = dt("x3", [TOWN, D], F32, kind="ExternalInput")
        c.d["xn3i"] = dt("xn3i", [TOWN, D], BF16, kind="ExternalInput")
        c.d["m_gwtd"] = dt("m_gwtd", [D, 16], F32, kind="ExternalInput")
        c.d["m_egd"] = dt("m_egd", [16, D, 512], F32, kind="ExternalInput")
        c.d["m_eud"] = dt("m_eud", [16, D, 512], F32, kind="ExternalInput")
        c.d["m_edd"] = dt("m_edd", [16, 512, D], F32, kind="ExternalInput")
        c.d["m_sgd"] = dt("m_sgd", [D, D], F32, kind="ExternalInput")
        c.d["m_sud"] = dt("m_sud", [D, D], F32, kind="ExternalInput")
        c.d["m_sdd"] = dt("m_sdd", [D, D], F32, kind="ExternalInput")
        c.d["c_sel16d"] = dt("c_sel16d", [16, 16 * 128], F32,
                             kind="ExternalInput")
        c.d["c_identfd"] = dt("c_identfd", [128, 128], F32,
                              kind="ExternalInput")
        c.d["y3"] = dt("y3", [D, TOWN], BF16)
        c.d["cm_d"] = dt("cm_d", [16, TOWN], BF16)
        c.d["out"] = dt("out", [TOWN, D], F32, kind="ExternalOutput")

        sbuf(c, "m_x", [128, DC * TT], BF16)
        sbuf(c, "m_acc", [128, DC * TT], F32)
        sbuf(c, "m_wg", [128, 2 * DC * 512], BF16)
        sbuf(c, "m_wu", [128, 2 * DC * 512], BF16)
        sbuf(c, "m_wd", [128, 2 * 4 * D], BF16)
        sbuf(c, "m_g", [128, 4 * 512], BF16)
        sbuf(c, "m_gu", [128, 4 * 512], BF16)
        sbuf(c, "m_hm", [128, 8 * 512], BF16)
        sbuf(c, "m_cm", [16, TT], BF16)
        sbuf(c, "m_scr", [128, 32], F32)
        sbuf(c, "m_w16", [128, 32], F32)
        sbuf(c, "m_t16", [128, 32], F32)
        sbuf(c, "m_mc", [128, 16], F32)
        sbuf(c, "m_gwt", [128, DC * 16], BF16)
        sbuf(c, "c_sel16", [16, 16 * 128], BF16)
        sbuf(c, "c_identf", [128, 128], F32)
        rb = sbuf(c, "r_ybuf_t", [128, 2 * D], BF16)
        c.t["r_ybuf"] = rb[:, :]
        c.t["r_lobuf"] = rb[:, :]

        c.block = stack.enter_context(nc.Block())
        alloc_common(c)
        emit_l1f(c)
        c.wait("sync", "s_yst0")
        c.wait("sync", "s_yst1")
        c.wait("sync", "s_pe")
        emit_res(c, c.d["x3"].ap(), c.d["y3"].ap(), c.d["out"].ap(),
                 None, None)
        finish(c, ["s_rst0", "s_rst1"])
    return nc


def phase_barrier(c):
    names = list(c.sems.keys())
    for eng in ("sync", "gpsimd", "vector", "scalar", "tensor"):
        for sn in names:
            c.wait(eng, sn)


from contextlib import contextmanager


@contextmanager
def phase_stack(c):
    outer = c.stack
    inner = ExitStack()
    c.stack = inner
    try:
        with inner:
            yield
    finally:
        c.stack = outer


SEM_NAMES = ["s_ld", "s_wt", "s_in0", "s_in1", "s_xn0", "s_xn1", "s_tr",
             "s_pe", "s_dv", "s_ac", "s_yst0", "s_yst1", "s_rin0", "s_rin1",
             "s_rst0", "s_rst1", "s_w0", "s_w1", "s_st0", "s_st1", "s_q0",
             "s_q1", "s_kv0", "s_kv1", "s_ao0", "s_ao1", "s_y0", "s_y1",
             "s_cc"]


def build_fused():
    nc = bass.Bass("TRN2", target_bir_lowering=False)
    stack = ExitStack()
    c = Ctx(nc, stack)
    with stack:
        dt = nc.dram_tensor
        c.d["x_pre"] = dt("x_pre", [TOWN, D], F32, kind="ExternalInput")
        c.d["x_own"] = dt("x_own", [TOWN, D], F32, kind="ExternalInput")
        c.d["w_q"] = dt("w_q", [D, D], F32, kind="ExternalInput")
        c.d["w_k"] = dt("w_k", [D, D], F32, kind="ExternalInput")
        c.d["w_v"] = dt("w_v", [D, D], F32, kind="ExternalInput")
        c.d["w_o"] = dt("w_o", [D, D], F32, kind="ExternalInput")
        c.d["w_f"] = dt("w_f", [HD, NF], F32, kind="ExternalInput")
        c.d["c_trild"] = dt("c_trild", [128, 128], F32, kind="ExternalInput")
        c.d["c_sheadd"] = dt("c_sheadd", [16, 16 * 128], F32,
                             kind="ExternalInput")
        c.d["c_ucold"] = dt("c_ucold", [64, 16 * 16], F32,
                            kind="ExternalInput")
        c.d["c_flagd"] = dt("c_flagd", [128, 1], F32, kind="ExternalInput")
        c.d["f_wgd"] = dt("f_wgd", [D, 8], F32, kind="ExternalInput")
        c.d["f_bgd"] = dt("f_bgd", [1, 8], F32, kind="ExternalInput")
        c.d["f_w1d"] = dt("f_w1d", [8, D, D], F32, kind="ExternalInput")
        c.d["f_b1d"] = dt("f_b1d", [128, 64], F32, kind="ExternalInput")
        c.d["f_w2d"] = dt("f_w2d", [8, D, D], F32, kind="ExternalInput")
        c.d["f_b2d"] = dt("f_b2d", [8, D], F32, kind="ExternalInput")
        c.d["c_sel8d"] = dt("c_sel8d", [8, 8 * 128], F32,
                            kind="ExternalInput")
        c.d["c_identfd"] = dt("c_identfd", [128, 128], F32,
                              kind="ExternalInput")
        c.d["gw_qd"] = dt("gw_qd", [D, D], F32, kind="ExternalInput")
        c.d["gw_kd"] = dt("gw_kd", [D, D], F32, kind="ExternalInput")
        c.d["gw_vd"] = dt("gw_vd", [D, D], F32, kind="ExternalInput")
        c.d["gw_od"] = dt("gw_od", [D, D], F32, kind="ExternalInput")
        c.d["g_bqkd"] = dt("g_bqkd", [128, 16], F32, kind="ExternalInput")
        c.d["g_bvrd"] = dt("g_bvrd", [1, D], F32, kind="ExternalInput")
        c.d["g_bolsd"] = dt("g_bolsd", [128, 16], F32, kind="ExternalInput")
        c.d["m_gwtd"] = dt("m_gwtd", [D, 16], F32, kind="ExternalInput")
        c.d["m_egd"] = dt("m_egd", [16, D, 512], F32, kind="ExternalInput")
        c.d["m_eud"] = dt("m_eud", [16, D, 512], F32, kind="ExternalInput")
        c.d["m_edd"] = dt("m_edd", [16, 512, D], F32, kind="ExternalInput")
        c.d["m_sgd"] = dt("m_sgd", [D, D], F32, kind="ExternalInput")
        c.d["m_sud"] = dt("m_sud", [D, D], F32, kind="ExternalInput")
        c.d["m_sdd"] = dt("m_sdd", [D, D], F32, kind="ExternalInput")
        c.d["c_sel16d"] = dt("c_sel16d", [16, 16 * 128], F32,
                             kind="ExternalInput")
        c.d["xn0"] = dt("xn0", [2 * TOWN, D], BF16)
        c.d["y0"] = dt("y0", [D, TOWN], BF16)
        c.d["x1"] = dt("x1", [TOWN, D], F32)
        c.d["xn1"] = dt("xn1", [TOWN, D], BF16)
        c.d["xn1lo"] = dt("xn1lo", [TOWN, D], BF16)
        c.d["y1"] = dt("y1", [D, TOWN], BF16)
        c.d["x2"] = dt("x2", [TOWN, D], F32)
        c.d["xn2i"] = dt("xn2i", [TOWN, D], BF16)
        c.d["xn2f"] = dt("xn2f", [2 * TOWN, D], BF16)
        c.d["kn_d"] = dt("kn_d", [D, 2 * TOWN], BF16)
        c.d["v_d"] = dt("v_d", [2 * TOWN, D], BF16)
        c.d["qn_d"] = dt("qn_d", [D, TOWN], BF16)
        c.d["ao_tm"] = dt("ao_tm", [TOWN, D], BF16)
        c.d["y2"] = dt("y2", [D, TOWN], BF16)
        c.d["x3"] = dt("x3", [TOWN, D], F32)
        c.d["xn3i"] = dt("xn3i", [TOWN, D], BF16)
        c.d["y3"] = dt("y3", [D, TOWN], BF16)
        c.d["cm_d"] = dt("cm_d", [16, TOWN], BF16)
        c.d["out"] = dt("out", [TOWN, D], F32, kind="ExternalOutput")

        c.block = stack.enter_context(nc.Block())
        for sn in SEM_NAMES:
            c.sem(sn)
        alloc_common(c)

        with phase_stack(c):
            for n in ("wq", "wk", "wv", "wo"):
                sbuf(c, n, [128, DC * D], BF16)
            sbuf(c, "wfeat", [64, 128], BF16)
            sbuf(c, "c_tril", [128, 128], BF16)
            sbuf(c, "c_shead", [16, 16 * 128], BF16)
            sbuf(c, "c_ucol", [64, 16 * 16], BF16)
            sbuf(c, "c_flag", [128, 1], F32)
            sbuf(c, "a_xnf", [128, DC * BLK], BF16)
            sbuf(c, "a_khh", [64, H * BLK], BF16)
            sbuf(c, "a_qhh", [64, H * BLK], BF16)
            sbuf(c, "a_sqh", [64, 4 * BLK], BF16)
            sbuf(c, "a_vaug", [128, 4 * 1040], BF16)
            sbuf(c, "a_kpf", [128, H * BLK], BF16)
            sbuf(c, "a_qpf", [128, H * BLK], BF16)
            sbuf(c, "a_kptm", [128, H * 128], BF16)
            sbuf(c, "a_sqm", [16, 2 * BLK], BF16)
            sbuf(c, "a_am", [128, 2 * 128], BF16)
            sbuf(c, "a_numb", [64, 2 * 128], BF16)
            sbuf(c, "a_yall", [128, DC * BLK], BF16)
            sbuf(c, "a_yo", [128, 2 * BLK], BF16)
            sbuf(c, "a_sf32", [128, H * 65], F32)
            sbuf(c, "a_sbf", [128, H * 65], BF16)
            sbuf(c, "a_dn", [1, 512], F32)
            sbuf(c, "a_rec", [1, 256], BF16)
            c.t["r_ybuf"] = c.t["a_yall"][:, 0:2 * D]
            c.t["r_lobuf"] = c.t["a_yall"][:, 2 * D:4 * D]

            c.dma("gpsimd", c.t["c_tril"][:], c.d["c_trild"].ap(), "s_wt")
            c.dma("gpsimd", c.t["c_shead"][:], c.d["c_sheadd"].ap(), "s_wt")
            c.dma("gpsimd", c.t["c_ucol"][:], c.d["c_ucold"].ap(), "s_wt")
            c.dma("sync", c.t["c_flag"][:], c.d["c_flagd"].ap(), "s_ld")
            load_w_dmajor(c, c.t["wq"], c.d["w_q"].ap(), D)
            load_w_dmajor(c, c.t["wk"], c.d["w_k"].ap(), D)
            load_w_dmajor(c, c.t["wv"], c.d["w_v"].ap(), D)
            load_w_dmajor(c, c.t["wo"], c.d["w_o"].ap(), D)
            c.dma("gpsimd", c.t["wfeat"][:], c.d["w_f"].ap(), "s_wt")
            for eng in ("scalar", "vector"):
                c.wait(eng, "s_wt")
                c.wait(eng, "s_ld")
            c.wait("tensor", "s_ld")
            emit_l0a(c)
            c.wait("sync", "s_yst0")
            c.wait("sync", "s_yst1")
            c.wait("vector", "s_xn0")
            c.wait("vector", "s_xn1")
            c.wait("scalar", "s_xn0")
            c.wait("scalar", "s_xn1")
            c.wait("sync", "s_pe")
            emit_res(c, c.d["x_own"].ap(), c.d["y0"].ap(), c.d["x1"].ap(),
                     c.d["xn1"].ap(), c.d["xn1lo"].ap())
        phase_barrier(c)

        with phase_stack(c):
            sbuf(c, "f_x2", [128, DC * TT], BF16)
            sbuf(c, "f_acc", [128, DC * TT], F32)
            sbuf(c, "f_w1", [128, 2 * DC * 512], BF16)
            sbuf(c, "f_w2", [128, 2 * 4 * D], BF16)
            sbuf(c, "f_b1", [128, 64], F32)
            sbuf(c, "f_h", [128, 4 * 512], BF16)
            sbuf(c, "f_hm", [128, 8 * 512], BF16)
            sbuf(c, "f_y", [128, 2 * TT], BF16)
            sbuf(c, "f_mk", [8, TOWN], BF16)
            sbuf(c, "f_lg", [128, 16], F32)
            sbuf(c, "f_mx", [128, 2], F32)
            sbuf(c, "f_mk32", [128, 16], F32)
            sbuf(c, "f_wgh", [128, 64], BF16)
            sbuf(c, "f_wgl", [128, 64], BF16)
            sbuf(c, "f_wgf", [128, 64], F32)
            sbuf(c, "f_bgf", [1, 8], F32)
            sbuf(c, "f_bgb", [128, 8], F32)
            sbuf(c, "f_b2b", [8, D], BF16)
            sbuf(c, "f_b2f", [8, D], F32)
            sbuf(c, "c_sel8", [8, 8 * 128], BF16)
            sbuf(c, "c_identf", [128, 128], F32)
            c.t["r_ybuf"] = c.t["f_y"][:, 0:2 * D]
            c.t["r_lobuf"] = c.t["f_y"][:, 2 * D:4 * D]
            emit_l0f(c)
            c.wait("sync", "s_yst0")
            c.wait("sync", "s_yst1")
            c.wait("sync", "s_pe")
            emit_res(c, c.d["x1"].ap(), c.d["y1"].ap(), c.d["x2"].ap(),
                     c.d["xn2i"].ap(), None)
        c.wait("gpsimd", "s_rst0")
        c.wait("gpsimd", "s_rst1")
        c.op("gpsimd", lambda e: e.collective_compute(
            "AllGather", ALU.bypass,
            replica_groups=[[0, 1], [2, 3], [4, 5], [6, 7]],
            ins=[c.d["xn2i"].ap().opt()], outs=[c.d["xn2f"].ap().opt()]),
            "s_cc")
        phase_barrier(c)

        with phase_stack(c):
            for n in ("gw_q", "gw_k", "gw_v", "gw_o"):
                sbuf(c, n, [128, DC * D], BF16)
            sbuf(c, "g_xnf", [128, DC * BLK], BF16)
            sbuf(c, "g_kraw", [128, DC * BLK], BF16)
            sbuf(c, "g_ksq", [128, DC * BLK], BF16)
            sbuf(c, "g_kn", [128, DC * BLK], BF16)
            sbuf(c, "g_v", [128, 4 * D], BF16)
            sbuf(c, "g_rows", [1, 1024], F32)
            sbuf(c, "g_r2", [1, 1024], F32)
            sbuf(c, "g_r3", [1, 1024], F32)
            sbuf(c, "g_ab", [1, 1024], BF16)
            sbuf(c, "g_bqk", [128, 16], F32)
            sbuf(c, "g_bvr", [1, D], BF16)
            sbuf(c, "g_bols", [128, 16], F32)
            sbuf(c, "c_ones128", [128, 1], BF16)
            sbuf(c, "g_qnt", [128, DC * 256], BF16)
            sbuf(c, "g_att", [128, 4 * 256], BF16)
            sbuf(c, "g_outb", [128, 2 * 2048], BF16)
            sbuf(c, "g_knl", [128, 2 * 4096], BF16)
            sbuf(c, "g_vbl", [128, 2 * 4096], BF16)
            rb = sbuf(c, "r_ybuf_t2", [128, 2 * D], BF16)
            c.t["r_ybuf"] = rb[:, :]
            c.t["r_lobuf"] = rb[:, :]
            load_w_dmajor(c, c.t["gw_q"], c.d["gw_qd"].ap(), D)
            load_w_dmajor(c, c.t["gw_k"], c.d["gw_kd"].ap(), D)
            load_w_dmajor(c, c.t["gw_v"], c.d["gw_vd"].ap(), D)
            load_w_dmajor(c, c.t["gw_o"], c.d["gw_od"].ap(), D)
            c.dma("sync", c.t["g_bqk"][:], c.d["g_bqkd"].ap(), "s_ld")
            c.dma("gpsimd", c.t["g_bvr"][:], c.d["g_bvrd"].ap(), "s_wt")
            c.dma("sync", c.t["g_bols"][:], c.d["g_bolsd"].ap(), "s_ld")
            for eng in ("scalar", "vector"):
                c.wait(eng, "s_ld")
                c.wait(eng, "s_wt")
            c.wait("tensor", "s_ld")
            emit_l1a(c)
            c.wait("sync", "s_pe")
            emit_res(c, c.d["x2"].ap(), c.d["y2"].ap(), c.d["x3"].ap(),
                     c.d["xn3i"].ap(), None)
        phase_barrier(c)

        with phase_stack(c):
            sbuf(c, "m_x", [128, DC * TT], BF16)
            sbuf(c, "m_acc", [128, DC * TT], F32)
            sbuf(c, "m_wg", [128, 2 * DC * 512], BF16)
            sbuf(c, "m_wu", [128, 2 * DC * 512], BF16)
            sbuf(c, "m_wd", [128, 2 * 4 * D], BF16)
            sbuf(c, "m_g", [128, 4 * 512], BF16)
            sbuf(c, "m_gu", [128, 4 * 512], BF16)
            sbuf(c, "m_hm", [128, 8 * 512], BF16)
            sbuf(c, "m_cm", [16, TT], BF16)
            sbuf(c, "m_scr", [128, 32], F32)
            sbuf(c, "m_w16", [128, 32], F32)
            sbuf(c, "m_t16", [128, 32], F32)
            sbuf(c, "m_mc", [128, 16], F32)
            sbuf(c, "m_gwt", [128, DC * 16], BF16)
            sbuf(c, "c_sel16", [16, 16 * 128], BF16)
            t2 = c.stack.enter_context(
                c.nc.sbuf_tensor("c_identf2", [128, 128], F32))
            c.t["c_identf"] = t2
            c.t["r_ybuf"] = c.t["m_hm"][:, 0:2 * D]
            c.t["r_lobuf"] = c.t["m_hm"][:, 2 * D:4 * D]
            emit_l1f(c)
            c.wait("sync", "s_yst0")
            c.wait("sync", "s_yst1")
            c.wait("sync", "s_pe")
            emit_res(c, c.d["x3"].ap(), c.d["y3"].ap(), c.d["out"].ap(),
                     None, None)
        finish(c, ["s_rst0", "s_rst1"])
    return nc


def host_consts():
    tril = (np.tril(np.ones((128, 128), np.float32), 0).T).copy()
    shead = np.zeros((16, 16 * 128), np.float32)
    for h in range(16):
        shead[h, h * 128:(h + 1) * 128] = 1.0
    ucol = np.zeros((64, 16 * 16), np.float32)
    for h in range(16):
        ucol[:, h * 16 + h] = 1.0
    cpool = np.tile(np.array(CONST_VALS, np.float32)[None, :], (128, 1))
    return dict(c_trild=np.ascontiguousarray(tril),
                c_sheadd=shead, c_ucold=ucol, c_pool=cpool)


def fused_in_maps(inputs):
    x = np.asarray(inputs["L0_attn_x"])
    consts = host_consts()
    ident = np.eye(128, dtype=np.float32)
    sel8 = np.zeros((8, 8 * 128), np.float32)
    for h in range(8):
        sel8[h, h * 128:(h + 1) * 128] = 1.0
    sel16 = np.zeros((16, 16 * 128), np.float32)
    for h in range(16):
        sel16[h, h * 128:(h + 1) * 128] = 1.0
    b1 = np.asarray(inputs["L0_ffn_b1"])
    b1r = np.ascontiguousarray(
        b1.reshape(8, 8, 128).transpose(2, 0, 1).reshape(128, 64))
    bq = np.asarray(inputs["L1_attn_bq"]).reshape(8, 128).T
    bk = np.asarray(inputs["L1_attn_bk"]).reshape(8, 128).T
    bo = np.asarray(inputs["L1_attn_bo"]).reshape(8, 128).T
    ls = np.asarray(inputs["L1_attn_layerscale"]).reshape(8, 128).T
    gwt = np.ascontiguousarray(np.asarray(inputs["L1_ffn_gate_w"]).T)
    common = {
        "c_pool": consts["c_pool"], "c_trild": consts["c_trild"],
        "c_sheadd": consts["c_sheadd"], "c_ucold": consts["c_ucold"],
        "c_sel8d": sel8, "c_sel16d": sel16, "c_identfd": ident,
        "w_q": np.asarray(inputs["L0_attn_Wq"]),
        "w_k": np.asarray(inputs["L0_attn_Wk"]),
        "w_v": np.asarray(inputs["L0_attn_Wv"]),
        "w_o": np.asarray(inputs["L0_attn_Wo"]),
        "w_f": np.asarray(inputs["L0_attn_W_feat"]),
        "f_wgd": np.asarray(inputs["L0_ffn_Wg"]),
        "f_bgd": np.asarray(inputs["L0_ffn_bg"]).reshape(1, 8),
        "f_w1d": np.asarray(inputs["L0_ffn_W1"]),
        "f_b1d": b1r,
        "f_w2d": np.asarray(inputs["L0_ffn_W2"]),
        "f_b2d": np.asarray(inputs["L0_ffn_b2"]),
        "gw_qd": np.asarray(inputs["L1_attn_Wq"]),
        "gw_kd": np.asarray(inputs["L1_attn_Wk"]),
        "gw_vd": np.asarray(inputs["L1_attn_Wv"]),
        "gw_od": np.asarray(inputs["L1_attn_Wo"]),
        "g_bqkd": np.ascontiguousarray(np.concatenate([bq, bk], axis=1)),
        "g_bvrd": np.asarray(inputs["L1_attn_bv"]).reshape(1, D),
        "g_bolsd": np.ascontiguousarray(np.concatenate([ls, bo], axis=1)),
        "m_gwtd": gwt,
        "m_egd": np.asarray(inputs["L1_ffn_expert_gate"]),
        "m_eud": np.asarray(inputs["L1_ffn_expert_up"]),
        "m_edd": np.asarray(inputs["L1_ffn_expert_down"]),
        "m_sgd": np.asarray(inputs["L1_ffn_shared_gate"]),
        "m_sud": np.asarray(inputs["L1_ffn_shared_up"]),
        "m_sdd": np.asarray(inputs["L1_ffn_shared_down"]),
    }
    in_maps = []
    for core in range(8):
        b, half = core // 2, core % 2
        m = dict(common)
        m["x_own"] = np.ascontiguousarray(x[b, half * TOWN:(half + 1) * TOWN])
        m["x_pre"] = np.ascontiguousarray(x[b, 0:TOWN])
        m["c_flagd"] = np.full((128, 1), float(half), np.float32)
        in_maps.append(m)
    return in_maps


def kernel(**inputs):
    in_maps = fused_in_maps(inputs)
    nc = build_fused()
    res = run_bass_kernel_spmd(nc, in_maps, core_ids=list(range(8)))
    out = np.zeros((4, 8192, D), np.float32)
    for core in range(8):
        b, half = core // 2, core % 2
        out[b, half * TOWN:(half + 1) * TOWN] = res.results[core]["out"]
    return out


# version 11
# speedup vs baseline: 1.2413x; 1.2413x over previous
import math
import os
from contextlib import ExitStack

DEV_STAGE = int(os.environ.get("DEV_STAGE", "9"))
DEV_NBLK = int(os.environ.get("DEV_NBLK", "0"))
DEV_S3 = int(os.environ.get("DEV_S3", "9"))

import numpy as np
import ml_dtypes

import concourse.bass as bass
import concourse.mybir as mybir
from concourse.bass_utils import run_bass_kernel_spmd

F32 = mybir.dt.float32
BF16 = mybir.dt.bfloat16
AF = mybir.ActivationFunctionType
ALU = mybir.AluOpType
BFD = ml_dtypes.bfloat16

D = 1024
DC = 8
TOWN = 4096
BLK = 512
NSUB = 4
NBLK = TOWN // BLK
H, HD, NF = 16, 64, 128
EPS_FAVOR = 1e-6
LN_EPS = 1e-5
LNINV = -0.5 * math.log(NF)
LOG_S = math.log(8192.0)

CONST_VALS = sorted(set([2.0, -2.0, 0.5, -0.5, -1.0, 1.0 / D, LN_EPS,
                         EPS_FAVOR, LNINV, -LOG_S, 1.0 / 32.0, 1e-20]))


class Ctx:
    def __init__(self, nc, stack):
        self.nc = nc
        self.stack = stack
        self.q = {"sync": [], "gpsimd": [], "vector": [], "scalar": [],
                  "tensor": []}
        self.sems = {}
        self.t = {}
        self.d = {}
        self._l0f_wfree = {}

    def sem(self, name):
        if name not in self.sems:
            h = self.stack.enter_context(self.nc.semaphore(name))
            self.sems[name] = [h, 0]
        return self.sems[name]

    def tick(self, name, k):
        s = self.sem(name)
        s[1] += k
        return s[1]

    def cur(self, name):
        return self.sem(name)[1]

    def emit(self, eng, fn):
        self.q[eng].append(fn)

    def wait(self, eng, semname, tgt=None):
        s = self.sem(semname)
        n = s[1] if tgt is None else tgt
        if n <= 0:
            return
        h = s[0]
        self.emit(eng, lambda e, h=h, n=n: e.wait_ge(h, n))

    def dma(self, eng, out, in_, semname, **kw):
        n = self.tick(semname, 16)
        h = self.sem(semname)[0]
        self.emit(eng, lambda e, out=out, in_=in_, h=h, kw=kw:
                  e.dma_start(out=out, in_=in_, **kw).then_inc(h, 16))
        return n

    def op(self, eng, fn, semname=None, k=1):
        if semname is None:
            self.emit(eng, fn)
            return None
        n = self.tick(semname, k)
        h = self.sem(semname)[0]
        self.emit(eng, lambda e, fn=fn, h=h, k=k: fn(e).then_inc(h, k))
        return n

    def finalize(self, block):
        for name in ("sync", "gpsimd", "vector", "scalar", "tensor"):
            fns = self.q[name]
            if not fns:
                continue
            getattr(block, name)(lambda e, fns=fns: [f(e) for f in fns])


def sbuf(c, name, shape, dtype):
    t = c.stack.enter_context(c.nc.sbuf_tensor(name, shape, dtype))
    c.t[name] = t
    return t


def psum_t(c, name, shape, dtype=F32):
    t = c.stack.enter_context(c.nc.psum_tensor(name, shape, dtype))
    c.t[name] = t
    return t


def load_w_dmajor(c, dst, w_ap, E):
    c.dma("gpsimd", dst.ap().rearrange("p (dc e) -> p dc e", e=E),
          w_ap.rearrange("(dc p) e -> p dc e", p=128), "s_wt")


def emit_ln_chunk(c, xsrc, xn_dst, lo_dst, prev_dv=0):
    st = c.t["r_stats"]
    sq = c.t["r_sq"]
    xb = c.t["r_xb"]
    f32s = c.t["r_f32"]
    c.wait("scalar", "s_dv", prev_dv)
    c.op("scalar", lambda e: e.activation(sq[:], xsrc, AF.Square,
                                          accum_out=st[:, 1:2]))
    c.op("scalar", lambda e: e.activation(xb[:], xsrc, AF.Identity,
                                          accum_out=st[:, 0:1]), "s_ac")
    c.wait("vector", "s_ac")
    n = c.op("vector", lambda e: e.tensor_scalar(st[:, 2:3], st[:, 0:1],
                                                 1.0 / D, None, ALU.mult),
             "s_dv")
    n2 = c.op("vector", lambda e: e.tensor_scalar(st[:, 3:4], st[:, 1:2],
                                                  1.0 / D, None, ALU.mult),
              "s_dv")
    c.wait("vector", "s_dv", n)
    n = c.op("vector", lambda e: e.tensor_tensor(st[:, 5:6], st[:, 2:3],
                                                 st[:, 2:3], ALU.mult),
             "s_dv")
    c.wait("vector", "s_dv", max(n, n2))
    n = c.op("vector", lambda e: e.tensor_tensor(st[:, 3:4], st[:, 3:4],
                                                 st[:, 5:6], ALU.subtract),
             "s_dv")
    c.wait("vector", "s_dv", n)
    n = c.op("vector", lambda e: e.tensor_scalar(st[:, 3:4], st[:, 3:4],
                                                 LN_EPS, None, ALU.add),
             "s_dv")
    c.wait("vector", "s_dv", n)
    c.op("vector", lambda e: e.reciprocal(st[:, 6:7], st[:, 3:4]), "s_dv")
    c.wait("scalar", "s_dv")
    c.op("scalar", lambda e: e.activation(st[:, 4:5], st[:, 6:7], AF.Sqrt),
         "s_ac")
    c.wait("vector", "s_ac")
    if lo_dst is None:
        return c.op("vector", lambda e: e.tensor_scalar(
            xn_dst, xsrc, st[:, 2:3], st[:, 4:5], ALU.subtract, ALU.mult),
            "s_dv")
    n = c.op("vector", lambda e: e.tensor_scalar(
        f32s[:], xsrc, st[:, 2:3], st[:, 4:5], ALU.subtract, ALU.mult),
        "s_dv")
    c.wait("vector", "s_dv", n)
    n = c.op("vector", lambda e: e.tensor_scalar(xn_dst, f32s[:], 0.0, None,
                                                 ALU.add), "s_dv")
    c.wait("vector", "s_dv", n)
    return c.op("vector", lambda e: e.tensor_tensor(lo_dst, f32s[:], xn_dst,
                                                    ALU.subtract), "s_dv")


def emit_res(c, x_in, y_fm, x_out, xn_out, lo_out):
    rx, xs = c.t["r_x"], c.t["r_xs"]
    ry = c.t["r_ybuf"]
    lob = c.t["r_lobuf"]
    xnb = c.t["r_xn"]
    done = {}
    prev_dv = 0
    for ch in range(TOWN // 128):
        p = ch % 2
        t0 = ch * 128
        if ch >= 2:
            c.wait("sync", "s_rst%d" % p, done[ch - 2])
        c.dma("sync", rx[:, p * D:(p + 1) * D], x_in[t0:t0 + 128, :],
              "s_rin%d" % p)
        if y_fm is not None:
            c.dma("sync", ry[:, p * D:(p + 1) * D], y_fm[0:D, t0:t0 + 128],
                  "s_rin%d" % p, transpose=True)
        c.wait("vector", "s_rin%d" % p)
        c.op("vector", lambda e, p=p, has_y=y_fm is not None: (
            e.tensor_tensor(xs[:, p * D:(p + 1) * D], rx[:, p * D:(p + 1) * D],
                            ry[:, p * D:(p + 1) * D], ALU.add) if has_y else
            e.tensor_scalar(xs[:, p * D:(p + 1) * D], rx[:, p * D:(p + 1) * D],
                            0.0, None, ALU.add)), "s_dv")
        c.wait("scalar", "s_dv")
        c.dma("scalar", x_out[t0:t0 + 128, :], xs[:, p * D:(p + 1) * D],
              "s_rst%d" % p)
        if xn_out is not None:
            xnd = xnb[:, p * D:(p + 1) * D]
            lod = lob[:, p * D:(p + 1) * D] if lo_out is not None else None
            prev_dv = emit_ln_chunk(c, xs[:, p * D:(p + 1) * D], xnd, lod,
                                    prev_dv)
            c.wait("scalar", "s_dv")
            c.dma("scalar", xn_out[t0:t0 + 128, :], xnd, "s_rst%d" % p)
            if lo_out is not None:
                c.dma("scalar", lo_out[t0:t0 + 128, :], lod, "s_rst%d" % p)
        done[ch] = c.cur("s_rst%d" % p)


def emit_l0a(c):
    wq, wk, wv, wo, wfe = (c.t["wq"], c.t["wk"], c.t["wv"], c.t["wo"],
                           c.t["wfeat"])
    P0, P1, P2, P3, P4 = (c.t["pp0"], c.t["pp1"], c.t["pp2"], c.t["pp3"],
                          c.t["pp4"])
    ST = [c.t["st0"], c.t["st1"], c.t["st2"]]
    STW = [455, 455, 130]
    PP = [P0, P1]
    PN = [P2, P3]

    xnf, khh, qhh = c.t["a_xnf"], c.t["a_khh"], c.t["a_qhh"]
    sqh, vaug = c.t["a_sqh"], c.t["a_vaug"]
    kpf, qpf, kptm = c.t["a_kpf"], c.t["a_qpf"], c.t["a_kptm"]
    sqm, am, numb = c.t["a_sqm"], c.t["a_am"], c.t["a_numb"]
    yall, yo = c.t["a_yall"], c.t["a_yo"]
    sf, sb = c.t["a_sf32"], c.t["a_sbf"]
    dnr, recb = c.t["a_dn"], c.t["a_rec"]
    sx, xnc = c.t["r_x"], c.t["r_xn"]
    tril4, shead, ucol, ones1 = (c.t["c_tril4"], c.t["c_shead"],
                                 c.t["c_ucol"], c.t["c_ones1"])
    ones1f = c.t["c_ones1f"]
    onesr, epsv = c.t["a_onesr"], c.t["a_epsv"]
    flag = c.t["c_flag"]
    xn0, x_pre, x_own, y0 = (c.d["xn0"].ap(), c.d["x_pre"].ap(),
                             c.d["x_own"].ap(), c.d["y0"].ap())
    SREG = [0, 455, 910]

    c.op("vector", lambda e: e.memset(sf[:], 0.0))
    c.op("vector", lambda e: e.memset(onesr[:], 1.0))
    c.op("vector", lambda e: e.memset(epsv[:], 0.0), "s_dv")
    c.wait("vector", "s_dv")
    c.op("vector", lambda e: e.memset(epsv[0:1, 64:65], EPS_FAVOR))
    c.op("vector", lambda e: e.memset(
        vaug.ap().rearrange("p (s hh ee) -> p s hh ee", s=4, ee=65)
        [:, :, :, 64:65], 1.0), "s_dv")
    init_dv = c.cur("s_dv")

    state_done = 0
    state_na = 0
    prev_y = {}
    ln_store = {}
    prev_ln_dv = 0
    xnf_free = 0

    def sqrows(src, slot):
        c.wait("scalar", "s_ac")
        pe_g = {}
        for g in range(8):
            p = g % 2
            if g >= 2:
                c.wait("scalar", "s_pe", pe_g[g - 2])
            c.op("scalar", lambda e, g=g, p=p: e.activation(
                sqh[0:64, p * 2 * BLK:(p + 1) * 2 * BLK],
                src[0:64, 2 * g * BLK:2 * (g + 1) * BLK], AF.Square), "s_ac")
            c.wait("tensor", "s_ac")
            for hh in range(2):
                h = 2 * g + hh
                pe_g[g] = c.op("tensor", lambda e, h=h, p=p, hh=hh: e.matmul(
                    P4[0:16, 0:BLK], ucol[0:64, h * 16:h * 16 + 16],
                    sqh[0:64, (p * 2 + hh) * BLK:(p * 2 + hh + 1) * BLK],
                    start=(h == 0), stop=(h == H - 1)), "s_pe")
        c.wait("scalar", "s_pe")
        return c.op("scalar", lambda e, slot=slot: e.activation(
            sqm[:, slot * BLK:(slot + 1) * BLK], P4[0:16, 0:BLK],
            AF.Identity, scale=-0.5), "s_ac")

    blk_list = list(range(2 * NBLK))
    if DEV_NBLK:
        blk_list = list(range(NBLK, NBLK + DEV_NBLK))
    for blk in blk_list:
        own = blk >= NBLK
        xsrc_d = x_own if own else x_pre
        t0 = (blk - NBLK) * BLK if own else blk * BLK
        g0 = blk * BLK

        for j in range(NSUB):
            p = j % 2
            c.wait("sync", "s_dv", prev_ln_dv)
            c.dma("sync", sx[:, p * D:(p + 1) * D],
                  xsrc_d[t0 + j * 128:t0 + j * 128 + 128, :], "s_in%d" % p)
            c.wait("scalar", "s_in%d" % p)
            c.wait("vector", "s_in%d" % p)
            c.wait("vector", "s_xn%d" % p)
            prev_ln_dv = emit_ln_chunk(c, sx[:, p * D:(p + 1) * D],
                                       xnc[:, p * D:(p + 1) * D], None,
                                       prev_ln_dv)
            c.wait("scalar", "s_dv")
            c.dma("scalar", xn0[g0 + j * 128:g0 + j * 128 + 128, :],
                  xnc[:, p * D:(p + 1) * D], "s_xn%d" % p)
        c.wait("sync", "s_xn0")
        c.wait("sync", "s_xn1")
        c.wait("sync", "s_pe", xnf_free)
        for dc in range(DC):
            c.dma("sync", xnf[:, dc * BLK:(dc + 1) * BLK],
                  xn0[g0:g0 + BLK, dc * 128:(dc + 1) * 128], "s_tr",
                  transpose=True)

        if DEV_STAGE < 2:
            continue
        c.wait("tensor", "s_tr")
        if blk == 0:
            c.wait("tensor", "s_wt")
            c.wait("tensor", "s_dv", init_dv)

        def projh(dst, w, pp):
            ev = {}
            c.wait("tensor", "s_ac")
            for h in range(H):
                pb = pp[h % 2]
                if h >= 2:
                    c.wait("tensor", "s_ac", ev[h - 2])
                for dc in range(DC):
                    c.op("tensor", lambda e, pb=pb, w=w, dc=dc, h=h: e.matmul(
                        pb[0:64, 0:BLK],
                        w[:, dc * D + h * 64:dc * D + h * 64 + 64],
                        xnf[:, dc * BLK:(dc + 1) * BLK],
                        start=(dc == 0), stop=(dc == DC - 1)),
                        "s_pe" if dc == DC - 1 else None)
                c.wait("scalar", "s_pe")
                ev[h] = c.op("scalar", lambda e, pb=pb, dst=dst, h=h: e.copy(
                    dst[0:64, h * BLK:(h + 1) * BLK], pb[0:64, 0:BLK]), "s_ac")

        projh(khh, wk, PP)
        if own:
            projh(qhh, wq, PP)

        vev = {}
        c.wait("tensor", "s_ac")
        for g in range(8):
            half, j = g // 4, g % 4
            pb = PN[g % 2]
            if g >= 2:
                c.wait("tensor", "s_ac", vev[g - 2])
            for dc in range(DC):
                c.op("tensor", lambda e, pb=pb, dc=dc, half=half, j=j: e.matmul(
                    pb[:, 0:512],
                    xnf[:, dc * BLK + j * 128:dc * BLK + j * 128 + 128],
                    wv[:, dc * D + half * 512:dc * D + half * 512 + 512],
                    start=(dc == 0), stop=(dc == DC - 1)),
                    "s_pe" if dc == DC - 1 else None)
            c.wait("scalar", "s_pe")
            vdst = vaug.ap().rearrange("p (s hh ee) -> p s hh ee", s=4, ee=65)[
                :, j, half * 8:half * 8 + 8, 0:64]
            vev[g] = c.op("scalar", lambda e, pb=pb, vdst=vdst: e.copy(
                vdst, pb[:, 0:512].rearrange("p (hh ee) -> p hh ee", ee=64)),
                "s_ac")
        xnf_free = c.cur("s_pe")

        if DEV_STAGE < 3:
            continue
        sqrows(khh, 0)
        c.wait("tensor", "s_ac")
        if DEV_S3 < 2:
            continue

        def featmaps(srcf, dstf, slot):
            ev = {}
            c.wait("tensor", "s_ac")
            for h in range(H):
                pb = PP[h % 2]
                if h >= 2:
                    c.wait("tensor", "s_ac", ev[h - 2])
                c.op("tensor", lambda e, pb=pb, h=h, slot=slot: e.matmul(
                    pb[:, 0:BLK], shead[:, h * 128:h * 128 + 128],
                    sqm[:, slot * BLK:(slot + 1) * BLK],
                    start=True, stop=False))
                c.op("tensor", lambda e, pb=pb, h=h, srcf=srcf: e.matmul(
                    pb[:, 0:BLK], wfe[0:64, 0:128],
                    srcf[0:64, h * BLK:(h + 1) * BLK],
                    start=False, stop=True), "s_pe")
                c.wait("scalar", "s_pe")
                ev[h] = c.op("scalar", lambda e, pb=pb, h=h, dstf=dstf:
                             e.activation(dstf[:, h * BLK:(h + 1) * BLK],
                                          pb[:, 0:BLK], AF.Exp, bias=LNINV),
                             "s_ac")

        if own:
            featmaps(khh, kpf, 0)
        if DEV_S3 < 4:
            continue
        if own:
            sqrows(qhh, 1)
            c.wait("tensor", "s_ac")
            featmaps(qhh, qpf, 1)

        if DEV_STAGE < 4:
            continue
        c.wait("tensor", "s_ac")
        last_ymult = 0
        for j in range(NSUB):
            c.wait("tensor", "s_dv", last_ymult)
            for qd in range(4):
                c.wait("tensor", "s_ac")
                for hh in range(4):
                    h = 4 * qd + hh
                    c.op("tensor", lambda e, h=h, hh=hh, j=j: e.matmul(
                        P4[:, hh * 128:(hh + 1) * 128],
                        sqm[:, j * 128:(j + 1) * 128],
                        shead[:, h * 128:h * 128 + 128],
                        start=True, stop=False))
                    c.op("tensor", lambda e, h=h, hh=hh, j=j: e.matmul(
                        P4[:, hh * 128:(hh + 1) * 128],
                        khh[0:64, h * BLK + j * 128:h * BLK + j * 128 + 128],
                        wfe[0:64, 0:128], start=False, stop=True),
                        "s_pe" if hh == 3 else None)
                c.wait("scalar", "s_pe")
                c.op("scalar", lambda e, qd=qd: e.activation(
                    kptm[:, qd * 512:(qd + 1) * 512], P4[:, 0:512], AF.Exp,
                    bias=LNINV), "s_ac")
            c.wait("tensor", "s_ac")
            if own:
                PACKS = ((0, 2, 4, 6), (1, 3, 5, 7),
                         (8, 10, 12, 14), (9, 11, 13, 15))
                am_done, numb_done, rec_done, y_done = {}, {}, {}, {}
                pe_a, pe_n, pe_y = {}, {}, {}

                def eAp(p):
                    pa = PP[p % 2]
                    if p >= 2:
                        c.wait("tensor", "s_dv", am_done[p - 2])
                    for i, h in enumerate(PACKS[p]):
                        pe_a[p] = c.op(
                            "tensor", lambda e, pa=pa, h=h, i=i, j=j: e.matmul(
                                pa[:, i * 128:(i + 1) * 128],
                                kpf[:, h * BLK + j * 128:
                                    h * BLK + j * 128 + 128],
                                qpf[:, h * BLK + j * 128:
                                    h * BLK + j * 128 + 128],
                                start=True, stop=True),
                            "s_pe" if i == 3 else None)

                def eAm(p):
                    pa = PP[p % 2]
                    c.wait("vector", "s_pe", pe_a[p])
                    am_done[p] = c.op("vector", lambda e, pa=pa, p=p:
                                      e.tensor_tensor(
                                          am[:, (p % 2) * 512:
                                             (p % 2) * 512 + 512],
                                          pa[:, 0:512], tril4[:], ALU.mult),
                                      "s_dv")

                def eNum(p):
                    pn = PN[p % 2]
                    c.wait("tensor", "s_dv", am_done[p])
                    if p == 0:
                        c.wait("tensor", "s_ac", state_done)
                    if p >= 2:
                        c.wait("tensor", "s_ac", numb_done[p - 2])
                    for i, h in enumerate(PACKS[p]):
                        vsl = vaug[:, j * 1040 + h * 65:
                                   j * 1040 + (h + 1) * 65]
                        c.op("tensor", lambda e, pn=pn, h=h, i=i, j=j:
                             e.matmul(
                                 pn[0:65, i * 128:(i + 1) * 128],
                                 sb[:, h * 65:(h + 1) * 65],
                                 qpf[:, h * BLK + j * 128:
                                     h * BLK + j * 128 + 128],
                                 start=True, stop=False))
                        c.op("tensor", lambda e, pn=pn, i=i, vsl=vsl, p=p:
                             e.matmul(
                                 pn[0:65, i * 128:(i + 1) * 128], vsl,
                                 am[:, (p % 2) * 512 + i * 128:
                                    (p % 2) * 512 + (i + 1) * 128],
                                 start=False, stop=False))
                        pe_n[p] = c.op("tensor", lambda e, pn=pn, i=i:
                                       e.matmul(
                                           pn[0:65, i * 128:(i + 1) * 128],
                                           epsv[0:1, 0:65],
                                           onesr[0:1, 0:128],
                                           start=False, stop=True),
                                       "s_pe" if i == 3 else None)

                def eDs(p):
                    for i, h in enumerate(PACKS[p]):
                        sth, sto = ST[h // 7], (h % 7) * 65
                        vsl = vaug[:, j * 1040 + h * 65:
                                   j * 1040 + (h + 1) * 65]
                        c.op("tensor", lambda e, sth=sth, sto=sto, h=h,
                             vsl=vsl: e.matmul(
                                 sth[:, sto:sto + 65],
                                 kptm[:, h * 128:(h + 1) * 128],
                                 vsl, start=True, stop=True),
                             "s_pe" if i == 3 else None)

                def eRec(p):
                    pn = PN[p % 2]
                    c.wait("scalar", "s_pe", pe_n[p])
                    if p >= 2:
                        c.wait("scalar", "s_pe", pe_y[p - 2])
                    c.op("scalar", lambda e, pn=pn: e.activation(
                        dnr[0:1, 0:512], pn[64:65, 0:512], AF.Ln), "s_ac")
                    c.wait("scalar", "s_ac")
                    rec_done[p] = c.op("scalar", lambda e, p=p: e.activation(
                        recb[0:1, (p % 2) * 512:(p % 2) * 512 + 512],
                        dnr[0:1, 0:512], AF.Exp, scale=-1.0), "s_ac")

                def eYbc(p):
                    c.wait("tensor", "s_ac", rec_done[p])
                    if p >= 1:
                        c.wait("tensor", "s_dv", y_done[p - 1])
                    pe_y[p] = c.op("tensor", lambda e, p=p: e.matmul(
                        P4[0:64, 0:512], ones1[0:1, 0:64],
                        recb[0:1, (p % 2) * 512:(p % 2) * 512 + 512],
                        start=True, stop=True), "s_pe")

                def eNumbEv(p):
                    pn = PN[p % 2]
                    c.wait("scalar", "s_pe", pe_n[p])
                    yw = y_done[p - 2] if p >= 2 else prev_y.get(p + 2, 0)
                    c.wait("scalar", "s_dv", yw)
                    numb_done[p] = c.op("scalar", lambda e, pn=pn, p=p:
                                        e.copy(
                                            numb[:, (p % 2) * 512:
                                                 (p % 2) * 512 + 512],
                                            pn[0:64, 0:512]), "s_ac")

                def eYm(p):
                    c.wait("vector", "s_ac", numb_done[p])
                    c.wait("vector", "s_pe", pe_y[p])
                    r0 = (p % 2) * 64
                    c0 = (p // 2) * 4
                    ydst = yall[r0:r0 + 64, :].rearrange(
                        "p (cb b) -> p cb b", b=BLK)[
                        :, c0:c0 + 4, j * 128:j * 128 + 128]
                    nsrc = numb[0:64, (p % 2) * 512:
                                (p % 2) * 512 + 512].rearrange(
                        "p (cb b) -> p cb b", b=128)
                    psrc = P4[0:64, 0:512].rearrange(
                        "p (cb b) -> p cb b", b=128)
                    y_done[p] = c.op("vector", lambda e, ydst=ydst, nsrc=nsrc,
                                     psrc=psrc: e.tensor_tensor(
                                         ydst, nsrc, psrc, ALU.mult), "s_dv")

                eAp(0); eAm(0)
                eAp(1); eAm(1)
                eNum(0); eDs(0); eRec(0); eNumbEv(0)
                eAp(2); eAm(2)
                eYbc(0)
                eNum(1); eDs(1); eRec(1); eNumbEv(1)
                eYm(0)
                eAp(3); eAm(3)
                eYbc(1)
                eNum(2); eDs(2); eRec(2); eNumbEv(2)
                eYm(1)
                eYbc(2)
                eNum(3); eDs(3); eRec(3); eNumbEv(3)
                eYm(2)
                eYbc(3)
                eYm(3)
                last_ymult = y_done[3]
                prev_y = {2: y_done[2], 3: y_done[3]}
            else:
                c.wait("tensor", "s_dv", state_na)
                for h in range(H):
                    sth, sto = ST[h // 7], (h % 7) * 65
                    vsl = vaug[:, j * 1040 + h * 65:j * 1040 + (h + 1) * 65]
                    c.op("tensor", lambda e, sth=sth, sto=sto, h=h, j=j,
                         vsl=vsl: e.matmul(
                             sth[:, sto:sto + 65],
                             kptm[:, h * 128:(h + 1) * 128],
                             vsl, start=True, stop=True),
                         "s_pe" if h == H - 1 else None)
            c.wait("vector", "s_pe")
            c.wait("vector", "s_dv", state_na)
            for si in range(3):
                state_na = c.op("vector", lambda e, si=si: e.tensor_tensor(
                    sf[:, SREG[si]:SREG[si] + STW[si]],
                    sf[:, SREG[si]:SREG[si] + STW[si]],
                    ST[si][:, 0:STW[si]], ALU.add), "s_dv")
            if own and not (blk == 2 * NBLK - 1 and j == NSUB - 1):
                c.wait("scalar", "s_dv", state_na)
                state_done = c.op("scalar", lambda e: e.copy(sb[:], sf[:]),
                                  "s_ac")

        if blk == NBLK - 1:
            c.wait("vector", "s_dv", state_na)
            state_na = c.op("vector", lambda e: e.tensor_scalar(
                sf[:], sf[:], flag[:, 0:1], None, ALU.mult), "s_dv")
            c.wait("scalar", "s_dv", state_na)
            state_done = c.op("scalar", lambda e: e.copy(sb[:], sf[:]),
                              "s_ac")

        if DEV_STAGE < 5:
            continue
        if own:
            c.wait("tensor", "s_dv")
            c.wait("scalar", "s_yst0")
            c.wait("scalar", "s_yst1")
            yst = {}
            for oc in range(DC):
                pb = PP[oc % 2]
                if oc >= 2:
                    c.wait("tensor", "s_ac")
                for ec in range(DC):
                    c.op("tensor", lambda e, pb=pb, oc=oc, ec=ec: e.matmul(
                        pb[:, 0:BLK],
                        wo[:, ec * D + oc * 128:ec * D + oc * 128 + 128],
                        yall[:, ec * BLK:(ec + 1) * BLK],
                        start=(ec == 0), stop=(ec == DC - 1)),
                        "s_pe" if ec == DC - 1 else None)
                c.wait("scalar", "s_pe")
                if oc >= 2:
                    c.wait("scalar", "s_yst%d" % (oc % 2), yst[oc - 2])
                nev = c.op("scalar", lambda e, pb=pb, oc=oc: e.copy(
                    yo[:, (oc % 2) * BLK:(oc % 2) * BLK + BLK], pb[:, 0:BLK]),
                    "s_ac")
                c.wait("scalar", "s_ac", nev)
                yst[oc] = c.dma("scalar",
                                y0[oc * 128:(oc + 1) * 128, t0:t0 + BLK],
                                yo[:, (oc % 2) * BLK:(oc % 2) * BLK + BLK],
                                "s_yst%d" % (oc % 2))




TT = 1024
NTT = TOWN // TT


def emit_l0f(c):
    P0, P1, P2, P3, P4 = (c.t["pp0"], c.t["pp1"], c.t["pp2"], c.t["pp3"],
                          c.t["pp4"])
    ST0, ST1 = c.t["st0"], c.t["st1"]
    PP, PN = [P0, P1], [P2, P3]
    fxh, fxl = c.t["f_xh"], c.t["f_xl"]
    acc = c.t["f_acc"]
    w1s, w2s, b1r = c.t["f_w1"], c.t["f_w2"], c.t["f_b1"]
    hsb, hms, ybf = c.t["f_h"], c.t["f_hm"], c.t["f_y"]
    mkf = c.t["f_mk"]
    lg, mx, mk32 = c.t["f_lg"], c.t["f_mx"], c.t["f_mk32"]
    wgh, wgl, wgf, bgb = c.t["f_wgh"], c.t["f_wgl"], c.t["f_wgf"], c.t["f_bgb"]
    b2b, b2f, sel8 = c.t["f_b2b"], c.t["f_b2f"], c.t["c_sel8"]
    identf = c.t["c_identf"]
    ones1 = c.t["c_ones1"]
    xn1, xn1lo = c.d["xn1"].ap(), c.d["xn1lo"].ap()
    w1d, w2d = c.d["f_w1d"].ap(), c.d["f_w2d"].ap()
    y1 = c.d["y1"].ap()

    c.dma("sync", wgf[:].rearrange("p (dc e) -> p dc e", e=8),
          c.d["f_wgd"].ap().rearrange("(dc p) e -> p dc e", p=128), "s_ld")
    c.dma("sync", c.t["f_bgf"][:], c.d["f_bgd"].ap(), "s_ld")
    c.dma("sync", b2f[:], c.d["f_b2d"].ap(), "s_ld")
    c.dma("sync", b1r[:], c.d["f_b1d"].ap(), "s_ld")
    c.dma("gpsimd", sel8[:], c.d["c_sel8d"].ap(), "s_wt")
    c.dma("sync", identf[:], c.d["c_identfd"].ap(), "s_ld")
    c.wait("vector", "s_ld")
    c.wait("scalar", "s_ld")
    c.wait("vector", "s_wt")
    n = c.op("vector", lambda e: e.tensor_scalar(wgh[:], wgf[:], 0.0, None,
                                                 ALU.add), "s_dv")
    c.wait("vector", "s_dv", n)
    c.op("vector", lambda e: e.tensor_tensor(wgl[:], wgf[:], wgh[:],
                                             ALU.subtract), "s_dv")
    n = c.op("vector", lambda e: e.tensor_scalar(b2b[:], b2f[:], 0.0, None,
                                                 ALU.add), "s_dv")
    c.op("vector", lambda e: e.memset(c.t["c_ones1f"][:], 1.0), "s_dv")
    c.wait("tensor", "s_dv")
    c.wait("tensor", "s_ld")
    n = c.op("tensor", lambda e: e.matmul(
        ST1[0:128, 0:8], c.t["c_ones1f"][0:1, :], c.t["f_bgf"][0:1, :],
        start=True, stop=True), "s_pe")
    c.wait("vector", "s_pe", n)
    nbg = c.op("vector", lambda e: e.tensor_scalar(
        bgb[:], ST1[0:128, 0:8], 0.0, None, ALU.add), "s_dv")

    pe_fx_free = 0
    last_hm = 0
    conv_dv = 0
    last_store = {0: 0, 1: 0}
    ntt = int(os.environ.get("DEV_NTT", "0")) or NTT
    for tt in range(ntt):
        t0 = tt * TT
        c.wait("sync", "s_pe", pe_fx_free)
        for dc in range(DC):
            c.dma("sync", fxh[:, dc * TT:(dc + 1) * TT],
                  xn1[t0:t0 + TT, dc * 128:(dc + 1) * 128], "s_tr",
                  transpose=True)
            c.dma("sync", fxl[:, dc * TT:(dc + 1) * TT],
                  xn1lo[t0:t0 + TT, dc * 128:(dc + 1) * 128], "s_tr",
                  transpose=True)

        c.wait("tensor", "s_tr")
        if tt == 0:
            c.wait("tensor", "s_dv", nbg)
        lgf_done = nbg
        tr_evac = 0
        for ch in range(TT // 128):
            c.wait("tensor", "s_dv", lgf_done)
            mms = []
            for src, wgx in ((fxh, wgh), (fxh, wgl), (fxl, wgh)):
                for dc in range(DC):
                    mms.append((src, wgx, dc))
            for i, (src, wgx, dc) in enumerate(mms):
                c.op("tensor", lambda e, src=src, wgx=wgx, dc=dc, ch=ch,
                     i=i, last=(i == len(mms) - 1): e.matmul(
                         ST0[0:128, 0:8],
                         src[:, dc * TT + ch * 128:dc * TT + ch * 128 + 128],
                         wgx[:, dc * 8:(dc + 1) * 8],
                         start=(i == 0), stop=last),
                     "s_pe" if i == len(mms) - 1 else None)
            c.wait("vector", "s_pe")
            n = c.op("vector", lambda e: e.tensor_tensor(
                lg[:], ST0[0:128, 0:8], bgb[:], ALU.add), "s_dv")
            lgf_done = n
            c.wait("vector", "s_dv", n)
            n = c.op("vector", lambda e: e.reduce_max(
                mx[:], lg[:], axis=mybir.AxisListType.X), "s_dv")
            c.wait("vector", "s_dv", n)
            n = c.op("vector", lambda e: e.tensor_scalar(
                mk32[:], lg[:], mx[:, 0:1], None, ALU.is_ge), "s_dv")
            c.wait("tensor", "s_dv", n)
            c.wait("tensor", "s_ac", tr_evac)
            n = c.op("tensor", lambda e: e.transpose(
                ST1[0:8, 128:256], mk32[:], identf[:]), "s_pe")
            c.wait("scalar", "s_pe", n)
            tr_evac = c.op("scalar", lambda e, ch=ch: e.copy(
                mkf[0:8, ch * 128:(ch + 1) * 128], ST1[0:8, 128:256]), "s_ac")

        for u in range(16):
            eix, hf = u // 2, u % 2
            pu = u % 2
            c.wait("gpsimd", "s_pe", c._l0f_wfree.get(pu, 0))
            c.dma("gpsimd",
                  w1s.ap().rearrange("p (b dc i) -> p b dc i", b=2, i=512)
                  [:, pu], w1d[eix].rearrange("(dc p) i -> p dc i", p=128)
                  [:, :, hf * 512:(hf + 1) * 512], "s_w%d" % pu)
            c.dma("gpsimd",
                  w2s.ap().rearrange("p (b ic o) -> p b ic o", b=2, o=1024)
                  [:, pu],
                  w2d[eix, hf * 512:(hf + 1) * 512, :]
                  .rearrange("(ic p) o -> p ic o", p=128), "s_w%d" % pu)
            c.wait("tensor", "s_w%d" % pu)
            if u < 2:
                c.wait("tensor", "s_ac", tr_evac)
            last_mm = 0
            for sub in range(TT // 512):
                hev = {}
                c.wait("tensor", "s_ac")
                for ic in range(4):
                    pb = PP[ic % 2]
                    if ic >= 2:
                        c.wait("tensor", "s_ac", hev[ic - 2])
                    for dc in range(DC):
                        c.op("tensor", lambda e, pb=pb, ic=ic, dc=dc, pu=pu,
                             sub=sub: e.matmul(
                                 pb[:, 0:512],
                                 w1s[:, (pu * 8 + dc) * 512 + ic * 128:
                                     (pu * 8 + dc) * 512 + ic * 128 + 128],
                                 fxh[:, dc * TT + sub * 512:
                                     dc * TT + sub * 512 + 512],
                                 start=(dc == 0), stop=(dc == DC - 1)),
                             "s_pe" if dc == DC - 1 else None)
                    c.wait("scalar", "s_pe")
                    hev[ic] = c.op("scalar", lambda e, pb=pb, ic=ic, u=u:
                                   e.activation(
                                       hsb[:, ic * 512:(ic + 1) * 512],
                                       pb[:, 0:512], AF.Gelu,
                                       bias=b1r[:, u * 4 + ic:u * 4 + ic + 1]),
                                   "s_ac")
                c.wait("tensor", "s_dv", last_hm)
                pe_mb = c.op("tensor", lambda e, eix=eix, sub=sub: e.matmul(
                    P4[:, 0:512], sel8[:, eix * 128:(eix + 1) * 128],
                    mkf[0:8, sub * 512:(sub + 1) * 512],
                    start=True, stop=True), "s_pe")
                c.wait("vector", "s_pe", pe_mb)
                c.wait("vector", "s_ac")
                for ic in range(4):
                    last_hm = c.op("vector", lambda e, ic=ic: e.tensor_tensor(
                        hms[:, ic * 512:(ic + 1) * 512],
                        hsb[:, ic * 512:(ic + 1) * 512], P4[:, 0:512],
                        ALU.mult), "s_dv")
                c.wait("tensor", "s_dv", last_hm)
                aev = {}
                for oc in range(DC):
                    pd = PN[oc % 2]
                    if oc >= 2:
                        c.wait("tensor", "s_dv", aev[oc - 2])
                    for ic in range(4):
                        last_mm = c.op(
                            "tensor", lambda e, pd=pd, oc=oc, ic=ic, pu=pu:
                            e.matmul(
                                pd[:, 0:512],
                                w2s[:, (pu * 4 + ic) * 1024 + oc * 128:
                                    (pu * 4 + ic) * 1024 + oc * 128 + 128],
                                hms[:, ic * 512:(ic + 1) * 512],
                                start=(ic == 0), stop=(ic == 3)),
                            "s_pe" if ic == 3 else None)
                    c.wait("vector", "s_pe")
                    if u == 0:
                        c.wait("vector", "s_dv", conv_dv)
                    dst = acc[:, oc * TT + sub * 512:oc * TT + sub * 512 + 512]
                    aev[oc] = c.op("vector", lambda e, pd=pd, dst=dst,
                                   first=(u == 0): (
                                       e.tensor_scalar(dst, pd[:, 0:512], 0.0,
                                                       None, ALU.add)
                                       if first else
                                       e.tensor_tensor(dst, dst, pd[:, 0:512],
                                                       ALU.add)), "s_dv")
            c._l0f_wfree[pu] = last_mm

        for sub in range(TT // 512):
            for oc in range(DC):
                pd = PN[oc % 2]
                c.wait("tensor", "s_dv")
                n = c.op("tensor", lambda e, pd=pd, oc=oc, sub=sub: e.matmul(
                    pd[:, 0:512], b2b[0:8, oc * 128:(oc + 1) * 128],
                    mkf[0:8, sub * 512:(sub + 1) * 512],
                    start=True, stop=True), "s_pe")
                c.wait("vector", "s_pe", n)
                dst = acc[:, oc * TT + sub * 512:oc * TT + sub * 512 + 512]
                c.op("vector", lambda e, pd=pd, dst=dst: e.tensor_tensor(
                    dst, dst, pd[:, 0:512], ALU.add), "s_dv")
        pe_fx_free = c.cur("s_pe")
        c.wait("vector", "s_dv")
        for oc in range(DC):
            p = oc % 2
            c.wait("vector", "s_yst%d" % p, last_store[p])
            conv_dv = c.op("vector", lambda e, oc=oc, p=p: e.tensor_scalar(
                ybf[:, p * TT:(p + 1) * TT], acc[:, oc * TT:(oc + 1) * TT],
                0.0, None, ALU.add), "s_dv")
            c.wait("scalar", "s_dv", conv_dv)
            last_store[p] = c.dma(
                "scalar", y1[oc * 128:(oc + 1) * 128, t0:t0 + TT],
                ybf[:, p * TT:(p + 1) * TT], "s_yst%d" % p)


def emit_l1a(c):
    P0, P1, P2, P3, P4 = (c.t["pp0"], c.t["pp1"], c.t["pp2"], c.t["pp3"],
                          c.t["pp4"])
    ST0 = c.t["st0"]
    PP = [P0, P1]
    xnf = c.t["g_xnf"]
    kraw = c.t["g_kraw"]
    ksq = c.t["g_ksq"]
    knb = c.t["g_kn"]
    vsb = c.t["g_v"]
    rows = c.t["g_rows"]
    arow, brow = c.t["g_ab"][0:1, 0:512], c.t["g_ab"][0:1, 512:1024]
    bqk = c.t["g_bqk"]
    bvr = c.t["g_bvr"]
    bols = c.t["g_bols"]
    wq, wk, wv, wo = c.t["gw_q"], c.t["gw_k"], c.t["gw_v"], c.t["gw_o"]
    ones1, ones128 = c.t["c_ones1"], c.t["c_ones128"]
    xn2 = c.d["xn2i"].ap()
    xn2f = c.d["xn2f"].ap()
    kn_d, v_d, qn_d = c.d["kn_d"].ap(), c.d["v_d"].ap(), c.d["qn_d"].ap()
    ao_tm, y2 = c.d["ao_tm"].ap(), c.d["y2"].ap()

    c.op("vector", lambda e: e.memset(ones128[:], 1.0), "s_dv")
    c.wait("vector", "s_ld")
    n = c.op("vector", lambda e: e.tensor_tensor(
        bols[:, 8:16], bols[:, 8:16], bols[:, 0:8], ALU.mult), "s_dv")

    pe_xnf_free = 0
    st_par = {0: 0, 1: 0}
    for til in range(24):
        qpass = til >= 16
        src = xn2 if qpass else xn2f
        t0 = (til - 16) * BLK if qpass else til * BLK
        g0 = til * BLK
        c.wait("sync", "s_pe", pe_xnf_free)
        for dc in range(DC):
            c.dma("sync", xnf[:, dc * BLK:(dc + 1) * BLK],
                  src[t0:t0 + BLK, dc * 128:(dc + 1) * 128], "s_tr",
                  transpose=True)
        c.wait("tensor", "s_tr")
        if til == 0:
            c.wait("tensor", "s_wt")
            c.wait("tensor", "s_dv", n)

        def qkproj(dst, w, bcol):
            ev = {}
            c.wait("tensor", "s_ac")
            c.wait("scalar", "s_dv")
            for ec in range(DC):
                pb = PP[ec % 2]
                if ec >= 2:
                    c.wait("tensor", "s_ac", ev[ec - 2])
                for dc in range(DC):
                    c.op("tensor", lambda e, pb=pb, w=w, dc=dc, ec=ec: e.matmul(
                        pb[:, 0:BLK],
                        w[:, dc * D + ec * 128:dc * D + ec * 128 + 128],
                        xnf[:, dc * BLK:(dc + 1) * BLK],
                        start=(dc == 0), stop=(dc == DC - 1)),
                        "s_pe" if dc == DC - 1 else None)
                c.wait("scalar", "s_pe")
                ev[ec] = c.op("scalar", lambda e, pb=pb, dst=dst, ec=ec,
                              bcol=bcol: e.activation(
                                  dst[:, ec * BLK:(ec + 1) * BLK],
                                  pb[:, 0:BLK], AF.Identity,
                                  bias=bqk[:, bcol + ec:bcol + ec + 1]),
                              "s_ac")
            return ev

        def lnrows(src_sb, dst_sb, dst_d, grow):
            c.wait("vector", "s_st0")
            c.wait("vector", "s_st1")
            c.wait("scalar", "s_ac")
            nsq = c.op("scalar", lambda e: e.activation(ksq[:], src_sb[:],
                                                        AF.Square), "s_ac")
            c.wait("tensor", "s_ac", nsq)
            c.wait("tensor", "s_dv")
            for ec in range(DC):
                c.op("tensor", lambda e, ec=ec: e.matmul(
                    P2[0:1, 0:BLK], ones128[:, 0:1],
                    src_sb[:, ec * BLK:(ec + 1) * BLK],
                    start=(ec == 0), stop=(ec == DC - 1)),
                    "s_pe" if ec == DC - 1 else None)
            for ec in range(DC):
                c.op("tensor", lambda e, ec=ec: e.matmul(
                    P3[0:1, 0:BLK], ones128[:, 0:1],
                    ksq[:, ec * BLK:(ec + 1) * BLK],
                    start=(ec == 0), stop=(ec == DC - 1)),
                    "s_pe" if ec == DC - 1 else None)
            c.wait("vector", "s_pe")
            nn = c.op("vector", lambda e: e.tensor_scalar(
                rows[0:1, 0:512], P2[0:1, 0:BLK], 1.0 / D, None, ALU.mult),
                "s_dv")
            nn2 = c.op("vector", lambda e: e.tensor_scalar(
                rows[0:1, 512:1024], P3[0:1, 0:BLK], 1.0 / D, None, ALU.mult),
                "s_dv")
            c.wait("vector", "s_dv", nn)
            nn3 = c.op("vector", lambda e: e.tensor_tensor(
                c.t["g_r2"][0:1, 0:512], rows[0:1, 0:512], rows[0:1, 0:512],
                ALU.mult), "s_dv")
            c.wait("vector", "s_dv", max(nn2, nn3))
            nn = c.op("vector", lambda e: e.tensor_tensor(
                c.t["g_r2"][0:1, 512:1024], rows[0:1, 512:1024],
                c.t["g_r2"][0:1, 0:512], ALU.subtract), "s_dv")
            c.wait("vector", "s_dv", nn)
            nn = c.op("vector", lambda e: e.tensor_scalar(
                c.t["g_r2"][0:1, 512:1024], c.t["g_r2"][0:1, 512:1024],
                LN_EPS, None, ALU.add), "s_dv")
            c.wait("vector", "s_dv", nn)
            nn = c.op("vector", lambda e: e.reciprocal(
                c.t["g_r3"][0:1, 0:512], c.t["g_r2"][0:1, 512:1024]), "s_dv")
            c.wait("scalar", "s_dv", nn)
            nn = c.op("scalar", lambda e: e.activation(
                c.t["g_r3"][0:1, 512:1024], c.t["g_r3"][0:1, 0:512], AF.Sqrt),
                "s_ac")
            c.wait("vector", "s_ac", nn)
            nn = c.op("vector", lambda e: e.tensor_scalar(
                arow[:], c.t["g_r3"][0:1, 512:1024], 0.0, None, ALU.add),
                "s_dv")
            c.wait("vector", "s_dv", nn)
            nn = c.op("vector", lambda e: e.tensor_tensor(
                brow[:], rows[0:1, 0:512], arow[:], ALU.mult), "s_dv")
            c.wait("vector", "s_dv", nn)
            nb = c.op("vector", lambda e: e.tensor_scalar(
                brow[:], brow[:], -1.0, None, ALU.mult), "s_dv")
            c.wait("tensor", "s_dv", nb)
            c.op("tensor", lambda e: e.matmul(
                P2[:, 0:BLK], ones1[0:1, :], arow[:], start=True, stop=True))
            nbc = c.op("tensor", lambda e: e.matmul(
                P3[:, 0:BLK], ones1[0:1, :], brow[:], start=True, stop=True),
                "s_pe")
            c.wait("vector", "s_pe", nbc)
            last = 0
            for ec in range(DC):
                n1 = c.op("vector", lambda e, ec=ec: e.tensor_tensor(
                    ksq[:, ec * BLK:(ec + 1) * BLK],
                    src_sb[:, ec * BLK:(ec + 1) * BLK], P2[:, 0:BLK],
                    ALU.mult), "s_dv")
                c.wait("vector", "s_dv", n1)
                last = c.op("vector", lambda e, ec=ec: e.tensor_tensor(
                    dst_sb[:, ec * BLK:(ec + 1) * BLK],
                    ksq[:, ec * BLK:(ec + 1) * BLK], P3[:, 0:BLK],
                    ALU.add), "s_dv")
            c.wait("scalar", "s_dv", last)
            par = til % 2
            c.wait("scalar", "s_st%d" % par, st_par[par])
            st_par[par] = c.dma(
                "scalar",
                dst_d.rearrange("(dc p) t -> p dc t", p=128)[:, :, grow:grow + BLK],
                dst_sb[:].rearrange("p (dc t) -> p dc t", t=BLK),
                "s_st%d" % par)
            return last

        if qpass:
            qkproj(kraw, wq, 0)
            lnrows(kraw, knb, qn_d, t0)
            pe_xnf_free = c.cur("s_pe")
            continue
        qkproj(kraw, wk, 8)
        lnrows(kraw, knb, kn_d, g0)

        vev = {}
        c.wait("tensor", "s_ac")
        for g in range(8):
            half, j = g // 4, g % 4
            pb = PP[g % 2]
            if g >= 2:
                c.wait("tensor", "s_ac", vev[g - 2])
            c.op("tensor", lambda e, pb=pb, half=half: e.matmul(
                pb[:, 0:512], ones1[0:1, 0:128],
                bvr[0:1, half * 512:(half + 1) * 512],
                start=True, stop=False))
            for dc in range(DC):
                c.op("tensor", lambda e, pb=pb, dc=dc, half=half, j=j: e.matmul(
                    pb[:, 0:512],
                    xnf[:, dc * BLK + j * 128:dc * BLK + j * 128 + 128],
                    wv[:, dc * D + half * 512:dc * D + half * 512 + 512],
                    start=False, stop=(dc == DC - 1)),
                    "s_pe" if dc == DC - 1 else None)
            c.wait("scalar", "s_pe")
            vev[g] = c.op("scalar", lambda e, pb=pb, j=j, half=half: e.copy(
                vsb[:, j * 1024 + half * 512:j * 1024 + half * 512 + 512],
                pb[:, 0:512]), "s_ac")
        pe_xnf_free = c.cur("s_pe")
        par = til % 2
        c.wait("scalar", "s_ac")
        c.wait("scalar", "s_st%d" % par, st_par[par])
        st_par[par] = c.dma(
            "scalar",
            v_d[g0:g0 + BLK, :].rearrange("(s p) e -> p s e", p=128),
            vsb[:].rearrange("p (s e) -> p s e", e=1024),
            "s_st%d" % par)
    c.wait("sync", "s_st0")
    c.wait("sync", "s_st1")

    qnt = c.t["g_qnt"]
    att = c.t["g_att"]
    outb = c.t["g_outb"]
    knl = c.t["g_knl"]
    vbl = c.t["g_vbl"]
    OUTB = [P0, P1, P2, P3]
    ao_par = {0: 0, 1: 0}
    kv_free = {0: 0, 1: 0}
    out_ev = 0
    qn_free = 0
    for qt in range(16):
        s0 = qt * 256
        c.wait("sync", "s_pe", qn_free)
        c.dma("sync", qnt[:].rearrange("p (dc s) -> p dc s", s=256),
              qn_d.rearrange("(dc p) t -> p dc t", p=128)[:, :, s0:s0 + 256],
              "s_q%d" % (qt % 2))
        c.wait("tensor", "s_q%d" % (qt % 2))
        c.wait("tensor", "s_ac", out_ev)
        def emit_scores(gsub, spb_free):
            kb, sub = gsub // 4, gsub % 4
            kpar = kb % 2
            if sub == 0:
                c.wait("tensor", "s_kv%d" % kpar)
            c.wait("tensor", "s_ac", spb_free)
            spb = P4 if gsub % 2 == 0 else ST0
            for ec in range(DC):
                c.op("tensor", lambda e, spb=spb, ec=ec, sub=sub,
                     kpar=kpar: e.matmul(
                         spb[0:128, 0:256],
                         knl[:, kpar * 4096 + ec * 512 + sub * 128:
                             kpar * 4096 + ec * 512 + sub * 128 + 128],
                         qnt[:, ec * 256:(ec + 1) * 256],
                         start=(ec == 0), stop=(ec == DC - 1)),
                     "s_pe" if ec == DC - 1 else None)
            nsc = c.cur("s_pe")
            nse = c.op("scalar", lambda e, spb=spb, sub=gsub, nsc=nsc: (
                e.wait_ge(c.sem("s_pe")[0], nsc),
                e.activation(att[:, (sub % 2) * 256:(sub % 2) * 256 + 256],
                             spb[0:128, 0:256], AF.Sigmoid, bias=-LOG_S,
                             scale=1.0 / 32.0))[-1], "s_ac")
            return nse

        def emit_out(gsub, nse):
            kb, sub = gsub // 4, gsub % 4
            kpar = kb % 2
            c.wait("tensor", "s_ac", nse)
            for g in range(4):
                sc, eh = g // 2, g % 2
                c.op("tensor", lambda e, g=g, sc=sc, eh=eh, sub=sub,
                     gsub=gsub, kpar=kpar, first=(gsub == 0),
                     lastm=(gsub == 63): e.matmul(
                         OUTB[g][0:128, 0:512],
                         att[:, (gsub % 2) * 256 + sc * 128:
                             (gsub % 2) * 256 + sc * 128 + 128],
                         vbl[:, kpar * 4096 + sub * 1024 + eh * 512:
                             kpar * 4096 + sub * 1024 + eh * 512 + 512],
                         start=first, stop=lastm),
                     "s_pe" if g == 3 else None)

        def emit_kv_load(kb):
            kpar = kb % 2
            c.wait("sync", "s_pe", kv_free[kpar])
            c.dma("sync",
                  knl[:, kpar * 4096:(kpar + 1) * 4096]
                  .rearrange("p (dc t) -> p dc t", t=512),
                  kn_d.rearrange("(dc p) t -> p dc t", p=128)
                  [:, :, kb * 512:(kb + 1) * 512], "s_kv%d" % kpar)
            c.dma("sync",
                  vbl[:, kpar * 4096:(kpar + 1) * 4096]
                  .rearrange("p (s e) -> p s e", e=1024),
                  v_d[kb * 512:(kb + 1) * 512, :]
                  .rearrange("(s p) e -> p s e", p=128), "s_kv%d" % kpar)

        if qt == 0:
            emit_kv_load(0)
            emit_kv_load(1)
        prev = None
        nse_hist = {}
        for gsub in range(64):
            kb, sub = gsub // 4, gsub % 4
            nse = emit_scores(gsub, nse_hist.get(gsub - 2, 0))
            nse_hist[gsub] = nse
            if prev is not None:
                emit_out(*prev)
                pkb, psub = prev[0] // 4, prev[0] % 4
                if psub == 3:
                    kv_free[pkb % 2] = c.cur("s_pe")
                    nkb = pkb + 2 if qt < 15 else pkb + 2
                    if pkb + 2 <= 15:
                        emit_kv_load(pkb + 2)
                    elif qt < 15:
                        emit_kv_load(pkb + 2 - 16)
            prev = (gsub, nse)
        emit_out(*prev)
        kv_free[1] = c.cur("s_pe")
        if qt < 15:
            emit_kv_load(1)
        qn_free = c.cur("s_pe")
        c.wait("scalar", "s_pe")
        par = qt % 2
        c.wait("scalar", "s_ao%d" % par, ao_par[par])
        for g in range(4):
            sc, eh = g // 2, g % 2
            out_ev = c.op("scalar", lambda e, g=g, sc=sc, eh=eh, par=par:
                          e.copy(
                              outb[:, par * 2048 + sc * 1024 + eh * 512:
                                   par * 2048 + sc * 1024 + eh * 512 + 512],
                              OUTB[g][0:128, 0:512]), "s_ac")
        c.wait("scalar", "s_ac", out_ev)
        ao_par[par] = c.dma(
            "scalar",
            ao_tm[s0:s0 + 256, :].rearrange("(s p) e -> p s e", p=128),
            outb[:, par * 2048:(par + 1) * 2048]
            .rearrange("p (s e) -> p s e", e=1024), "s_ao%d" % par)
    c.wait("sync", "s_ao0")
    c.wait("sync", "s_ao1")

    aof = c.t["g_xnf"]
    ybf = c.t["g_kraw"]
    yst = {0: 0, 1: 0}
    pe_aof_free = c.cur("s_pe")
    for til in range(8):
        t0 = til * BLK
        c.wait("sync", "s_pe", pe_aof_free)
        for ec in range(DC):
            c.dma("sync", aof[:, ec * BLK:(ec + 1) * BLK],
                  ao_tm[t0:t0 + BLK, ec * 128:(ec + 1) * 128], "s_tr",
                  transpose=True)
        c.wait("tensor", "s_tr")
        c.wait("tensor", "s_ac")
        ev = {}
        for oc in range(DC):
            pb = PP[oc % 2]
            if oc >= 2:
                c.wait("tensor", "s_ac", ev[oc - 2])
            for ec in range(DC):
                c.op("tensor", lambda e, pb=pb, oc=oc, ec=ec: e.matmul(
                    pb[:, 0:BLK],
                    wo[:, ec * D + oc * 128:ec * D + oc * 128 + 128],
                    aof[:, ec * BLK:(ec + 1) * BLK],
                    start=(ec == 0), stop=(ec == DC - 1)),
                    "s_pe" if ec == DC - 1 else None)
            c.wait("scalar", "s_pe")
            par = oc % 2
            c.wait("scalar", "s_y%d" % par, yst[par])
            ev[oc] = c.op("scalar", lambda e, pb=pb, oc=oc, par=par:
                          e.activation(
                              ybf[:, par * BLK:(par + 1) * BLK], pb[:, 0:BLK],
                              AF.Identity,
                              bias=bols[:, 8 + oc:8 + oc + 1],
                              scale=bols[:, oc:oc + 1]), "s_ac")
            c.wait("scalar", "s_ac", ev[oc])
            yst[par] = c.dma("scalar",
                             y2[oc * 128:(oc + 1) * 128, t0:t0 + BLK],
                             ybf[:, par * BLK:(par + 1) * BLK], "s_y%d" % par)
        pe_aof_free = c.cur("s_pe")
    c.wait("sync", "s_y0")
    c.wait("sync", "s_y1")


def alloc_common(c):
    nc = c.nc
    dt = nc.dram_tensor
    c.d["c_pool"] = dt("c_pool", [128, len(CONST_VALS)], F32,
                       kind="ExternalInput")
    sbuf(c, "c_cpool", [128, len(CONST_VALS)], F32)
    for i, v in enumerate(CONST_VALS):
        nc.const_aps.register(F32, v, c.t["c_cpool"][:, i:i + 1])
    sbuf(c, "c_ones1", [1, 128], BF16)
    sbuf(c, "c_ones1f", [1, 128], F32)
    sbuf(c, "r_x", [128, 2 * D], F32)
    sbuf(c, "r_xs", [128, 2 * D], F32)
    sbuf(c, "r_stats", [128, 8], F32)
    sbuf(c, "r_sq", [128, D], BF16)
    sbuf(c, "r_xb", [128, D], BF16)
    sbuf(c, "r_f32", [128, D], F32)
    sbuf(c, "r_xn", [128, 2 * D], BF16)
    psum_t(c, "pp0", [128, 512])
    psum_t(c, "pp1", [128, 512])
    psum_t(c, "pp2", [128, 512])
    psum_t(c, "pp3", [128, 512])
    psum_t(c, "pp4", [128, 512])
    psum_t(c, "st0", [128, 455])
    psum_t(c, "st1", [128, 455])
    psum_t(c, "st2", [128, 130])
    c.dma("sync", c.t["c_cpool"][:], c.d["c_pool"].ap(), "s_ld")
    c.op("vector", lambda e: e.memset(c.t["c_ones1"][:], 1.0))
    c.op("vector", lambda e: e.memset(c.t["c_ones1f"][:], 1.0))


def finish(c, store_sems):
    for sname in store_sems:
        c.wait("sync", sname)
    c.finalize(c.block)


def build_program_a():
    nc = bass.Bass("TRN2", target_bir_lowering=False)
    stack = ExitStack()
    c = Ctx(nc, stack)
    with stack:
        dt = nc.dram_tensor
        c.d["x_pre"] = dt("x_pre", [TOWN, D], F32, kind="ExternalInput")
        c.d["x_own"] = dt("x_own", [TOWN, D], F32, kind="ExternalInput")
        c.d["w_q"] = dt("w_q", [D, D], F32, kind="ExternalInput")
        c.d["w_k"] = dt("w_k", [D, D], F32, kind="ExternalInput")
        c.d["w_v"] = dt("w_v", [D, D], F32, kind="ExternalInput")
        c.d["w_o"] = dt("w_o", [D, D], F32, kind="ExternalInput")
        c.d["w_f"] = dt("w_f", [HD, NF], F32, kind="ExternalInput")
        c.d["c_trild"] = dt("c_trild", [128, 128], F32, kind="ExternalInput")
        c.d["c_sheadd"] = dt("c_sheadd", [16, 16 * 128], F32,
                             kind="ExternalInput")
        c.d["c_ucold"] = dt("c_ucold", [64, 16 * 16], F32,
                            kind="ExternalInput")
        c.d["c_flagd"] = dt("c_flagd", [128, 1], F32, kind="ExternalInput")
        c.d["c_identfd"] = dt("c_identfd", [128, 128], F32,
                              kind="ExternalInput")
        c.d["xn0"] = dt("xn0", [2 * TOWN, D], BF16)
        c.d["y0"] = dt("y0", [D, TOWN], BF16)
        c.d["x1"] = dt("x1", [TOWN, D], F32, kind="ExternalOutput")
        c.d["xn1"] = dt("xn1", [TOWN, D], BF16, kind="ExternalOutput")
        c.d["xn1lo"] = dt("xn1lo", [TOWN, D], BF16, kind="ExternalOutput")

        for n in ("wq", "wk", "wv", "wo"):
            sbuf(c, n, [128, DC * D], BF16)
        sbuf(c, "wfeat", [64, 128], BF16)
        sbuf(c, "c_tril4", [128, 4 * 128], BF16)
        sbuf(c, "c_shead", [16, 16 * 128], BF16)
        sbuf(c, "c_ucol", [64, 16 * 16], BF16)
        sbuf(c, "c_flag", [128, 1], F32)
        sbuf(c, "a_xnf", [128, DC * BLK], BF16)
        sbuf(c, "a_khh", [64, H * BLK], BF16)
        sbuf(c, "a_qhh", [64, H * BLK], BF16)
        sbuf(c, "a_vaug", [128, 4 * 1040], BF16)
        sbuf(c, "a_kpf", [128, H * BLK], BF16)
        sbuf(c, "a_qpf", [128, H * BLK], BF16)
        sbuf(c, "a_kptm", [128, H * 128], BF16)
        sbuf(c, "a_sqm", [16, 2 * BLK], BF16)
        sbuf(c, "a_am", [128, 2 * 512], BF16)
        sbuf(c, "a_numb", [64, 2 * 512], BF16)
        sbuf(c, "a_yall", [128, DC * BLK], BF16)
        sbuf(c, "a_yo", [128, 2 * BLK], BF16)
        sbuf(c, "a_sbf", [128, H * 65], BF16)
        sbuf(c, "a_dn", [1, 512], F32)
        sbuf(c, "a_rec", [1, 2 * 512], BF16)
        sbuf(c, "a_onesr", [1, 512], BF16)
        sbuf(c, "a_epsv", [1, 65], BF16)
        c.t["a_sqh"] = c.t["a_yall"][0:64, 4 * BLK:8 * BLK]
        c.t["r_ybuf"] = c.t["a_yall"][:, 0:2 * D]
        c.t["r_lobuf"] = c.t["a_yall"][:, 2 * D:4 * D]

        c.block = stack.enter_context(nc.Block())
        alloc_common(c)
        c.t["a_sf32"] = c.t["r_xs"][:, 0:H * 65]
        for i in range(4):
            c.dma("gpsimd", c.t["c_tril4"][:, i * 128:(i + 1) * 128],
                  c.d["c_trild"].ap(), "s_wt")
        c.dma("gpsimd", c.t["c_shead"][:], c.d["c_sheadd"].ap(), "s_wt")
        c.dma("gpsimd", c.t["c_ucol"][:], c.d["c_ucold"].ap(), "s_wt")
        c.dma("sync", c.t["c_flag"][:], c.d["c_flagd"].ap(), "s_ld")
        load_w_dmajor(c, c.t["wq"], c.d["w_q"].ap(), D)
        load_w_dmajor(c, c.t["wk"], c.d["w_k"].ap(), D)
        load_w_dmajor(c, c.t["wv"], c.d["w_v"].ap(), D)
        load_w_dmajor(c, c.t["wo"], c.d["w_o"].ap(), D)
        c.dma("gpsimd", c.t["wfeat"][:], c.d["w_f"].ap(), "s_wt")
        for eng in ("scalar", "vector"):
            c.wait(eng, "s_wt")
            c.wait(eng, "s_ld")
        c.wait("tensor", "s_ld")

        emit_l0a(c)
        c.wait("sync", "s_yst0")
        c.wait("sync", "s_yst1")
        c.wait("vector", "s_xn0")
        c.wait("vector", "s_xn1")
        c.wait("scalar", "s_xn0")
        c.wait("scalar", "s_xn1")
        c.wait("sync", "s_pe")
        emit_res(c, c.d["x_own"].ap(), c.d["y0"].ap(), c.d["x1"].ap(),
                 c.d["xn1"].ap(), c.d["xn1lo"].ap())
        finish(c, ["s_rst0", "s_rst1"])
    return nc


def build_program_b():
    nc = bass.Bass("TRN2", target_bir_lowering=False)
    stack = ExitStack()
    c = Ctx(nc, stack)
    with stack:
        dt = nc.dram_tensor
        c.d["x1"] = dt("x1", [TOWN, D], F32, kind="ExternalInput")
        c.d["xn1"] = dt("xn1", [TOWN, D], BF16, kind="ExternalInput")
        c.d["xn1lo"] = dt("xn1lo", [TOWN, D], BF16, kind="ExternalInput")
        c.d["f_wgd"] = dt("f_wgd", [D, 8], F32, kind="ExternalInput")
        c.d["f_bgd"] = dt("f_bgd", [1, 8], F32, kind="ExternalInput")
        c.d["f_w1d"] = dt("f_w1d", [8, D, D], F32, kind="ExternalInput")
        c.d["f_b1d"] = dt("f_b1d", [128, 64], F32, kind="ExternalInput")
        c.d["f_w2d"] = dt("f_w2d", [8, D, D], F32, kind="ExternalInput")
        c.d["f_b2d"] = dt("f_b2d", [8, D], F32, kind="ExternalInput")
        c.d["c_sel8d"] = dt("c_sel8d", [8, 8 * 128], F32,
                            kind="ExternalInput")
        c.d["c_identfd"] = dt("c_identfd", [128, 128], F32,
                              kind="ExternalInput")
        c.d["y1"] = dt("y1", [D, TOWN], BF16)
        c.d["x2"] = dt("x2", [TOWN, D], F32, kind="ExternalOutput")
        c.d["xn2"] = dt("xn2", [TOWN, D], BF16, kind="ExternalOutput")

        sbuf(c, "f_xh", [128, DC * TT], BF16)
        sbuf(c, "f_xl", [128, DC * TT], BF16)
        sbuf(c, "f_acc", [128, DC * TT], F32)
        sbuf(c, "f_w1", [128, 2 * DC * 512], BF16)
        sbuf(c, "f_w2", [128, 2 * 4 * D], BF16)
        sbuf(c, "f_b1", [128, 64], F32)
        sbuf(c, "f_h", [128, 4 * 512], BF16)
        sbuf(c, "f_hm", [128, 4 * 512], BF16)
        sbuf(c, "f_y", [128, 2 * TT], BF16)
        sbuf(c, "f_mk", [8, TT], BF16)
        sbuf(c, "f_lg", [128, 8], F32)
        sbuf(c, "f_mx", [128, 1], F32)
        sbuf(c, "f_mk32", [128, 8], F32)
        sbuf(c, "f_wgh", [128, 64], BF16)
        sbuf(c, "f_wgl", [128, 64], BF16)
        sbuf(c, "f_wgf", [128, 64], F32)
        sbuf(c, "f_bgf", [1, 8], F32)
        sbuf(c, "f_bgb", [128, 8], F32)
        sbuf(c, "f_b2b", [8, D], BF16)
        sbuf(c, "f_b2f", [8, D], F32)
        sbuf(c, "c_sel8", [8, 8 * 128], BF16)
        sbuf(c, "c_identf", [128, 128], F32)
        rb = sbuf(c, "r_ybuf_t", [128, 2 * D], BF16)
        c.t["r_ybuf"] = rb[:, :]
        c.t["r_lobuf"] = rb[:, :]

        c.block = stack.enter_context(nc.Block())
        alloc_common(c)
        emit_l0f(c)
        c.wait("sync", "s_yst0")
        c.wait("sync", "s_yst1")
        c.wait("sync", "s_pe")
        emit_res(c, c.d["x1"].ap(), c.d["y1"].ap(), c.d["x2"].ap(),
                 c.d["xn2"].ap(), None)
        finish(c, ["s_rst0", "s_rst1"])
    return nc


def build_program_c():
    nc = bass.Bass("TRN2", target_bir_lowering=False)
    stack = ExitStack()
    c = Ctx(nc, stack)
    with stack:
        dt = nc.dram_tensor
        c.d["x2"] = dt("x2", [TOWN, D], F32, kind="ExternalInput")
        c.d["xn2i"] = dt("xn2i", [TOWN, D], BF16, kind="ExternalInput")
        c.d["xn2f"] = dt("xn2f", [2 * TOWN, D], BF16, kind="ExternalInput")
        c.d["gw_qd"] = dt("gw_qd", [D, D], F32, kind="ExternalInput")
        c.d["gw_kd"] = dt("gw_kd", [D, D], F32, kind="ExternalInput")
        c.d["gw_vd"] = dt("gw_vd", [D, D], F32, kind="ExternalInput")
        c.d["gw_od"] = dt("gw_od", [D, D], F32, kind="ExternalInput")
        c.d["g_bqkd"] = dt("g_bqkd", [128, 16], F32, kind="ExternalInput")
        c.d["g_bvrd"] = dt("g_bvrd", [1, D], F32, kind="ExternalInput")
        c.d["g_bolsd"] = dt("g_bolsd", [128, 16], F32, kind="ExternalInput")
        c.d["kn_d"] = dt("kn_d", [D, 2 * TOWN], BF16)
        c.d["v_d"] = dt("v_d", [2 * TOWN, D], BF16)
        c.d["qn_d"] = dt("qn_d", [D, TOWN], BF16)
        c.d["ao_tm"] = dt("ao_tm", [TOWN, D], BF16)
        c.d["y2"] = dt("y2", [D, TOWN], BF16)
        c.d["x3"] = dt("x3", [TOWN, D], F32, kind="ExternalOutput")
        c.d["xn3"] = dt("xn3", [TOWN, D], BF16, kind="ExternalOutput")

        for n in ("gw_q", "gw_k", "gw_v", "gw_o"):
            sbuf(c, n, [128, DC * D], BF16)
        sbuf(c, "g_xnf", [128, DC * BLK], BF16)
        sbuf(c, "g_kraw", [128, DC * BLK], BF16)
        sbuf(c, "g_ksq", [128, DC * BLK], BF16)
        sbuf(c, "g_kn", [128, DC * BLK], BF16)
        sbuf(c, "g_v", [128, 4 * D], BF16)
        sbuf(c, "g_rows", [1, 1024], F32)
        sbuf(c, "g_r2", [1, 1024], F32)
        sbuf(c, "g_r3", [1, 1024], F32)
        sbuf(c, "g_ab", [1, 1024], BF16)
        sbuf(c, "g_bqk", [128, 16], F32)
        sbuf(c, "g_bvr", [1, D], BF16)
        sbuf(c, "g_bols", [128, 16], F32)
        sbuf(c, "c_ones128", [128, 1], BF16)
        sbuf(c, "g_qnt", [128, DC * 256], BF16)
        sbuf(c, "g_att", [128, 2 * 256], BF16)
        sbuf(c, "g_outb", [128, 2 * 2048], BF16)
        sbuf(c, "g_knl", [128, 2 * 4096], BF16)
        sbuf(c, "g_vbl", [128, 2 * 4096], BF16)
        rb = sbuf(c, "r_ybuf_t", [128, 2 * D], BF16)
        c.t["r_ybuf"] = rb[:, :]
        c.t["r_lobuf"] = rb[:, :]

        c.block = stack.enter_context(nc.Block())
        alloc_common(c)
        load_w_dmajor(c, c.t["gw_q"], c.d["gw_qd"].ap(), D)
        load_w_dmajor(c, c.t["gw_k"], c.d["gw_kd"].ap(), D)
        load_w_dmajor(c, c.t["gw_v"], c.d["gw_vd"].ap(), D)
        load_w_dmajor(c, c.t["gw_o"], c.d["gw_od"].ap(), D)
        c.dma("sync", c.t["g_bqk"][:], c.d["g_bqkd"].ap(), "s_ld")
        c.dma("gpsimd", c.t["g_bvr"][:], c.d["g_bvrd"].ap(), "s_wt")
        c.dma("sync", c.t["g_bols"][:], c.d["g_bolsd"].ap(), "s_ld")
        for eng in ("scalar", "vector"):
            c.wait(eng, "s_ld")
            c.wait(eng, "s_wt")
        c.wait("tensor", "s_ld")
        emit_l1a(c)
        c.wait("sync", "s_pe")
        emit_res(c, c.d["x2"].ap(), c.d["y2"].ap(), c.d["x3"].ap(),
                 c.d["xn3"].ap(), None)
        finish(c, ["s_rst0", "s_rst1"])
    return nc




def emit_l1f(c):
    P0, P1, P2, P3, P4 = (c.t["pp0"], c.t["pp1"], c.t["pp2"], c.t["pp3"],
                          c.t["pp4"])
    ST0, ST1 = c.t["st0"], c.t["st1"]
    PP, PN = [P0, P1], [P2, P3]
    fx = c.t["m_x"]
    acc = c.t["m_acc"]
    wgx, wux, wdx = c.t["m_wg"], c.t["m_wu"], c.t["m_wd"]
    gsb, usb, gub, hms = c.t["m_g"], c.t["m_u"], c.t["m_gu"], c.t["m_hm"]
    ybf = c.t["m_y"]
    cmf = c.t["m_cm"]
    scr, w16, t16, mcol = c.t["m_scr"], c.t["m_w16"], c.t["m_t16"], c.t["m_mc"]
    gwt, sel16, identf = c.t["m_gwt"], c.t["c_sel16"], c.t["c_identf"]
    xn3 = c.d["xn3i"].ap()
    egd, eud, edd = c.d["m_egd"].ap(), c.d["m_eud"].ap(), c.d["m_edd"].ap()
    sgd, sud, sdd = c.d["m_sgd"].ap(), c.d["m_sud"].ap(), c.d["m_sdd"].ap()
    y3 = c.d["y3"].ap()

    c.dma("gpsimd", gwt[:].rearrange("p (dc e) -> p dc e", e=16),
          c.d["m_gwtd"].ap().rearrange("(dc p) e -> p dc e", p=128), "s_wt")
    c.dma("gpsimd", sel16[:], c.d["c_sel16d"].ap(), "s_wt")
    c.dma("sync", identf[:], c.d["c_identfd"].ap(), "s_ld")
    c.wait("vector", "s_ld")
    c.wait("vector", "s_wt")
    c.wait("scalar", "s_ld")
    c.wait("scalar", "s_wt")

    pe_fx_free = 0
    last_hm = 0
    conv_dv = 0
    last_store = {0: 0, 1: 0}
    wfree = {}
    for tt in range(NTT):
        t0 = tt * TT
        c.wait("sync", "s_pe", pe_fx_free)
        for dc in range(DC):
            c.dma("sync", fx[:, dc * TT:(dc + 1) * TT],
                  xn3[t0:t0 + TT, dc * 128:(dc + 1) * 128], "s_tr",
                  transpose=True)

        c.wait("tensor", "s_tr")
        if tt == 0:
            c.wait("tensor", "s_wt")
            c.wait("tensor", "s_ld")
        st0_free = 0
        scr_free = 0
        tr_evac = 0
        for ch in range(TT // 128):
            c.wait("tensor", "s_ac", st0_free)
            for dc in range(DC):
                c.op("tensor", lambda e, dc=dc, ch=ch: e.matmul(
                    ST0[0:128, 0:16],
                    fx[:, dc * TT + ch * 128:dc * TT + ch * 128 + 128],
                    gwt[:, dc * 16:(dc + 1) * 16],
                    start=(dc == 0), stop=(dc == DC - 1)),
                    "s_pe" if dc == DC - 1 else None)
            c.wait("scalar", "s_pe")
            c.wait("scalar", "s_dv", scr_free)
            nsg = c.op("scalar", lambda e: e.activation(
                scr[:], ST0[0:128, 0:16], AF.Sigmoid), "s_ac")
            st0_free = nsg
            c.wait("vector", "s_ac", nsg)
            n = c.op("vector", lambda e: e.tensor_scalar(
                w16[:], scr[:], 0.0, None, ALU.add), "s_dv")
            for r in range(4):
                c.wait("vector", "s_dv", n)
                n = c.op("vector", lambda e, r=r: e.reduce_max(
                    mcol[:, r:r + 1], w16[:], axis=mybir.AxisListType.X),
                    "s_dv")
                if r < 3:
                    c.wait("vector", "s_dv", n)
                    n = c.op("vector", lambda e, r=r: e.tensor_scalar(
                        t16[:], w16[:], mcol[:, r:r + 1], None, ALU.is_ge),
                        "s_dv")
                    c.wait("vector", "s_dv", n)
                    n = c.op("vector", lambda e: e.scalar_tensor_tensor(
                        w16[:], t16[:], -2.0, w16[:], ALU.mult, ALU.add),
                        "s_dv")
            c.wait("vector", "s_dv", n)
            n = c.op("vector", lambda e: e.reduce_sum(
                mcol[:, 4:5], mcol[:, 0:4], axis=mybir.AxisListType.X), "s_dv")
            c.wait("vector", "s_dv", n)
            n = c.op("vector", lambda e: e.tensor_scalar(
                mcol[:, 4:5], mcol[:, 4:5], 1e-20, None, ALU.add), "s_dv")
            c.wait("vector", "s_dv", n)
            n = c.op("vector", lambda e: e.reciprocal(
                mcol[:, 5:6], mcol[:, 4:5]), "s_dv")
            c.wait("vector", "s_dv", n)
            n = c.op("vector", lambda e: e.tensor_scalar(
                t16[:], scr[:], mcol[:, 3:4], None, ALU.is_ge), "s_dv")
            c.wait("vector", "s_dv", n)
            n = c.op("vector", lambda e: e.tensor_tensor(
                t16[:], t16[:], scr[:], ALU.mult), "s_dv")
            c.wait("vector", "s_dv", n)
            n = c.op("vector", lambda e: e.tensor_scalar(
                t16[:], t16[:], mcol[:, 5:6], None, ALU.mult), "s_dv")
            scr_free = n
            c.wait("tensor", "s_dv", n)
            c.wait("tensor", "s_ac", tr_evac)
            n = c.op("tensor", lambda e: e.transpose(
                ST1[0:16, 128:256], t16[:], identf[:]), "s_pe")
            c.wait("scalar", "s_pe", n)
            tr_evac = c.op("scalar", lambda e, ch=ch: e.copy(
                cmf[0:16, ch * 128:(ch + 1) * 128], ST1[0:16, 128:256]),
                "s_ac")

        for u in range(18):
            pu = u % 2
            c.wait("gpsimd", "s_pe", wfree.get(pu, 0))
            if u < 16:
                c.dma("gpsimd",
                      wgx.ap().rearrange("p (b dc i) -> p b dc i", b=2, i=512)
                      [:, pu], egd[u].rearrange("(dc p) i -> p dc i", p=128),
                      "s_w%d" % pu)
                c.dma("gpsimd",
                      wux.ap().rearrange("p (b dc i) -> p b dc i", b=2, i=512)
                      [:, pu], eud[u].rearrange("(dc p) i -> p dc i", p=128),
                      "s_w%d" % pu)
                c.dma("gpsimd",
                      wdx.ap().rearrange("p (b ic o) -> p b ic o", b=2, o=1024)
                      [:, pu], edd[u].rearrange("(ic p) o -> p ic o", p=128),
                      "s_w%d" % pu)
            else:
                hs = u - 16
                c.dma("gpsimd",
                      wgx.ap().rearrange("p (b dc i) -> p b dc i", b=2, i=512)
                      [:, pu],
                      sgd.rearrange("(dc p) i -> p dc i", p=128)
                      [:, :, hs * 512:(hs + 1) * 512], "s_w%d" % pu)
                c.dma("gpsimd",
                      wux.ap().rearrange("p (b dc i) -> p b dc i", b=2, i=512)
                      [:, pu],
                      sud.rearrange("(dc p) i -> p dc i", p=128)
                      [:, :, hs * 512:(hs + 1) * 512], "s_w%d" % pu)
                c.dma("gpsimd",
                      wdx.ap().rearrange("p (b ic o) -> p b ic o", b=2, o=1024)
                      [:, pu],
                      sdd[hs * 512:(hs + 1) * 512, :]
                      .rearrange("(ic p) o -> p ic o", p=128), "s_w%d" % pu)
            c.wait("tensor", "s_w%d" % pu)
            if u < 2:
                c.wait("tensor", "s_ac", tr_evac)
            last_mm = 0
            for sub in range(TT // 512):
                gev = {}
                c.wait("tensor", "s_ac")
                c.wait("tensor", "s_dv")
                for ic in range(4):
                    pb = PP[ic % 2]
                    if ic >= 2:
                        c.wait("tensor", "s_ac", gev[ic - 2])
                    for dc in range(DC):
                        c.op("tensor", lambda e, pb=pb, ic=ic, dc=dc, pu=pu,
                             sub=sub: e.matmul(
                                 pb[:, 0:512],
                                 wgx[:, (pu * 8 + dc) * 512 + ic * 128:
                                     (pu * 8 + dc) * 512 + ic * 128 + 128],
                                 fx[:, dc * TT + sub * 512:
                                    dc * TT + sub * 512 + 512],
                                 start=(dc == 0), stop=(dc == DC - 1)),
                             "s_pe" if dc == DC - 1 else None)
                    c.wait("scalar", "s_pe")
                    gev[ic] = c.op("scalar", lambda e, pb=pb, ic=ic:
                                   e.activation(
                                       gsb[:, ic * 512:(ic + 1) * 512],
                                       pb[:, 0:512], AF.Silu), "s_ac")
                uev = {}
                for ic in range(4):
                    pb = PN[ic % 2]
                    if ic >= 2:
                        c.wait("tensor", "s_ac", uev[ic - 2])
                    for dc in range(DC):
                        c.op("tensor", lambda e, pb=pb, ic=ic, dc=dc, pu=pu,
                             sub=sub: e.matmul(
                                 pb[:, 0:512],
                                 wux[:, (pu * 8 + dc) * 512 + ic * 128:
                                     (pu * 8 + dc) * 512 + ic * 128 + 128],
                                 fx[:, dc * TT + sub * 512:
                                    dc * TT + sub * 512 + 512],
                                 start=(dc == 0), stop=(dc == DC - 1)),
                             "s_pe" if dc == DC - 1 else None)
                    c.wait("scalar", "s_pe")
                    uev[ic] = c.op("scalar", lambda e, pb=pb, ic=ic: e.copy(
                        usb[:, ic * 512:(ic + 1) * 512], pb[:, 0:512]), "s_ac")
                c.wait("vector", "s_ac", uev[3])
                gu_done = 0
                for ic in range(4):
                    gu_done = c.op("vector", lambda e, ic=ic: e.tensor_tensor(
                        gub[:, ic * 512:(ic + 1) * 512],
                        gsb[:, ic * 512:(ic + 1) * 512],
                        usb[:, ic * 512:(ic + 1) * 512], ALU.mult), "s_dv")
                rhs_h = gub
                if u < 16:
                    c.wait("tensor", "s_dv", last_hm)
                    pe_mb = c.op("tensor", lambda e, u=u, sub=sub: e.matmul(
                        P4[:, 0:512], sel16[:, u * 128:(u + 1) * 128],
                        cmf[0:16, sub * 512:(sub + 1) * 512],
                        start=True, stop=True), "s_pe")
                    c.wait("vector", "s_pe", pe_mb)
                    c.wait("vector", "s_dv", gu_done)
                    for ic in range(4):
                        last_hm = c.op("vector", lambda e, ic=ic:
                                       e.tensor_tensor(
                                           hms[:, ic * 512:(ic + 1) * 512],
                                           gub[:, ic * 512:(ic + 1) * 512],
                                           P4[:, 0:512], ALU.mult), "s_dv")
                    rhs_h = hms
                    hm_rdy = last_hm
                else:
                    hm_rdy = gu_done
                c.wait("tensor", "s_dv", hm_rdy)
                aev = {}
                for oc in range(DC):
                    pd = PP[oc % 2]
                    if oc >= 2:
                        c.wait("tensor", "s_dv", aev[oc - 2])
                    c.wait("tensor", "s_ac")
                    for ic in range(4):
                        last_mm = c.op(
                            "tensor", lambda e, pd=pd, oc=oc, ic=ic, pu=pu,
                            rhs_h=rhs_h: e.matmul(
                                pd[:, 0:512],
                                wdx[:, (pu * 4 + ic) * 1024 + oc * 128:
                                    (pu * 4 + ic) * 1024 + oc * 128 + 128],
                                rhs_h[:, ic * 512:(ic + 1) * 512],
                                start=(ic == 0), stop=(ic == 3)),
                            "s_pe" if ic == 3 else None)
                    c.wait("vector", "s_pe")
                    if u == 0:
                        c.wait("vector", "s_dv", conv_dv)
                    dst = acc[:, oc * TT + sub * 512:oc * TT + sub * 512 + 512]
                    aev[oc] = c.op("vector", lambda e, pd=pd, dst=dst,
                                   first=(u == 0): (
                                       e.tensor_scalar(dst, pd[:, 0:512], 0.0,
                                                       None, ALU.add)
                                       if first else
                                       e.tensor_tensor(dst, dst, pd[:, 0:512],
                                                       ALU.add)), "s_dv")
            wfree[pu] = last_mm
        pe_fx_free = c.cur("s_pe")
        c.wait("vector", "s_dv")
        for oc in range(DC):
            p = oc % 2
            c.wait("vector", "s_yst%d" % p, last_store[p])
            conv_dv = c.op("vector", lambda e, oc=oc, p=p: e.tensor_scalar(
                ybf[:, p * TT:(p + 1) * TT], acc[:, oc * TT:(oc + 1) * TT],
                0.0, None, ALU.add), "s_dv")
            c.wait("scalar", "s_dv", conv_dv)
            last_store[p] = c.dma(
                "scalar", y3[oc * 128:(oc + 1) * 128, t0:t0 + TT],
                ybf[:, p * TT:(p + 1) * TT], "s_yst%d" % p)


def build_program_d():
    nc = bass.Bass("TRN2", target_bir_lowering=False)
    stack = ExitStack()
    c = Ctx(nc, stack)
    with stack:
        dt = nc.dram_tensor
        c.d["x3"] = dt("x3", [TOWN, D], F32, kind="ExternalInput")
        c.d["xn3i"] = dt("xn3i", [TOWN, D], BF16, kind="ExternalInput")
        c.d["m_gwtd"] = dt("m_gwtd", [D, 16], F32, kind="ExternalInput")
        c.d["m_egd"] = dt("m_egd", [16, D, 512], F32, kind="ExternalInput")
        c.d["m_eud"] = dt("m_eud", [16, D, 512], F32, kind="ExternalInput")
        c.d["m_edd"] = dt("m_edd", [16, 512, D], F32, kind="ExternalInput")
        c.d["m_sgd"] = dt("m_sgd", [D, D], F32, kind="ExternalInput")
        c.d["m_sud"] = dt("m_sud", [D, D], F32, kind="ExternalInput")
        c.d["m_sdd"] = dt("m_sdd", [D, D], F32, kind="ExternalInput")
        c.d["c_sel16d"] = dt("c_sel16d", [16, 16 * 128], F32,
                             kind="ExternalInput")
        c.d["c_identfd"] = dt("c_identfd", [128, 128], F32,
                              kind="ExternalInput")
        c.d["y3"] = dt("y3", [D, TOWN], BF16)
        c.d["out"] = dt("out", [TOWN, D], F32, kind="ExternalOutput")

        sbuf(c, "m_x", [128, DC * TT], BF16)
        sbuf(c, "m_acc", [128, DC * TT], F32)
        sbuf(c, "m_wg", [128, 2 * DC * 512], BF16)
        sbuf(c, "m_wu", [128, 2 * DC * 512], BF16)
        sbuf(c, "m_wd", [128, 2 * 4 * D], BF16)
        sbuf(c, "m_g", [128, 4 * 512], BF16)
        sbuf(c, "m_u", [128, 4 * 512], BF16)
        sbuf(c, "m_gu", [128, 4 * 512], BF16)
        sbuf(c, "m_hm", [128, 4 * 512], BF16)
        sbuf(c, "m_y", [128, 2 * TT], BF16)
        sbuf(c, "m_cm", [16, TT], BF16)
        sbuf(c, "m_scr", [128, 16], F32)
        sbuf(c, "m_w16", [128, 16], F32)
        sbuf(c, "m_t16", [128, 16], F32)
        sbuf(c, "m_mc", [128, 8], F32)
        sbuf(c, "m_gwt", [128, DC * 16], BF16)
        sbuf(c, "c_sel16", [16, 16 * 128], BF16)
        sbuf(c, "c_identf", [128, 128], F32)
        rb = sbuf(c, "r_ybuf_t", [128, 2 * D], BF16)
        c.t["r_ybuf"] = rb[:, :]
        c.t["r_lobuf"] = rb[:, :]

        c.block = stack.enter_context(nc.Block())
        alloc_common(c)
        emit_l1f(c)
        c.wait("sync", "s_yst0")
        c.wait("sync", "s_yst1")
        c.wait("sync", "s_pe")
        emit_res(c, c.d["x3"].ap(), c.d["y3"].ap(), c.d["out"].ap(),
                 None, None)
        finish(c, ["s_rst0", "s_rst1"])
    return nc


def phase_barrier(c):
    names = list(c.sems.keys())
    for eng in ("sync", "gpsimd", "vector", "scalar", "tensor"):
        for sn in names:
            c.wait(eng, sn)


from contextlib import contextmanager


@contextmanager
def phase_stack(c):
    outer = c.stack
    inner = ExitStack()
    c.stack = inner
    try:
        with inner:
            yield
    finally:
        c.stack = outer


SEM_NAMES = ["s_ld", "s_wt", "s_in0", "s_in1", "s_xn0", "s_xn1", "s_tr",
             "s_pe", "s_dv", "s_ac", "s_yst0", "s_yst1", "s_rin0", "s_rin1",
             "s_rst0", "s_rst1", "s_w0", "s_w1", "s_st0", "s_st1", "s_q0",
             "s_q1", "s_kv0", "s_kv1", "s_ao0", "s_ao1", "s_y0", "s_y1",
             "s_cc"]


def build_fused():
    nc = bass.Bass("TRN2", target_bir_lowering=False)
    stack = ExitStack()
    c = Ctx(nc, stack)
    with stack:
        dt = nc.dram_tensor
        c.d["x_pre"] = dt("x_pre", [TOWN, D], F32, kind="ExternalInput")
        c.d["x_own"] = dt("x_own", [TOWN, D], F32, kind="ExternalInput")
        c.d["w_q"] = dt("w_q", [D, D], F32, kind="ExternalInput")
        c.d["w_k"] = dt("w_k", [D, D], F32, kind="ExternalInput")
        c.d["w_v"] = dt("w_v", [D, D], F32, kind="ExternalInput")
        c.d["w_o"] = dt("w_o", [D, D], F32, kind="ExternalInput")
        c.d["w_f"] = dt("w_f", [HD, NF], F32, kind="ExternalInput")
        c.d["c_trild"] = dt("c_trild", [128, 128], F32, kind="ExternalInput")
        c.d["c_sheadd"] = dt("c_sheadd", [16, 16 * 128], F32,
                             kind="ExternalInput")
        c.d["c_ucold"] = dt("c_ucold", [64, 16 * 16], F32,
                            kind="ExternalInput")
        c.d["c_flagd"] = dt("c_flagd", [128, 1], F32, kind="ExternalInput")
        c.d["f_wgd"] = dt("f_wgd", [D, 8], F32, kind="ExternalInput")
        c.d["f_bgd"] = dt("f_bgd", [1, 8], F32, kind="ExternalInput")
        c.d["f_w1d"] = dt("f_w1d", [8, D, D], F32, kind="ExternalInput")
        c.d["f_b1d"] = dt("f_b1d", [128, 64], F32, kind="ExternalInput")
        c.d["f_w2d"] = dt("f_w2d", [8, D, D], F32, kind="ExternalInput")
        c.d["f_b2d"] = dt("f_b2d", [8, D], F32, kind="ExternalInput")
        c.d["c_sel8d"] = dt("c_sel8d", [8, 8 * 128], F32,
                            kind="ExternalInput")
        c.d["c_identfd"] = dt("c_identfd", [128, 128], F32,
                              kind="ExternalInput")
        c.d["gw_qd"] = dt("gw_qd", [D, D], F32, kind="ExternalInput")
        c.d["gw_kd"] = dt("gw_kd", [D, D], F32, kind="ExternalInput")
        c.d["gw_vd"] = dt("gw_vd", [D, D], F32, kind="ExternalInput")
        c.d["gw_od"] = dt("gw_od", [D, D], F32, kind="ExternalInput")
        c.d["g_bqkd"] = dt("g_bqkd", [128, 16], F32, kind="ExternalInput")
        c.d["g_bvrd"] = dt("g_bvrd", [1, D], F32, kind="ExternalInput")
        c.d["g_bolsd"] = dt("g_bolsd", [128, 16], F32, kind="ExternalInput")
        c.d["m_gwtd"] = dt("m_gwtd", [D, 16], F32, kind="ExternalInput")
        c.d["m_egd"] = dt("m_egd", [16, D, 512], F32, kind="ExternalInput")
        c.d["m_eud"] = dt("m_eud", [16, D, 512], F32, kind="ExternalInput")
        c.d["m_edd"] = dt("m_edd", [16, 512, D], F32, kind="ExternalInput")
        c.d["m_sgd"] = dt("m_sgd", [D, D], F32, kind="ExternalInput")
        c.d["m_sud"] = dt("m_sud", [D, D], F32, kind="ExternalInput")
        c.d["m_sdd"] = dt("m_sdd", [D, D], F32, kind="ExternalInput")
        c.d["c_sel16d"] = dt("c_sel16d", [16, 16 * 128], F32,
                             kind="ExternalInput")
        c.d["xn0"] = dt("xn0", [2 * TOWN, D], BF16)
        c.d["y0"] = dt("y0", [D, TOWN], BF16)
        c.d["x1"] = dt("x1", [TOWN, D], F32)
        c.d["xn1"] = dt("xn1", [TOWN, D], BF16)
        c.d["xn1lo"] = dt("xn1lo", [TOWN, D], BF16)
        c.d["y1"] = dt("y1", [D, TOWN], BF16)
        c.d["x2"] = dt("x2", [TOWN, D], F32)
        c.d["xn2i"] = dt("xn2i", [TOWN, D], BF16)
        c.d["xn2f"] = dt("xn2f", [2 * TOWN, D], BF16)
        c.d["kn_d"] = dt("kn_d", [D, 2 * TOWN], BF16)
        c.d["v_d"] = dt("v_d", [2 * TOWN, D], BF16)
        c.d["qn_d"] = dt("qn_d", [D, TOWN], BF16)
        c.d["ao_tm"] = dt("ao_tm", [TOWN, D], BF16)
        c.d["y2"] = dt("y2", [D, TOWN], BF16)
        c.d["x3"] = dt("x3", [TOWN, D], F32)
        c.d["xn3i"] = dt("xn3i", [TOWN, D], BF16)
        c.d["y3"] = dt("y3", [D, TOWN], BF16)
        c.d["out"] = dt("out", [TOWN, D], F32, kind="ExternalOutput")

        c.block = stack.enter_context(nc.Block())
        for sn in SEM_NAMES:
            c.sem(sn)
        alloc_common(c)

        with phase_stack(c):
            for n in ("wq", "wk", "wv", "wo"):
                sbuf(c, n, [128, DC * D], BF16)
            sbuf(c, "wfeat", [64, 128], BF16)
            sbuf(c, "c_tril4", [128, 4 * 128], BF16)
            sbuf(c, "c_shead", [16, 16 * 128], BF16)
            sbuf(c, "c_ucol", [64, 16 * 16], BF16)
            sbuf(c, "c_flag", [128, 1], F32)
            sbuf(c, "a_xnf", [128, DC * BLK], BF16)
            sbuf(c, "a_khh", [64, H * BLK], BF16)
            sbuf(c, "a_qhh", [64, H * BLK], BF16)
            sbuf(c, "a_vaug", [128, 4 * 1040], BF16)
            sbuf(c, "a_kpf", [128, H * BLK], BF16)
            sbuf(c, "a_qpf", [128, H * BLK], BF16)
            sbuf(c, "a_kptm", [128, H * 128], BF16)
            sbuf(c, "a_sqm", [16, 2 * BLK], BF16)
            sbuf(c, "a_am", [128, 2 * 512], BF16)
            sbuf(c, "a_numb", [64, 2 * 512], BF16)
            sbuf(c, "a_yall", [128, DC * BLK], BF16)
            sbuf(c, "a_yo", [128, 2 * BLK], BF16)
            c.t["a_sf32"] = c.t["r_xs"][:, 0:H * 65]
            sbuf(c, "a_sbf", [128, H * 65], BF16)
            sbuf(c, "a_dn", [1, 512], F32)
            sbuf(c, "a_rec", [1, 2 * 512], BF16)
            sbuf(c, "a_onesr", [1, 512], BF16)
            sbuf(c, "a_epsv", [1, 65], BF16)
            c.t["a_sqh"] = c.t["a_yall"][0:64, 4 * BLK:8 * BLK]
            c.t["r_ybuf"] = c.t["a_yall"][:, 0:2 * D]
            c.t["r_lobuf"] = c.t["a_yall"][:, 2 * D:4 * D]

            for i in range(4):
                c.dma("gpsimd", c.t["c_tril4"][:, i * 128:(i + 1) * 128],
                      c.d["c_trild"].ap(), "s_wt")
            c.dma("gpsimd", c.t["c_shead"][:], c.d["c_sheadd"].ap(), "s_wt")
            c.dma("gpsimd", c.t["c_ucol"][:], c.d["c_ucold"].ap(), "s_wt")
            c.dma("sync", c.t["c_flag"][:], c.d["c_flagd"].ap(), "s_ld")
            load_w_dmajor(c, c.t["wq"], c.d["w_q"].ap(), D)
            load_w_dmajor(c, c.t["wk"], c.d["w_k"].ap(), D)
            load_w_dmajor(c, c.t["wv"], c.d["w_v"].ap(), D)
            load_w_dmajor(c, c.t["wo"], c.d["w_o"].ap(), D)
            c.dma("gpsimd", c.t["wfeat"][:], c.d["w_f"].ap(), "s_wt")
            for eng in ("scalar", "vector"):
                c.wait(eng, "s_wt")
                c.wait(eng, "s_ld")
            c.wait("tensor", "s_ld")
            emit_l0a(c)
            c.wait("sync", "s_yst0")
            c.wait("sync", "s_yst1")
            c.wait("vector", "s_xn0")
            c.wait("vector", "s_xn1")
            c.wait("scalar", "s_xn0")
            c.wait("scalar", "s_xn1")
            c.wait("sync", "s_pe")
            emit_res(c, c.d["x_own"].ap(), c.d["y0"].ap(), c.d["x1"].ap(),
                     c.d["xn1"].ap(), c.d["xn1lo"].ap())
        phase_barrier(c)

        with phase_stack(c):
            sbuf(c, "f_xh", [128, DC * TT], BF16)
            sbuf(c, "f_xl", [128, DC * TT], BF16)
            sbuf(c, "f_acc", [128, DC * TT], F32)
            sbuf(c, "f_w1", [128, 2 * DC * 512], BF16)
            sbuf(c, "f_w2", [128, 2 * 4 * D], BF16)
            sbuf(c, "f_b1", [128, 64], F32)
            sbuf(c, "f_h", [128, 4 * 512], BF16)
            sbuf(c, "f_hm", [128, 4 * 512], BF16)
            sbuf(c, "f_y", [128, 2 * TT], BF16)
            sbuf(c, "f_mk", [8, TT], BF16)
            sbuf(c, "f_lg", [128, 8], F32)
            sbuf(c, "f_mx", [128, 1], F32)
            sbuf(c, "f_mk32", [128, 8], F32)
            sbuf(c, "f_wgh", [128, 64], BF16)
            sbuf(c, "f_wgl", [128, 64], BF16)
            sbuf(c, "f_wgf", [128, 64], F32)
            sbuf(c, "f_bgf", [1, 8], F32)
            sbuf(c, "f_bgb", [128, 8], F32)
            sbuf(c, "f_b2b", [8, D], BF16)
            sbuf(c, "f_b2f", [8, D], F32)
            sbuf(c, "c_sel8", [8, 8 * 128], BF16)
            sbuf(c, "c_identf", [128, 128], F32)
            rb = sbuf(c, "r_ybuf_t", [128, 2 * D], BF16)
            c.t["r_ybuf"] = rb[:, :]
            c.t["r_lobuf"] = rb[:, :]
            emit_l0f(c)
            c.wait("sync", "s_yst0")
            c.wait("sync", "s_yst1")
            c.wait("sync", "s_pe")
            emit_res(c, c.d["x1"].ap(), c.d["y1"].ap(), c.d["x2"].ap(),
                     c.d["xn2i"].ap(), None)
        c.wait("gpsimd", "s_rst0")
        c.wait("gpsimd", "s_rst1")
        c.op("gpsimd", lambda e: e.collective_compute(
            "AllGather", ALU.bypass,
            replica_groups=[[0, 1], [2, 3], [4, 5], [6, 7]],
            ins=[c.d["xn2i"].ap().opt()], outs=[c.d["xn2f"].ap().opt()]),
            "s_cc")
        phase_barrier(c)

        with phase_stack(c):
            for n in ("gw_q", "gw_k", "gw_v", "gw_o"):
                sbuf(c, n, [128, DC * D], BF16)
            sbuf(c, "g_xnf", [128, DC * BLK], BF16)
            sbuf(c, "g_kraw", [128, DC * BLK], BF16)
            sbuf(c, "g_ksq", [128, DC * BLK], BF16)
            sbuf(c, "g_kn", [128, DC * BLK], BF16)
            sbuf(c, "g_v", [128, 4 * D], BF16)
            sbuf(c, "g_rows", [1, 1024], F32)
            sbuf(c, "g_r2", [1, 1024], F32)
            sbuf(c, "g_r3", [1, 1024], F32)
            sbuf(c, "g_ab", [1, 1024], BF16)
            sbuf(c, "g_bqk", [128, 16], F32)
            sbuf(c, "g_bvr", [1, D], BF16)
            sbuf(c, "g_bols", [128, 16], F32)
            sbuf(c, "c_ones128", [128, 1], BF16)
            sbuf(c, "g_qnt", [128, DC * 256], BF16)
            sbuf(c, "g_att", [128, 2 * 256], BF16)
            sbuf(c, "g_outb", [128, 2 * 2048], BF16)
            sbuf(c, "g_knl", [128, 2 * 4096], BF16)
            sbuf(c, "g_vbl", [128, 2 * 4096], BF16)
            rb = sbuf(c, "r_ybuf_t2", [128, 2 * D], BF16)
            c.t["r_ybuf"] = rb[:, :]
            c.t["r_lobuf"] = rb[:, :]
            load_w_dmajor(c, c.t["gw_q"], c.d["gw_qd"].ap(), D)
            load_w_dmajor(c, c.t["gw_k"], c.d["gw_kd"].ap(), D)
            load_w_dmajor(c, c.t["gw_v"], c.d["gw_vd"].ap(), D)
            load_w_dmajor(c, c.t["gw_o"], c.d["gw_od"].ap(), D)
            c.dma("sync", c.t["g_bqk"][:], c.d["g_bqkd"].ap(), "s_ld")
            c.dma("gpsimd", c.t["g_bvr"][:], c.d["g_bvrd"].ap(), "s_wt")
            c.dma("sync", c.t["g_bols"][:], c.d["g_bolsd"].ap(), "s_ld")
            for eng in ("scalar", "vector"):
                c.wait(eng, "s_ld")
                c.wait(eng, "s_wt")
            c.wait("tensor", "s_ld")
            emit_l1a(c)
            c.wait("sync", "s_pe")
            emit_res(c, c.d["x2"].ap(), c.d["y2"].ap(), c.d["x3"].ap(),
                     c.d["xn3i"].ap(), None)
        phase_barrier(c)

        with phase_stack(c):
            sbuf(c, "m_x", [128, DC * TT], BF16)
            sbuf(c, "m_acc", [128, DC * TT], F32)
            sbuf(c, "m_wg", [128, 2 * DC * 512], BF16)
            sbuf(c, "m_wu", [128, 2 * DC * 512], BF16)
            sbuf(c, "m_wd", [128, 2 * 4 * D], BF16)
            sbuf(c, "m_g", [128, 4 * 512], BF16)
            sbuf(c, "m_u", [128, 4 * 512], BF16)
            sbuf(c, "m_gu", [128, 4 * 512], BF16)
            sbuf(c, "m_hm", [128, 4 * 512], BF16)
            sbuf(c, "m_y", [128, 2 * TT], BF16)
            sbuf(c, "m_cm", [16, TT], BF16)
            sbuf(c, "m_scr", [128, 16], F32)
            sbuf(c, "m_w16", [128, 16], F32)
            sbuf(c, "m_t16", [128, 16], F32)
            sbuf(c, "m_mc", [128, 8], F32)
            sbuf(c, "m_gwt", [128, DC * 16], BF16)
            sbuf(c, "c_sel16", [16, 16 * 128], BF16)
            t2 = c.stack.enter_context(
                c.nc.sbuf_tensor("c_identf2", [128, 128], F32))
            c.t["c_identf"] = t2
            rb = sbuf(c, "r_ybuf_t3", [128, 2 * D], BF16)
            c.t["r_ybuf"] = rb[:, :]
            c.t["r_lobuf"] = rb[:, :]
            emit_l1f(c)
            c.wait("sync", "s_yst0")
            c.wait("sync", "s_yst1")
            c.wait("sync", "s_pe")
            emit_res(c, c.d["x3"].ap(), c.d["y3"].ap(), c.d["out"].ap(),
                     None, None)
        finish(c, ["s_rst0", "s_rst1"])
    return nc


def host_consts():
    tril = (np.tril(np.ones((128, 128), np.float32), 0).T).copy()
    shead = np.zeros((16, 16 * 128), np.float32)
    for h in range(16):
        shead[h, h * 128:(h + 1) * 128] = 1.0
    ucol = np.zeros((64, 16 * 16), np.float32)
    for h in range(16):
        ucol[:, h * 16 + h] = 1.0
    cpool = np.tile(np.array(CONST_VALS, np.float32)[None, :], (128, 1))
    return dict(c_trild=np.ascontiguousarray(tril),
                c_sheadd=shead, c_ucold=ucol, c_pool=cpool)


def fused_in_maps(inputs):
    x = np.asarray(inputs["L0_attn_x"])
    consts = host_consts()
    ident = np.eye(128, dtype=np.float32)
    sel8 = np.zeros((8, 8 * 128), np.float32)
    for h in range(8):
        sel8[h, h * 128:(h + 1) * 128] = 1.0
    sel16 = np.zeros((16, 16 * 128), np.float32)
    for h in range(16):
        sel16[h, h * 128:(h + 1) * 128] = 1.0
    b1 = np.asarray(inputs["L0_ffn_b1"])
    b1r = np.ascontiguousarray(
        b1.reshape(8, 8, 128).transpose(2, 0, 1).reshape(128, 64))
    bq = np.asarray(inputs["L1_attn_bq"]).reshape(8, 128).T
    bk = np.asarray(inputs["L1_attn_bk"]).reshape(8, 128).T
    bo = np.asarray(inputs["L1_attn_bo"]).reshape(8, 128).T
    ls = np.asarray(inputs["L1_attn_layerscale"]).reshape(8, 128).T
    gwt = np.ascontiguousarray(np.asarray(inputs["L1_ffn_gate_w"]).T)
    common = {
        "c_pool": consts["c_pool"], "c_trild": consts["c_trild"],
        "c_sheadd": consts["c_sheadd"], "c_ucold": consts["c_ucold"],
        "c_sel8d": sel8, "c_sel16d": sel16, "c_identfd": ident,
        "w_q": np.asarray(inputs["L0_attn_Wq"]),
        "w_k": np.asarray(inputs["L0_attn_Wk"]),
        "w_v": np.asarray(inputs["L0_attn_Wv"]),
        "w_o": np.asarray(inputs["L0_attn_Wo"]),
        "w_f": np.asarray(inputs["L0_attn_W_feat"]),
        "f_wgd": np.asarray(inputs["L0_ffn_Wg"]),
        "f_bgd": np.asarray(inputs["L0_ffn_bg"]).reshape(1, 8),
        "f_w1d": np.asarray(inputs["L0_ffn_W1"]),
        "f_b1d": b1r,
        "f_w2d": np.asarray(inputs["L0_ffn_W2"]),
        "f_b2d": np.asarray(inputs["L0_ffn_b2"]),
        "gw_qd": np.asarray(inputs["L1_attn_Wq"]),
        "gw_kd": np.asarray(inputs["L1_attn_Wk"]),
        "gw_vd": np.asarray(inputs["L1_attn_Wv"]),
        "gw_od": np.asarray(inputs["L1_attn_Wo"]),
        "g_bqkd": np.ascontiguousarray(np.concatenate([bq, bk], axis=1)),
        "g_bvrd": np.asarray(inputs["L1_attn_bv"]).reshape(1, D),
        "g_bolsd": np.ascontiguousarray(np.concatenate([ls, bo], axis=1)),
        "m_gwtd": gwt,
        "m_egd": np.asarray(inputs["L1_ffn_expert_gate"]),
        "m_eud": np.asarray(inputs["L1_ffn_expert_up"]),
        "m_edd": np.asarray(inputs["L1_ffn_expert_down"]),
        "m_sgd": np.asarray(inputs["L1_ffn_shared_gate"]),
        "m_sud": np.asarray(inputs["L1_ffn_shared_up"]),
        "m_sdd": np.asarray(inputs["L1_ffn_shared_down"]),
    }
    in_maps = []
    for core in range(8):
        b, half = core // 2, core % 2
        m = dict(common)
        m["x_own"] = np.ascontiguousarray(x[b, half * TOWN:(half + 1) * TOWN])
        m["x_pre"] = np.ascontiguousarray(x[b, 0:TOWN])
        m["c_flagd"] = np.full((128, 1), float(half), np.float32)
        in_maps.append(m)
    return in_maps


def kernel(**inputs):
    in_maps = fused_in_maps(inputs)
    nc = build_fused()
    res = run_bass_kernel_spmd(nc, in_maps, core_ids=list(range(8)))
    out = np.zeros((4, 8192, D), np.float32)
    for core in range(8):
        b, half = core // 2, core % 2
        out[b, half * TOWN:(half + 1) * TOWN] = res.results[core]["out"]
    return out


# version 12
# speedup vs baseline: 1.2895x; 1.0388x over previous
import math
import os
from contextlib import ExitStack

DEV_STAGE = int(os.environ.get("DEV_STAGE", "9"))
DEV_NBLK = int(os.environ.get("DEV_NBLK", "0"))
DEV_S3 = int(os.environ.get("DEV_S3", "9"))

import numpy as np
import ml_dtypes

import concourse.bass as bass
import concourse.mybir as mybir
from concourse.bass_utils import run_bass_kernel_spmd

F32 = mybir.dt.float32
BF16 = mybir.dt.bfloat16
AF = mybir.ActivationFunctionType
ALU = mybir.AluOpType
BFD = ml_dtypes.bfloat16

D = 1024
DC = 8
TOWN = 4096
BLK = 512
NSUB = 4
NBLK = TOWN // BLK
H, HD, NF = 16, 64, 128
EPS_FAVOR = 1e-6
LN_EPS = 1e-5
LNINV = -0.5 * math.log(NF)
LOG_S = math.log(8192.0)

CONST_VALS = sorted(set([2.0, -2.0, 0.5, -0.5, -1.0, 1.0 / D, LN_EPS,
                         EPS_FAVOR, LNINV, -LOG_S, 1.0 / 32.0, 1e-20]))


class Ctx:
    def __init__(self, nc, stack):
        self.nc = nc
        self.stack = stack
        self.q = {"sync": [], "gpsimd": [], "vector": [], "scalar": [],
                  "tensor": []}
        self.sems = {}
        self.t = {}
        self.d = {}
        self._l0f_wfree = {}

    def sem(self, name):
        if name not in self.sems:
            h = self.stack.enter_context(self.nc.semaphore(name))
            self.sems[name] = [h, 0]
        return self.sems[name]

    def tick(self, name, k):
        s = self.sem(name)
        s[1] += k
        return s[1]

    def cur(self, name):
        return self.sem(name)[1]

    def emit(self, eng, fn):
        self.q[eng].append(fn)

    def wait(self, eng, semname, tgt=None):
        s = self.sem(semname)
        n = s[1] if tgt is None else tgt
        if n <= 0:
            return
        h = s[0]
        self.emit(eng, lambda e, h=h, n=n: e.wait_ge(h, n))

    def dma(self, eng, out, in_, semname, **kw):
        n = self.tick(semname, 16)
        h = self.sem(semname)[0]
        self.emit(eng, lambda e, out=out, in_=in_, h=h, kw=kw:
                  e.dma_start(out=out, in_=in_, **kw).then_inc(h, 16))
        return n

    def op(self, eng, fn, semname=None, k=1):
        if semname is None:
            self.emit(eng, fn)
            return None
        n = self.tick(semname, k)
        h = self.sem(semname)[0]
        self.emit(eng, lambda e, fn=fn, h=h, k=k: fn(e).then_inc(h, k))
        return n

    def finalize(self, block):
        for name in ("sync", "gpsimd", "vector", "scalar", "tensor"):
            fns = self.q[name]
            if not fns:
                continue
            getattr(block, name)(lambda e, fns=fns: [f(e) for f in fns])


def sbuf(c, name, shape, dtype):
    t = c.stack.enter_context(c.nc.sbuf_tensor(name, shape, dtype))
    c.t[name] = t
    return t


def psum_t(c, name, shape, dtype=F32):
    t = c.stack.enter_context(c.nc.psum_tensor(name, shape, dtype))
    c.t[name] = t
    return t


def load_w_dmajor(c, dst, w_ap, E):
    c.dma("gpsimd", dst.ap().rearrange("p (dc e) -> p dc e", e=E),
          w_ap.rearrange("(dc p) e -> p dc e", p=128), "s_wt")


def emit_ln_chunk(c, xsrc, xn_dst, lo_dst, prev_dv=0):
    st = c.t["r_stats"]
    sq = c.t["r_sq"]
    xb = c.t["r_xb"]
    f32s = c.t["r_f32"]
    c.wait("scalar", "s_dv", prev_dv)
    c.op("scalar", lambda e: e.activation(sq[:], xsrc, AF.Square,
                                          accum_out=st[:, 1:2]))
    c.op("scalar", lambda e: e.activation(xb[:], xsrc, AF.Identity,
                                          accum_out=st[:, 0:1]), "s_ac")
    c.wait("vector", "s_ac")
    n = c.op("vector", lambda e: e.tensor_scalar(st[:, 2:3], st[:, 0:1],
                                                 1.0 / D, None, ALU.mult),
             "s_dv")
    n2 = c.op("vector", lambda e: e.tensor_scalar(st[:, 3:4], st[:, 1:2],
                                                  1.0 / D, None, ALU.mult),
              "s_dv")
    c.wait("vector", "s_dv", n)
    n = c.op("vector", lambda e: e.tensor_tensor(st[:, 5:6], st[:, 2:3],
                                                 st[:, 2:3], ALU.mult),
             "s_dv")
    c.wait("vector", "s_dv", max(n, n2))
    n = c.op("vector", lambda e: e.tensor_tensor(st[:, 3:4], st[:, 3:4],
                                                 st[:, 5:6], ALU.subtract),
             "s_dv")
    c.wait("vector", "s_dv", n)
    n = c.op("vector", lambda e: e.tensor_scalar(st[:, 3:4], st[:, 3:4],
                                                 LN_EPS, None, ALU.add),
             "s_dv")
    c.wait("vector", "s_dv", n)
    c.op("vector", lambda e: e.reciprocal(st[:, 6:7], st[:, 3:4]), "s_dv")
    c.wait("scalar", "s_dv")
    c.op("scalar", lambda e: e.activation(st[:, 4:5], st[:, 6:7], AF.Sqrt),
         "s_ac")
    c.wait("vector", "s_ac")
    if lo_dst is None:
        return c.op("vector", lambda e: e.tensor_scalar(
            xn_dst, xsrc, st[:, 2:3], st[:, 4:5], ALU.subtract, ALU.mult),
            "s_dv")
    n = c.op("vector", lambda e: e.tensor_scalar(
        f32s[:], xsrc, st[:, 2:3], st[:, 4:5], ALU.subtract, ALU.mult),
        "s_dv")
    c.wait("vector", "s_dv", n)
    n = c.op("vector", lambda e: e.tensor_scalar(xn_dst, f32s[:], 0.0, None,
                                                 ALU.add), "s_dv")
    c.wait("vector", "s_dv", n)
    return c.op("vector", lambda e: e.tensor_tensor(lo_dst, f32s[:], xn_dst,
                                                    ALU.subtract), "s_dv")


def emit_res(c, x_in, y_fm, x_out, xn_out, lo_out):
    rx, xs = c.t["r_x"], c.t["r_xs"]
    ry = c.t["r_ybuf"]
    lob = c.t["r_lobuf"]
    xnb = c.t["r_xn"]
    done = {}
    prev_dv = 0
    for ch in range(TOWN // 128):
        p = ch % 2
        t0 = ch * 128
        if ch >= 2:
            c.wait("sync", "s_rst%d" % p, done[ch - 2])
        c.dma("sync", rx[:, p * D:(p + 1) * D], x_in[t0:t0 + 128, :],
              "s_rin%d" % p)
        if y_fm is not None:
            c.dma("sync", ry[:, p * D:(p + 1) * D], y_fm[0:D, t0:t0 + 128],
                  "s_rin%d" % p, transpose=True)
        c.wait("vector", "s_rin%d" % p)
        c.op("vector", lambda e, p=p, has_y=y_fm is not None: (
            e.tensor_tensor(xs[:, p * D:(p + 1) * D], rx[:, p * D:(p + 1) * D],
                            ry[:, p * D:(p + 1) * D], ALU.add) if has_y else
            e.tensor_scalar(xs[:, p * D:(p + 1) * D], rx[:, p * D:(p + 1) * D],
                            0.0, None, ALU.add)), "s_dv")
        c.wait("scalar", "s_dv")
        c.dma("scalar", x_out[t0:t0 + 128, :], xs[:, p * D:(p + 1) * D],
              "s_rst%d" % p)
        if xn_out is not None:
            xnd = xnb[:, p * D:(p + 1) * D]
            lod = lob[:, p * D:(p + 1) * D] if lo_out is not None else None
            prev_dv = emit_ln_chunk(c, xs[:, p * D:(p + 1) * D], xnd, lod,
                                    prev_dv)
            c.wait("scalar", "s_dv")
            c.dma("scalar", xn_out[t0:t0 + 128, :], xnd, "s_rst%d" % p)
            if lo_out is not None:
                c.dma("scalar", lo_out[t0:t0 + 128, :], lod, "s_rst%d" % p)
        done[ch] = c.cur("s_rst%d" % p)


def emit_l0a(c):
    wq, wk, wv, wo, wfe = (c.t["wq"], c.t["wk"], c.t["wv"], c.t["wo"],
                           c.t["wfeat"])
    P0, P1, P2, P3, P4 = (c.t["pp0"], c.t["pp1"], c.t["pp2"], c.t["pp3"],
                          c.t["pp4"])
    ST = [c.t["st0"], c.t["st1"], c.t["st2"]]
    STW = [455, 455, 130]
    PP = [P0, P1]
    PN = [P2, P3]

    xnf, khh, qhh = c.t["a_xnf"], c.t["a_khh"], c.t["a_qhh"]
    sqh, vaug = c.t["a_sqh"], c.t["a_vaug"]
    kpf, qpf, kptm = c.t["a_kpf"], c.t["a_qpf"], c.t["a_kptm"]
    sqm, am, numb = c.t["a_sqm"], c.t["a_am"], c.t["a_numb"]
    yall, yo = c.t["a_yall"], c.t["a_yo"]
    sf, sb = c.t["a_sf32"], c.t["a_sbf"]
    dnr, recb = c.t["a_dn"], c.t["a_rec"]
    sx, xnc = c.t["r_x"], c.t["r_xn"]
    tril4, shead, ucol, ones1 = (c.t["c_tril4"], c.t["c_shead"],
                                 c.t["c_ucol"], c.t["c_ones1"])
    ones1f = c.t["c_ones1f"]
    onesr, epsv = c.t["a_onesr"], c.t["a_epsv"]
    flag = c.t["c_flag"]
    xn0, x_pre, x_own, y0 = (c.d["xn0"].ap(), c.d["x_pre"].ap(),
                             c.d["x_own"].ap(), c.d["y0"].ap())
    SREG = [0, 455, 910]

    c.op("vector", lambda e: e.memset(sf[:], 0.0))
    c.op("vector", lambda e: e.memset(onesr[:], 1.0))
    c.op("vector", lambda e: e.memset(epsv[:], 0.0), "s_dv")
    c.wait("vector", "s_dv")
    c.op("vector", lambda e: e.memset(epsv[0:1, 64:65], EPS_FAVOR))
    c.op("vector", lambda e: e.memset(
        vaug.ap().rearrange("p (s hh ee) -> p s hh ee", s=4, ee=65)
        [:, :, :, 64:65], 1.0), "s_dv")
    init_dv = c.cur("s_dv")

    state_done = 0
    state_na = 0
    prev_y = {}
    ln_store = {}
    prev_ln_dv = 0
    xnf_free = 0

    def sqrows(src, slot):
        c.wait("scalar", "s_ac")
        pe_g = {}
        for g in range(8):
            p = g % 2
            if g >= 2:
                c.wait("scalar", "s_pe", pe_g[g - 2])
            c.op("scalar", lambda e, g=g, p=p: e.activation(
                sqh[0:64, p * 2 * BLK:(p + 1) * 2 * BLK],
                src[0:64, 2 * g * BLK:2 * (g + 1) * BLK], AF.Square), "s_ac")
            c.wait("tensor", "s_ac")
            for hh in range(2):
                h = 2 * g + hh
                pe_g[g] = c.op("tensor", lambda e, h=h, p=p, hh=hh: e.matmul(
                    P4[0:16, 0:BLK], ucol[0:64, h * 16:h * 16 + 16],
                    sqh[0:64, (p * 2 + hh) * BLK:(p * 2 + hh + 1) * BLK],
                    start=(h == 0), stop=(h == H - 1)), "s_pe")
        c.wait("scalar", "s_pe")
        return c.op("scalar", lambda e, slot=slot: e.activation(
            sqm[:, slot * BLK:(slot + 1) * BLK], P4[0:16, 0:BLK],
            AF.Identity, scale=-0.5), "s_ac")

    blk_list = list(range(2 * NBLK))
    if DEV_NBLK:
        blk_list = list(range(NBLK, NBLK + DEV_NBLK))
    for blk in blk_list:
        own = blk >= NBLK
        xsrc_d = x_own if own else x_pre
        t0 = (blk - NBLK) * BLK if own else blk * BLK
        g0 = blk * BLK

        for j in range(NSUB):
            p = j % 2
            c.wait("sync", "s_dv", prev_ln_dv)
            c.dma("sync", sx[:, p * D:(p + 1) * D],
                  xsrc_d[t0 + j * 128:t0 + j * 128 + 128, :], "s_in%d" % p)
            c.wait("scalar", "s_in%d" % p)
            c.wait("vector", "s_in%d" % p)
            c.wait("vector", "s_xn%d" % p)
            prev_ln_dv = emit_ln_chunk(c, sx[:, p * D:(p + 1) * D],
                                       xnc[:, p * D:(p + 1) * D], None,
                                       prev_ln_dv)
            c.wait("scalar", "s_dv")
            c.dma("scalar", xn0[g0 + j * 128:g0 + j * 128 + 128, :],
                  xnc[:, p * D:(p + 1) * D], "s_xn%d" % p)
        c.wait("sync", "s_xn0")
        c.wait("sync", "s_xn1")
        c.wait("sync", "s_pe", xnf_free)
        for dc in range(DC):
            c.dma("sync", xnf[:, dc * BLK:(dc + 1) * BLK],
                  xn0[g0:g0 + BLK, dc * 128:(dc + 1) * 128], "s_tr",
                  transpose=True)

        if DEV_STAGE < 2:
            continue
        c.wait("tensor", "s_tr")
        if blk == 0:
            c.wait("tensor", "s_wt")
            c.wait("tensor", "s_dv", init_dv)

        def projh(dst, w, pp):
            ev = {}
            c.wait("tensor", "s_ac")
            for h in range(H):
                pb = pp[h % 2]
                if h >= 2:
                    c.wait("tensor", "s_ac", ev[h - 2])
                for dc in range(DC):
                    c.op("tensor", lambda e, pb=pb, w=w, dc=dc, h=h: e.matmul(
                        pb[0:64, 0:BLK],
                        w[:, dc * D + h * 64:dc * D + h * 64 + 64],
                        xnf[:, dc * BLK:(dc + 1) * BLK],
                        start=(dc == 0), stop=(dc == DC - 1)),
                        "s_pe" if dc == DC - 1 else None)
                c.wait("scalar", "s_pe")
                ev[h] = c.op("scalar", lambda e, pb=pb, dst=dst, h=h: e.copy(
                    dst[0:64, h * BLK:(h + 1) * BLK], pb[0:64, 0:BLK]), "s_ac")

        projh(khh, wk, PP)
        if own:
            projh(qhh, wq, PP)

        vev = {}
        c.wait("tensor", "s_ac")
        for g in range(8):
            half, j = g // 4, g % 4
            pb = PN[g % 2]
            if g >= 2:
                c.wait("tensor", "s_ac", vev[g - 2])
            for dc in range(DC):
                c.op("tensor", lambda e, pb=pb, dc=dc, half=half, j=j: e.matmul(
                    pb[:, 0:512],
                    xnf[:, dc * BLK + j * 128:dc * BLK + j * 128 + 128],
                    wv[:, dc * D + half * 512:dc * D + half * 512 + 512],
                    start=(dc == 0), stop=(dc == DC - 1)),
                    "s_pe" if dc == DC - 1 else None)
            c.wait("scalar", "s_pe")
            vdst = vaug.ap().rearrange("p (s hh ee) -> p s hh ee", s=4, ee=65)[
                :, j, half * 8:half * 8 + 8, 0:64]
            vev[g] = c.op("scalar", lambda e, pb=pb, vdst=vdst: e.copy(
                vdst, pb[:, 0:512].rearrange("p (hh ee) -> p hh ee", ee=64)),
                "s_ac")
        xnf_free = c.cur("s_pe")

        if DEV_STAGE < 3:
            continue
        sqrows(khh, 0)
        c.wait("tensor", "s_ac")
        if DEV_S3 < 2:
            continue

        def featmaps(srcf, dstf, slot):
            ev = {}
            c.wait("tensor", "s_ac")
            for h in range(H):
                pb = PP[h % 2]
                if h >= 2:
                    c.wait("tensor", "s_ac", ev[h - 2])
                c.op("tensor", lambda e, pb=pb, h=h, slot=slot: e.matmul(
                    pb[:, 0:BLK], shead[:, h * 128:h * 128 + 128],
                    sqm[:, slot * BLK:(slot + 1) * BLK],
                    start=True, stop=False))
                c.op("tensor", lambda e, pb=pb, h=h, srcf=srcf: e.matmul(
                    pb[:, 0:BLK], wfe[0:64, 0:128],
                    srcf[0:64, h * BLK:(h + 1) * BLK],
                    start=False, stop=True), "s_pe")
                c.wait("scalar", "s_pe")
                ev[h] = c.op("scalar", lambda e, pb=pb, h=h, dstf=dstf:
                             e.activation(dstf[:, h * BLK:(h + 1) * BLK],
                                          pb[:, 0:BLK], AF.Exp, bias=LNINV),
                             "s_ac")

        if own:
            featmaps(khh, kpf, 0)
        if DEV_S3 < 4:
            continue
        if own:
            sqrows(qhh, 1)
            c.wait("tensor", "s_ac")
            featmaps(qhh, qpf, 1)

        if DEV_STAGE < 4:
            continue
        c.wait("tensor", "s_ac")
        last_ymult = 0
        for j in range(NSUB):
            c.wait("tensor", "s_dv", last_ymult)
            for qd in range(4):
                c.wait("tensor", "s_ac")
                for hh in range(4):
                    h = 4 * qd + hh
                    c.op("tensor", lambda e, h=h, hh=hh, j=j: e.matmul(
                        P4[:, hh * 128:(hh + 1) * 128],
                        sqm[:, j * 128:(j + 1) * 128],
                        shead[:, h * 128:h * 128 + 128],
                        start=True, stop=False))
                    c.op("tensor", lambda e, h=h, hh=hh, j=j: e.matmul(
                        P4[:, hh * 128:(hh + 1) * 128],
                        khh[0:64, h * BLK + j * 128:h * BLK + j * 128 + 128],
                        wfe[0:64, 0:128], start=False, stop=True),
                        "s_pe" if hh == 3 else None)
                c.wait("scalar", "s_pe")
                c.op("scalar", lambda e, qd=qd: e.activation(
                    kptm[:, qd * 512:(qd + 1) * 512], P4[:, 0:512], AF.Exp,
                    bias=LNINV), "s_ac")
            c.wait("tensor", "s_ac")
            if own:
                PACKS = ((0, 2, 4, 6), (1, 3, 5, 7),
                         (8, 10, 12, 14), (9, 11, 13, 15))
                am_done, numb_done, rec_done, y_done = {}, {}, {}, {}
                pe_a, pe_n, pe_y = {}, {}, {}

                def eAp(p):
                    pa = PP[p % 2]
                    if p >= 2:
                        c.wait("tensor", "s_dv", am_done[p - 2])
                    for i, h in enumerate(PACKS[p]):
                        pe_a[p] = c.op(
                            "tensor", lambda e, pa=pa, h=h, i=i, j=j: e.matmul(
                                pa[:, i * 128:(i + 1) * 128],
                                kpf[:, h * BLK + j * 128:
                                    h * BLK + j * 128 + 128],
                                qpf[:, h * BLK + j * 128:
                                    h * BLK + j * 128 + 128],
                                start=True, stop=True),
                            "s_pe" if i == 3 else None)

                def eAm(p):
                    pa = PP[p % 2]
                    c.wait("vector", "s_pe", pe_a[p])
                    am_done[p] = c.op("vector", lambda e, pa=pa, p=p:
                                      e.tensor_tensor(
                                          am[:, (p % 2) * 512:
                                             (p % 2) * 512 + 512],
                                          pa[:, 0:512], tril4[:], ALU.mult),
                                      "s_dv")

                def eNum(p):
                    pn = PN[p % 2]
                    c.wait("tensor", "s_dv", am_done[p])
                    if p == 0:
                        c.wait("tensor", "s_ac", state_done)
                    if p >= 2:
                        c.wait("tensor", "s_ac", numb_done[p - 2])
                    for i, h in enumerate(PACKS[p]):
                        vsl = vaug[:, j * 1040 + h * 65:
                                   j * 1040 + (h + 1) * 65]
                        c.op("tensor", lambda e, pn=pn, h=h, i=i, j=j:
                             e.matmul(
                                 pn[0:65, i * 128:(i + 1) * 128],
                                 sb[:, h * 65:(h + 1) * 65],
                                 qpf[:, h * BLK + j * 128:
                                     h * BLK + j * 128 + 128],
                                 start=True, stop=False))
                        c.op("tensor", lambda e, pn=pn, i=i, vsl=vsl, p=p:
                             e.matmul(
                                 pn[0:65, i * 128:(i + 1) * 128], vsl,
                                 am[:, (p % 2) * 512 + i * 128:
                                    (p % 2) * 512 + (i + 1) * 128],
                                 start=False, stop=False))
                        pe_n[p] = c.op("tensor", lambda e, pn=pn, i=i:
                                       e.matmul(
                                           pn[0:65, i * 128:(i + 1) * 128],
                                           epsv[0:1, 0:65],
                                           onesr[0:1, 0:128],
                                           start=False, stop=True),
                                       "s_pe" if i == 3 else None)

                def eDs(p):
                    for i, h in enumerate(PACKS[p]):
                        sth, sto = ST[h // 7], (h % 7) * 65
                        vsl = vaug[:, j * 1040 + h * 65:
                                   j * 1040 + (h + 1) * 65]
                        c.op("tensor", lambda e, sth=sth, sto=sto, h=h,
                             vsl=vsl: e.matmul(
                                 sth[:, sto:sto + 65],
                                 kptm[:, h * 128:(h + 1) * 128],
                                 vsl, start=True, stop=True),
                             "s_pe" if i == 3 else None)

                def eRec(p):
                    pn = PN[p % 2]
                    c.wait("scalar", "s_pe", pe_n[p])
                    if p >= 2:
                        c.wait("scalar", "s_pe", pe_y[p - 2])
                    c.op("scalar", lambda e, pn=pn: e.activation(
                        dnr[0:1, 0:512], pn[64:65, 0:512], AF.Ln), "s_ac")
                    c.wait("scalar", "s_ac")
                    rec_done[p] = c.op("scalar", lambda e, p=p: e.activation(
                        recb[0:1, (p % 2) * 512:(p % 2) * 512 + 512],
                        dnr[0:1, 0:512], AF.Exp, scale=-1.0), "s_ac")

                def eYbc(p):
                    c.wait("tensor", "s_ac", rec_done[p])
                    if p >= 1:
                        c.wait("tensor", "s_dv", y_done[p - 1])
                    pe_y[p] = c.op("tensor", lambda e, p=p: e.matmul(
                        P4[0:64, 0:512], ones1[0:1, 0:64],
                        recb[0:1, (p % 2) * 512:(p % 2) * 512 + 512],
                        start=True, stop=True), "s_pe")

                def eNumbEv(p):
                    pn = PN[p % 2]
                    c.wait("scalar", "s_pe", pe_n[p])
                    yw = y_done[p - 2] if p >= 2 else prev_y.get(p + 2, 0)
                    c.wait("scalar", "s_dv", yw)
                    numb_done[p] = c.op("scalar", lambda e, pn=pn, p=p:
                                        e.copy(
                                            numb[:, (p % 2) * 512:
                                                 (p % 2) * 512 + 512],
                                            pn[0:64, 0:512]), "s_ac")

                def eYm(p):
                    c.wait("vector", "s_ac", numb_done[p])
                    c.wait("vector", "s_pe", pe_y[p])
                    r0 = (p % 2) * 64
                    c0 = (p // 2) * 4
                    ydst = yall[r0:r0 + 64, :].rearrange(
                        "p (cb b) -> p cb b", b=BLK)[
                        :, c0:c0 + 4, j * 128:j * 128 + 128]
                    nsrc = numb[0:64, (p % 2) * 512:
                                (p % 2) * 512 + 512].rearrange(
                        "p (cb b) -> p cb b", b=128)
                    psrc = P4[0:64, 0:512].rearrange(
                        "p (cb b) -> p cb b", b=128)
                    y_done[p] = c.op("vector", lambda e, ydst=ydst, nsrc=nsrc,
                                     psrc=psrc: e.tensor_tensor(
                                         ydst, nsrc, psrc, ALU.mult), "s_dv")

                eAp(0); eAm(0)
                eAp(1); eAm(1)
                eNum(0); eDs(0); eRec(0); eNumbEv(0)
                eAp(2); eAm(2)
                eYbc(0)
                eNum(1); eDs(1); eRec(1); eNumbEv(1)
                eYm(0)
                eAp(3); eAm(3)
                eYbc(1)
                eNum(2); eDs(2); eRec(2); eNumbEv(2)
                eYm(1)
                eYbc(2)
                eNum(3); eDs(3); eRec(3); eNumbEv(3)
                eYm(2)
                eYbc(3)
                eYm(3)
                last_ymult = y_done[3]
                prev_y = {2: y_done[2], 3: y_done[3]}
            else:
                c.wait("tensor", "s_dv", state_na)
                for h in range(H):
                    sth, sto = ST[h // 7], (h % 7) * 65
                    vsl = vaug[:, j * 1040 + h * 65:j * 1040 + (h + 1) * 65]
                    c.op("tensor", lambda e, sth=sth, sto=sto, h=h, j=j,
                         vsl=vsl: e.matmul(
                             sth[:, sto:sto + 65],
                             kptm[:, h * 128:(h + 1) * 128],
                             vsl, start=True, stop=True),
                         "s_pe" if h == H - 1 else None)
            c.wait("vector", "s_pe")
            c.wait("vector", "s_dv", state_na)
            for si in range(3):
                state_na = c.op("vector", lambda e, si=si: e.tensor_tensor(
                    sf[:, SREG[si]:SREG[si] + STW[si]],
                    sf[:, SREG[si]:SREG[si] + STW[si]],
                    ST[si][:, 0:STW[si]], ALU.add), "s_dv")
            if own and not (blk == 2 * NBLK - 1 and j == NSUB - 1):
                c.wait("scalar", "s_dv", state_na)
                state_done = c.op("scalar", lambda e: e.copy(sb[:], sf[:]),
                                  "s_ac")

        if blk == NBLK - 1:
            c.wait("vector", "s_dv", state_na)
            state_na = c.op("vector", lambda e: e.tensor_scalar(
                sf[:], sf[:], flag[:, 0:1], None, ALU.mult), "s_dv")
            c.wait("scalar", "s_dv", state_na)
            state_done = c.op("scalar", lambda e: e.copy(sb[:], sf[:]),
                              "s_ac")

        if DEV_STAGE < 5:
            continue
        if own:
            c.wait("tensor", "s_dv")
            c.wait("scalar", "s_yst0")
            c.wait("scalar", "s_yst1")
            yst = {}
            for oc in range(DC):
                pb = PP[oc % 2]
                if oc >= 2:
                    c.wait("tensor", "s_ac")
                for ec in range(DC):
                    c.op("tensor", lambda e, pb=pb, oc=oc, ec=ec: e.matmul(
                        pb[:, 0:BLK],
                        wo[:, ec * D + oc * 128:ec * D + oc * 128 + 128],
                        yall[:, ec * BLK:(ec + 1) * BLK],
                        start=(ec == 0), stop=(ec == DC - 1)),
                        "s_pe" if ec == DC - 1 else None)
                c.wait("scalar", "s_pe")
                if oc >= 2:
                    c.wait("scalar", "s_yst%d" % (oc % 2), yst[oc - 2])
                nev = c.op("scalar", lambda e, pb=pb, oc=oc: e.copy(
                    yo[:, (oc % 2) * BLK:(oc % 2) * BLK + BLK], pb[:, 0:BLK]),
                    "s_ac")
                c.wait("scalar", "s_ac", nev)
                yst[oc] = c.dma("scalar",
                                y0[oc * 128:(oc + 1) * 128, t0:t0 + BLK],
                                yo[:, (oc % 2) * BLK:(oc % 2) * BLK + BLK],
                                "s_yst%d" % (oc % 2))




TT = 1024
NTT = TOWN // TT


def emit_l0f(c):
    P0, P1, P2, P3, P4 = (c.t["pp0"], c.t["pp1"], c.t["pp2"], c.t["pp3"],
                          c.t["pp4"])
    ST0, ST1 = c.t["st0"], c.t["st1"]
    PP, PN = [P0, P1], [P2, P3]
    fxh, fxl = c.t["f_xh"], c.t["f_xl"]
    acc = c.t["f_acc"]
    w1s, w2s, b1r = c.t["f_w1"], c.t["f_w2"], c.t["f_b1"]
    hsb, hms, ybf = c.t["f_h"], c.t["f_hm"], c.t["f_y"]
    mkf = c.t["f_mk"]
    lg, mx, mk32 = c.t["f_lg"], c.t["f_mx"], c.t["f_mk32"]
    wgh, wgl, wgf, bgb = c.t["f_wgh"], c.t["f_wgl"], c.t["f_wgf"], c.t["f_bgb"]
    b2b, b2f, sel8 = c.t["f_b2b"], c.t["f_b2f"], c.t["c_sel8"]
    identf = c.t["c_identf"]
    ones1 = c.t["c_ones1"]
    xn1, xn1lo = c.d["xn1"].ap(), c.d["xn1lo"].ap()
    w1d, w2d = c.d["f_w1d"].ap(), c.d["f_w2d"].ap()
    y1 = c.d["y1"].ap()

    c.dma("sync", wgf[:].rearrange("p (dc e) -> p dc e", e=8),
          c.d["f_wgd"].ap().rearrange("(dc p) e -> p dc e", p=128), "s_ld")
    c.dma("sync", c.t["f_bgf"][:], c.d["f_bgd"].ap(), "s_ld")
    c.dma("sync", b2f[:], c.d["f_b2d"].ap(), "s_ld")
    c.dma("sync", b1r[:], c.d["f_b1d"].ap(), "s_ld")
    c.dma("gpsimd", sel8[:], c.d["c_sel8d"].ap(), "s_wt")
    c.dma("sync", identf[:], c.d["c_identfd"].ap(), "s_ld")
    c.wait("vector", "s_ld")
    c.wait("scalar", "s_ld")
    c.wait("vector", "s_wt")
    n = c.op("vector", lambda e: e.tensor_scalar(wgh[:], wgf[:], 0.0, None,
                                                 ALU.add), "s_dv")
    c.wait("vector", "s_dv", n)
    c.op("vector", lambda e: e.tensor_tensor(wgl[:], wgf[:], wgh[:],
                                             ALU.subtract), "s_dv")
    n = c.op("vector", lambda e: e.tensor_scalar(b2b[:], b2f[:], 0.0, None,
                                                 ALU.add), "s_dv")
    c.op("vector", lambda e: e.memset(c.t["c_ones1f"][:], 1.0), "s_dv")
    c.wait("tensor", "s_dv")
    c.wait("tensor", "s_ld")
    n = c.op("tensor", lambda e: e.matmul(
        ST1[0:128, 0:8], c.t["c_ones1f"][0:1, :], c.t["f_bgf"][0:1, :],
        start=True, stop=True), "s_pe")
    c.wait("vector", "s_pe", n)
    nbg = c.op("vector", lambda e: e.tensor_scalar(
        bgb[:], ST1[0:128, 0:8], 0.0, None, ALU.add), "s_dv")

    pe_fx_free = 0
    last_hm = 0
    conv_dv = 0
    last_store = {0: 0, 1: 0}
    ntt = int(os.environ.get("DEV_NTT", "0")) or NTT
    for tt in range(ntt):
        t0 = tt * TT
        c.wait("sync", "s_pe", pe_fx_free)
        for dc in range(DC):
            c.dma("sync", fxh[:, dc * TT:(dc + 1) * TT],
                  xn1[t0:t0 + TT, dc * 128:(dc + 1) * 128], "s_tr",
                  transpose=True)
            c.dma("sync", fxl[:, dc * TT:(dc + 1) * TT],
                  xn1lo[t0:t0 + TT, dc * 128:(dc + 1) * 128], "s_tr",
                  transpose=True)

        c.wait("tensor", "s_tr")
        if tt == 0:
            c.wait("tensor", "s_dv", nbg)
        lgf_done = nbg
        tr_evac = 0
        for ch in range(TT // 128):
            c.wait("tensor", "s_dv", lgf_done)
            mms = []
            for src, wgx in ((fxh, wgh), (fxh, wgl), (fxl, wgh)):
                for dc in range(DC):
                    mms.append((src, wgx, dc))
            for i, (src, wgx, dc) in enumerate(mms):
                c.op("tensor", lambda e, src=src, wgx=wgx, dc=dc, ch=ch,
                     i=i, last=(i == len(mms) - 1): e.matmul(
                         ST0[0:128, 0:8],
                         src[:, dc * TT + ch * 128:dc * TT + ch * 128 + 128],
                         wgx[:, dc * 8:(dc + 1) * 8],
                         start=(i == 0), stop=last),
                     "s_pe" if i == len(mms) - 1 else None)
            c.wait("vector", "s_pe")
            n = c.op("vector", lambda e: e.tensor_tensor(
                lg[:], ST0[0:128, 0:8], bgb[:], ALU.add), "s_dv")
            lgf_done = n
            c.wait("vector", "s_dv", n)
            n = c.op("vector", lambda e: e.reduce_max(
                mx[:], lg[:], axis=mybir.AxisListType.X), "s_dv")
            c.wait("vector", "s_dv", n)
            n = c.op("vector", lambda e: e.tensor_scalar(
                mk32[:], lg[:], mx[:, 0:1], None, ALU.is_ge), "s_dv")
            c.wait("tensor", "s_dv", n)
            c.wait("tensor", "s_ac", tr_evac)
            n = c.op("tensor", lambda e: e.transpose(
                ST1[0:8, 128:256], mk32[:], identf[:]), "s_pe")
            c.wait("scalar", "s_pe", n)
            tr_evac = c.op("scalar", lambda e, ch=ch: e.copy(
                mkf[0:8, ch * 128:(ch + 1) * 128], ST1[0:8, 128:256]), "s_ac")

        for u in range(16):
            eix, hf = u // 2, u % 2
            pu = u % 2
            c.wait("gpsimd", "s_pe", c._l0f_wfree.get(pu, 0))
            c.dma("gpsimd",
                  w1s.ap().rearrange("p (b dc i) -> p b dc i", b=2, i=512)
                  [:, pu], w1d[eix].rearrange("(dc p) i -> p dc i", p=128)
                  [:, :, hf * 512:(hf + 1) * 512], "s_w%d" % pu)
            c.dma("gpsimd",
                  w2s.ap().rearrange("p (b ic o) -> p b ic o", b=2, o=1024)
                  [:, pu],
                  w2d[eix, hf * 512:(hf + 1) * 512, :]
                  .rearrange("(ic p) o -> p ic o", p=128), "s_w%d" % pu)
            c.wait("tensor", "s_w%d" % pu)
            if u < 2:
                c.wait("tensor", "s_ac", tr_evac)
            last_mm = 0
            for sub in range(TT // 512):
                hev = {}
                c.wait("tensor", "s_ac")
                for ic in range(4):
                    pb = PP[ic % 2]
                    if ic >= 2:
                        c.wait("tensor", "s_ac", hev[ic - 2])
                    for dc in range(DC):
                        c.op("tensor", lambda e, pb=pb, ic=ic, dc=dc, pu=pu,
                             sub=sub: e.matmul(
                                 pb[:, 0:512],
                                 w1s[:, (pu * 8 + dc) * 512 + ic * 128:
                                     (pu * 8 + dc) * 512 + ic * 128 + 128],
                                 fxh[:, dc * TT + sub * 512:
                                     dc * TT + sub * 512 + 512],
                                 start=(dc == 0), stop=(dc == DC - 1)),
                             "s_pe" if dc == DC - 1 else None)
                    c.wait("scalar", "s_pe")
                    hev[ic] = c.op("scalar", lambda e, pb=pb, ic=ic, u=u:
                                   e.activation(
                                       hsb[:, ic * 512:(ic + 1) * 512],
                                       pb[:, 0:512], AF.Gelu,
                                       bias=b1r[:, u * 4 + ic:u * 4 + ic + 1]),
                                   "s_ac")
                c.wait("tensor", "s_dv", last_hm)
                pe_mb = c.op("tensor", lambda e, eix=eix, sub=sub: e.matmul(
                    P4[:, 0:512], sel8[:, eix * 128:(eix + 1) * 128],
                    mkf[0:8, sub * 512:(sub + 1) * 512],
                    start=True, stop=True), "s_pe")
                c.wait("vector", "s_pe", pe_mb)
                c.wait("vector", "s_ac")
                for ic in range(4):
                    last_hm = c.op("vector", lambda e, ic=ic: e.tensor_tensor(
                        hms[:, ic * 512:(ic + 1) * 512],
                        hsb[:, ic * 512:(ic + 1) * 512], P4[:, 0:512],
                        ALU.mult), "s_dv")
                c.wait("tensor", "s_dv", last_hm)
                aev = {}
                for oc in range(DC):
                    pd = PN[oc % 2]
                    if oc >= 2:
                        c.wait("tensor", "s_dv", aev[oc - 2])
                    for ic in range(4):
                        last_mm = c.op(
                            "tensor", lambda e, pd=pd, oc=oc, ic=ic, pu=pu:
                            e.matmul(
                                pd[:, 0:512],
                                w2s[:, (pu * 4 + ic) * 1024 + oc * 128:
                                    (pu * 4 + ic) * 1024 + oc * 128 + 128],
                                hms[:, ic * 512:(ic + 1) * 512],
                                start=(ic == 0), stop=(ic == 3)),
                            "s_pe" if ic == 3 else None)
                    c.wait("vector", "s_pe")
                    if u == 0:
                        c.wait("vector", "s_dv", conv_dv)
                    dst = acc[:, oc * TT + sub * 512:oc * TT + sub * 512 + 512]
                    aev[oc] = c.op("vector", lambda e, pd=pd, dst=dst,
                                   first=(u == 0): (
                                       e.tensor_scalar(dst, pd[:, 0:512], 0.0,
                                                       None, ALU.add)
                                       if first else
                                       e.tensor_tensor(dst, dst, pd[:, 0:512],
                                                       ALU.add)), "s_dv")
            c._l0f_wfree[pu] = last_mm

        for sub in range(TT // 512):
            for oc in range(DC):
                pd = PN[oc % 2]
                c.wait("tensor", "s_dv")
                n = c.op("tensor", lambda e, pd=pd, oc=oc, sub=sub: e.matmul(
                    pd[:, 0:512], b2b[0:8, oc * 128:(oc + 1) * 128],
                    mkf[0:8, sub * 512:(sub + 1) * 512],
                    start=True, stop=True), "s_pe")
                c.wait("vector", "s_pe", n)
                dst = acc[:, oc * TT + sub * 512:oc * TT + sub * 512 + 512]
                c.op("vector", lambda e, pd=pd, dst=dst: e.tensor_tensor(
                    dst, dst, pd[:, 0:512], ALU.add), "s_dv")
        pe_fx_free = c.cur("s_pe")
        c.wait("vector", "s_dv")
        for oc in range(DC):
            p = oc % 2
            c.wait("vector", "s_yst%d" % p, last_store[p])
            conv_dv = c.op("vector", lambda e, oc=oc, p=p: e.tensor_scalar(
                ybf[:, p * TT:(p + 1) * TT], acc[:, oc * TT:(oc + 1) * TT],
                0.0, None, ALU.add), "s_dv")
            c.wait("scalar", "s_dv", conv_dv)
            last_store[p] = c.dma(
                "scalar", y1[oc * 128:(oc + 1) * 128, t0:t0 + TT],
                ybf[:, p * TT:(p + 1) * TT], "s_yst%d" % p)


def emit_l1a(c):
    P0, P1, P2, P3, P4 = (c.t["pp0"], c.t["pp1"], c.t["pp2"], c.t["pp3"],
                          c.t["pp4"])
    ST0 = c.t["st0"]
    PP = [P0, P1]
    xnf = c.t["g_xnf"]
    kraw = c.t["g_kraw"]
    ksq = c.t["g_ksq"]
    knb = c.t["g_kn"]
    vsb = c.t["g_v"]
    rows = c.t["g_rows"]
    arow, brow = c.t["g_ab"][0:1, 0:512], c.t["g_ab"][0:1, 512:1024]
    bqk = c.t["g_bqk"]
    bvr = c.t["g_bvr"]
    bols = c.t["g_bols"]
    wq, wk, wv, wo = c.t["gw_q"], c.t["gw_k"], c.t["gw_v"], c.t["gw_o"]
    ones1, ones128 = c.t["c_ones1"], c.t["c_ones128"]
    xn2 = c.d["xn2i"].ap()
    xn2f = c.d["xn2f"].ap()
    kn_d, v_d, qn_d = c.d["kn_d"].ap(), c.d["v_d"].ap(), c.d["qn_d"].ap()
    ao_tm, y2 = c.d["ao_tm"].ap(), c.d["y2"].ap()

    c.op("vector", lambda e: e.memset(ones128[:], 1.0), "s_dv")
    c.wait("vector", "s_ld")
    n = c.op("vector", lambda e: e.tensor_tensor(
        bols[:, 8:16], bols[:, 8:16], bols[:, 0:8], ALU.mult), "s_dv")

    pe_xnf_free = 0
    st_par = {0: 0, 1: 0}
    for til in range(24):
        qpass = til >= 16
        src = xn2 if qpass else xn2f
        t0 = (til - 16) * BLK if qpass else til * BLK
        g0 = til * BLK
        c.wait("sync", "s_pe", pe_xnf_free)
        for dc in range(DC):
            c.dma("sync", xnf[:, dc * BLK:(dc + 1) * BLK],
                  src[t0:t0 + BLK, dc * 128:(dc + 1) * 128], "s_tr",
                  transpose=True)
        c.wait("tensor", "s_tr")
        if til == 0:
            c.wait("tensor", "s_wt")
            c.wait("tensor", "s_dv", n)

        def qkproj(dst, w, bcol):
            ev = {}
            c.wait("tensor", "s_ac")
            c.wait("scalar", "s_dv")
            for ec in range(DC):
                pb = PP[ec % 2]
                if ec >= 2:
                    c.wait("tensor", "s_ac", ev[ec - 2])
                for dc in range(DC):
                    c.op("tensor", lambda e, pb=pb, w=w, dc=dc, ec=ec: e.matmul(
                        pb[:, 0:BLK],
                        w[:, dc * D + ec * 128:dc * D + ec * 128 + 128],
                        xnf[:, dc * BLK:(dc + 1) * BLK],
                        start=(dc == 0), stop=(dc == DC - 1)),
                        "s_pe" if dc == DC - 1 else None)
                c.wait("scalar", "s_pe")
                ev[ec] = c.op("scalar", lambda e, pb=pb, dst=dst, ec=ec,
                              bcol=bcol: e.activation(
                                  dst[:, ec * BLK:(ec + 1) * BLK],
                                  pb[:, 0:BLK], AF.Identity,
                                  bias=bqk[:, bcol + ec:bcol + ec + 1]),
                              "s_ac")
            return ev

        def lnrows(src_sb, dst_sb, dst_d, grow):
            c.wait("vector", "s_st0")
            c.wait("vector", "s_st1")
            c.wait("scalar", "s_ac")
            nsq = c.op("scalar", lambda e: e.activation(ksq[:], src_sb[:],
                                                        AF.Square), "s_ac")
            c.wait("tensor", "s_ac", nsq)
            c.wait("tensor", "s_dv")
            for ec in range(DC):
                c.op("tensor", lambda e, ec=ec: e.matmul(
                    P2[0:1, 0:BLK], ones128[:, 0:1],
                    src_sb[:, ec * BLK:(ec + 1) * BLK],
                    start=(ec == 0), stop=(ec == DC - 1)),
                    "s_pe" if ec == DC - 1 else None)
            for ec in range(DC):
                c.op("tensor", lambda e, ec=ec: e.matmul(
                    P3[0:1, 0:BLK], ones128[:, 0:1],
                    ksq[:, ec * BLK:(ec + 1) * BLK],
                    start=(ec == 0), stop=(ec == DC - 1)),
                    "s_pe" if ec == DC - 1 else None)
            c.wait("vector", "s_pe")
            nn = c.op("vector", lambda e: e.tensor_scalar(
                rows[0:1, 0:512], P2[0:1, 0:BLK], 1.0 / D, None, ALU.mult),
                "s_dv")
            nn2 = c.op("vector", lambda e: e.tensor_scalar(
                rows[0:1, 512:1024], P3[0:1, 0:BLK], 1.0 / D, None, ALU.mult),
                "s_dv")
            c.wait("vector", "s_dv", nn)
            nn3 = c.op("vector", lambda e: e.tensor_tensor(
                c.t["g_r2"][0:1, 0:512], rows[0:1, 0:512], rows[0:1, 0:512],
                ALU.mult), "s_dv")
            c.wait("vector", "s_dv", max(nn2, nn3))
            nn = c.op("vector", lambda e: e.tensor_tensor(
                c.t["g_r2"][0:1, 512:1024], rows[0:1, 512:1024],
                c.t["g_r2"][0:1, 0:512], ALU.subtract), "s_dv")
            c.wait("vector", "s_dv", nn)
            nn = c.op("vector", lambda e: e.tensor_scalar(
                c.t["g_r2"][0:1, 512:1024], c.t["g_r2"][0:1, 512:1024],
                LN_EPS, None, ALU.add), "s_dv")
            c.wait("vector", "s_dv", nn)
            nn = c.op("vector", lambda e: e.reciprocal(
                c.t["g_r3"][0:1, 0:512], c.t["g_r2"][0:1, 512:1024]), "s_dv")
            c.wait("scalar", "s_dv", nn)
            nn = c.op("scalar", lambda e: e.activation(
                c.t["g_r3"][0:1, 512:1024], c.t["g_r3"][0:1, 0:512], AF.Sqrt),
                "s_ac")
            c.wait("vector", "s_ac", nn)
            nn = c.op("vector", lambda e: e.tensor_scalar(
                arow[:], c.t["g_r3"][0:1, 512:1024], 0.0, None, ALU.add),
                "s_dv")
            c.wait("vector", "s_dv", nn)
            nn = c.op("vector", lambda e: e.tensor_tensor(
                brow[:], rows[0:1, 0:512], arow[:], ALU.mult), "s_dv")
            c.wait("vector", "s_dv", nn)
            nb = c.op("vector", lambda e: e.tensor_scalar(
                brow[:], brow[:], -1.0, None, ALU.mult), "s_dv")
            c.wait("tensor", "s_dv", nb)
            c.op("tensor", lambda e: e.matmul(
                P2[:, 0:BLK], ones1[0:1, :], arow[:], start=True, stop=True))
            nbc = c.op("tensor", lambda e: e.matmul(
                P3[:, 0:BLK], ones1[0:1, :], brow[:], start=True, stop=True),
                "s_pe")
            c.wait("vector", "s_pe", nbc)
            last = 0
            for ec in range(DC):
                n1 = c.op("vector", lambda e, ec=ec: e.tensor_tensor(
                    ksq[:, ec * BLK:(ec + 1) * BLK],
                    src_sb[:, ec * BLK:(ec + 1) * BLK], P2[:, 0:BLK],
                    ALU.mult), "s_dv")
                c.wait("vector", "s_dv", n1)
                last = c.op("vector", lambda e, ec=ec: e.tensor_tensor(
                    dst_sb[:, ec * BLK:(ec + 1) * BLK],
                    ksq[:, ec * BLK:(ec + 1) * BLK], P3[:, 0:BLK],
                    ALU.add), "s_dv")
            c.wait("scalar", "s_dv", last)
            par = til % 2
            c.wait("scalar", "s_st%d" % par, st_par[par])
            st_par[par] = c.dma(
                "scalar",
                dst_d.rearrange("(dc p) t -> p dc t", p=128)[:, :, grow:grow + BLK],
                dst_sb[:].rearrange("p (dc t) -> p dc t", t=BLK),
                "s_st%d" % par)
            return last

        if qpass:
            qkproj(kraw, wq, 0)
            lnrows(kraw, knb, qn_d, t0)
            pe_xnf_free = c.cur("s_pe")
            continue
        qkproj(kraw, wk, 8)
        lnrows(kraw, knb, kn_d, g0)

        vev = {}
        c.wait("tensor", "s_ac")
        for g in range(8):
            half, j = g // 4, g % 4
            pb = PP[g % 2]
            if g >= 2:
                c.wait("tensor", "s_ac", vev[g - 2])
            c.op("tensor", lambda e, pb=pb, half=half: e.matmul(
                pb[:, 0:512], ones1[0:1, 0:128],
                bvr[0:1, half * 512:(half + 1) * 512],
                start=True, stop=False))
            for dc in range(DC):
                c.op("tensor", lambda e, pb=pb, dc=dc, half=half, j=j: e.matmul(
                    pb[:, 0:512],
                    xnf[:, dc * BLK + j * 128:dc * BLK + j * 128 + 128],
                    wv[:, dc * D + half * 512:dc * D + half * 512 + 512],
                    start=False, stop=(dc == DC - 1)),
                    "s_pe" if dc == DC - 1 else None)
            c.wait("scalar", "s_pe")
            vev[g] = c.op("scalar", lambda e, pb=pb, j=j, half=half: e.copy(
                vsb[:, j * 1024 + half * 512:j * 1024 + half * 512 + 512],
                pb[:, 0:512]), "s_ac")
        pe_xnf_free = c.cur("s_pe")
        par = til % 2
        c.wait("scalar", "s_ac")
        c.wait("scalar", "s_st%d" % par, st_par[par])
        st_par[par] = c.dma(
            "scalar",
            v_d[g0:g0 + BLK, :].rearrange("(s p) e -> p s e", p=128),
            vsb[:].rearrange("p (s e) -> p s e", e=1024),
            "s_st%d" % par)
    c.wait("sync", "s_st0")
    c.wait("sync", "s_st1")

    qnt4 = c.t["gw_v"][:, :]
    accq = c.t["gw_q"][:, :].bitcast(F32)
    acck = c.t["gw_k"][:, :].bitcast(F32)
    att = c.t["g_att"]
    outb = c.t["g_outb"]
    knl = c.t["g_knl"]
    vbl = c.t["g_vbl"]
    OUTB = [P0, P1, P2, P3]
    GQ = 4

    def acc_ap(ql):
        base = accq if ql < 2 else acck
        return base[:, (ql % 2) * 2048:(ql % 2) * 2048 + 2048]

    ao_par = {0: 0, 1: 0}
    kv_free = {0: 0, 1: 0}
    qn_free = 0
    evac_ac = {}
    add_dv = {}
    last_add = [0]
    for sweep in range(16 // GQ):
        c.wait("sync", "s_pe", qn_free)
        for ql in range(GQ):
            qt = sweep * GQ + ql
            c.dma("sync",
                  qnt4[:, ql * 2048:(ql + 1) * 2048]
                  .rearrange("p (dc s) -> p dc s", s=256),
                  qn_d.rearrange("(dc p) t -> p dc t", p=128)
                  [:, :, qt * 256:qt * 256 + 256], "s_q0")
        c.wait("tensor", "s_q0")

        def emit_scores(u, spb_free):
            kb, ql, sub = u // 16, (u // 4) % 4, u % 4
            kpar = kb % 2
            if ql == 0 and sub == 0:
                c.wait("tensor", "s_kv%d" % kpar)
            c.wait("tensor", "s_ac", spb_free)
            spb = P4 if u % 2 == 0 else ST0
            for ec in range(DC):
                c.op("tensor", lambda e, spb=spb, ec=ec, sub=sub, ql=ql,
                     kpar=kpar: e.matmul(
                         spb[0:128, 0:256],
                         knl[:, kpar * 4096 + ec * 512 + sub * 128:
                             kpar * 4096 + ec * 512 + sub * 128 + 128],
                         qnt4[:, ql * 2048 + ec * 256:
                              ql * 2048 + (ec + 1) * 256],
                         start=(ec == 0), stop=(ec == DC - 1)),
                     "s_pe" if ec == DC - 1 else None)
            nsc = c.cur("s_pe")
            nse = c.op("scalar", lambda e, spb=spb, u=u, nsc=nsc: (
                e.wait_ge(c.sem("s_pe")[0], nsc),
                e.activation(att[:, (u % 2) * 256:(u % 2) * 256 + 256],
                             spb[0:128, 0:256], AF.Sigmoid, bias=-LOG_S,
                             scale=1.0 / 32.0))[-1], "s_ac")
            return nse

        def emit_acc(kb, ql, pe_tick):
            c.wait("vector", "s_pe", pe_tick)
            c.wait("vector", "s_dv", add_dv.get(ql, 0))
            if kb == 0:
                c.wait("vector", "s_ac", evac_ac.get(ql, 0))
            for g in range(4):
                dst = acc_ap(ql)[:, g * 512:(g + 1) * 512]
                if kb == 0:
                    add_dv[ql] = c.op(
                        "vector", lambda e, dst=dst, g=g: e.tensor_scalar(
                            dst, OUTB[g][0:128, 0:512], 0.0, None, ALU.add),
                        "s_dv")
                else:
                    add_dv[ql] = c.op(
                        "vector", lambda e, dst=dst, g=g: e.tensor_tensor(
                            dst, dst, OUTB[g][0:128, 0:512], ALU.add), "s_dv")
            last_add[0] = add_dv[ql]

        def emit_out(u, nse):
            kb, ql, sub = u // 16, (u // 4) % 4, u % 4
            kpar = kb % 2
            c.wait("tensor", "s_ac", nse)
            if sub == 0:
                c.wait("tensor", "s_dv", last_add[0])
            for g in range(4):
                c.op("tensor", lambda e, g=g, sub=sub, u=u, kpar=kpar:
                     e.matmul(
                         OUTB[g][0:128, 0:512],
                         att[:, (u % 2) * 256 + (g // 2) * 128:
                             (u % 2) * 256 + (g // 2) * 128 + 128],
                         vbl[:, kpar * 4096 + sub * 1024 + (g % 2) * 512:
                             kpar * 4096 + sub * 1024 + (g % 2) * 512 + 512],
                         start=(sub == 0), stop=(sub == 3)),
                     "s_pe" if g == 3 else None)
            if sub == 3:
                emit_acc(kb, ql, c.cur("s_pe"))
            if ql == GQ - 1 and sub == 3:
                kv_free[kpar] = c.cur("s_pe")
                if kb + 2 <= 15:
                    emit_kv_load(kb + 2)
                elif sweep < 16 // GQ - 1:
                    emit_kv_load(kb + 2 - 16)

        def emit_kv_load(kb):
            kpar = kb % 2
            c.wait("sync", "s_pe", kv_free[kpar])
            c.dma("sync",
                  knl[:, kpar * 4096:(kpar + 1) * 4096]
                  .rearrange("p (dc t) -> p dc t", t=512),
                  kn_d.rearrange("(dc p) t -> p dc t", p=128)
                  [:, :, kb * 512:(kb + 1) * 512], "s_kv%d" % kpar)
            c.dma("sync",
                  vbl[:, kpar * 4096:(kpar + 1) * 4096]
                  .rearrange("p (s e) -> p s e", e=1024),
                  v_d[kb * 512:(kb + 1) * 512, :]
                  .rearrange("(s p) e -> p s e", p=128), "s_kv%d" % kpar)

        if sweep == 0:
            emit_kv_load(0)
            emit_kv_load(1)
        prev = None
        nse_hist = {}
        for u in range(16 * GQ * 4):
            nse = emit_scores(u, nse_hist.get(u - 2, 0))
            nse_hist[u] = nse
            if prev is not None:
                emit_out(*prev)
            prev = (u, nse)
        emit_out(*prev)
        qn_free = c.cur("s_pe")

        c.wait("scalar", "s_pe")
        for ql in range(GQ):
            qt = sweep * GQ + ql
            par = qt % 2
            c.wait("scalar", "s_dv", add_dv[ql])
            c.wait("scalar", "s_ao%d" % par, ao_par[par])
            ev = 0
            for g in range(4):
                ev = c.op("scalar", lambda e, ql=ql, g=g, par=par: e.copy(
                    outb[:, par * 2048 + g * 512:par * 2048 + (g + 1) * 512],
                    acc_ap(ql)[:, g * 512:(g + 1) * 512]), "s_ac")
            evac_ac[ql] = ev
            c.wait("scalar", "s_ac", ev)
            ao_par[par] = c.dma(
                "scalar",
                ao_tm[qt * 256:qt * 256 + 256, :]
                .rearrange("(s p) e -> p s e", p=128),
                outb[:, par * 2048:(par + 1) * 2048]
                .rearrange("p (s e) -> p s e", e=1024), "s_ao%d" % par)
    c.wait("sync", "s_ao0")
    c.wait("sync", "s_ao1")

    aof = c.t["g_xnf"]
    ybf = c.t["g_kraw"]
    yst = {0: 0, 1: 0}
    pe_aof_free = c.cur("s_pe")
    for til in range(8):
        t0 = til * BLK
        c.wait("sync", "s_pe", pe_aof_free)
        for ec in range(DC):
            c.dma("sync", aof[:, ec * BLK:(ec + 1) * BLK],
                  ao_tm[t0:t0 + BLK, ec * 128:(ec + 1) * 128], "s_tr",
                  transpose=True)
        c.wait("tensor", "s_tr")
        c.wait("tensor", "s_ac")
        ev = {}
        for oc in range(DC):
            pb = PP[oc % 2]
            if oc >= 2:
                c.wait("tensor", "s_ac", ev[oc - 2])
            for ec in range(DC):
                c.op("tensor", lambda e, pb=pb, oc=oc, ec=ec: e.matmul(
                    pb[:, 0:BLK],
                    wo[:, ec * D + oc * 128:ec * D + oc * 128 + 128],
                    aof[:, ec * BLK:(ec + 1) * BLK],
                    start=(ec == 0), stop=(ec == DC - 1)),
                    "s_pe" if ec == DC - 1 else None)
            c.wait("scalar", "s_pe")
            par = oc % 2
            c.wait("scalar", "s_y%d" % par, yst[par])
            ev[oc] = c.op("scalar", lambda e, pb=pb, oc=oc, par=par:
                          e.activation(
                              ybf[:, par * BLK:(par + 1) * BLK], pb[:, 0:BLK],
                              AF.Identity,
                              bias=bols[:, 8 + oc:8 + oc + 1],
                              scale=bols[:, oc:oc + 1]), "s_ac")
            c.wait("scalar", "s_ac", ev[oc])
            yst[par] = c.dma("scalar",
                             y2[oc * 128:(oc + 1) * 128, t0:t0 + BLK],
                             ybf[:, par * BLK:(par + 1) * BLK], "s_y%d" % par)
        pe_aof_free = c.cur("s_pe")
    c.wait("sync", "s_y0")
    c.wait("sync", "s_y1")


def alloc_common(c):
    nc = c.nc
    dt = nc.dram_tensor
    c.d["c_pool"] = dt("c_pool", [128, len(CONST_VALS)], F32,
                       kind="ExternalInput")
    sbuf(c, "c_cpool", [128, len(CONST_VALS)], F32)
    for i, v in enumerate(CONST_VALS):
        nc.const_aps.register(F32, v, c.t["c_cpool"][:, i:i + 1])
    sbuf(c, "c_ones1", [1, 128], BF16)
    sbuf(c, "c_ones1f", [1, 128], F32)
    sbuf(c, "r_x", [128, 2 * D], F32)
    sbuf(c, "r_xs", [128, 2 * D], F32)
    sbuf(c, "r_stats", [128, 8], F32)
    sbuf(c, "r_sq", [128, D], BF16)
    sbuf(c, "r_xb", [128, D], BF16)
    sbuf(c, "r_f32", [128, D], F32)
    sbuf(c, "r_xn", [128, 2 * D], BF16)
    psum_t(c, "pp0", [128, 512])
    psum_t(c, "pp1", [128, 512])
    psum_t(c, "pp2", [128, 512])
    psum_t(c, "pp3", [128, 512])
    psum_t(c, "pp4", [128, 512])
    psum_t(c, "st0", [128, 455])
    psum_t(c, "st1", [128, 455])
    psum_t(c, "st2", [128, 130])
    c.dma("sync", c.t["c_cpool"][:], c.d["c_pool"].ap(), "s_ld")
    c.op("vector", lambda e: e.memset(c.t["c_ones1"][:], 1.0))
    c.op("vector", lambda e: e.memset(c.t["c_ones1f"][:], 1.0))


def finish(c, store_sems):
    for sname in store_sems:
        c.wait("sync", sname)
    c.finalize(c.block)


def build_program_a():
    nc = bass.Bass("TRN2", target_bir_lowering=False)
    stack = ExitStack()
    c = Ctx(nc, stack)
    with stack:
        dt = nc.dram_tensor
        c.d["x_pre"] = dt("x_pre", [TOWN, D], F32, kind="ExternalInput")
        c.d["x_own"] = dt("x_own", [TOWN, D], F32, kind="ExternalInput")
        c.d["w_q"] = dt("w_q", [D, D], F32, kind="ExternalInput")
        c.d["w_k"] = dt("w_k", [D, D], F32, kind="ExternalInput")
        c.d["w_v"] = dt("w_v", [D, D], F32, kind="ExternalInput")
        c.d["w_o"] = dt("w_o", [D, D], F32, kind="ExternalInput")
        c.d["w_f"] = dt("w_f", [HD, NF], F32, kind="ExternalInput")
        c.d["c_trild"] = dt("c_trild", [128, 128], F32, kind="ExternalInput")
        c.d["c_sheadd"] = dt("c_sheadd", [16, 16 * 128], F32,
                             kind="ExternalInput")
        c.d["c_ucold"] = dt("c_ucold", [64, 16 * 16], F32,
                            kind="ExternalInput")
        c.d["c_flagd"] = dt("c_flagd", [128, 1], F32, kind="ExternalInput")
        c.d["c_identfd"] = dt("c_identfd", [128, 128], F32,
                              kind="ExternalInput")
        c.d["xn0"] = dt("xn0", [2 * TOWN, D], BF16)
        c.d["y0"] = dt("y0", [D, TOWN], BF16)
        c.d["x1"] = dt("x1", [TOWN, D], F32, kind="ExternalOutput")
        c.d["xn1"] = dt("xn1", [TOWN, D], BF16, kind="ExternalOutput")
        c.d["xn1lo"] = dt("xn1lo", [TOWN, D], BF16, kind="ExternalOutput")

        for n in ("wq", "wk", "wv", "wo"):
            sbuf(c, n, [128, DC * D], BF16)
        sbuf(c, "wfeat", [64, 128], BF16)
        sbuf(c, "c_tril4", [128, 4 * 128], BF16)
        sbuf(c, "c_shead", [16, 16 * 128], BF16)
        sbuf(c, "c_ucol", [64, 16 * 16], BF16)
        sbuf(c, "c_flag", [128, 1], F32)
        sbuf(c, "a_xnf", [128, DC * BLK], BF16)
        sbuf(c, "a_khh", [64, H * BLK], BF16)
        sbuf(c, "a_qhh", [64, H * BLK], BF16)
        sbuf(c, "a_vaug", [128, 4 * 1040], BF16)
        sbuf(c, "a_kpf", [128, H * BLK], BF16)
        sbuf(c, "a_qpf", [128, H * BLK], BF16)
        sbuf(c, "a_kptm", [128, H * 128], BF16)
        sbuf(c, "a_sqm", [16, 2 * BLK], BF16)
        sbuf(c, "a_am", [128, 2 * 512], BF16)
        sbuf(c, "a_numb", [64, 2 * 512], BF16)
        sbuf(c, "a_yall", [128, DC * BLK], BF16)
        sbuf(c, "a_yo", [128, 2 * BLK], BF16)
        sbuf(c, "a_sbf", [128, H * 65], BF16)
        sbuf(c, "a_dn", [1, 512], F32)
        sbuf(c, "a_rec", [1, 2 * 512], BF16)
        sbuf(c, "a_onesr", [1, 512], BF16)
        sbuf(c, "a_epsv", [1, 65], BF16)
        c.t["a_sqh"] = c.t["a_yall"][0:64, 4 * BLK:8 * BLK]
        c.t["r_ybuf"] = c.t["a_yall"][:, 0:2 * D]
        c.t["r_lobuf"] = c.t["a_yall"][:, 2 * D:4 * D]

        c.block = stack.enter_context(nc.Block())
        alloc_common(c)
        c.t["a_sf32"] = c.t["r_xs"][:, 0:H * 65]
        for i in range(4):
            c.dma("gpsimd", c.t["c_tril4"][:, i * 128:(i + 1) * 128],
                  c.d["c_trild"].ap(), "s_wt")
        c.dma("gpsimd", c.t["c_shead"][:], c.d["c_sheadd"].ap(), "s_wt")
        c.dma("gpsimd", c.t["c_ucol"][:], c.d["c_ucold"].ap(), "s_wt")
        c.dma("sync", c.t["c_flag"][:], c.d["c_flagd"].ap(), "s_ld")
        load_w_dmajor(c, c.t["wq"], c.d["w_q"].ap(), D)
        load_w_dmajor(c, c.t["wk"], c.d["w_k"].ap(), D)
        load_w_dmajor(c, c.t["wv"], c.d["w_v"].ap(), D)
        load_w_dmajor(c, c.t["wo"], c.d["w_o"].ap(), D)
        c.dma("gpsimd", c.t["wfeat"][:], c.d["w_f"].ap(), "s_wt")
        for eng in ("scalar", "vector"):
            c.wait(eng, "s_wt")
            c.wait(eng, "s_ld")
        c.wait("tensor", "s_ld")

        emit_l0a(c)
        c.wait("sync", "s_yst0")
        c.wait("sync", "s_yst1")
        c.wait("vector", "s_xn0")
        c.wait("vector", "s_xn1")
        c.wait("scalar", "s_xn0")
        c.wait("scalar", "s_xn1")
        c.wait("sync", "s_pe")
        emit_res(c, c.d["x_own"].ap(), c.d["y0"].ap(), c.d["x1"].ap(),
                 c.d["xn1"].ap(), c.d["xn1lo"].ap())
        finish(c, ["s_rst0", "s_rst1"])
    return nc


def build_program_b():
    nc = bass.Bass("TRN2", target_bir_lowering=False)
    stack = ExitStack()
    c = Ctx(nc, stack)
    with stack:
        dt = nc.dram_tensor
        c.d["x1"] = dt("x1", [TOWN, D], F32, kind="ExternalInput")
        c.d["xn1"] = dt("xn1", [TOWN, D], BF16, kind="ExternalInput")
        c.d["xn1lo"] = dt("xn1lo", [TOWN, D], BF16, kind="ExternalInput")
        c.d["f_wgd"] = dt("f_wgd", [D, 8], F32, kind="ExternalInput")
        c.d["f_bgd"] = dt("f_bgd", [1, 8], F32, kind="ExternalInput")
        c.d["f_w1d"] = dt("f_w1d", [8, D, D], F32, kind="ExternalInput")
        c.d["f_b1d"] = dt("f_b1d", [128, 64], F32, kind="ExternalInput")
        c.d["f_w2d"] = dt("f_w2d", [8, D, D], F32, kind="ExternalInput")
        c.d["f_b2d"] = dt("f_b2d", [8, D], F32, kind="ExternalInput")
        c.d["c_sel8d"] = dt("c_sel8d", [8, 8 * 128], F32,
                            kind="ExternalInput")
        c.d["c_identfd"] = dt("c_identfd", [128, 128], F32,
                              kind="ExternalInput")
        c.d["y1"] = dt("y1", [D, TOWN], BF16)
        c.d["x2"] = dt("x2", [TOWN, D], F32, kind="ExternalOutput")
        c.d["xn2"] = dt("xn2", [TOWN, D], BF16, kind="ExternalOutput")

        sbuf(c, "f_xh", [128, DC * TT], BF16)
        sbuf(c, "f_xl", [128, DC * TT], BF16)
        sbuf(c, "f_acc", [128, DC * TT], F32)
        sbuf(c, "f_w1", [128, 2 * DC * 512], BF16)
        sbuf(c, "f_w2", [128, 2 * 4 * D], BF16)
        sbuf(c, "f_b1", [128, 64], F32)
        sbuf(c, "f_h", [128, 4 * 512], BF16)
        sbuf(c, "f_hm", [128, 4 * 512], BF16)
        sbuf(c, "f_y", [128, 2 * TT], BF16)
        sbuf(c, "f_mk", [8, TT], BF16)
        sbuf(c, "f_lg", [128, 8], F32)
        sbuf(c, "f_mx", [128, 1], F32)
        sbuf(c, "f_mk32", [128, 8], F32)
        sbuf(c, "f_wgh", [128, 64], BF16)
        sbuf(c, "f_wgl", [128, 64], BF16)
        sbuf(c, "f_wgf", [128, 64], F32)
        sbuf(c, "f_bgf", [1, 8], F32)
        sbuf(c, "f_bgb", [128, 8], F32)
        sbuf(c, "f_b2b", [8, D], BF16)
        sbuf(c, "f_b2f", [8, D], F32)
        sbuf(c, "c_sel8", [8, 8 * 128], BF16)
        sbuf(c, "c_identf", [128, 128], F32)
        rb = sbuf(c, "r_ybuf_t", [128, 2 * D], BF16)
        c.t["r_ybuf"] = rb[:, :]
        c.t["r_lobuf"] = rb[:, :]

        c.block = stack.enter_context(nc.Block())
        alloc_common(c)
        emit_l0f(c)
        c.wait("sync", "s_yst0")
        c.wait("sync", "s_yst1")
        c.wait("sync", "s_pe")
        emit_res(c, c.d["x1"].ap(), c.d["y1"].ap(), c.d["x2"].ap(),
                 c.d["xn2"].ap(), None)
        finish(c, ["s_rst0", "s_rst1"])
    return nc


def build_program_c():
    nc = bass.Bass("TRN2", target_bir_lowering=False)
    stack = ExitStack()
    c = Ctx(nc, stack)
    with stack:
        dt = nc.dram_tensor
        c.d["x2"] = dt("x2", [TOWN, D], F32, kind="ExternalInput")
        c.d["xn2i"] = dt("xn2i", [TOWN, D], BF16, kind="ExternalInput")
        c.d["xn2f"] = dt("xn2f", [2 * TOWN, D], BF16, kind="ExternalInput")
        c.d["gw_qd"] = dt("gw_qd", [D, D], F32, kind="ExternalInput")
        c.d["gw_kd"] = dt("gw_kd", [D, D], F32, kind="ExternalInput")
        c.d["gw_vd"] = dt("gw_vd", [D, D], F32, kind="ExternalInput")
        c.d["gw_od"] = dt("gw_od", [D, D], F32, kind="ExternalInput")
        c.d["g_bqkd"] = dt("g_bqkd", [128, 16], F32, kind="ExternalInput")
        c.d["g_bvrd"] = dt("g_bvrd", [1, D], F32, kind="ExternalInput")
        c.d["g_bolsd"] = dt("g_bolsd", [128, 16], F32, kind="ExternalInput")
        c.d["kn_d"] = dt("kn_d", [D, 2 * TOWN], BF16)
        c.d["v_d"] = dt("v_d", [2 * TOWN, D], BF16)
        c.d["qn_d"] = dt("qn_d", [D, TOWN], BF16)
        c.d["ao_tm"] = dt("ao_tm", [TOWN, D], BF16)
        c.d["y2"] = dt("y2", [D, TOWN], BF16)
        c.d["x3"] = dt("x3", [TOWN, D], F32, kind="ExternalOutput")
        c.d["xn3"] = dt("xn3", [TOWN, D], BF16, kind="ExternalOutput")

        for n in ("gw_q", "gw_k", "gw_v", "gw_o"):
            sbuf(c, n, [128, DC * D], BF16)
        sbuf(c, "g_xnf", [128, DC * BLK], BF16)
        sbuf(c, "g_kraw", [128, DC * BLK], BF16)
        sbuf(c, "g_ksq", [128, DC * BLK], BF16)
        sbuf(c, "g_kn", [128, DC * BLK], BF16)
        sbuf(c, "g_v", [128, 4 * D], BF16)
        sbuf(c, "g_rows", [1, 1024], F32)
        sbuf(c, "g_r2", [1, 1024], F32)
        sbuf(c, "g_r3", [1, 1024], F32)
        sbuf(c, "g_ab", [1, 1024], BF16)
        sbuf(c, "g_bqk", [128, 16], F32)
        sbuf(c, "g_bvr", [1, D], BF16)
        sbuf(c, "g_bols", [128, 16], F32)
        sbuf(c, "c_ones128", [128, 1], BF16)
        sbuf(c, "g_qnt", [128, DC * 256], BF16)
        sbuf(c, "g_att", [128, 2 * 256], BF16)
        sbuf(c, "g_outb", [128, 2 * 2048], BF16)
        sbuf(c, "g_knl", [128, 2 * 4096], BF16)
        sbuf(c, "g_vbl", [128, 2 * 4096], BF16)
        rb = sbuf(c, "r_ybuf_t", [128, 2 * D], BF16)
        c.t["r_ybuf"] = rb[:, :]
        c.t["r_lobuf"] = rb[:, :]

        c.block = stack.enter_context(nc.Block())
        alloc_common(c)
        load_w_dmajor(c, c.t["gw_q"], c.d["gw_qd"].ap(), D)
        load_w_dmajor(c, c.t["gw_k"], c.d["gw_kd"].ap(), D)
        load_w_dmajor(c, c.t["gw_v"], c.d["gw_vd"].ap(), D)
        load_w_dmajor(c, c.t["gw_o"], c.d["gw_od"].ap(), D)
        c.dma("sync", c.t["g_bqk"][:], c.d["g_bqkd"].ap(), "s_ld")
        c.dma("gpsimd", c.t["g_bvr"][:], c.d["g_bvrd"].ap(), "s_wt")
        c.dma("sync", c.t["g_bols"][:], c.d["g_bolsd"].ap(), "s_ld")
        for eng in ("scalar", "vector"):
            c.wait(eng, "s_ld")
            c.wait(eng, "s_wt")
        c.wait("tensor", "s_ld")
        emit_l1a(c)
        c.wait("sync", "s_pe")
        emit_res(c, c.d["x2"].ap(), c.d["y2"].ap(), c.d["x3"].ap(),
                 c.d["xn3"].ap(), None)
        finish(c, ["s_rst0", "s_rst1"])
    return nc




def emit_l1f(c):
    P0, P1, P2, P3, P4 = (c.t["pp0"], c.t["pp1"], c.t["pp2"], c.t["pp3"],
                          c.t["pp4"])
    ST0, ST1 = c.t["st0"], c.t["st1"]
    PP, PN = [P0, P1], [P2, P3]
    fx = c.t["m_x"]
    acc = c.t["m_acc"]
    wgx, wux, wdx = c.t["m_wg"], c.t["m_wu"], c.t["m_wd"]
    gsb, usb, gub, hms = c.t["m_g"], c.t["m_u"], c.t["m_gu"], c.t["m_hm"]
    ybf = c.t["m_y"]
    cmf = c.t["m_cm"]
    scr, w16, t16, mcol = c.t["m_scr"], c.t["m_w16"], c.t["m_t16"], c.t["m_mc"]
    gwt, sel16, identf = c.t["m_gwt"], c.t["c_sel16"], c.t["c_identf"]
    xn3 = c.d["xn3i"].ap()
    egd, eud, edd = c.d["m_egd"].ap(), c.d["m_eud"].ap(), c.d["m_edd"].ap()
    sgd, sud, sdd = c.d["m_sgd"].ap(), c.d["m_sud"].ap(), c.d["m_sdd"].ap()
    y3 = c.d["y3"].ap()

    c.dma("gpsimd", gwt[:].rearrange("p (dc e) -> p dc e", e=16),
          c.d["m_gwtd"].ap().rearrange("(dc p) e -> p dc e", p=128), "s_wt")
    c.dma("gpsimd", sel16[:], c.d["c_sel16d"].ap(), "s_wt")
    c.dma("sync", identf[:], c.d["c_identfd"].ap(), "s_ld")
    c.wait("vector", "s_ld")
    c.wait("vector", "s_wt")
    c.wait("scalar", "s_ld")
    c.wait("scalar", "s_wt")

    pe_fx_free = 0
    last_hm = 0
    conv_dv = 0
    last_store = {0: 0, 1: 0}
    wfree = {}
    for tt in range(NTT):
        t0 = tt * TT
        c.wait("sync", "s_pe", pe_fx_free)
        for dc in range(DC):
            c.dma("sync", fx[:, dc * TT:(dc + 1) * TT],
                  xn3[t0:t0 + TT, dc * 128:(dc + 1) * 128], "s_tr",
                  transpose=True)

        c.wait("tensor", "s_tr")
        if tt == 0:
            c.wait("tensor", "s_wt")
            c.wait("tensor", "s_ld")
        st0_free = 0
        scr_free = 0
        tr_evac = 0
        for ch in range(TT // 128):
            c.wait("tensor", "s_ac", st0_free)
            for dc in range(DC):
                c.op("tensor", lambda e, dc=dc, ch=ch: e.matmul(
                    ST0[0:128, 0:16],
                    fx[:, dc * TT + ch * 128:dc * TT + ch * 128 + 128],
                    gwt[:, dc * 16:(dc + 1) * 16],
                    start=(dc == 0), stop=(dc == DC - 1)),
                    "s_pe" if dc == DC - 1 else None)
            c.wait("scalar", "s_pe")
            c.wait("scalar", "s_dv", scr_free)
            nsg = c.op("scalar", lambda e: e.activation(
                scr[:], ST0[0:128, 0:16], AF.Sigmoid), "s_ac")
            st0_free = nsg
            c.wait("vector", "s_ac", nsg)
            n = c.op("vector", lambda e: e.tensor_scalar(
                w16[:], scr[:], 0.0, None, ALU.add), "s_dv")
            for r in range(4):
                c.wait("vector", "s_dv", n)
                n = c.op("vector", lambda e, r=r: e.reduce_max(
                    mcol[:, r:r + 1], w16[:], axis=mybir.AxisListType.X),
                    "s_dv")
                if r < 3:
                    c.wait("vector", "s_dv", n)
                    n = c.op("vector", lambda e, r=r: e.tensor_scalar(
                        t16[:], w16[:], mcol[:, r:r + 1], None, ALU.is_ge),
                        "s_dv")
                    c.wait("vector", "s_dv", n)
                    n = c.op("vector", lambda e: e.scalar_tensor_tensor(
                        w16[:], t16[:], -2.0, w16[:], ALU.mult, ALU.add),
                        "s_dv")
            c.wait("vector", "s_dv", n)
            n = c.op("vector", lambda e: e.reduce_sum(
                mcol[:, 4:5], mcol[:, 0:4], axis=mybir.AxisListType.X), "s_dv")
            c.wait("vector", "s_dv", n)
            n = c.op("vector", lambda e: e.tensor_scalar(
                mcol[:, 4:5], mcol[:, 4:5], 1e-20, None, ALU.add), "s_dv")
            c.wait("vector", "s_dv", n)
            n = c.op("vector", lambda e: e.reciprocal(
                mcol[:, 5:6], mcol[:, 4:5]), "s_dv")
            c.wait("vector", "s_dv", n)
            n = c.op("vector", lambda e: e.tensor_scalar(
                t16[:], scr[:], mcol[:, 3:4], None, ALU.is_ge), "s_dv")
            c.wait("vector", "s_dv", n)
            n = c.op("vector", lambda e: e.tensor_tensor(
                t16[:], t16[:], scr[:], ALU.mult), "s_dv")
            c.wait("vector", "s_dv", n)
            n = c.op("vector", lambda e: e.tensor_scalar(
                t16[:], t16[:], mcol[:, 5:6], None, ALU.mult), "s_dv")
            scr_free = n
            c.wait("tensor", "s_dv", n)
            c.wait("tensor", "s_ac", tr_evac)
            n = c.op("tensor", lambda e: e.transpose(
                ST1[0:16, 128:256], t16[:], identf[:]), "s_pe")
            c.wait("scalar", "s_pe", n)
            tr_evac = c.op("scalar", lambda e, ch=ch: e.copy(
                cmf[0:16, ch * 128:(ch + 1) * 128], ST1[0:16, 128:256]),
                "s_ac")

        for u in range(18):
            pu = u % 2
            c.wait("gpsimd", "s_pe", wfree.get(pu, 0))
            if u < 16:
                c.dma("gpsimd",
                      wgx.ap().rearrange("p (b dc i) -> p b dc i", b=2, i=512)
                      [:, pu], egd[u].rearrange("(dc p) i -> p dc i", p=128),
                      "s_w%d" % pu)
                c.dma("gpsimd",
                      wux.ap().rearrange("p (b dc i) -> p b dc i", b=2, i=512)
                      [:, pu], eud[u].rearrange("(dc p) i -> p dc i", p=128),
                      "s_w%d" % pu)
                c.dma("gpsimd",
                      wdx.ap().rearrange("p (b ic o) -> p b ic o", b=2, o=1024)
                      [:, pu], edd[u].rearrange("(ic p) o -> p ic o", p=128),
                      "s_w%d" % pu)
            else:
                hs = u - 16
                c.dma("gpsimd",
                      wgx.ap().rearrange("p (b dc i) -> p b dc i", b=2, i=512)
                      [:, pu],
                      sgd.rearrange("(dc p) i -> p dc i", p=128)
                      [:, :, hs * 512:(hs + 1) * 512], "s_w%d" % pu)
                c.dma("gpsimd",
                      wux.ap().rearrange("p (b dc i) -> p b dc i", b=2, i=512)
                      [:, pu],
                      sud.rearrange("(dc p) i -> p dc i", p=128)
                      [:, :, hs * 512:(hs + 1) * 512], "s_w%d" % pu)
                c.dma("gpsimd",
                      wdx.ap().rearrange("p (b ic o) -> p b ic o", b=2, o=1024)
                      [:, pu],
                      sdd[hs * 512:(hs + 1) * 512, :]
                      .rearrange("(ic p) o -> p ic o", p=128), "s_w%d" % pu)
            c.wait("tensor", "s_w%d" % pu)
            if u < 2:
                c.wait("tensor", "s_ac", tr_evac)
            last_mm = 0
            for sub in range(TT // 512):
                gev = {}
                c.wait("tensor", "s_ac")
                c.wait("tensor", "s_dv")
                for ic in range(4):
                    pb = PP[ic % 2]
                    if ic >= 2:
                        c.wait("tensor", "s_ac", gev[ic - 2])
                    for dc in range(DC):
                        c.op("tensor", lambda e, pb=pb, ic=ic, dc=dc, pu=pu,
                             sub=sub: e.matmul(
                                 pb[:, 0:512],
                                 wgx[:, (pu * 8 + dc) * 512 + ic * 128:
                                     (pu * 8 + dc) * 512 + ic * 128 + 128],
                                 fx[:, dc * TT + sub * 512:
                                    dc * TT + sub * 512 + 512],
                                 start=(dc == 0), stop=(dc == DC - 1)),
                             "s_pe" if dc == DC - 1 else None)
                    c.wait("scalar", "s_pe")
                    gev[ic] = c.op("scalar", lambda e, pb=pb, ic=ic:
                                   e.activation(
                                       gsb[:, ic * 512:(ic + 1) * 512],
                                       pb[:, 0:512], AF.Silu), "s_ac")
                uev = {}
                for ic in range(4):
                    pb = PN[ic % 2]
                    if ic >= 2:
                        c.wait("tensor", "s_ac", uev[ic - 2])
                    for dc in range(DC):
                        c.op("tensor", lambda e, pb=pb, ic=ic, dc=dc, pu=pu,
                             sub=sub: e.matmul(
                                 pb[:, 0:512],
                                 wux[:, (pu * 8 + dc) * 512 + ic * 128:
                                     (pu * 8 + dc) * 512 + ic * 128 + 128],
                                 fx[:, dc * TT + sub * 512:
                                    dc * TT + sub * 512 + 512],
                                 start=(dc == 0), stop=(dc == DC - 1)),
                             "s_pe" if dc == DC - 1 else None)
                    c.wait("scalar", "s_pe")
                    uev[ic] = c.op("scalar", lambda e, pb=pb, ic=ic: e.copy(
                        usb[:, ic * 512:(ic + 1) * 512], pb[:, 0:512]), "s_ac")
                c.wait("vector", "s_ac", uev[3])
                gu_done = 0
                for ic in range(4):
                    gu_done = c.op("vector", lambda e, ic=ic: e.tensor_tensor(
                        gub[:, ic * 512:(ic + 1) * 512],
                        gsb[:, ic * 512:(ic + 1) * 512],
                        usb[:, ic * 512:(ic + 1) * 512], ALU.mult), "s_dv")
                rhs_h = gub
                if u < 16:
                    c.wait("tensor", "s_dv", last_hm)
                    pe_mb = c.op("tensor", lambda e, u=u, sub=sub: e.matmul(
                        P4[:, 0:512], sel16[:, u * 128:(u + 1) * 128],
                        cmf[0:16, sub * 512:(sub + 1) * 512],
                        start=True, stop=True), "s_pe")
                    c.wait("vector", "s_pe", pe_mb)
                    c.wait("vector", "s_dv", gu_done)
                    for ic in range(4):
                        last_hm = c.op("vector", lambda e, ic=ic:
                                       e.tensor_tensor(
                                           hms[:, ic * 512:(ic + 1) * 512],
                                           gub[:, ic * 512:(ic + 1) * 512],
                                           P4[:, 0:512], ALU.mult), "s_dv")
                    rhs_h = hms
                    hm_rdy = last_hm
                else:
                    hm_rdy = gu_done
                c.wait("tensor", "s_dv", hm_rdy)
                aev = {}
                for oc in range(DC):
                    pd = PP[oc % 2]
                    if oc >= 2:
                        c.wait("tensor", "s_dv", aev[oc - 2])
                    c.wait("tensor", "s_ac")
                    for ic in range(4):
                        last_mm = c.op(
                            "tensor", lambda e, pd=pd, oc=oc, ic=ic, pu=pu,
                            rhs_h=rhs_h: e.matmul(
                                pd[:, 0:512],
                                wdx[:, (pu * 4 + ic) * 1024 + oc * 128:
                                    (pu * 4 + ic) * 1024 + oc * 128 + 128],
                                rhs_h[:, ic * 512:(ic + 1) * 512],
                                start=(ic == 0), stop=(ic == 3)),
                            "s_pe" if ic == 3 else None)
                    c.wait("vector", "s_pe")
                    if u == 0:
                        c.wait("vector", "s_dv", conv_dv)
                    dst = acc[:, oc * TT + sub * 512:oc * TT + sub * 512 + 512]
                    aev[oc] = c.op("vector", lambda e, pd=pd, dst=dst,
                                   first=(u == 0): (
                                       e.tensor_scalar(dst, pd[:, 0:512], 0.0,
                                                       None, ALU.add)
                                       if first else
                                       e.tensor_tensor(dst, dst, pd[:, 0:512],
                                                       ALU.add)), "s_dv")
            wfree[pu] = last_mm
        pe_fx_free = c.cur("s_pe")
        c.wait("vector", "s_dv")
        for oc in range(DC):
            p = oc % 2
            c.wait("vector", "s_yst%d" % p, last_store[p])
            conv_dv = c.op("vector", lambda e, oc=oc, p=p: e.tensor_scalar(
                ybf[:, p * TT:(p + 1) * TT], acc[:, oc * TT:(oc + 1) * TT],
                0.0, None, ALU.add), "s_dv")
            c.wait("scalar", "s_dv", conv_dv)
            last_store[p] = c.dma(
                "scalar", y3[oc * 128:(oc + 1) * 128, t0:t0 + TT],
                ybf[:, p * TT:(p + 1) * TT], "s_yst%d" % p)


def build_program_d():
    nc = bass.Bass("TRN2", target_bir_lowering=False)
    stack = ExitStack()
    c = Ctx(nc, stack)
    with stack:
        dt = nc.dram_tensor
        c.d["x3"] = dt("x3", [TOWN, D], F32, kind="ExternalInput")
        c.d["xn3i"] = dt("xn3i", [TOWN, D], BF16, kind="ExternalInput")
        c.d["m_gwtd"] = dt("m_gwtd", [D, 16], F32, kind="ExternalInput")
        c.d["m_egd"] = dt("m_egd", [16, D, 512], F32, kind="ExternalInput")
        c.d["m_eud"] = dt("m_eud", [16, D, 512], F32, kind="ExternalInput")
        c.d["m_edd"] = dt("m_edd", [16, 512, D], F32, kind="ExternalInput")
        c.d["m_sgd"] = dt("m_sgd", [D, D], F32, kind="ExternalInput")
        c.d["m_sud"] = dt("m_sud", [D, D], F32, kind="ExternalInput")
        c.d["m_sdd"] = dt("m_sdd", [D, D], F32, kind="ExternalInput")
        c.d["c_sel16d"] = dt("c_sel16d", [16, 16 * 128], F32,
                             kind="ExternalInput")
        c.d["c_identfd"] = dt("c_identfd", [128, 128], F32,
                              kind="ExternalInput")
        c.d["y3"] = dt("y3", [D, TOWN], BF16)
        c.d["out"] = dt("out", [TOWN, D], F32, kind="ExternalOutput")

        sbuf(c, "m_x", [128, DC * TT], BF16)
        sbuf(c, "m_acc", [128, DC * TT], F32)
        sbuf(c, "m_wg", [128, 2 * DC * 512], BF16)
        sbuf(c, "m_wu", [128, 2 * DC * 512], BF16)
        sbuf(c, "m_wd", [128, 2 * 4 * D], BF16)
        sbuf(c, "m_g", [128, 4 * 512], BF16)
        sbuf(c, "m_u", [128, 4 * 512], BF16)
        sbuf(c, "m_gu", [128, 4 * 512], BF16)
        sbuf(c, "m_hm", [128, 4 * 512], BF16)
        sbuf(c, "m_y", [128, 2 * TT], BF16)
        sbuf(c, "m_cm", [16, TT], BF16)
        sbuf(c, "m_scr", [128, 16], F32)
        sbuf(c, "m_w16", [128, 16], F32)
        sbuf(c, "m_t16", [128, 16], F32)
        sbuf(c, "m_mc", [128, 8], F32)
        sbuf(c, "m_gwt", [128, DC * 16], BF16)
        sbuf(c, "c_sel16", [16, 16 * 128], BF16)
        sbuf(c, "c_identf", [128, 128], F32)
        rb = sbuf(c, "r_ybuf_t", [128, 2 * D], BF16)
        c.t["r_ybuf"] = rb[:, :]
        c.t["r_lobuf"] = rb[:, :]

        c.block = stack.enter_context(nc.Block())
        alloc_common(c)
        emit_l1f(c)
        c.wait("sync", "s_yst0")
        c.wait("sync", "s_yst1")
        c.wait("sync", "s_pe")
        emit_res(c, c.d["x3"].ap(), c.d["y3"].ap(), c.d["out"].ap(),
                 None, None)
        finish(c, ["s_rst0", "s_rst1"])
    return nc


def phase_barrier(c):
    names = list(c.sems.keys())
    for eng in ("sync", "gpsimd", "vector", "scalar", "tensor"):
        for sn in names:
            c.wait(eng, sn)


from contextlib import contextmanager


@contextmanager
def phase_stack(c):
    outer = c.stack
    inner = ExitStack()
    c.stack = inner
    try:
        with inner:
            yield
    finally:
        c.stack = outer


SEM_NAMES = ["s_ld", "s_wt", "s_in0", "s_in1", "s_xn0", "s_xn1", "s_tr",
             "s_pe", "s_dv", "s_ac", "s_yst0", "s_yst1", "s_rin0", "s_rin1",
             "s_rst0", "s_rst1", "s_w0", "s_w1", "s_st0", "s_st1", "s_q0",
             "s_q1", "s_kv0", "s_kv1", "s_ao0", "s_ao1", "s_y0", "s_y1",
             "s_cc"]


def build_fused():
    nc = bass.Bass("TRN2", target_bir_lowering=False)
    stack = ExitStack()
    c = Ctx(nc, stack)
    with stack:
        dt = nc.dram_tensor
        c.d["x_pre"] = dt("x_pre", [TOWN, D], F32, kind="ExternalInput")
        c.d["x_own"] = dt("x_own", [TOWN, D], F32, kind="ExternalInput")
        c.d["w_q"] = dt("w_q", [D, D], F32, kind="ExternalInput")
        c.d["w_k"] = dt("w_k", [D, D], F32, kind="ExternalInput")
        c.d["w_v"] = dt("w_v", [D, D], F32, kind="ExternalInput")
        c.d["w_o"] = dt("w_o", [D, D], F32, kind="ExternalInput")
        c.d["w_f"] = dt("w_f", [HD, NF], F32, kind="ExternalInput")
        c.d["c_trild"] = dt("c_trild", [128, 128], F32, kind="ExternalInput")
        c.d["c_sheadd"] = dt("c_sheadd", [16, 16 * 128], F32,
                             kind="ExternalInput")
        c.d["c_ucold"] = dt("c_ucold", [64, 16 * 16], F32,
                            kind="ExternalInput")
        c.d["c_flagd"] = dt("c_flagd", [128, 1], F32, kind="ExternalInput")
        c.d["f_wgd"] = dt("f_wgd", [D, 8], F32, kind="ExternalInput")
        c.d["f_bgd"] = dt("f_bgd", [1, 8], F32, kind="ExternalInput")
        c.d["f_w1d"] = dt("f_w1d", [8, D, D], F32, kind="ExternalInput")
        c.d["f_b1d"] = dt("f_b1d", [128, 64], F32, kind="ExternalInput")
        c.d["f_w2d"] = dt("f_w2d", [8, D, D], F32, kind="ExternalInput")
        c.d["f_b2d"] = dt("f_b2d", [8, D], F32, kind="ExternalInput")
        c.d["c_sel8d"] = dt("c_sel8d", [8, 8 * 128], F32,
                            kind="ExternalInput")
        c.d["c_identfd"] = dt("c_identfd", [128, 128], F32,
                              kind="ExternalInput")
        c.d["gw_qd"] = dt("gw_qd", [D, D], F32, kind="ExternalInput")
        c.d["gw_kd"] = dt("gw_kd", [D, D], F32, kind="ExternalInput")
        c.d["gw_vd"] = dt("gw_vd", [D, D], F32, kind="ExternalInput")
        c.d["gw_od"] = dt("gw_od", [D, D], F32, kind="ExternalInput")
        c.d["g_bqkd"] = dt("g_bqkd", [128, 16], F32, kind="ExternalInput")
        c.d["g_bvrd"] = dt("g_bvrd", [1, D], F32, kind="ExternalInput")
        c.d["g_bolsd"] = dt("g_bolsd", [128, 16], F32, kind="ExternalInput")
        c.d["m_gwtd"] = dt("m_gwtd", [D, 16], F32, kind="ExternalInput")
        c.d["m_egd"] = dt("m_egd", [16, D, 512], F32, kind="ExternalInput")
        c.d["m_eud"] = dt("m_eud", [16, D, 512], F32, kind="ExternalInput")
        c.d["m_edd"] = dt("m_edd", [16, 512, D], F32, kind="ExternalInput")
        c.d["m_sgd"] = dt("m_sgd", [D, D], F32, kind="ExternalInput")
        c.d["m_sud"] = dt("m_sud", [D, D], F32, kind="ExternalInput")
        c.d["m_sdd"] = dt("m_sdd", [D, D], F32, kind="ExternalInput")
        c.d["c_sel16d"] = dt("c_sel16d", [16, 16 * 128], F32,
                             kind="ExternalInput")
        c.d["xn0"] = dt("xn0", [2 * TOWN, D], BF16)
        c.d["y0"] = dt("y0", [D, TOWN], BF16)
        c.d["x1"] = dt("x1", [TOWN, D], F32)
        c.d["xn1"] = dt("xn1", [TOWN, D], BF16)
        c.d["xn1lo"] = dt("xn1lo", [TOWN, D], BF16)
        c.d["y1"] = dt("y1", [D, TOWN], BF16)
        c.d["x2"] = dt("x2", [TOWN, D], F32)
        c.d["xn2i"] = dt("xn2i", [TOWN, D], BF16)
        c.d["xn2f"] = dt("xn2f", [2 * TOWN, D], BF16)
        c.d["kn_d"] = dt("kn_d", [D, 2 * TOWN], BF16)
        c.d["v_d"] = dt("v_d", [2 * TOWN, D], BF16)
        c.d["qn_d"] = dt("qn_d", [D, TOWN], BF16)
        c.d["ao_tm"] = dt("ao_tm", [TOWN, D], BF16)
        c.d["y2"] = dt("y2", [D, TOWN], BF16)
        c.d["x3"] = dt("x3", [TOWN, D], F32)
        c.d["xn3i"] = dt("xn3i", [TOWN, D], BF16)
        c.d["y3"] = dt("y3", [D, TOWN], BF16)
        c.d["out"] = dt("out", [TOWN, D], F32, kind="ExternalOutput")

        c.block = stack.enter_context(nc.Block())
        for sn in SEM_NAMES:
            c.sem(sn)
        alloc_common(c)

        with phase_stack(c):
            for n in ("wq", "wk", "wv", "wo"):
                sbuf(c, n, [128, DC * D], BF16)
            sbuf(c, "wfeat", [64, 128], BF16)
            sbuf(c, "c_tril4", [128, 4 * 128], BF16)
            sbuf(c, "c_shead", [16, 16 * 128], BF16)
            sbuf(c, "c_ucol", [64, 16 * 16], BF16)
            sbuf(c, "c_flag", [128, 1], F32)
            sbuf(c, "a_xnf", [128, DC * BLK], BF16)
            sbuf(c, "a_khh", [64, H * BLK], BF16)
            sbuf(c, "a_qhh", [64, H * BLK], BF16)
            sbuf(c, "a_vaug", [128, 4 * 1040], BF16)
            sbuf(c, "a_kpf", [128, H * BLK], BF16)
            sbuf(c, "a_qpf", [128, H * BLK], BF16)
            sbuf(c, "a_kptm", [128, H * 128], BF16)
            sbuf(c, "a_sqm", [16, 2 * BLK], BF16)
            sbuf(c, "a_am", [128, 2 * 512], BF16)
            sbuf(c, "a_numb", [64, 2 * 512], BF16)
            sbuf(c, "a_yall", [128, DC * BLK], BF16)
            sbuf(c, "a_yo", [128, 2 * BLK], BF16)
            c.t["a_sf32"] = c.t["r_xs"][:, 0:H * 65]
            sbuf(c, "a_sbf", [128, H * 65], BF16)
            sbuf(c, "a_dn", [1, 512], F32)
            sbuf(c, "a_rec", [1, 2 * 512], BF16)
            sbuf(c, "a_onesr", [1, 512], BF16)
            sbuf(c, "a_epsv", [1, 65], BF16)
            c.t["a_sqh"] = c.t["a_yall"][0:64, 4 * BLK:8 * BLK]
            c.t["r_ybuf"] = c.t["a_yall"][:, 0:2 * D]
            c.t["r_lobuf"] = c.t["a_yall"][:, 2 * D:4 * D]

            for i in range(4):
                c.dma("gpsimd", c.t["c_tril4"][:, i * 128:(i + 1) * 128],
                      c.d["c_trild"].ap(), "s_wt")
            c.dma("gpsimd", c.t["c_shead"][:], c.d["c_sheadd"].ap(), "s_wt")
            c.dma("gpsimd", c.t["c_ucol"][:], c.d["c_ucold"].ap(), "s_wt")
            c.dma("sync", c.t["c_flag"][:], c.d["c_flagd"].ap(), "s_ld")
            load_w_dmajor(c, c.t["wq"], c.d["w_q"].ap(), D)
            load_w_dmajor(c, c.t["wk"], c.d["w_k"].ap(), D)
            load_w_dmajor(c, c.t["wv"], c.d["w_v"].ap(), D)
            load_w_dmajor(c, c.t["wo"], c.d["w_o"].ap(), D)
            c.dma("gpsimd", c.t["wfeat"][:], c.d["w_f"].ap(), "s_wt")
            for eng in ("scalar", "vector"):
                c.wait(eng, "s_wt")
                c.wait(eng, "s_ld")
            c.wait("tensor", "s_ld")
            emit_l0a(c)
            c.wait("sync", "s_yst0")
            c.wait("sync", "s_yst1")
            c.wait("vector", "s_xn0")
            c.wait("vector", "s_xn1")
            c.wait("scalar", "s_xn0")
            c.wait("scalar", "s_xn1")
            c.wait("sync", "s_pe")
            emit_res(c, c.d["x_own"].ap(), c.d["y0"].ap(), c.d["x1"].ap(),
                     c.d["xn1"].ap(), c.d["xn1lo"].ap())
        phase_barrier(c)

        with phase_stack(c):
            sbuf(c, "f_xh", [128, DC * TT], BF16)
            sbuf(c, "f_xl", [128, DC * TT], BF16)
            sbuf(c, "f_acc", [128, DC * TT], F32)
            sbuf(c, "f_w1", [128, 2 * DC * 512], BF16)
            sbuf(c, "f_w2", [128, 2 * 4 * D], BF16)
            sbuf(c, "f_b1", [128, 64], F32)
            sbuf(c, "f_h", [128, 4 * 512], BF16)
            sbuf(c, "f_hm", [128, 4 * 512], BF16)
            sbuf(c, "f_y", [128, 2 * TT], BF16)
            sbuf(c, "f_mk", [8, TT], BF16)
            sbuf(c, "f_lg", [128, 8], F32)
            sbuf(c, "f_mx", [128, 1], F32)
            sbuf(c, "f_mk32", [128, 8], F32)
            sbuf(c, "f_wgh", [128, 64], BF16)
            sbuf(c, "f_wgl", [128, 64], BF16)
            sbuf(c, "f_wgf", [128, 64], F32)
            sbuf(c, "f_bgf", [1, 8], F32)
            sbuf(c, "f_bgb", [128, 8], F32)
            sbuf(c, "f_b2b", [8, D], BF16)
            sbuf(c, "f_b2f", [8, D], F32)
            sbuf(c, "c_sel8", [8, 8 * 128], BF16)
            sbuf(c, "c_identf", [128, 128], F32)
            rb = sbuf(c, "r_ybuf_t", [128, 2 * D], BF16)
            c.t["r_ybuf"] = rb[:, :]
            c.t["r_lobuf"] = rb[:, :]
            emit_l0f(c)
            c.wait("sync", "s_yst0")
            c.wait("sync", "s_yst1")
            c.wait("sync", "s_pe")
            emit_res(c, c.d["x1"].ap(), c.d["y1"].ap(), c.d["x2"].ap(),
                     c.d["xn2i"].ap(), None)
        c.wait("gpsimd", "s_rst0")
        c.wait("gpsimd", "s_rst1")
        c.op("gpsimd", lambda e: e.collective_compute(
            "AllGather", ALU.bypass,
            replica_groups=[[0, 1], [2, 3], [4, 5], [6, 7]],
            ins=[c.d["xn2i"].ap().opt()], outs=[c.d["xn2f"].ap().opt()]),
            "s_cc")
        phase_barrier(c)

        with phase_stack(c):
            for n in ("gw_q", "gw_k", "gw_v", "gw_o"):
                sbuf(c, n, [128, DC * D], BF16)
            sbuf(c, "g_xnf", [128, DC * BLK], BF16)
            sbuf(c, "g_kraw", [128, DC * BLK], BF16)
            sbuf(c, "g_ksq", [128, DC * BLK], BF16)
            sbuf(c, "g_kn", [128, DC * BLK], BF16)
            sbuf(c, "g_v", [128, 4 * D], BF16)
            sbuf(c, "g_rows", [1, 1024], F32)
            sbuf(c, "g_r2", [1, 1024], F32)
            sbuf(c, "g_r3", [1, 1024], F32)
            sbuf(c, "g_ab", [1, 1024], BF16)
            sbuf(c, "g_bqk", [128, 16], F32)
            sbuf(c, "g_bvr", [1, D], BF16)
            sbuf(c, "g_bols", [128, 16], F32)
            sbuf(c, "c_ones128", [128, 1], BF16)
            sbuf(c, "g_qnt", [128, DC * 256], BF16)
            sbuf(c, "g_att", [128, 2 * 256], BF16)
            sbuf(c, "g_outb", [128, 2 * 2048], BF16)
            sbuf(c, "g_knl", [128, 2 * 4096], BF16)
            sbuf(c, "g_vbl", [128, 2 * 4096], BF16)
            rb = sbuf(c, "r_ybuf_t2", [128, 2 * D], BF16)
            c.t["r_ybuf"] = rb[:, :]
            c.t["r_lobuf"] = rb[:, :]
            load_w_dmajor(c, c.t["gw_q"], c.d["gw_qd"].ap(), D)
            load_w_dmajor(c, c.t["gw_k"], c.d["gw_kd"].ap(), D)
            load_w_dmajor(c, c.t["gw_v"], c.d["gw_vd"].ap(), D)
            load_w_dmajor(c, c.t["gw_o"], c.d["gw_od"].ap(), D)
            c.dma("sync", c.t["g_bqk"][:], c.d["g_bqkd"].ap(), "s_ld")
            c.dma("gpsimd", c.t["g_bvr"][:], c.d["g_bvrd"].ap(), "s_wt")
            c.dma("sync", c.t["g_bols"][:], c.d["g_bolsd"].ap(), "s_ld")
            for eng in ("scalar", "vector"):
                c.wait(eng, "s_ld")
                c.wait(eng, "s_wt")
            c.wait("tensor", "s_ld")
            emit_l1a(c)
            c.wait("sync", "s_pe")
            emit_res(c, c.d["x2"].ap(), c.d["y2"].ap(), c.d["x3"].ap(),
                     c.d["xn3i"].ap(), None)
        phase_barrier(c)

        with phase_stack(c):
            sbuf(c, "m_x", [128, DC * TT], BF16)
            sbuf(c, "m_acc", [128, DC * TT], F32)
            sbuf(c, "m_wg", [128, 2 * DC * 512], BF16)
            sbuf(c, "m_wu", [128, 2 * DC * 512], BF16)
            sbuf(c, "m_wd", [128, 2 * 4 * D], BF16)
            sbuf(c, "m_g", [128, 4 * 512], BF16)
            sbuf(c, "m_u", [128, 4 * 512], BF16)
            sbuf(c, "m_gu", [128, 4 * 512], BF16)
            sbuf(c, "m_hm", [128, 4 * 512], BF16)
            sbuf(c, "m_y", [128, 2 * TT], BF16)
            sbuf(c, "m_cm", [16, TT], BF16)
            sbuf(c, "m_scr", [128, 16], F32)
            sbuf(c, "m_w16", [128, 16], F32)
            sbuf(c, "m_t16", [128, 16], F32)
            sbuf(c, "m_mc", [128, 8], F32)
            sbuf(c, "m_gwt", [128, DC * 16], BF16)
            sbuf(c, "c_sel16", [16, 16 * 128], BF16)
            t2 = c.stack.enter_context(
                c.nc.sbuf_tensor("c_identf2", [128, 128], F32))
            c.t["c_identf"] = t2
            rb = sbuf(c, "r_ybuf_t3", [128, 2 * D], BF16)
            c.t["r_ybuf"] = rb[:, :]
            c.t["r_lobuf"] = rb[:, :]
            emit_l1f(c)
            c.wait("sync", "s_yst0")
            c.wait("sync", "s_yst1")
            c.wait("sync", "s_pe")
            emit_res(c, c.d["x3"].ap(), c.d["y3"].ap(), c.d["out"].ap(),
                     None, None)
        finish(c, ["s_rst0", "s_rst1"])
    return nc


def host_consts():
    tril = (np.tril(np.ones((128, 128), np.float32), 0).T).copy()
    shead = np.zeros((16, 16 * 128), np.float32)
    for h in range(16):
        shead[h, h * 128:(h + 1) * 128] = 1.0
    ucol = np.zeros((64, 16 * 16), np.float32)
    for h in range(16):
        ucol[:, h * 16 + h] = 1.0
    cpool = np.tile(np.array(CONST_VALS, np.float32)[None, :], (128, 1))
    return dict(c_trild=np.ascontiguousarray(tril),
                c_sheadd=shead, c_ucold=ucol, c_pool=cpool)


def fused_in_maps(inputs):
    x = np.asarray(inputs["L0_attn_x"])
    consts = host_consts()
    ident = np.eye(128, dtype=np.float32)
    sel8 = np.zeros((8, 8 * 128), np.float32)
    for h in range(8):
        sel8[h, h * 128:(h + 1) * 128] = 1.0
    sel16 = np.zeros((16, 16 * 128), np.float32)
    for h in range(16):
        sel16[h, h * 128:(h + 1) * 128] = 1.0
    b1 = np.asarray(inputs["L0_ffn_b1"])
    b1r = np.ascontiguousarray(
        b1.reshape(8, 8, 128).transpose(2, 0, 1).reshape(128, 64))
    bq = np.asarray(inputs["L1_attn_bq"]).reshape(8, 128).T
    bk = np.asarray(inputs["L1_attn_bk"]).reshape(8, 128).T
    bo = np.asarray(inputs["L1_attn_bo"]).reshape(8, 128).T
    ls = np.asarray(inputs["L1_attn_layerscale"]).reshape(8, 128).T
    gwt = np.ascontiguousarray(np.asarray(inputs["L1_ffn_gate_w"]).T)
    common = {
        "c_pool": consts["c_pool"], "c_trild": consts["c_trild"],
        "c_sheadd": consts["c_sheadd"], "c_ucold": consts["c_ucold"],
        "c_sel8d": sel8, "c_sel16d": sel16, "c_identfd": ident,
        "w_q": np.asarray(inputs["L0_attn_Wq"]),
        "w_k": np.asarray(inputs["L0_attn_Wk"]),
        "w_v": np.asarray(inputs["L0_attn_Wv"]),
        "w_o": np.asarray(inputs["L0_attn_Wo"]),
        "w_f": np.asarray(inputs["L0_attn_W_feat"]),
        "f_wgd": np.asarray(inputs["L0_ffn_Wg"]),
        "f_bgd": np.asarray(inputs["L0_ffn_bg"]).reshape(1, 8),
        "f_w1d": np.asarray(inputs["L0_ffn_W1"]),
        "f_b1d": b1r,
        "f_w2d": np.asarray(inputs["L0_ffn_W2"]),
        "f_b2d": np.asarray(inputs["L0_ffn_b2"]),
        "gw_qd": np.asarray(inputs["L1_attn_Wq"]),
        "gw_kd": np.asarray(inputs["L1_attn_Wk"]),
        "gw_vd": np.asarray(inputs["L1_attn_Wv"]),
        "gw_od": np.asarray(inputs["L1_attn_Wo"]),
        "g_bqkd": np.ascontiguousarray(np.concatenate([bq, bk], axis=1)),
        "g_bvrd": np.asarray(inputs["L1_attn_bv"]).reshape(1, D),
        "g_bolsd": np.ascontiguousarray(np.concatenate([ls, bo], axis=1)),
        "m_gwtd": gwt,
        "m_egd": np.asarray(inputs["L1_ffn_expert_gate"]),
        "m_eud": np.asarray(inputs["L1_ffn_expert_up"]),
        "m_edd": np.asarray(inputs["L1_ffn_expert_down"]),
        "m_sgd": np.asarray(inputs["L1_ffn_shared_gate"]),
        "m_sud": np.asarray(inputs["L1_ffn_shared_up"]),
        "m_sdd": np.asarray(inputs["L1_ffn_shared_down"]),
    }
    in_maps = []
    for core in range(8):
        b, half = core // 2, core % 2
        m = dict(common)
        m["x_own"] = np.ascontiguousarray(x[b, half * TOWN:(half + 1) * TOWN])
        m["x_pre"] = np.ascontiguousarray(x[b, 0:TOWN])
        m["c_flagd"] = np.full((128, 1), float(half), np.float32)
        in_maps.append(m)
    return in_maps


def kernel(**inputs):
    in_maps = fused_in_maps(inputs)
    nc = build_fused()
    res = run_bass_kernel_spmd(nc, in_maps, core_ids=list(range(8)))
    out = np.zeros((4, 8192, D), np.float32)
    for core in range(8):
        b, half = core // 2, core % 2
        out[b, half * TOWN:(half + 1) * TOWN] = res.results[core]["out"]
    return out


# version 13
# speedup vs baseline: 1.3523x; 1.0487x over previous
import math
import os
from contextlib import ExitStack

DEV_STAGE = int(os.environ.get("DEV_STAGE", "9"))
DEV_NBLK = int(os.environ.get("DEV_NBLK", "0"))
DEV_S3 = int(os.environ.get("DEV_S3", "9"))

import numpy as np
import ml_dtypes

import concourse.bass as bass
import concourse.mybir as mybir
from concourse.bass_utils import run_bass_kernel_spmd

F32 = mybir.dt.float32
BF16 = mybir.dt.bfloat16
AF = mybir.ActivationFunctionType
ALU = mybir.AluOpType
BFD = ml_dtypes.bfloat16

D = 1024
DC = 8
TOWN = 4096
BLK = 512
NSUB = 4
NBLK = TOWN // BLK
H, HD, NF = 16, 64, 128
EPS_FAVOR = 1e-6
LN_EPS = 1e-5
LNINV = -0.5 * math.log(NF)
LOG_S = math.log(8192.0)

CONST_VALS = sorted(set([2.0, -2.0, 0.5, -0.5, -1.0, 1.0 / D, LN_EPS,
                         EPS_FAVOR, LNINV, -LOG_S, 1.0 / 32.0, 1e-20]))


class Ctx:
    def __init__(self, nc, stack):
        self.nc = nc
        self.stack = stack
        self.q = {"sync": [], "gpsimd": [], "vector": [], "scalar": [],
                  "tensor": []}
        self.sems = {}
        self.t = {}
        self.d = {}
        self._l0f_wfree = {}

    def sem(self, name):
        if name not in self.sems:
            h = self.stack.enter_context(self.nc.semaphore(name))
            self.sems[name] = [h, 0]
        return self.sems[name]

    def tick(self, name, k):
        s = self.sem(name)
        s[1] += k
        return s[1]

    def cur(self, name):
        return self.sem(name)[1]

    def emit(self, eng, fn):
        self.q[eng].append(fn)

    def wait(self, eng, semname, tgt=None):
        s = self.sem(semname)
        n = s[1] if tgt is None else tgt
        if n <= 0:
            return
        h = s[0]
        self.emit(eng, lambda e, h=h, n=n: e.wait_ge(h, n))

    def dma(self, eng, out, in_, semname, **kw):
        n = self.tick(semname, 16)
        h = self.sem(semname)[0]
        self.emit(eng, lambda e, out=out, in_=in_, h=h, kw=kw:
                  e.dma_start(out=out, in_=in_, **kw).then_inc(h, 16))
        return n

    def op(self, eng, fn, semname=None, k=1):
        if semname is None:
            self.emit(eng, fn)
            return None
        n = self.tick(semname, k)
        h = self.sem(semname)[0]
        self.emit(eng, lambda e, fn=fn, h=h, k=k: fn(e).then_inc(h, k))
        return n

    def finalize(self, block):
        for name in ("sync", "gpsimd", "vector", "scalar", "tensor"):
            fns = self.q[name]
            if not fns:
                continue
            getattr(block, name)(lambda e, fns=fns: [f(e) for f in fns])


def sbuf(c, name, shape, dtype):
    t = c.stack.enter_context(c.nc.sbuf_tensor(name, shape, dtype))
    c.t[name] = t
    return t


def psum_t(c, name, shape, dtype=F32):
    t = c.stack.enter_context(c.nc.psum_tensor(name, shape, dtype))
    c.t[name] = t
    return t


def load_w_dmajor(c, dst, w_ap, E):
    c.dma("gpsimd", dst.ap().rearrange("p (dc e) -> p dc e", e=E),
          w_ap.rearrange("(dc p) e -> p dc e", p=128), "s_wt")


def emit_ln_chunk(c, xsrc, xn_dst, lo_dst, prev_dv=0):
    st = c.t["r_stats"]
    sq = c.t["r_sq"]
    xb = c.t["r_xb"]
    f32s = c.t["r_f32"]
    c.wait("scalar", "s_dv", prev_dv)
    c.op("scalar", lambda e: e.activation(sq[:], xsrc, AF.Square,
                                          accum_out=st[:, 1:2]))
    c.op("scalar", lambda e: e.activation(xb[:], xsrc, AF.Identity,
                                          accum_out=st[:, 0:1]), "s_ac")
    c.wait("vector", "s_ac")
    n = c.op("vector", lambda e: e.tensor_scalar(st[:, 2:3], st[:, 0:1],
                                                 1.0 / D, None, ALU.mult),
             "s_dv")
    n2 = c.op("vector", lambda e: e.tensor_scalar(st[:, 3:4], st[:, 1:2],
                                                  1.0 / D, None, ALU.mult),
              "s_dv")
    c.wait("vector", "s_dv", n)
    n = c.op("vector", lambda e: e.tensor_tensor(st[:, 5:6], st[:, 2:3],
                                                 st[:, 2:3], ALU.mult),
             "s_dv")
    c.wait("vector", "s_dv", max(n, n2))
    n = c.op("vector", lambda e: e.tensor_tensor(st[:, 3:4], st[:, 3:4],
                                                 st[:, 5:6], ALU.subtract),
             "s_dv")
    c.wait("vector", "s_dv", n)
    n = c.op("vector", lambda e: e.tensor_scalar(st[:, 3:4], st[:, 3:4],
                                                 LN_EPS, None, ALU.add),
             "s_dv")
    c.wait("scalar", "s_dv", n)
    c.op("scalar", lambda e: e.activation(st[:, 6:7], st[:, 3:4], AF.Ln),
         "s_ac")
    c.wait("scalar", "s_ac")
    c.op("scalar", lambda e: e.activation(st[:, 4:5], st[:, 6:7], AF.Exp,
                                          scale=-0.5), "s_ac")
    c.wait("vector", "s_ac")
    if lo_dst is None:
        return c.op("vector", lambda e: e.tensor_scalar(
            xn_dst, xsrc, st[:, 2:3], st[:, 4:5], ALU.subtract, ALU.mult),
            "s_dv")
    n = c.op("vector", lambda e: e.tensor_scalar(
        f32s[:], xsrc, st[:, 2:3], st[:, 4:5], ALU.subtract, ALU.mult),
        "s_dv")
    c.wait("vector", "s_dv", n)
    n = c.op("vector", lambda e: e.tensor_scalar(xn_dst, f32s[:], 0.0, None,
                                                 ALU.add), "s_dv")
    c.wait("vector", "s_dv", n)
    return c.op("vector", lambda e: e.tensor_tensor(lo_dst, f32s[:], xn_dst,
                                                    ALU.subtract), "s_dv")


def emit_res(c, x_in, y_fm, x_out, xn_out, lo_out):
    rx, xs = c.t["r_x"], c.t["r_xs"]
    ry = c.t["r_ybuf"]
    lob = c.t["r_lobuf"]
    xnb = c.t["r_xn"]
    done = {}
    prev_dv = 0
    for ch in range(TOWN // 128):
        p = ch % 2
        t0 = ch * 128
        if ch >= 2:
            c.wait("sync", "s_rst%d" % p, done[ch - 2])
        c.dma("sync", rx[:, p * D:(p + 1) * D], x_in[t0:t0 + 128, :],
              "s_rin%d" % p)
        if y_fm is not None:
            c.dma("sync", ry[:, p * D:(p + 1) * D], y_fm[0:D, t0:t0 + 128],
                  "s_rin%d" % p, transpose=True)
        c.wait("vector", "s_rin%d" % p)
        c.op("vector", lambda e, p=p, has_y=y_fm is not None: (
            e.tensor_tensor(xs[:, p * D:(p + 1) * D], rx[:, p * D:(p + 1) * D],
                            ry[:, p * D:(p + 1) * D], ALU.add) if has_y else
            e.tensor_scalar(xs[:, p * D:(p + 1) * D], rx[:, p * D:(p + 1) * D],
                            0.0, None, ALU.add)), "s_dv")
        c.wait("scalar", "s_dv")
        c.dma("scalar", x_out[t0:t0 + 128, :], xs[:, p * D:(p + 1) * D],
              "s_rst%d" % p)
        if xn_out is not None:
            xnd = xnb[:, p * D:(p + 1) * D]
            lod = lob[:, p * D:(p + 1) * D] if lo_out is not None else None
            prev_dv = emit_ln_chunk(c, xs[:, p * D:(p + 1) * D], xnd, lod,
                                    prev_dv)
            c.wait("scalar", "s_dv")
            c.dma("scalar", xn_out[t0:t0 + 128, :], xnd, "s_rst%d" % p)
            if lo_out is not None:
                c.dma("scalar", lo_out[t0:t0 + 128, :], lod, "s_rst%d" % p)
        done[ch] = c.cur("s_rst%d" % p)


def emit_l0a(c):
    wq, wk, wv, wo, wfe = (c.t["wq"], c.t["wk"], c.t["wv"], c.t["wo"],
                           c.t["wfeat"])
    P0, P1, P2, P3, P4 = (c.t["pp0"], c.t["pp1"], c.t["pp2"], c.t["pp3"],
                          c.t["pp4"])
    ST = [c.t["st0"], c.t["st1"], c.t["st2"]]
    STW = [455, 455, 130]
    PP = [P0, P1]
    PN = [P2, P3]

    xnf, khh, qhh = c.t["a_xnf"], c.t["a_khh"], c.t["a_qhh"]
    sqh, vaug = c.t["a_sqh"], c.t["a_vaug"]
    kpf, qpf, kptm = c.t["a_kpf"], c.t["a_qpf"], c.t["a_kptm"]
    sqm, am, numb = c.t["a_sqm"], c.t["a_am"], c.t["a_numb"]
    yall, yo = c.t["a_yall"], c.t["a_yo"]
    sf, sb = c.t["a_sf32"], c.t["a_sbf"]
    dnr, recb = c.t["a_dn"], c.t["a_rec"]
    sx, xnc = c.t["r_x"], c.t["r_xn"]
    tril4, shead, ucol, ones1 = (c.t["c_tril4"], c.t["c_shead"],
                                 c.t["c_ucol"], c.t["c_ones1"])
    ones1f = c.t["c_ones1f"]
    onesr, epsv = c.t["a_onesr"], c.t["a_epsv"]
    flag = c.t["c_flag"]
    xn0, x_pre, x_own, y0 = (c.d["xn0"].ap(), c.d["x_pre"].ap(),
                             c.d["x_own"].ap(), c.d["y0"].ap())
    SREG = [0, 455, 910]

    c.op("vector", lambda e: e.memset(sf[:], 0.0))
    c.op("vector", lambda e: e.memset(onesr[:], 1.0))
    c.op("vector", lambda e: e.memset(epsv[:], 0.0), "s_dv")
    c.wait("vector", "s_dv")
    c.op("vector", lambda e: e.memset(epsv[0:1, 64:65], EPS_FAVOR))
    c.op("vector", lambda e: e.memset(
        vaug.ap().rearrange("p (s hh ee) -> p s hh ee", s=4, ee=65)
        [:, :, :, 64:65], 1.0), "s_dv")
    init_dv = c.cur("s_dv")

    state_done = 0
    state_na = 0
    prev_y = {}
    ln_store = {}
    prev_ln_dv = 0
    xnf_free = 0

    def sqrows(src, slot):
        c.wait("scalar", "s_ac")
        pe_g = {}
        for g in range(8):
            p = g % 2
            if g >= 2:
                c.wait("scalar", "s_pe", pe_g[g - 2])
            c.op("scalar", lambda e, g=g, p=p: e.activation(
                sqh[0:64, p * 2 * BLK:(p + 1) * 2 * BLK],
                src[0:64, 2 * g * BLK:2 * (g + 1) * BLK], AF.Square), "s_ac")
            c.wait("tensor", "s_ac")
            for hh in range(2):
                h = 2 * g + hh
                pe_g[g] = c.op("tensor", lambda e, h=h, p=p, hh=hh: e.matmul(
                    P4[0:16, 0:BLK], ucol[0:64, h * 16:h * 16 + 16],
                    sqh[0:64, (p * 2 + hh) * BLK:(p * 2 + hh + 1) * BLK],
                    start=(h == 0), stop=(h == H - 1)), "s_pe")
        c.wait("scalar", "s_pe")
        return c.op("scalar", lambda e, slot=slot: e.activation(
            sqm[:, slot * BLK:(slot + 1) * BLK], P4[0:16, 0:BLK],
            AF.Identity, scale=-0.5), "s_ac")

    blk_list = list(range(2 * NBLK))
    if DEV_NBLK:
        blk_list = list(range(NBLK, NBLK + DEV_NBLK))
    blk_set = set(blk_list)
    s1_inline = DEV_STAGE >= 4
    lnst = {"dv": 0}

    def emit_s1_chunk(bb, j):
        own_b = bb >= NBLK
        xsrc = x_own if own_b else x_pre
        tb = (bb - NBLK) * BLK if own_b else bb * BLK
        gb = bb * BLK
        p = j % 2
        c.wait("sync", "s_dv", lnst["dv"])
        c.dma("sync", sx[:, p * D:(p + 1) * D],
              xsrc[tb + j * 128:tb + j * 128 + 128, :], "s_in%d" % p)
        c.wait("scalar", "s_in%d" % p)
        c.wait("vector", "s_in%d" % p)
        c.wait("vector", "s_xn%d" % p)
        lnst["dv"] = emit_ln_chunk(c, sx[:, p * D:(p + 1) * D],
                                   xnc[:, p * D:(p + 1) * D], None,
                                   lnst["dv"])
        c.wait("scalar", "s_dv")
        c.dma("scalar", xn0[gb + j * 128:gb + j * 128 + 128, :],
              xnc[:, p * D:(p + 1) * D], "s_xn%d" % p)

    for blk in blk_list:
        own = blk >= NBLK
        t0 = (blk - NBLK) * BLK if own else blk * BLK
        g0 = blk * BLK

        if (not s1_inline) or blk == blk_list[0]:
            for j in range(NSUB):
                emit_s1_chunk(blk, j)
        c.wait("sync", "s_xn0")
        c.wait("sync", "s_xn1")
        c.wait("sync", "s_pe", xnf_free)
        for dc in range(DC):
            c.dma("sync", xnf[:, dc * BLK:(dc + 1) * BLK],
                  xn0[g0:g0 + BLK, dc * 128:(dc + 1) * 128], "s_tr",
                  transpose=True)

        if DEV_STAGE < 2:
            continue
        c.wait("tensor", "s_tr")
        if blk == 0:
            c.wait("tensor", "s_wt")
            c.wait("tensor", "s_dv", init_dv)

        def projh(dst, w, pp):
            ev = {}
            c.wait("tensor", "s_ac")
            for h in range(H):
                pb = pp[h % 2]
                if h >= 2:
                    c.wait("tensor", "s_ac", ev[h - 2])
                for dc in range(DC):
                    c.op("tensor", lambda e, pb=pb, w=w, dc=dc, h=h: e.matmul(
                        pb[0:64, 0:BLK],
                        w[:, dc * D + h * 64:dc * D + h * 64 + 64],
                        xnf[:, dc * BLK:(dc + 1) * BLK],
                        start=(dc == 0), stop=(dc == DC - 1)),
                        "s_pe" if dc == DC - 1 else None)
                c.wait("scalar", "s_pe")
                ev[h] = c.op("scalar", lambda e, pb=pb, dst=dst, h=h: e.copy(
                    dst[0:64, h * BLK:(h + 1) * BLK], pb[0:64, 0:BLK]), "s_ac")

        projh(khh, wk, PP)
        if own:
            projh(qhh, wq, PP)

        vev = {}
        c.wait("tensor", "s_ac")
        for g in range(8):
            half, j = g // 4, g % 4
            pb = PN[g % 2]
            if g >= 2:
                c.wait("tensor", "s_ac", vev[g - 2])
            for dc in range(DC):
                c.op("tensor", lambda e, pb=pb, dc=dc, half=half, j=j: e.matmul(
                    pb[:, 0:512],
                    xnf[:, dc * BLK + j * 128:dc * BLK + j * 128 + 128],
                    wv[:, dc * D + half * 512:dc * D + half * 512 + 512],
                    start=(dc == 0), stop=(dc == DC - 1)),
                    "s_pe" if dc == DC - 1 else None)
            c.wait("scalar", "s_pe")
            vdst = vaug.ap().rearrange("p (s hh ee) -> p s hh ee", s=4, ee=65)[
                :, j, half * 8:half * 8 + 8, 0:64]
            vev[g] = c.op("scalar", lambda e, pb=pb, vdst=vdst: e.copy(
                vdst, pb[:, 0:512].rearrange("p (hh ee) -> p hh ee", ee=64)),
                "s_ac")
        xnf_free = c.cur("s_pe")

        if DEV_STAGE < 3:
            continue
        sqrows(khh, 0)
        c.wait("tensor", "s_ac")
        if DEV_S3 < 2:
            continue

        def featmaps(srcf, dstf, slot):
            ev = {}
            c.wait("tensor", "s_ac")
            for h in range(H):
                pb = PP[h % 2]
                if h >= 2:
                    c.wait("tensor", "s_ac", ev[h - 2])
                c.op("tensor", lambda e, pb=pb, h=h, slot=slot: e.matmul(
                    pb[:, 0:BLK], shead[:, h * 128:h * 128 + 128],
                    sqm[:, slot * BLK:(slot + 1) * BLK],
                    start=True, stop=False))
                c.op("tensor", lambda e, pb=pb, h=h, srcf=srcf: e.matmul(
                    pb[:, 0:BLK], wfe[0:64, 0:128],
                    srcf[0:64, h * BLK:(h + 1) * BLK],
                    start=False, stop=True), "s_pe")
                c.wait("scalar", "s_pe")
                ev[h] = c.op("scalar", lambda e, pb=pb, h=h, dstf=dstf:
                             e.activation(dstf[:, h * BLK:(h + 1) * BLK],
                                          pb[:, 0:BLK], AF.Exp, bias=LNINV),
                             "s_ac")

        if own:
            featmaps(khh, kpf, 0)
        if DEV_S3 < 4:
            continue
        if own:
            sqrows(qhh, 1)
            c.wait("tensor", "s_ac")
            featmaps(qhh, qpf, 1)

        if DEV_STAGE < 4:
            continue
        c.wait("tensor", "s_ac")
        last_ymult = 0
        for j in range(NSUB):
            c.wait("tensor", "s_dv", last_ymult)
            for qd in range(4):
                c.wait("tensor", "s_ac")
                for hh in range(4):
                    h = 4 * qd + hh
                    c.op("tensor", lambda e, h=h, hh=hh, j=j: e.matmul(
                        P4[:, hh * 128:(hh + 1) * 128],
                        sqm[:, j * 128:(j + 1) * 128],
                        shead[:, h * 128:h * 128 + 128],
                        start=True, stop=False))
                    c.op("tensor", lambda e, h=h, hh=hh, j=j: e.matmul(
                        P4[:, hh * 128:(hh + 1) * 128],
                        khh[0:64, h * BLK + j * 128:h * BLK + j * 128 + 128],
                        wfe[0:64, 0:128], start=False, stop=True),
                        "s_pe" if hh == 3 else None)
                c.wait("scalar", "s_pe")
                c.op("scalar", lambda e, qd=qd: e.activation(
                    kptm[:, qd * 512:(qd + 1) * 512], P4[:, 0:512], AF.Exp,
                    bias=LNINV), "s_ac")
            c.wait("tensor", "s_ac")
            if own:
                PACKS = ((0, 2, 4, 6), (1, 3, 5, 7),
                         (8, 10, 12, 14), (9, 11, 13, 15))
                am_done, numb_done, rec_done, y_done = {}, {}, {}, {}
                pe_a, pe_n, pe_y = {}, {}, {}

                def eAp(p):
                    pa = PP[p % 2]
                    if p >= 2:
                        c.wait("tensor", "s_dv", am_done[p - 2])
                    for i, h in enumerate(PACKS[p]):
                        pe_a[p] = c.op(
                            "tensor", lambda e, pa=pa, h=h, i=i, j=j: e.matmul(
                                pa[:, i * 128:(i + 1) * 128],
                                kpf[:, h * BLK + j * 128:
                                    h * BLK + j * 128 + 128],
                                qpf[:, h * BLK + j * 128:
                                    h * BLK + j * 128 + 128],
                                start=True, stop=True),
                            "s_pe" if i == 3 else None)

                def eAm(p):
                    pa = PP[p % 2]
                    c.wait("vector", "s_pe", pe_a[p])
                    am_done[p] = c.op("vector", lambda e, pa=pa, p=p:
                                      e.tensor_tensor(
                                          am[:, (p % 2) * 512:
                                             (p % 2) * 512 + 512],
                                          pa[:, 0:512], tril4[:], ALU.mult),
                                      "s_dv")

                def eNum(p):
                    pn = PN[p % 2]
                    c.wait("tensor", "s_dv", am_done[p])
                    if p == 0:
                        c.wait("tensor", "s_ac", state_done)
                    if p >= 2:
                        c.wait("tensor", "s_ac", numb_done[p - 2])
                    for i, h in enumerate(PACKS[p]):
                        vsl = vaug[:, j * 1040 + h * 65:
                                   j * 1040 + (h + 1) * 65]
                        c.op("tensor", lambda e, pn=pn, h=h, i=i, j=j:
                             e.matmul(
                                 pn[0:65, i * 128:(i + 1) * 128],
                                 sb[:, h * 65:(h + 1) * 65],
                                 qpf[:, h * BLK + j * 128:
                                     h * BLK + j * 128 + 128],
                                 start=True, stop=False))
                        c.op("tensor", lambda e, pn=pn, i=i, vsl=vsl, p=p:
                             e.matmul(
                                 pn[0:65, i * 128:(i + 1) * 128], vsl,
                                 am[:, (p % 2) * 512 + i * 128:
                                    (p % 2) * 512 + (i + 1) * 128],
                                 start=False, stop=False))
                        pe_n[p] = c.op("tensor", lambda e, pn=pn, i=i:
                                       e.matmul(
                                           pn[0:65, i * 128:(i + 1) * 128],
                                           epsv[0:1, 0:65],
                                           onesr[0:1, 0:128],
                                           start=False, stop=True),
                                       "s_pe" if i == 3 else None)

                def eDs(p):
                    for i, h in enumerate(PACKS[p]):
                        sth, sto = ST[h // 7], (h % 7) * 65
                        vsl = vaug[:, j * 1040 + h * 65:
                                   j * 1040 + (h + 1) * 65]
                        c.op("tensor", lambda e, sth=sth, sto=sto, h=h,
                             vsl=vsl: e.matmul(
                                 sth[:, sto:sto + 65],
                                 kptm[:, h * 128:(h + 1) * 128],
                                 vsl, start=True, stop=True),
                             "s_pe" if i == 3 else None)

                def eRec(p):
                    pn = PN[p % 2]
                    c.wait("scalar", "s_pe", pe_n[p])
                    if p >= 2:
                        c.wait("scalar", "s_pe", pe_y[p - 2])
                    c.op("scalar", lambda e, pn=pn: e.activation(
                        dnr[0:1, 0:512], pn[64:65, 0:512], AF.Ln), "s_ac")
                    c.wait("scalar", "s_ac")
                    rec_done[p] = c.op("scalar", lambda e, p=p: e.activation(
                        recb[0:1, (p % 2) * 512:(p % 2) * 512 + 512],
                        dnr[0:1, 0:512], AF.Exp, scale=-1.0), "s_ac")

                def eYbc(p):
                    c.wait("tensor", "s_ac", rec_done[p])
                    if p >= 1:
                        c.wait("tensor", "s_dv", y_done[p - 1])
                    pe_y[p] = c.op("tensor", lambda e, p=p: e.matmul(
                        P4[0:64, 0:512], ones1[0:1, 0:64],
                        recb[0:1, (p % 2) * 512:(p % 2) * 512 + 512],
                        start=True, stop=True), "s_pe")

                def eNumbEv(p):
                    pn = PN[p % 2]
                    c.wait("scalar", "s_pe", pe_n[p])
                    yw = y_done[p - 2] if p >= 2 else prev_y.get(p + 2, 0)
                    c.wait("scalar", "s_dv", yw)
                    numb_done[p] = c.op("scalar", lambda e, pn=pn, p=p:
                                        e.copy(
                                            numb[:, (p % 2) * 512:
                                                 (p % 2) * 512 + 512],
                                            pn[0:64, 0:512]), "s_ac")

                def eYm(p):
                    c.wait("vector", "s_ac", numb_done[p])
                    c.wait("vector", "s_pe", pe_y[p])
                    r0 = (p % 2) * 64
                    c0 = (p // 2) * 4
                    ydst = yall[r0:r0 + 64, :].rearrange(
                        "p (cb b) -> p cb b", b=BLK)[
                        :, c0:c0 + 4, j * 128:j * 128 + 128]
                    nsrc = numb[0:64, (p % 2) * 512:
                                (p % 2) * 512 + 512].rearrange(
                        "p (cb b) -> p cb b", b=128)
                    psrc = P4[0:64, 0:512].rearrange(
                        "p (cb b) -> p cb b", b=128)
                    y_done[p] = c.op("vector", lambda e, ydst=ydst, nsrc=nsrc,
                                     psrc=psrc: e.tensor_tensor(
                                         ydst, nsrc, psrc, ALU.mult), "s_dv")

                eAp(0); eAm(0)
                eAp(1); eAm(1)
                eNum(0); eDs(0); eRec(0); eNumbEv(0)
                eAp(2); eAm(2)
                eYbc(0)
                eNum(1); eDs(1); eRec(1); eNumbEv(1)
                eYm(0)
                eAp(3); eAm(3)
                eYbc(1)
                eNum(2); eDs(2); eRec(2); eNumbEv(2)
                eYm(1)
                eYbc(2)
                eNum(3); eDs(3); eRec(3); eNumbEv(3)
                eYm(2)
                eYbc(3)
                eYm(3)
                last_ymult = y_done[3]
                prev_y = {2: y_done[2], 3: y_done[3]}
            else:
                c.wait("tensor", "s_dv", state_na)
                for h in range(H):
                    sth, sto = ST[h // 7], (h % 7) * 65
                    vsl = vaug[:, j * 1040 + h * 65:j * 1040 + (h + 1) * 65]
                    c.op("tensor", lambda e, sth=sth, sto=sto, h=h, j=j,
                         vsl=vsl: e.matmul(
                             sth[:, sto:sto + 65],
                             kptm[:, h * 128:(h + 1) * 128],
                             vsl, start=True, stop=True),
                         "s_pe" if h == H - 1 else None)
            c.wait("vector", "s_pe")
            c.wait("vector", "s_dv", state_na)
            for si in range(3):
                state_na = c.op("vector", lambda e, si=si: e.tensor_tensor(
                    sf[:, SREG[si]:SREG[si] + STW[si]],
                    sf[:, SREG[si]:SREG[si] + STW[si]],
                    ST[si][:, 0:STW[si]], ALU.add), "s_dv")
            if own and not (blk == 2 * NBLK - 1 and j == NSUB - 1):
                c.wait("scalar", "s_dv", state_na)
                state_done = c.op("scalar", lambda e: e.copy(sb[:], sf[:]),
                                  "s_ac")
            if s1_inline and blk + 1 in blk_set:
                emit_s1_chunk(blk + 1, j)

        if blk == NBLK - 1:
            c.wait("vector", "s_dv", state_na)
            state_na = c.op("vector", lambda e: e.tensor_scalar(
                sf[:], sf[:], flag[:, 0:1], None, ALU.mult), "s_dv")
            c.wait("scalar", "s_dv", state_na)
            state_done = c.op("scalar", lambda e: e.copy(sb[:], sf[:]),
                              "s_ac")

        if DEV_STAGE < 5:
            continue
        if own:
            c.wait("tensor", "s_dv")
            c.wait("scalar", "s_yst0")
            c.wait("scalar", "s_yst1")
            yst = {}
            for oc in range(DC):
                pb = PP[oc % 2]
                if oc >= 2:
                    c.wait("tensor", "s_ac")
                for ec in range(DC):
                    c.op("tensor", lambda e, pb=pb, oc=oc, ec=ec: e.matmul(
                        pb[:, 0:BLK],
                        wo[:, ec * D + oc * 128:ec * D + oc * 128 + 128],
                        yall[:, ec * BLK:(ec + 1) * BLK],
                        start=(ec == 0), stop=(ec == DC - 1)),
                        "s_pe" if ec == DC - 1 else None)
                c.wait("scalar", "s_pe")
                if oc >= 2:
                    c.wait("scalar", "s_yst%d" % (oc % 2), yst[oc - 2])
                nev = c.op("scalar", lambda e, pb=pb, oc=oc: e.copy(
                    yo[:, (oc % 2) * BLK:(oc % 2) * BLK + BLK], pb[:, 0:BLK]),
                    "s_ac")
                c.wait("scalar", "s_ac", nev)
                yst[oc] = c.dma("scalar",
                                y0[oc * 128:(oc + 1) * 128, t0:t0 + BLK],
                                yo[:, (oc % 2) * BLK:(oc % 2) * BLK + BLK],
                                "s_yst%d" % (oc % 2))




TT = 1024
NTT = TOWN // TT


def emit_l0f(c):
    P0, P1, P2, P3, P4 = (c.t["pp0"], c.t["pp1"], c.t["pp2"], c.t["pp3"],
                          c.t["pp4"])
    ST0, ST1 = c.t["st0"], c.t["st1"]
    PP, PN = [P0, P1], [P2, P3]
    fxh, fxl = c.t["f_xh"], c.t["f_xl"]
    acc = c.t["f_acc"]
    w1s, w2s, b1r = c.t["f_w1"], c.t["f_w2"], c.t["f_b1"]
    hsb, hms, ybf = c.t["f_h"], c.t["f_hm"], c.t["f_y"]
    mkf = c.t["f_mk"]
    lg, mx, mk32 = c.t["f_lg"], c.t["f_mx"], c.t["f_mk32"]
    wgh, wgl, wgf, bgb = c.t["f_wgh"], c.t["f_wgl"], c.t["f_wgf"], c.t["f_bgb"]
    b2b, b2f, sel8 = c.t["f_b2b"], c.t["f_b2f"], c.t["c_sel8"]
    identf = c.t["c_identf"]
    ones1 = c.t["c_ones1"]
    xn1, xn1lo = c.d["xn1"].ap(), c.d["xn1lo"].ap()
    w1d, w2d = c.d["f_w1d"].ap(), c.d["f_w2d"].ap()
    y1 = c.d["y1"].ap()

    c.dma("sync", wgf[:].rearrange("p (dc e) -> p dc e", e=8),
          c.d["f_wgd"].ap().rearrange("(dc p) e -> p dc e", p=128), "s_ld")
    c.dma("sync", c.t["f_bgf"][:], c.d["f_bgd"].ap(), "s_ld")
    c.dma("sync", b2f[:], c.d["f_b2d"].ap(), "s_ld")
    c.dma("sync", b1r[:], c.d["f_b1d"].ap(), "s_ld")
    c.dma("gpsimd", sel8[:], c.d["c_sel8d"].ap(), "s_wt")
    c.dma("sync", identf[:], c.d["c_identfd"].ap(), "s_ld")
    c.wait("vector", "s_ld")
    c.wait("scalar", "s_ld")
    c.wait("vector", "s_wt")
    n = c.op("vector", lambda e: e.tensor_scalar(wgh[:], wgf[:], 0.0, None,
                                                 ALU.add), "s_dv")
    c.wait("vector", "s_dv", n)
    c.op("vector", lambda e: e.tensor_tensor(wgl[:], wgf[:], wgh[:],
                                             ALU.subtract), "s_dv")
    n = c.op("vector", lambda e: e.tensor_scalar(b2b[:], b2f[:], 0.0, None,
                                                 ALU.add), "s_dv")
    c.op("vector", lambda e: e.memset(c.t["c_ones1f"][:], 1.0), "s_dv")
    c.wait("tensor", "s_dv")
    c.wait("tensor", "s_ld")
    n = c.op("tensor", lambda e: e.matmul(
        ST1[0:128, 0:8], c.t["c_ones1f"][0:1, :], c.t["f_bgf"][0:1, :],
        start=True, stop=True), "s_pe")
    c.wait("vector", "s_pe", n)
    nbg = c.op("vector", lambda e: e.tensor_scalar(
        bgb[:], ST1[0:128, 0:8], 0.0, None, ALU.add), "s_dv")

    pe_fx_free = 0
    last_hm = 0
    conv_dv = 0
    last_store = {0: 0, 1: 0}
    ntt = int(os.environ.get("DEV_NTT", "0")) or NTT
    for tt in range(ntt):
        t0 = tt * TT
        c.wait("sync", "s_pe", pe_fx_free)
        for dc in range(DC):
            c.dma("sync", fxh[:, dc * TT:(dc + 1) * TT],
                  xn1[t0:t0 + TT, dc * 128:(dc + 1) * 128], "s_tr",
                  transpose=True)
            c.dma("sync", fxl[:, dc * TT:(dc + 1) * TT],
                  xn1lo[t0:t0 + TT, dc * 128:(dc + 1) * 128], "s_tr",
                  transpose=True)

        c.wait("tensor", "s_tr")
        if tt == 0:
            c.wait("tensor", "s_dv", nbg)
        lgf_done = nbg
        tr_evac = 0
        for ch in range(TT // 128):
            c.wait("tensor", "s_dv", lgf_done)
            mms = []
            for src, wgx in ((fxh, wgh), (fxh, wgl), (fxl, wgh)):
                for dc in range(DC):
                    mms.append((src, wgx, dc))
            for i, (src, wgx, dc) in enumerate(mms):
                c.op("tensor", lambda e, src=src, wgx=wgx, dc=dc, ch=ch,
                     i=i, last=(i == len(mms) - 1): e.matmul(
                         ST0[0:128, 0:8],
                         src[:, dc * TT + ch * 128:dc * TT + ch * 128 + 128],
                         wgx[:, dc * 8:(dc + 1) * 8],
                         start=(i == 0), stop=last),
                     "s_pe" if i == len(mms) - 1 else None)
            c.wait("vector", "s_pe")
            n = c.op("vector", lambda e: e.tensor_tensor(
                lg[:], ST0[0:128, 0:8], bgb[:], ALU.add), "s_dv")
            lgf_done = n
            c.wait("vector", "s_dv", n)
            n = c.op("vector", lambda e: e.reduce_max(
                mx[:], lg[:], axis=mybir.AxisListType.X), "s_dv")
            c.wait("vector", "s_dv", n)
            n = c.op("vector", lambda e: e.tensor_scalar(
                mk32[:], lg[:], mx[:, 0:1], None, ALU.is_ge), "s_dv")
            c.wait("tensor", "s_dv", n)
            c.wait("tensor", "s_ac", tr_evac)
            n = c.op("tensor", lambda e: e.transpose(
                ST1[0:8, 128:256], mk32[:], identf[:]), "s_pe")
            c.wait("scalar", "s_pe", n)
            tr_evac = c.op("scalar", lambda e, ch=ch: e.copy(
                mkf[0:8, ch * 128:(ch + 1) * 128], ST1[0:8, 128:256]), "s_ac")

        for u in range(16):
            eix, hf = u // 2, u % 2
            pu = u % 2
            c.wait("gpsimd", "s_pe", c._l0f_wfree.get(pu, 0))
            c.dma("gpsimd",
                  w1s.ap().rearrange("p (b dc i) -> p b dc i", b=2, i=512)
                  [:, pu], w1d[eix].rearrange("(dc p) i -> p dc i", p=128)
                  [:, :, hf * 512:(hf + 1) * 512], "s_w%d" % pu)
            c.dma("gpsimd",
                  w2s.ap().rearrange("p (b ic o) -> p b ic o", b=2, o=1024)
                  [:, pu],
                  w2d[eix, hf * 512:(hf + 1) * 512, :]
                  .rearrange("(ic p) o -> p ic o", p=128), "s_w%d" % pu)
            c.wait("tensor", "s_w%d" % pu)
            if u < 2:
                c.wait("tensor", "s_ac", tr_evac)
            last_mm = 0
            for sub in range(TT // 512):
                hev = {}
                c.wait("tensor", "s_ac")
                for ic in range(4):
                    pb = PP[ic % 2]
                    if ic >= 2:
                        c.wait("tensor", "s_ac", hev[ic - 2])
                    for dc in range(DC):
                        c.op("tensor", lambda e, pb=pb, ic=ic, dc=dc, pu=pu,
                             sub=sub: e.matmul(
                                 pb[:, 0:512],
                                 w1s[:, (pu * 8 + dc) * 512 + ic * 128:
                                     (pu * 8 + dc) * 512 + ic * 128 + 128],
                                 fxh[:, dc * TT + sub * 512:
                                     dc * TT + sub * 512 + 512],
                                 start=(dc == 0), stop=(dc == DC - 1)),
                             "s_pe" if dc == DC - 1 else None)
                    c.wait("scalar", "s_pe")
                    hev[ic] = c.op("scalar", lambda e, pb=pb, ic=ic, u=u:
                                   e.activation(
                                       hsb[:, ic * 512:(ic + 1) * 512],
                                       pb[:, 0:512], AF.Gelu,
                                       bias=b1r[:, u * 4 + ic:u * 4 + ic + 1]),
                                   "s_ac")
                c.wait("tensor", "s_dv", last_hm)
                pe_mb = c.op("tensor", lambda e, eix=eix, sub=sub: e.matmul(
                    P4[:, 0:512], sel8[:, eix * 128:(eix + 1) * 128],
                    mkf[0:8, sub * 512:(sub + 1) * 512],
                    start=True, stop=True), "s_pe")
                c.wait("vector", "s_pe", pe_mb)
                c.wait("vector", "s_ac")
                for ic in range(4):
                    last_hm = c.op("vector", lambda e, ic=ic: e.tensor_tensor(
                        hms[:, ic * 512:(ic + 1) * 512],
                        hsb[:, ic * 512:(ic + 1) * 512], P4[:, 0:512],
                        ALU.mult), "s_dv")
                c.wait("tensor", "s_dv", last_hm)
                aev = {}
                for oc in range(DC):
                    pd = PN[oc % 2]
                    if oc >= 2:
                        c.wait("tensor", "s_dv", aev[oc - 2])
                    for ic in range(4):
                        last_mm = c.op(
                            "tensor", lambda e, pd=pd, oc=oc, ic=ic, pu=pu:
                            e.matmul(
                                pd[:, 0:512],
                                w2s[:, (pu * 4 + ic) * 1024 + oc * 128:
                                    (pu * 4 + ic) * 1024 + oc * 128 + 128],
                                hms[:, ic * 512:(ic + 1) * 512],
                                start=(ic == 0), stop=(ic == 3)),
                            "s_pe" if ic == 3 else None)
                    c.wait("vector", "s_pe")
                    if u == 0:
                        c.wait("vector", "s_dv", conv_dv)
                    dst = acc[:, oc * TT + sub * 512:oc * TT + sub * 512 + 512]
                    aev[oc] = c.op("vector", lambda e, pd=pd, dst=dst,
                                   first=(u == 0): (
                                       e.tensor_scalar(dst, pd[:, 0:512], 0.0,
                                                       None, ALU.add)
                                       if first else
                                       e.tensor_tensor(dst, dst, pd[:, 0:512],
                                                       ALU.add)), "s_dv")
            c._l0f_wfree[pu] = last_mm

        for sub in range(TT // 512):
            for oc in range(DC):
                pd = PN[oc % 2]
                c.wait("tensor", "s_dv")
                n = c.op("tensor", lambda e, pd=pd, oc=oc, sub=sub: e.matmul(
                    pd[:, 0:512], b2b[0:8, oc * 128:(oc + 1) * 128],
                    mkf[0:8, sub * 512:(sub + 1) * 512],
                    start=True, stop=True), "s_pe")
                c.wait("vector", "s_pe", n)
                dst = acc[:, oc * TT + sub * 512:oc * TT + sub * 512 + 512]
                c.op("vector", lambda e, pd=pd, dst=dst: e.tensor_tensor(
                    dst, dst, pd[:, 0:512], ALU.add), "s_dv")
        pe_fx_free = c.cur("s_pe")
        c.wait("vector", "s_dv")
        for oc in range(DC):
            p = oc % 2
            c.wait("vector", "s_yst%d" % p, last_store[p])
            conv_dv = c.op("vector", lambda e, oc=oc, p=p: e.tensor_scalar(
                ybf[:, p * TT:(p + 1) * TT], acc[:, oc * TT:(oc + 1) * TT],
                0.0, None, ALU.add), "s_dv")
            c.wait("scalar", "s_dv", conv_dv)
            last_store[p] = c.dma(
                "scalar", y1[oc * 128:(oc + 1) * 128, t0:t0 + TT],
                ybf[:, p * TT:(p + 1) * TT], "s_yst%d" % p)


def emit_l1a(c):
    P0, P1, P2, P3, P4 = (c.t["pp0"], c.t["pp1"], c.t["pp2"], c.t["pp3"],
                          c.t["pp4"])
    ST0 = c.t["st0"]
    PP = [P0, P1]
    xnf = c.t["g_xnf"]
    kraw = c.t["g_kraw"]
    ksq = c.t["g_ksq"]
    knb = c.t["g_kn"]
    vsb = c.t["g_v"]
    rows = c.t["g_rows"]
    arow, brow = c.t["g_ab"][0:1, 0:512], c.t["g_ab"][0:1, 512:1024]
    bqk = c.t["g_bqk"]
    bvr = c.t["g_bvr"]
    bols = c.t["g_bols"]
    wq, wk, wv, wo = c.t["gw_q"], c.t["gw_k"], c.t["gw_v"], c.t["gw_o"]
    ones1, ones128 = c.t["c_ones1"], c.t["c_ones128"]
    xn2 = c.d["xn2i"].ap()
    xn2f = c.d["xn2f"].ap()
    kn_d, v_d, qn_d = c.d["kn_d"].ap(), c.d["v_d"].ap(), c.d["qn_d"].ap()
    ao_tm, y2 = c.d["ao_tm"].ap(), c.d["y2"].ap()

    c.op("vector", lambda e: e.memset(ones128[:], 1.0), "s_dv")
    c.wait("vector", "s_ld")
    n = c.op("vector", lambda e: e.tensor_tensor(
        bols[:, 8:16], bols[:, 8:16], bols[:, 0:8], ALU.mult), "s_dv")

    pe_xnf_free = 0
    st_par = {0: 0, 1: 0}
    for til in range(24):
        qpass = til >= 16
        src = xn2 if qpass else xn2f
        t0 = (til - 16) * BLK if qpass else til * BLK
        g0 = til * BLK
        c.wait("sync", "s_pe", pe_xnf_free)
        for dc in range(DC):
            c.dma("sync", xnf[:, dc * BLK:(dc + 1) * BLK],
                  src[t0:t0 + BLK, dc * 128:(dc + 1) * 128], "s_tr",
                  transpose=True)
        c.wait("tensor", "s_tr")
        if til == 0:
            c.wait("tensor", "s_wt")
            c.wait("tensor", "s_dv", n)

        def qkproj(dst, w, bcol):
            ev = {}
            c.wait("tensor", "s_ac")
            c.wait("scalar", "s_dv")
            for ec in range(DC):
                pb = PP[ec % 2]
                if ec >= 2:
                    c.wait("tensor", "s_ac", ev[ec - 2])
                for dc in range(DC):
                    c.op("tensor", lambda e, pb=pb, w=w, dc=dc, ec=ec: e.matmul(
                        pb[:, 0:BLK],
                        w[:, dc * D + ec * 128:dc * D + ec * 128 + 128],
                        xnf[:, dc * BLK:(dc + 1) * BLK],
                        start=(dc == 0), stop=(dc == DC - 1)),
                        "s_pe" if dc == DC - 1 else None)
                c.wait("scalar", "s_pe")
                ev[ec] = c.op("scalar", lambda e, pb=pb, dst=dst, ec=ec,
                              bcol=bcol: e.activation(
                                  dst[:, ec * BLK:(ec + 1) * BLK],
                                  pb[:, 0:BLK], AF.Identity,
                                  bias=bqk[:, bcol + ec:bcol + ec + 1]),
                              "s_ac")
            return ev

        def lnrows(src_sb, dst_sb, dst_d, grow):
            c.wait("vector", "s_st0")
            c.wait("vector", "s_st1")
            c.wait("scalar", "s_ac")
            nsq = c.op("scalar", lambda e: e.activation(ksq[:], src_sb[:],
                                                        AF.Square), "s_ac")
            c.wait("tensor", "s_ac", nsq)
            c.wait("tensor", "s_dv")
            for ec in range(DC):
                c.op("tensor", lambda e, ec=ec: e.matmul(
                    P2[0:1, 0:BLK], ones128[:, 0:1],
                    src_sb[:, ec * BLK:(ec + 1) * BLK],
                    start=(ec == 0), stop=(ec == DC - 1)),
                    "s_pe" if ec == DC - 1 else None)
            for ec in range(DC):
                c.op("tensor", lambda e, ec=ec: e.matmul(
                    P3[0:1, 0:BLK], ones128[:, 0:1],
                    ksq[:, ec * BLK:(ec + 1) * BLK],
                    start=(ec == 0), stop=(ec == DC - 1)),
                    "s_pe" if ec == DC - 1 else None)
            c.wait("vector", "s_pe")
            nn = c.op("vector", lambda e: e.tensor_scalar(
                rows[0:1, 0:512], P2[0:1, 0:BLK], 1.0 / D, None, ALU.mult),
                "s_dv")
            nn2 = c.op("vector", lambda e: e.tensor_scalar(
                rows[0:1, 512:1024], P3[0:1, 0:BLK], 1.0 / D, None, ALU.mult),
                "s_dv")
            c.wait("vector", "s_dv", nn)
            nn3 = c.op("vector", lambda e: e.tensor_tensor(
                c.t["g_r2"][0:1, 0:512], rows[0:1, 0:512], rows[0:1, 0:512],
                ALU.mult), "s_dv")
            c.wait("vector", "s_dv", max(nn2, nn3))
            nn = c.op("vector", lambda e: e.tensor_tensor(
                c.t["g_r2"][0:1, 512:1024], rows[0:1, 512:1024],
                c.t["g_r2"][0:1, 0:512], ALU.subtract), "s_dv")
            c.wait("vector", "s_dv", nn)
            nn = c.op("vector", lambda e: e.tensor_scalar(
                c.t["g_r2"][0:1, 512:1024], c.t["g_r2"][0:1, 512:1024],
                LN_EPS, None, ALU.add), "s_dv")
            c.wait("vector", "s_dv", nn)
            nn = c.op("vector", lambda e: e.reciprocal(
                c.t["g_r3"][0:1, 0:512], c.t["g_r2"][0:1, 512:1024]), "s_dv")
            c.wait("scalar", "s_dv", nn)
            nn = c.op("scalar", lambda e: e.activation(
                c.t["g_r3"][0:1, 512:1024], c.t["g_r3"][0:1, 0:512], AF.Sqrt),
                "s_ac")
            c.wait("vector", "s_ac", nn)
            nn = c.op("vector", lambda e: e.tensor_scalar(
                arow[:], c.t["g_r3"][0:1, 512:1024], 0.0, None, ALU.add),
                "s_dv")
            c.wait("vector", "s_dv", nn)
            nn = c.op("vector", lambda e: e.tensor_tensor(
                brow[:], rows[0:1, 0:512], arow[:], ALU.mult), "s_dv")
            c.wait("vector", "s_dv", nn)
            nb = c.op("vector", lambda e: e.tensor_scalar(
                brow[:], brow[:], -1.0, None, ALU.mult), "s_dv")
            c.wait("tensor", "s_dv", nb)
            c.op("tensor", lambda e: e.matmul(
                P2[:, 0:BLK], ones1[0:1, :], arow[:], start=True, stop=True))
            nbc = c.op("tensor", lambda e: e.matmul(
                P3[:, 0:BLK], ones1[0:1, :], brow[:], start=True, stop=True),
                "s_pe")
            c.wait("vector", "s_pe", nbc)
            last = 0
            for ec in range(DC):
                n1 = c.op("vector", lambda e, ec=ec: e.tensor_tensor(
                    ksq[:, ec * BLK:(ec + 1) * BLK],
                    src_sb[:, ec * BLK:(ec + 1) * BLK], P2[:, 0:BLK],
                    ALU.mult), "s_dv")
                c.wait("vector", "s_dv", n1)
                last = c.op("vector", lambda e, ec=ec: e.tensor_tensor(
                    dst_sb[:, ec * BLK:(ec + 1) * BLK],
                    ksq[:, ec * BLK:(ec + 1) * BLK], P3[:, 0:BLK],
                    ALU.add), "s_dv")
            c.wait("scalar", "s_dv", last)
            par = til % 2
            c.wait("scalar", "s_st%d" % par, st_par[par])
            st_par[par] = c.dma(
                "scalar",
                dst_d.rearrange("(dc p) t -> p dc t", p=128)[:, :, grow:grow + BLK],
                dst_sb[:].rearrange("p (dc t) -> p dc t", t=BLK),
                "s_st%d" % par)
            return last

        if qpass:
            qkproj(kraw, wq, 0)
            lnrows(kraw, knb, qn_d, t0)
            pe_xnf_free = c.cur("s_pe")
            continue
        qkproj(kraw, wk, 8)
        lnrows(kraw, knb, kn_d, g0)

        vev = {}
        c.wait("tensor", "s_ac")
        for g in range(8):
            half, j = g // 4, g % 4
            pb = PP[g % 2]
            if g >= 2:
                c.wait("tensor", "s_ac", vev[g - 2])
            c.op("tensor", lambda e, pb=pb, half=half: e.matmul(
                pb[:, 0:512], ones1[0:1, 0:128],
                bvr[0:1, half * 512:(half + 1) * 512],
                start=True, stop=False))
            for dc in range(DC):
                c.op("tensor", lambda e, pb=pb, dc=dc, half=half, j=j: e.matmul(
                    pb[:, 0:512],
                    xnf[:, dc * BLK + j * 128:dc * BLK + j * 128 + 128],
                    wv[:, dc * D + half * 512:dc * D + half * 512 + 512],
                    start=False, stop=(dc == DC - 1)),
                    "s_pe" if dc == DC - 1 else None)
            c.wait("scalar", "s_pe")
            vev[g] = c.op("scalar", lambda e, pb=pb, j=j, half=half: e.copy(
                vsb[:, j * 1024 + half * 512:j * 1024 + half * 512 + 512],
                pb[:, 0:512]), "s_ac")
        pe_xnf_free = c.cur("s_pe")
        par = til % 2
        c.wait("scalar", "s_ac")
        c.wait("scalar", "s_st%d" % par, st_par[par])
        st_par[par] = c.dma(
            "scalar",
            v_d[g0:g0 + BLK, :].rearrange("(s p) e -> p s e", p=128),
            vsb[:].rearrange("p (s e) -> p s e", e=1024),
            "s_st%d" % par)
    c.wait("sync", "s_st0")
    c.wait("sync", "s_st1")

    qnt4 = c.t["gw_v"][:, :]
    accq = c.t["gw_q"][:, :].bitcast(F32)
    acck = c.t["gw_k"][:, :].bitcast(F32)
    att = c.t["g_att"]
    outb = c.t["g_outb"]
    knl = c.t["g_knl"]
    vbl = c.t["g_vbl"]
    OUTB = [P0, P1, P2, P3]
    GQ = 4

    def acc_ap(ql):
        base = accq if ql < 2 else acck
        return base[:, (ql % 2) * 2048:(ql % 2) * 2048 + 2048]

    ao_par = {0: 0, 1: 0}
    kv_free = {0: 0, 1: 0}
    qn_free = 0
    evac_ac = {}
    add_dv = {}
    last_add = [0]
    for sweep in range(16 // GQ):
        c.wait("sync", "s_pe", qn_free)
        for ql in range(GQ):
            qt = sweep * GQ + ql
            c.dma("sync",
                  qnt4[:, ql * 2048:(ql + 1) * 2048]
                  .rearrange("p (dc s) -> p dc s", s=256),
                  qn_d.rearrange("(dc p) t -> p dc t", p=128)
                  [:, :, qt * 256:qt * 256 + 256], "s_q0")
        c.wait("tensor", "s_q0")

        def emit_scores(u, spb_free):
            kb, ql, sub = u // 16, (u // 4) % 4, u % 4
            kpar = kb % 2
            if ql == 0 and sub == 0:
                c.wait("tensor", "s_kv%d" % kpar)
            c.wait("tensor", "s_ac", spb_free)
            spb = P4 if u % 2 == 0 else ST0
            for ec in range(DC):
                c.op("tensor", lambda e, spb=spb, ec=ec, sub=sub, ql=ql,
                     kpar=kpar: e.matmul(
                         spb[0:128, 0:256],
                         knl[:, kpar * 4096 + ec * 512 + sub * 128:
                             kpar * 4096 + ec * 512 + sub * 128 + 128],
                         qnt4[:, ql * 2048 + ec * 256:
                              ql * 2048 + (ec + 1) * 256],
                         start=(ec == 0), stop=(ec == DC - 1)),
                     "s_pe" if ec == DC - 1 else None)
            nsc = c.cur("s_pe")
            nse = c.op("scalar", lambda e, spb=spb, u=u, nsc=nsc: (
                e.wait_ge(c.sem("s_pe")[0], nsc),
                e.activation(att[:, (u % 2) * 256:(u % 2) * 256 + 256],
                             spb[0:128, 0:256], AF.Sigmoid, bias=-LOG_S,
                             scale=1.0 / 32.0))[-1], "s_ac")
            return nse

        def emit_acc(kb, ql, pe_tick):
            c.wait("vector", "s_pe", pe_tick)
            c.wait("vector", "s_dv", add_dv.get(ql, 0))
            if kb == 0:
                c.wait("vector", "s_ac", evac_ac.get(ql, 0))
            for g in range(4):
                dst = acc_ap(ql)[:, g * 512:(g + 1) * 512]
                if kb == 0:
                    add_dv[ql] = c.op(
                        "vector", lambda e, dst=dst, g=g: e.tensor_scalar(
                            dst, OUTB[g][0:128, 0:512], 0.0, None, ALU.add),
                        "s_dv")
                else:
                    add_dv[ql] = c.op(
                        "vector", lambda e, dst=dst, g=g: e.tensor_tensor(
                            dst, dst, OUTB[g][0:128, 0:512], ALU.add), "s_dv")
            last_add[0] = add_dv[ql]

        def emit_out(u, nse):
            kb, ql, sub = u // 16, (u // 4) % 4, u % 4
            kpar = kb % 2
            c.wait("tensor", "s_ac", nse)
            if sub == 0:
                c.wait("tensor", "s_dv", last_add[0])
            for g in range(4):
                c.op("tensor", lambda e, g=g, sub=sub, u=u, kpar=kpar:
                     e.matmul(
                         OUTB[g][0:128, 0:512],
                         att[:, (u % 2) * 256 + (g // 2) * 128:
                             (u % 2) * 256 + (g // 2) * 128 + 128],
                         vbl[:, kpar * 4096 + sub * 1024 + (g % 2) * 512:
                             kpar * 4096 + sub * 1024 + (g % 2) * 512 + 512],
                         start=(sub == 0), stop=(sub == 3)),
                     "s_pe" if g == 3 else None)
            if sub == 3:
                emit_acc(kb, ql, c.cur("s_pe"))
            if ql == GQ - 1 and sub == 3:
                kv_free[kpar] = c.cur("s_pe")
                if kb + 2 <= 15:
                    emit_kv_load(kb + 2)
                elif sweep < 16 // GQ - 1:
                    emit_kv_load(kb + 2 - 16)

        def emit_kv_load(kb):
            kpar = kb % 2
            c.wait("sync", "s_pe", kv_free[kpar])
            c.dma("sync",
                  knl[:, kpar * 4096:(kpar + 1) * 4096]
                  .rearrange("p (dc t) -> p dc t", t=512),
                  kn_d.rearrange("(dc p) t -> p dc t", p=128)
                  [:, :, kb * 512:(kb + 1) * 512], "s_kv%d" % kpar)
            c.dma("sync",
                  vbl[:, kpar * 4096:(kpar + 1) * 4096]
                  .rearrange("p (s e) -> p s e", e=1024),
                  v_d[kb * 512:(kb + 1) * 512, :]
                  .rearrange("(s p) e -> p s e", p=128), "s_kv%d" % kpar)

        if sweep == 0:
            emit_kv_load(0)
            emit_kv_load(1)
        prev = None
        nse_hist = {}
        for u in range(16 * GQ * 4):
            nse = emit_scores(u, nse_hist.get(u - 2, 0))
            nse_hist[u] = nse
            if prev is not None:
                emit_out(*prev)
            prev = (u, nse)
        emit_out(*prev)
        qn_free = c.cur("s_pe")

        c.wait("scalar", "s_pe")
        for ql in range(GQ):
            qt = sweep * GQ + ql
            par = qt % 2
            c.wait("scalar", "s_dv", add_dv[ql])
            c.wait("scalar", "s_ao%d" % par, ao_par[par])
            ev = 0
            for g in range(4):
                ev = c.op("scalar", lambda e, ql=ql, g=g, par=par: e.copy(
                    outb[:, par * 2048 + g * 512:par * 2048 + (g + 1) * 512],
                    acc_ap(ql)[:, g * 512:(g + 1) * 512]), "s_ac")
            evac_ac[ql] = ev
            c.wait("scalar", "s_ac", ev)
            ao_par[par] = c.dma(
                "scalar",
                ao_tm[qt * 256:qt * 256 + 256, :]
                .rearrange("(s p) e -> p s e", p=128),
                outb[:, par * 2048:(par + 1) * 2048]
                .rearrange("p (s e) -> p s e", e=1024), "s_ao%d" % par)
    c.wait("sync", "s_ao0")
    c.wait("sync", "s_ao1")

    aof = c.t["g_xnf"]
    ybf = c.t["g_kraw"]
    yst = {0: 0, 1: 0}
    pe_aof_free = c.cur("s_pe")
    for til in range(8):
        t0 = til * BLK
        c.wait("sync", "s_pe", pe_aof_free)
        for ec in range(DC):
            c.dma("sync", aof[:, ec * BLK:(ec + 1) * BLK],
                  ao_tm[t0:t0 + BLK, ec * 128:(ec + 1) * 128], "s_tr",
                  transpose=True)
        c.wait("tensor", "s_tr")
        c.wait("tensor", "s_ac")
        ev = {}
        for oc in range(DC):
            pb = PP[oc % 2]
            if oc >= 2:
                c.wait("tensor", "s_ac", ev[oc - 2])
            for ec in range(DC):
                c.op("tensor", lambda e, pb=pb, oc=oc, ec=ec: e.matmul(
                    pb[:, 0:BLK],
                    wo[:, ec * D + oc * 128:ec * D + oc * 128 + 128],
                    aof[:, ec * BLK:(ec + 1) * BLK],
                    start=(ec == 0), stop=(ec == DC - 1)),
                    "s_pe" if ec == DC - 1 else None)
            c.wait("scalar", "s_pe")
            par = oc % 2
            c.wait("scalar", "s_y%d" % par, yst[par])
            ev[oc] = c.op("scalar", lambda e, pb=pb, oc=oc, par=par:
                          e.activation(
                              ybf[:, par * BLK:(par + 1) * BLK], pb[:, 0:BLK],
                              AF.Identity,
                              bias=bols[:, 8 + oc:8 + oc + 1],
                              scale=bols[:, oc:oc + 1]), "s_ac")
            c.wait("scalar", "s_ac", ev[oc])
            yst[par] = c.dma("scalar",
                             y2[oc * 128:(oc + 1) * 128, t0:t0 + BLK],
                             ybf[:, par * BLK:(par + 1) * BLK], "s_y%d" % par)
        pe_aof_free = c.cur("s_pe")
    c.wait("sync", "s_y0")
    c.wait("sync", "s_y1")


def alloc_common(c):
    nc = c.nc
    dt = nc.dram_tensor
    c.d["c_pool"] = dt("c_pool", [128, len(CONST_VALS)], F32,
                       kind="ExternalInput")
    sbuf(c, "c_cpool", [128, len(CONST_VALS)], F32)
    for i, v in enumerate(CONST_VALS):
        nc.const_aps.register(F32, v, c.t["c_cpool"][:, i:i + 1])
    sbuf(c, "c_ones1", [1, 128], BF16)
    sbuf(c, "c_ones1f", [1, 128], F32)
    sbuf(c, "r_x", [128, 2 * D], F32)
    sbuf(c, "r_xs", [128, 2 * D], F32)
    sbuf(c, "r_stats", [128, 8], F32)
    sbuf(c, "r_sq", [128, D], BF16)
    sbuf(c, "r_xb", [128, D], BF16)
    sbuf(c, "r_f32", [128, D], F32)
    sbuf(c, "r_xn", [128, 2 * D], BF16)
    psum_t(c, "pp0", [128, 512])
    psum_t(c, "pp1", [128, 512])
    psum_t(c, "pp2", [128, 512])
    psum_t(c, "pp3", [128, 512])
    psum_t(c, "pp4", [128, 512])
    psum_t(c, "st0", [128, 455])
    psum_t(c, "st1", [128, 455])
    psum_t(c, "st2", [128, 130])
    c.dma("sync", c.t["c_cpool"][:], c.d["c_pool"].ap(), "s_ld")
    c.op("vector", lambda e: e.memset(c.t["c_ones1"][:], 1.0))
    c.op("vector", lambda e: e.memset(c.t["c_ones1f"][:], 1.0))


def finish(c, store_sems):
    for sname in store_sems:
        c.wait("sync", sname)
    c.finalize(c.block)


def build_program_a():
    nc = bass.Bass("TRN2", target_bir_lowering=False)
    stack = ExitStack()
    c = Ctx(nc, stack)
    with stack:
        dt = nc.dram_tensor
        c.d["x_pre"] = dt("x_pre", [TOWN, D], F32, kind="ExternalInput")
        c.d["x_own"] = dt("x_own", [TOWN, D], F32, kind="ExternalInput")
        c.d["w_q"] = dt("w_q", [D, D], F32, kind="ExternalInput")
        c.d["w_k"] = dt("w_k", [D, D], F32, kind="ExternalInput")
        c.d["w_v"] = dt("w_v", [D, D], F32, kind="ExternalInput")
        c.d["w_o"] = dt("w_o", [D, D], F32, kind="ExternalInput")
        c.d["w_f"] = dt("w_f", [HD, NF], F32, kind="ExternalInput")
        c.d["c_trild"] = dt("c_trild", [128, 128], F32, kind="ExternalInput")
        c.d["c_sheadd"] = dt("c_sheadd", [16, 16 * 128], F32,
                             kind="ExternalInput")
        c.d["c_ucold"] = dt("c_ucold", [64, 16 * 16], F32,
                            kind="ExternalInput")
        c.d["c_flagd"] = dt("c_flagd", [128, 1], F32, kind="ExternalInput")
        c.d["c_identfd"] = dt("c_identfd", [128, 128], F32,
                              kind="ExternalInput")
        c.d["xn0"] = dt("xn0", [2 * TOWN, D], BF16)
        c.d["y0"] = dt("y0", [D, TOWN], BF16)
        c.d["x1"] = dt("x1", [TOWN, D], F32, kind="ExternalOutput")
        c.d["xn1"] = dt("xn1", [TOWN, D], BF16, kind="ExternalOutput")
        c.d["xn1lo"] = dt("xn1lo", [TOWN, D], BF16, kind="ExternalOutput")

        for n in ("wq", "wk", "wv", "wo"):
            sbuf(c, n, [128, DC * D], BF16)
        sbuf(c, "wfeat", [64, 128], BF16)
        sbuf(c, "c_tril4", [128, 4 * 128], BF16)
        sbuf(c, "c_shead", [16, 16 * 128], BF16)
        sbuf(c, "c_ucol", [64, 16 * 16], BF16)
        sbuf(c, "c_flag", [128, 1], F32)
        sbuf(c, "a_xnf", [128, DC * BLK], BF16)
        sbuf(c, "a_khh", [64, H * BLK], BF16)
        sbuf(c, "a_qhh", [64, H * BLK], BF16)
        sbuf(c, "a_vaug", [128, 4 * 1040], BF16)
        sbuf(c, "a_kpf", [128, H * BLK], BF16)
        sbuf(c, "a_qpf", [128, H * BLK], BF16)
        sbuf(c, "a_kptm", [128, H * 128], BF16)
        sbuf(c, "a_sqm", [16, 2 * BLK], BF16)
        sbuf(c, "a_am", [128, 2 * 512], BF16)
        sbuf(c, "a_numb", [64, 2 * 512], BF16)
        sbuf(c, "a_yall", [128, DC * BLK], BF16)
        sbuf(c, "a_yo", [128, 2 * BLK], BF16)
        sbuf(c, "a_sbf", [128, H * 65], BF16)
        sbuf(c, "a_dn", [1, 512], F32)
        sbuf(c, "a_rec", [1, 2 * 512], BF16)
        sbuf(c, "a_onesr", [1, 512], BF16)
        sbuf(c, "a_epsv", [1, 65], BF16)
        c.t["a_sqh"] = c.t["a_yall"][0:64, 4 * BLK:8 * BLK]
        c.t["r_ybuf"] = c.t["a_yall"][:, 0:2 * D]
        c.t["r_lobuf"] = c.t["a_yall"][:, 2 * D:4 * D]

        c.block = stack.enter_context(nc.Block())
        alloc_common(c)
        c.t["a_sf32"] = c.t["r_xs"][:, 0:H * 65]
        for i in range(4):
            c.dma("gpsimd", c.t["c_tril4"][:, i * 128:(i + 1) * 128],
                  c.d["c_trild"].ap(), "s_wt")
        c.dma("gpsimd", c.t["c_shead"][:], c.d["c_sheadd"].ap(), "s_wt")
        c.dma("gpsimd", c.t["c_ucol"][:], c.d["c_ucold"].ap(), "s_wt")
        c.dma("sync", c.t["c_flag"][:], c.d["c_flagd"].ap(), "s_ld")
        load_w_dmajor(c, c.t["wq"], c.d["w_q"].ap(), D)
        load_w_dmajor(c, c.t["wk"], c.d["w_k"].ap(), D)
        load_w_dmajor(c, c.t["wv"], c.d["w_v"].ap(), D)
        load_w_dmajor(c, c.t["wo"], c.d["w_o"].ap(), D)
        c.dma("gpsimd", c.t["wfeat"][:], c.d["w_f"].ap(), "s_wt")
        for eng in ("scalar", "vector"):
            c.wait(eng, "s_wt")
            c.wait(eng, "s_ld")
        c.wait("tensor", "s_ld")

        emit_l0a(c)
        c.wait("sync", "s_yst0")
        c.wait("sync", "s_yst1")
        c.wait("vector", "s_xn0")
        c.wait("vector", "s_xn1")
        c.wait("scalar", "s_xn0")
        c.wait("scalar", "s_xn1")
        c.wait("sync", "s_pe")
        emit_res(c, c.d["x_own"].ap(), c.d["y0"].ap(), c.d["x1"].ap(),
                 c.d["xn1"].ap(), c.d["xn1lo"].ap())
        finish(c, ["s_rst0", "s_rst1"])
    return nc


def build_program_b():
    nc = bass.Bass("TRN2", target_bir_lowering=False)
    stack = ExitStack()
    c = Ctx(nc, stack)
    with stack:
        dt = nc.dram_tensor
        c.d["x1"] = dt("x1", [TOWN, D], F32, kind="ExternalInput")
        c.d["xn1"] = dt("xn1", [TOWN, D], BF16, kind="ExternalInput")
        c.d["xn1lo"] = dt("xn1lo", [TOWN, D], BF16, kind="ExternalInput")
        c.d["f_wgd"] = dt("f_wgd", [D, 8], F32, kind="ExternalInput")
        c.d["f_bgd"] = dt("f_bgd", [1, 8], F32, kind="ExternalInput")
        c.d["f_w1d"] = dt("f_w1d", [8, D, D], F32, kind="ExternalInput")
        c.d["f_b1d"] = dt("f_b1d", [128, 64], F32, kind="ExternalInput")
        c.d["f_w2d"] = dt("f_w2d", [8, D, D], F32, kind="ExternalInput")
        c.d["f_b2d"] = dt("f_b2d", [8, D], F32, kind="ExternalInput")
        c.d["c_sel8d"] = dt("c_sel8d", [8, 8 * 128], F32,
                            kind="ExternalInput")
        c.d["c_identfd"] = dt("c_identfd", [128, 128], F32,
                              kind="ExternalInput")
        c.d["y1"] = dt("y1", [D, TOWN], BF16)
        c.d["x2"] = dt("x2", [TOWN, D], F32, kind="ExternalOutput")
        c.d["xn2"] = dt("xn2", [TOWN, D], BF16, kind="ExternalOutput")

        sbuf(c, "f_xh", [128, DC * TT], BF16)
        sbuf(c, "f_xl", [128, DC * TT], BF16)
        sbuf(c, "f_acc", [128, DC * TT], F32)
        sbuf(c, "f_w1", [128, 2 * DC * 512], BF16)
        sbuf(c, "f_w2", [128, 2 * 4 * D], BF16)
        sbuf(c, "f_b1", [128, 64], F32)
        sbuf(c, "f_h", [128, 4 * 512], BF16)
        sbuf(c, "f_hm", [128, 4 * 512], BF16)
        sbuf(c, "f_y", [128, 2 * TT], BF16)
        sbuf(c, "f_mk", [8, TT], BF16)
        sbuf(c, "f_lg", [128, 8], F32)
        sbuf(c, "f_mx", [128, 1], F32)
        sbuf(c, "f_mk32", [128, 8], F32)
        sbuf(c, "f_wgh", [128, 64], BF16)
        sbuf(c, "f_wgl", [128, 64], BF16)
        sbuf(c, "f_wgf", [128, 64], F32)
        sbuf(c, "f_bgf", [1, 8], F32)
        sbuf(c, "f_bgb", [128, 8], F32)
        sbuf(c, "f_b2b", [8, D], BF16)
        sbuf(c, "f_b2f", [8, D], F32)
        sbuf(c, "c_sel8", [8, 8 * 128], BF16)
        sbuf(c, "c_identf", [128, 128], F32)
        rb = sbuf(c, "r_ybuf_t", [128, 2 * D], BF16)
        c.t["r_ybuf"] = rb[:, :]
        c.t["r_lobuf"] = rb[:, :]

        c.block = stack.enter_context(nc.Block())
        alloc_common(c)
        emit_l0f(c)
        c.wait("sync", "s_yst0")
        c.wait("sync", "s_yst1")
        c.wait("sync", "s_pe")
        emit_res(c, c.d["x1"].ap(), c.d["y1"].ap(), c.d["x2"].ap(),
                 c.d["xn2"].ap(), None)
        finish(c, ["s_rst0", "s_rst1"])
    return nc


def build_program_c():
    nc = bass.Bass("TRN2", target_bir_lowering=False)
    stack = ExitStack()
    c = Ctx(nc, stack)
    with stack:
        dt = nc.dram_tensor
        c.d["x2"] = dt("x2", [TOWN, D], F32, kind="ExternalInput")
        c.d["xn2i"] = dt("xn2i", [TOWN, D], BF16, kind="ExternalInput")
        c.d["xn2f"] = dt("xn2f", [2 * TOWN, D], BF16, kind="ExternalInput")
        c.d["gw_qd"] = dt("gw_qd", [D, D], F32, kind="ExternalInput")
        c.d["gw_kd"] = dt("gw_kd", [D, D], F32, kind="ExternalInput")
        c.d["gw_vd"] = dt("gw_vd", [D, D], F32, kind="ExternalInput")
        c.d["gw_od"] = dt("gw_od", [D, D], F32, kind="ExternalInput")
        c.d["g_bqkd"] = dt("g_bqkd", [128, 16], F32, kind="ExternalInput")
        c.d["g_bvrd"] = dt("g_bvrd", [1, D], F32, kind="ExternalInput")
        c.d["g_bolsd"] = dt("g_bolsd", [128, 16], F32, kind="ExternalInput")
        c.d["kn_d"] = dt("kn_d", [D, 2 * TOWN], BF16)
        c.d["v_d"] = dt("v_d", [2 * TOWN, D], BF16)
        c.d["qn_d"] = dt("qn_d", [D, TOWN], BF16)
        c.d["ao_tm"] = dt("ao_tm", [TOWN, D], BF16)
        c.d["y2"] = dt("y2", [D, TOWN], BF16)
        c.d["x3"] = dt("x3", [TOWN, D], F32, kind="ExternalOutput")
        c.d["xn3"] = dt("xn3", [TOWN, D], BF16, kind="ExternalOutput")

        for n in ("gw_q", "gw_k", "gw_v", "gw_o"):
            sbuf(c, n, [128, DC * D], BF16)
        sbuf(c, "g_xnf", [128, DC * BLK], BF16)
        sbuf(c, "g_kraw", [128, DC * BLK], BF16)
        sbuf(c, "g_ksq", [128, DC * BLK], BF16)
        sbuf(c, "g_kn", [128, DC * BLK], BF16)
        sbuf(c, "g_v", [128, 4 * D], BF16)
        sbuf(c, "g_rows", [1, 1024], F32)
        sbuf(c, "g_r2", [1, 1024], F32)
        sbuf(c, "g_r3", [1, 1024], F32)
        sbuf(c, "g_ab", [1, 1024], BF16)
        sbuf(c, "g_bqk", [128, 16], F32)
        sbuf(c, "g_bvr", [1, D], BF16)
        sbuf(c, "g_bols", [128, 16], F32)
        sbuf(c, "c_ones128", [128, 1], BF16)
        sbuf(c, "g_qnt", [128, DC * 256], BF16)
        sbuf(c, "g_att", [128, 2 * 256], BF16)
        sbuf(c, "g_outb", [128, 2 * 2048], BF16)
        sbuf(c, "g_knl", [128, 2 * 4096], BF16)
        sbuf(c, "g_vbl", [128, 2 * 4096], BF16)
        rb = sbuf(c, "r_ybuf_t", [128, 2 * D], BF16)
        c.t["r_ybuf"] = rb[:, :]
        c.t["r_lobuf"] = rb[:, :]

        c.block = stack.enter_context(nc.Block())
        alloc_common(c)
        load_w_dmajor(c, c.t["gw_q"], c.d["gw_qd"].ap(), D)
        load_w_dmajor(c, c.t["gw_k"], c.d["gw_kd"].ap(), D)
        load_w_dmajor(c, c.t["gw_v"], c.d["gw_vd"].ap(), D)
        load_w_dmajor(c, c.t["gw_o"], c.d["gw_od"].ap(), D)
        c.dma("sync", c.t["g_bqk"][:], c.d["g_bqkd"].ap(), "s_ld")
        c.dma("gpsimd", c.t["g_bvr"][:], c.d["g_bvrd"].ap(), "s_wt")
        c.dma("sync", c.t["g_bols"][:], c.d["g_bolsd"].ap(), "s_ld")
        for eng in ("scalar", "vector"):
            c.wait(eng, "s_ld")
            c.wait(eng, "s_wt")
        c.wait("tensor", "s_ld")
        emit_l1a(c)
        c.wait("sync", "s_pe")
        emit_res(c, c.d["x2"].ap(), c.d["y2"].ap(), c.d["x3"].ap(),
                 c.d["xn3"].ap(), None)
        finish(c, ["s_rst0", "s_rst1"])
    return nc




def emit_l1f(c):
    P0, P1, P2, P3, P4 = (c.t["pp0"], c.t["pp1"], c.t["pp2"], c.t["pp3"],
                          c.t["pp4"])
    ST0, ST1 = c.t["st0"], c.t["st1"]
    PP, PN = [P0, P1], [P2, P3]
    fx = c.t["m_x"]
    acc = c.t["m_acc"]
    wgx, wux, wdx = c.t["m_wg"], c.t["m_wu"], c.t["m_wd"]
    gsb, usb, gub, hms = c.t["m_g"], c.t["m_u"], c.t["m_gu"], c.t["m_hm"]
    ybf = c.t["m_y"]
    cmf = c.t["m_cm"]
    scr, w16, t16, mcol = c.t["m_scr"], c.t["m_w16"], c.t["m_t16"], c.t["m_mc"]
    gwt, sel16, identf = c.t["m_gwt"], c.t["c_sel16"], c.t["c_identf"]
    xn3 = c.d["xn3i"].ap()
    egd, eud, edd = c.d["m_egd"].ap(), c.d["m_eud"].ap(), c.d["m_edd"].ap()
    sgd, sud, sdd = c.d["m_sgd"].ap(), c.d["m_sud"].ap(), c.d["m_sdd"].ap()
    y3 = c.d["y3"].ap()

    c.dma("gpsimd", gwt[:].rearrange("p (dc e) -> p dc e", e=16),
          c.d["m_gwtd"].ap().rearrange("(dc p) e -> p dc e", p=128), "s_wt")
    c.dma("gpsimd", sel16[:], c.d["c_sel16d"].ap(), "s_wt")
    c.dma("sync", identf[:], c.d["c_identfd"].ap(), "s_ld")
    c.wait("vector", "s_ld")
    c.wait("vector", "s_wt")
    c.wait("scalar", "s_ld")
    c.wait("scalar", "s_wt")

    pe_fx_free = 0
    last_hm = 0
    conv_dv = 0
    last_store = {0: 0, 1: 0}
    wfree = {}
    for tt in range(NTT):
        t0 = tt * TT
        c.wait("sync", "s_pe", pe_fx_free)
        for dc in range(DC):
            c.dma("sync", fx[:, dc * TT:(dc + 1) * TT],
                  xn3[t0:t0 + TT, dc * 128:(dc + 1) * 128], "s_tr",
                  transpose=True)

        c.wait("tensor", "s_tr")
        if tt == 0:
            c.wait("tensor", "s_wt")
            c.wait("tensor", "s_ld")
        st0_free = 0
        scr_free = 0
        tr_evac = 0
        for ch in range(TT // 128):
            c.wait("tensor", "s_ac", st0_free)
            for dc in range(DC):
                c.op("tensor", lambda e, dc=dc, ch=ch: e.matmul(
                    ST0[0:128, 0:16],
                    fx[:, dc * TT + ch * 128:dc * TT + ch * 128 + 128],
                    gwt[:, dc * 16:(dc + 1) * 16],
                    start=(dc == 0), stop=(dc == DC - 1)),
                    "s_pe" if dc == DC - 1 else None)
            c.wait("scalar", "s_pe")
            c.wait("scalar", "s_dv", scr_free)
            nsg = c.op("scalar", lambda e: e.activation(
                scr[:], ST0[0:128, 0:16], AF.Sigmoid), "s_ac")
            st0_free = nsg
            c.wait("vector", "s_ac", nsg)
            n = c.op("vector", lambda e: e.tensor_scalar(
                w16[:], scr[:], 0.0, None, ALU.add), "s_dv")
            for r in range(4):
                c.wait("vector", "s_dv", n)
                n = c.op("vector", lambda e, r=r: e.reduce_max(
                    mcol[:, r:r + 1], w16[:], axis=mybir.AxisListType.X),
                    "s_dv")
                if r < 3:
                    c.wait("vector", "s_dv", n)
                    n = c.op("vector", lambda e, r=r: e.tensor_scalar(
                        t16[:], w16[:], mcol[:, r:r + 1], None, ALU.is_ge),
                        "s_dv")
                    c.wait("vector", "s_dv", n)
                    n = c.op("vector", lambda e: e.scalar_tensor_tensor(
                        w16[:], t16[:], -2.0, w16[:], ALU.mult, ALU.add),
                        "s_dv")
            c.wait("vector", "s_dv", n)
            n = c.op("vector", lambda e: e.reduce_sum(
                mcol[:, 4:5], mcol[:, 0:4], axis=mybir.AxisListType.X), "s_dv")
            c.wait("vector", "s_dv", n)
            n = c.op("vector", lambda e: e.tensor_scalar(
                mcol[:, 4:5], mcol[:, 4:5], 1e-20, None, ALU.add), "s_dv")
            c.wait("vector", "s_dv", n)
            n = c.op("vector", lambda e: e.reciprocal(
                mcol[:, 5:6], mcol[:, 4:5]), "s_dv")
            c.wait("vector", "s_dv", n)
            n = c.op("vector", lambda e: e.tensor_scalar(
                t16[:], scr[:], mcol[:, 3:4], None, ALU.is_ge), "s_dv")
            c.wait("vector", "s_dv", n)
            n = c.op("vector", lambda e: e.tensor_tensor(
                t16[:], t16[:], scr[:], ALU.mult), "s_dv")
            c.wait("vector", "s_dv", n)
            n = c.op("vector", lambda e: e.tensor_scalar(
                t16[:], t16[:], mcol[:, 5:6], None, ALU.mult), "s_dv")
            scr_free = n
            c.wait("tensor", "s_dv", n)
            c.wait("tensor", "s_ac", tr_evac)
            n = c.op("tensor", lambda e: e.transpose(
                ST1[0:16, 128:256], t16[:], identf[:]), "s_pe")
            c.wait("scalar", "s_pe", n)
            tr_evac = c.op("scalar", lambda e, ch=ch: e.copy(
                cmf[0:16, ch * 128:(ch + 1) * 128], ST1[0:16, 128:256]),
                "s_ac")

        for u in range(18):
            pu = u % 2
            c.wait("gpsimd", "s_pe", wfree.get(pu, 0))
            if u < 16:
                c.dma("gpsimd",
                      wgx.ap().rearrange("p (b dc i) -> p b dc i", b=2, i=512)
                      [:, pu], egd[u].rearrange("(dc p) i -> p dc i", p=128),
                      "s_w%d" % pu)
                c.dma("gpsimd",
                      wux.ap().rearrange("p (b dc i) -> p b dc i", b=2, i=512)
                      [:, pu], eud[u].rearrange("(dc p) i -> p dc i", p=128),
                      "s_w%d" % pu)
                c.dma("gpsimd",
                      wdx.ap().rearrange("p (b ic o) -> p b ic o", b=2, o=1024)
                      [:, pu], edd[u].rearrange("(ic p) o -> p ic o", p=128),
                      "s_w%d" % pu)
            else:
                hs = u - 16
                c.dma("gpsimd",
                      wgx.ap().rearrange("p (b dc i) -> p b dc i", b=2, i=512)
                      [:, pu],
                      sgd.rearrange("(dc p) i -> p dc i", p=128)
                      [:, :, hs * 512:(hs + 1) * 512], "s_w%d" % pu)
                c.dma("gpsimd",
                      wux.ap().rearrange("p (b dc i) -> p b dc i", b=2, i=512)
                      [:, pu],
                      sud.rearrange("(dc p) i -> p dc i", p=128)
                      [:, :, hs * 512:(hs + 1) * 512], "s_w%d" % pu)
                c.dma("gpsimd",
                      wdx.ap().rearrange("p (b ic o) -> p b ic o", b=2, o=1024)
                      [:, pu],
                      sdd[hs * 512:(hs + 1) * 512, :]
                      .rearrange("(ic p) o -> p ic o", p=128), "s_w%d" % pu)
            c.wait("tensor", "s_w%d" % pu)
            if u < 2:
                c.wait("tensor", "s_ac", tr_evac)
            last_mm = 0
            for sub in range(TT // 512):
                gev = {}
                c.wait("tensor", "s_ac")
                c.wait("tensor", "s_dv")
                for ic in range(4):
                    pb = PP[ic % 2]
                    if ic >= 2:
                        c.wait("tensor", "s_ac", gev[ic - 2])
                    for dc in range(DC):
                        c.op("tensor", lambda e, pb=pb, ic=ic, dc=dc, pu=pu,
                             sub=sub: e.matmul(
                                 pb[:, 0:512],
                                 wgx[:, (pu * 8 + dc) * 512 + ic * 128:
                                     (pu * 8 + dc) * 512 + ic * 128 + 128],
                                 fx[:, dc * TT + sub * 512:
                                    dc * TT + sub * 512 + 512],
                                 start=(dc == 0), stop=(dc == DC - 1)),
                             "s_pe" if dc == DC - 1 else None)
                    c.wait("scalar", "s_pe")
                    gev[ic] = c.op("scalar", lambda e, pb=pb, ic=ic:
                                   e.activation(
                                       gsb[:, ic * 512:(ic + 1) * 512],
                                       pb[:, 0:512], AF.Silu), "s_ac")
                uev = {}
                for ic in range(4):
                    pb = PN[ic % 2]
                    if ic >= 2:
                        c.wait("tensor", "s_ac", uev[ic - 2])
                    for dc in range(DC):
                        c.op("tensor", lambda e, pb=pb, ic=ic, dc=dc, pu=pu,
                             sub=sub: e.matmul(
                                 pb[:, 0:512],
                                 wux[:, (pu * 8 + dc) * 512 + ic * 128:
                                     (pu * 8 + dc) * 512 + ic * 128 + 128],
                                 fx[:, dc * TT + sub * 512:
                                    dc * TT + sub * 512 + 512],
                                 start=(dc == 0), stop=(dc == DC - 1)),
                             "s_pe" if dc == DC - 1 else None)
                    c.wait("scalar", "s_pe")
                    uev[ic] = c.op("scalar", lambda e, pb=pb, ic=ic: e.copy(
                        usb[:, ic * 512:(ic + 1) * 512], pb[:, 0:512]), "s_ac")
                c.wait("vector", "s_ac", uev[3])
                gu_done = 0
                for ic in range(4):
                    gu_done = c.op("vector", lambda e, ic=ic: e.tensor_tensor(
                        gub[:, ic * 512:(ic + 1) * 512],
                        gsb[:, ic * 512:(ic + 1) * 512],
                        usb[:, ic * 512:(ic + 1) * 512], ALU.mult), "s_dv")
                rhs_h = gub
                if u < 16:
                    c.wait("tensor", "s_dv", last_hm)
                    pe_mb = c.op("tensor", lambda e, u=u, sub=sub: e.matmul(
                        P4[:, 0:512], sel16[:, u * 128:(u + 1) * 128],
                        cmf[0:16, sub * 512:(sub + 1) * 512],
                        start=True, stop=True), "s_pe")
                    c.wait("vector", "s_pe", pe_mb)
                    c.wait("vector", "s_dv", gu_done)
                    for ic in range(4):
                        last_hm = c.op("vector", lambda e, ic=ic:
                                       e.tensor_tensor(
                                           hms[:, ic * 512:(ic + 1) * 512],
                                           gub[:, ic * 512:(ic + 1) * 512],
                                           P4[:, 0:512], ALU.mult), "s_dv")
                    rhs_h = hms
                    hm_rdy = last_hm
                else:
                    hm_rdy = gu_done
                c.wait("tensor", "s_dv", hm_rdy)
                aev = {}
                for oc in range(DC):
                    pd = PP[oc % 2]
                    if oc >= 2:
                        c.wait("tensor", "s_dv", aev[oc - 2])
                    c.wait("tensor", "s_ac")
                    for ic in range(4):
                        last_mm = c.op(
                            "tensor", lambda e, pd=pd, oc=oc, ic=ic, pu=pu,
                            rhs_h=rhs_h: e.matmul(
                                pd[:, 0:512],
                                wdx[:, (pu * 4 + ic) * 1024 + oc * 128:
                                    (pu * 4 + ic) * 1024 + oc * 128 + 128],
                                rhs_h[:, ic * 512:(ic + 1) * 512],
                                start=(ic == 0), stop=(ic == 3)),
                            "s_pe" if ic == 3 else None)
                    c.wait("vector", "s_pe")
                    if u == 0:
                        c.wait("vector", "s_dv", conv_dv)
                    dst = acc[:, oc * TT + sub * 512:oc * TT + sub * 512 + 512]
                    aev[oc] = c.op("vector", lambda e, pd=pd, dst=dst,
                                   first=(u == 0): (
                                       e.tensor_scalar(dst, pd[:, 0:512], 0.0,
                                                       None, ALU.add)
                                       if first else
                                       e.tensor_tensor(dst, dst, pd[:, 0:512],
                                                       ALU.add)), "s_dv")
            wfree[pu] = last_mm
        pe_fx_free = c.cur("s_pe")
        c.wait("vector", "s_dv")
        for oc in range(DC):
            p = oc % 2
            c.wait("vector", "s_yst%d" % p, last_store[p])
            conv_dv = c.op("vector", lambda e, oc=oc, p=p: e.tensor_scalar(
                ybf[:, p * TT:(p + 1) * TT], acc[:, oc * TT:(oc + 1) * TT],
                0.0, None, ALU.add), "s_dv")
            c.wait("scalar", "s_dv", conv_dv)
            last_store[p] = c.dma(
                "scalar", y3[oc * 128:(oc + 1) * 128, t0:t0 + TT],
                ybf[:, p * TT:(p + 1) * TT], "s_yst%d" % p)


def build_program_d():
    nc = bass.Bass("TRN2", target_bir_lowering=False)
    stack = ExitStack()
    c = Ctx(nc, stack)
    with stack:
        dt = nc.dram_tensor
        c.d["x3"] = dt("x3", [TOWN, D], F32, kind="ExternalInput")
        c.d["xn3i"] = dt("xn3i", [TOWN, D], BF16, kind="ExternalInput")
        c.d["m_gwtd"] = dt("m_gwtd", [D, 16], F32, kind="ExternalInput")
        c.d["m_egd"] = dt("m_egd", [16, D, 512], F32, kind="ExternalInput")
        c.d["m_eud"] = dt("m_eud", [16, D, 512], F32, kind="ExternalInput")
        c.d["m_edd"] = dt("m_edd", [16, 512, D], F32, kind="ExternalInput")
        c.d["m_sgd"] = dt("m_sgd", [D, D], F32, kind="ExternalInput")
        c.d["m_sud"] = dt("m_sud", [D, D], F32, kind="ExternalInput")
        c.d["m_sdd"] = dt("m_sdd", [D, D], F32, kind="ExternalInput")
        c.d["c_sel16d"] = dt("c_sel16d", [16, 16 * 128], F32,
                             kind="ExternalInput")
        c.d["c_identfd"] = dt("c_identfd", [128, 128], F32,
                              kind="ExternalInput")
        c.d["y3"] = dt("y3", [D, TOWN], BF16)
        c.d["out"] = dt("out", [TOWN, D], F32, kind="ExternalOutput")

        sbuf(c, "m_x", [128, DC * TT], BF16)
        sbuf(c, "m_acc", [128, DC * TT], F32)
        sbuf(c, "m_wg", [128, 2 * DC * 512], BF16)
        sbuf(c, "m_wu", [128, 2 * DC * 512], BF16)
        sbuf(c, "m_wd", [128, 2 * 4 * D], BF16)
        sbuf(c, "m_g", [128, 4 * 512], BF16)
        sbuf(c, "m_u", [128, 4 * 512], BF16)
        sbuf(c, "m_gu", [128, 4 * 512], BF16)
        sbuf(c, "m_hm", [128, 4 * 512], BF16)
        sbuf(c, "m_y", [128, 2 * TT], BF16)
        sbuf(c, "m_cm", [16, TT], BF16)
        sbuf(c, "m_scr", [128, 16], F32)
        sbuf(c, "m_w16", [128, 16], F32)
        sbuf(c, "m_t16", [128, 16], F32)
        sbuf(c, "m_mc", [128, 8], F32)
        sbuf(c, "m_gwt", [128, DC * 16], BF16)
        sbuf(c, "c_sel16", [16, 16 * 128], BF16)
        sbuf(c, "c_identf", [128, 128], F32)
        rb = sbuf(c, "r_ybuf_t", [128, 2 * D], BF16)
        c.t["r_ybuf"] = rb[:, :]
        c.t["r_lobuf"] = rb[:, :]

        c.block = stack.enter_context(nc.Block())
        alloc_common(c)
        emit_l1f(c)
        c.wait("sync", "s_yst0")
        c.wait("sync", "s_yst1")
        c.wait("sync", "s_pe")
        emit_res(c, c.d["x3"].ap(), c.d["y3"].ap(), c.d["out"].ap(),
                 None, None)
        finish(c, ["s_rst0", "s_rst1"])
    return nc


def phase_barrier(c):
    names = list(c.sems.keys())
    for eng in ("sync", "gpsimd", "vector", "scalar", "tensor"):
        for sn in names:
            c.wait(eng, sn)


from contextlib import contextmanager


@contextmanager
def phase_stack(c):
    outer = c.stack
    inner = ExitStack()
    c.stack = inner
    try:
        with inner:
            yield
    finally:
        c.stack = outer


SEM_NAMES = ["s_ld", "s_wt", "s_in0", "s_in1", "s_xn0", "s_xn1", "s_tr",
             "s_pe", "s_dv", "s_ac", "s_yst0", "s_yst1", "s_rin0", "s_rin1",
             "s_rst0", "s_rst1", "s_w0", "s_w1", "s_st0", "s_st1", "s_q0",
             "s_q1", "s_kv0", "s_kv1", "s_ao0", "s_ao1", "s_y0", "s_y1",
             "s_cc"]


def build_fused():
    nc = bass.Bass("TRN2", target_bir_lowering=False)
    stack = ExitStack()
    c = Ctx(nc, stack)
    with stack:
        dt = nc.dram_tensor
        c.d["x_pre"] = dt("x_pre", [TOWN, D], F32, kind="ExternalInput")
        c.d["x_own"] = dt("x_own", [TOWN, D], F32, kind="ExternalInput")
        c.d["w_q"] = dt("w_q", [D, D], F32, kind="ExternalInput")
        c.d["w_k"] = dt("w_k", [D, D], F32, kind="ExternalInput")
        c.d["w_v"] = dt("w_v", [D, D], F32, kind="ExternalInput")
        c.d["w_o"] = dt("w_o", [D, D], F32, kind="ExternalInput")
        c.d["w_f"] = dt("w_f", [HD, NF], F32, kind="ExternalInput")
        c.d["c_trild"] = dt("c_trild", [128, 128], F32, kind="ExternalInput")
        c.d["c_sheadd"] = dt("c_sheadd", [16, 16 * 128], F32,
                             kind="ExternalInput")
        c.d["c_ucold"] = dt("c_ucold", [64, 16 * 16], F32,
                            kind="ExternalInput")
        c.d["c_flagd"] = dt("c_flagd", [128, 1], F32, kind="ExternalInput")
        c.d["f_wgd"] = dt("f_wgd", [D, 8], F32, kind="ExternalInput")
        c.d["f_bgd"] = dt("f_bgd", [1, 8], F32, kind="ExternalInput")
        c.d["f_w1d"] = dt("f_w1d", [8, D, D], F32, kind="ExternalInput")
        c.d["f_b1d"] = dt("f_b1d", [128, 64], F32, kind="ExternalInput")
        c.d["f_w2d"] = dt("f_w2d", [8, D, D], F32, kind="ExternalInput")
        c.d["f_b2d"] = dt("f_b2d", [8, D], F32, kind="ExternalInput")
        c.d["c_sel8d"] = dt("c_sel8d", [8, 8 * 128], F32,
                            kind="ExternalInput")
        c.d["c_identfd"] = dt("c_identfd", [128, 128], F32,
                              kind="ExternalInput")
        c.d["gw_qd"] = dt("gw_qd", [D, D], F32, kind="ExternalInput")
        c.d["gw_kd"] = dt("gw_kd", [D, D], F32, kind="ExternalInput")
        c.d["gw_vd"] = dt("gw_vd", [D, D], F32, kind="ExternalInput")
        c.d["gw_od"] = dt("gw_od", [D, D], F32, kind="ExternalInput")
        c.d["g_bqkd"] = dt("g_bqkd", [128, 16], F32, kind="ExternalInput")
        c.d["g_bvrd"] = dt("g_bvrd", [1, D], F32, kind="ExternalInput")
        c.d["g_bolsd"] = dt("g_bolsd", [128, 16], F32, kind="ExternalInput")
        c.d["m_gwtd"] = dt("m_gwtd", [D, 16], F32, kind="ExternalInput")
        c.d["m_egd"] = dt("m_egd", [16, D, 512], F32, kind="ExternalInput")
        c.d["m_eud"] = dt("m_eud", [16, D, 512], F32, kind="ExternalInput")
        c.d["m_edd"] = dt("m_edd", [16, 512, D], F32, kind="ExternalInput")
        c.d["m_sgd"] = dt("m_sgd", [D, D], F32, kind="ExternalInput")
        c.d["m_sud"] = dt("m_sud", [D, D], F32, kind="ExternalInput")
        c.d["m_sdd"] = dt("m_sdd", [D, D], F32, kind="ExternalInput")
        c.d["c_sel16d"] = dt("c_sel16d", [16, 16 * 128], F32,
                             kind="ExternalInput")
        c.d["xn0"] = dt("xn0", [2 * TOWN, D], BF16)
        c.d["y0"] = dt("y0", [D, TOWN], BF16)
        c.d["x1"] = dt("x1", [TOWN, D], F32)
        c.d["xn1"] = dt("xn1", [TOWN, D], BF16)
        c.d["xn1lo"] = dt("xn1lo", [TOWN, D], BF16)
        c.d["y1"] = dt("y1", [D, TOWN], BF16)
        c.d["x2"] = dt("x2", [TOWN, D], F32)
        c.d["xn2i"] = dt("xn2i", [TOWN, D], BF16)
        c.d["xn2f"] = dt("xn2f", [2 * TOWN, D], BF16)
        c.d["kn_d"] = dt("kn_d", [D, 2 * TOWN], BF16)
        c.d["v_d"] = dt("v_d", [2 * TOWN, D], BF16)
        c.d["qn_d"] = dt("qn_d", [D, TOWN], BF16)
        c.d["ao_tm"] = dt("ao_tm", [TOWN, D], BF16)
        c.d["y2"] = dt("y2", [D, TOWN], BF16)
        c.d["x3"] = dt("x3", [TOWN, D], F32)
        c.d["xn3i"] = dt("xn3i", [TOWN, D], BF16)
        c.d["y3"] = dt("y3", [D, TOWN], BF16)
        c.d["out"] = dt("out", [TOWN, D], F32, kind="ExternalOutput")

        c.block = stack.enter_context(nc.Block())
        for sn in SEM_NAMES:
            c.sem(sn)
        alloc_common(c)

        with phase_stack(c):
            for n in ("wq", "wk", "wv", "wo"):
                sbuf(c, n, [128, DC * D], BF16)
            sbuf(c, "wfeat", [64, 128], BF16)
            sbuf(c, "c_tril4", [128, 4 * 128], BF16)
            sbuf(c, "c_shead", [16, 16 * 128], BF16)
            sbuf(c, "c_ucol", [64, 16 * 16], BF16)
            sbuf(c, "c_flag", [128, 1], F32)
            sbuf(c, "a_xnf", [128, DC * BLK], BF16)
            sbuf(c, "a_khh", [64, H * BLK], BF16)
            sbuf(c, "a_qhh", [64, H * BLK], BF16)
            sbuf(c, "a_vaug", [128, 4 * 1040], BF16)
            sbuf(c, "a_kpf", [128, H * BLK], BF16)
            sbuf(c, "a_qpf", [128, H * BLK], BF16)
            sbuf(c, "a_kptm", [128, H * 128], BF16)
            sbuf(c, "a_sqm", [16, 2 * BLK], BF16)
            sbuf(c, "a_am", [128, 2 * 512], BF16)
            sbuf(c, "a_numb", [64, 2 * 512], BF16)
            sbuf(c, "a_yall", [128, DC * BLK], BF16)
            sbuf(c, "a_yo", [128, 2 * BLK], BF16)
            c.t["a_sf32"] = c.t["r_xs"][:, 0:H * 65]
            sbuf(c, "a_sbf", [128, H * 65], BF16)
            sbuf(c, "a_dn", [1, 512], F32)
            sbuf(c, "a_rec", [1, 2 * 512], BF16)
            sbuf(c, "a_onesr", [1, 512], BF16)
            sbuf(c, "a_epsv", [1, 65], BF16)
            c.t["a_sqh"] = c.t["a_yall"][0:64, 4 * BLK:8 * BLK]
            c.t["r_ybuf"] = c.t["a_yall"][:, 0:2 * D]
            c.t["r_lobuf"] = c.t["a_yall"][:, 2 * D:4 * D]

            for i in range(4):
                c.dma("gpsimd", c.t["c_tril4"][:, i * 128:(i + 1) * 128],
                      c.d["c_trild"].ap(), "s_wt")
            c.dma("gpsimd", c.t["c_shead"][:], c.d["c_sheadd"].ap(), "s_wt")
            c.dma("gpsimd", c.t["c_ucol"][:], c.d["c_ucold"].ap(), "s_wt")
            c.dma("sync", c.t["c_flag"][:], c.d["c_flagd"].ap(), "s_ld")
            load_w_dmajor(c, c.t["wq"], c.d["w_q"].ap(), D)
            load_w_dmajor(c, c.t["wk"], c.d["w_k"].ap(), D)
            load_w_dmajor(c, c.t["wv"], c.d["w_v"].ap(), D)
            load_w_dmajor(c, c.t["wo"], c.d["w_o"].ap(), D)
            c.dma("gpsimd", c.t["wfeat"][:], c.d["w_f"].ap(), "s_wt")
            for eng in ("scalar", "vector"):
                c.wait(eng, "s_wt")
                c.wait(eng, "s_ld")
            c.wait("tensor", "s_ld")
            emit_l0a(c)
            c.wait("sync", "s_yst0")
            c.wait("sync", "s_yst1")
            c.wait("vector", "s_xn0")
            c.wait("vector", "s_xn1")
            c.wait("scalar", "s_xn0")
            c.wait("scalar", "s_xn1")
            c.wait("sync", "s_pe")
            emit_res(c, c.d["x_own"].ap(), c.d["y0"].ap(), c.d["x1"].ap(),
                     c.d["xn1"].ap(), c.d["xn1lo"].ap())
        phase_barrier(c)

        with phase_stack(c):
            sbuf(c, "f_xh", [128, DC * TT], BF16)
            sbuf(c, "f_xl", [128, DC * TT], BF16)
            sbuf(c, "f_acc", [128, DC * TT], F32)
            sbuf(c, "f_w1", [128, 2 * DC * 512], BF16)
            sbuf(c, "f_w2", [128, 2 * 4 * D], BF16)
            sbuf(c, "f_b1", [128, 64], F32)
            sbuf(c, "f_h", [128, 4 * 512], BF16)
            sbuf(c, "f_hm", [128, 4 * 512], BF16)
            sbuf(c, "f_y", [128, 2 * TT], BF16)
            sbuf(c, "f_mk", [8, TT], BF16)
            sbuf(c, "f_lg", [128, 8], F32)
            sbuf(c, "f_mx", [128, 1], F32)
            sbuf(c, "f_mk32", [128, 8], F32)
            sbuf(c, "f_wgh", [128, 64], BF16)
            sbuf(c, "f_wgl", [128, 64], BF16)
            sbuf(c, "f_wgf", [128, 64], F32)
            sbuf(c, "f_bgf", [1, 8], F32)
            sbuf(c, "f_bgb", [128, 8], F32)
            sbuf(c, "f_b2b", [8, D], BF16)
            sbuf(c, "f_b2f", [8, D], F32)
            sbuf(c, "c_sel8", [8, 8 * 128], BF16)
            sbuf(c, "c_identf", [128, 128], F32)
            rb = sbuf(c, "r_ybuf_t", [128, 2 * D], BF16)
            c.t["r_ybuf"] = rb[:, :]
            c.t["r_lobuf"] = rb[:, :]
            emit_l0f(c)
            c.wait("sync", "s_yst0")
            c.wait("sync", "s_yst1")
            c.wait("sync", "s_pe")
            emit_res(c, c.d["x1"].ap(), c.d["y1"].ap(), c.d["x2"].ap(),
                     c.d["xn2i"].ap(), None)
        c.wait("gpsimd", "s_rst0")
        c.wait("gpsimd", "s_rst1")
        c.op("gpsimd", lambda e: e.collective_compute(
            "AllGather", ALU.bypass,
            replica_groups=[[0, 1], [2, 3], [4, 5], [6, 7]],
            ins=[c.d["xn2i"].ap().opt()], outs=[c.d["xn2f"].ap().opt()]),
            "s_cc")
        phase_barrier(c)

        with phase_stack(c):
            for n in ("gw_q", "gw_k", "gw_v", "gw_o"):
                sbuf(c, n, [128, DC * D], BF16)
            sbuf(c, "g_xnf", [128, DC * BLK], BF16)
            sbuf(c, "g_kraw", [128, DC * BLK], BF16)
            sbuf(c, "g_ksq", [128, DC * BLK], BF16)
            sbuf(c, "g_kn", [128, DC * BLK], BF16)
            sbuf(c, "g_v", [128, 4 * D], BF16)
            sbuf(c, "g_rows", [1, 1024], F32)
            sbuf(c, "g_r2", [1, 1024], F32)
            sbuf(c, "g_r3", [1, 1024], F32)
            sbuf(c, "g_ab", [1, 1024], BF16)
            sbuf(c, "g_bqk", [128, 16], F32)
            sbuf(c, "g_bvr", [1, D], BF16)
            sbuf(c, "g_bols", [128, 16], F32)
            sbuf(c, "c_ones128", [128, 1], BF16)
            sbuf(c, "g_qnt", [128, DC * 256], BF16)
            sbuf(c, "g_att", [128, 2 * 256], BF16)
            sbuf(c, "g_outb", [128, 2 * 2048], BF16)
            sbuf(c, "g_knl", [128, 2 * 4096], BF16)
            sbuf(c, "g_vbl", [128, 2 * 4096], BF16)
            rb = sbuf(c, "r_ybuf_t2", [128, 2 * D], BF16)
            c.t["r_ybuf"] = rb[:, :]
            c.t["r_lobuf"] = rb[:, :]
            load_w_dmajor(c, c.t["gw_q"], c.d["gw_qd"].ap(), D)
            load_w_dmajor(c, c.t["gw_k"], c.d["gw_kd"].ap(), D)
            load_w_dmajor(c, c.t["gw_v"], c.d["gw_vd"].ap(), D)
            load_w_dmajor(c, c.t["gw_o"], c.d["gw_od"].ap(), D)
            c.dma("sync", c.t["g_bqk"][:], c.d["g_bqkd"].ap(), "s_ld")
            c.dma("gpsimd", c.t["g_bvr"][:], c.d["g_bvrd"].ap(), "s_wt")
            c.dma("sync", c.t["g_bols"][:], c.d["g_bolsd"].ap(), "s_ld")
            for eng in ("scalar", "vector"):
                c.wait(eng, "s_ld")
                c.wait(eng, "s_wt")
            c.wait("tensor", "s_ld")
            emit_l1a(c)
            c.wait("sync", "s_pe")
            emit_res(c, c.d["x2"].ap(), c.d["y2"].ap(), c.d["x3"].ap(),
                     c.d["xn3i"].ap(), None)
        phase_barrier(c)

        with phase_stack(c):
            sbuf(c, "m_x", [128, DC * TT], BF16)
            sbuf(c, "m_acc", [128, DC * TT], F32)
            sbuf(c, "m_wg", [128, 2 * DC * 512], BF16)
            sbuf(c, "m_wu", [128, 2 * DC * 512], BF16)
            sbuf(c, "m_wd", [128, 2 * 4 * D], BF16)
            sbuf(c, "m_g", [128, 4 * 512], BF16)
            sbuf(c, "m_u", [128, 4 * 512], BF16)
            sbuf(c, "m_gu", [128, 4 * 512], BF16)
            sbuf(c, "m_hm", [128, 4 * 512], BF16)
            sbuf(c, "m_y", [128, 2 * TT], BF16)
            sbuf(c, "m_cm", [16, TT], BF16)
            sbuf(c, "m_scr", [128, 16], F32)
            sbuf(c, "m_w16", [128, 16], F32)
            sbuf(c, "m_t16", [128, 16], F32)
            sbuf(c, "m_mc", [128, 8], F32)
            sbuf(c, "m_gwt", [128, DC * 16], BF16)
            sbuf(c, "c_sel16", [16, 16 * 128], BF16)
            t2 = c.stack.enter_context(
                c.nc.sbuf_tensor("c_identf2", [128, 128], F32))
            c.t["c_identf"] = t2
            rb = sbuf(c, "r_ybuf_t3", [128, 2 * D], BF16)
            c.t["r_ybuf"] = rb[:, :]
            c.t["r_lobuf"] = rb[:, :]
            emit_l1f(c)
            c.wait("sync", "s_yst0")
            c.wait("sync", "s_yst1")
            c.wait("sync", "s_pe")
            emit_res(c, c.d["x3"].ap(), c.d["y3"].ap(), c.d["out"].ap(),
                     None, None)
        finish(c, ["s_rst0", "s_rst1"])
    return nc


def host_consts():
    tril = (np.tril(np.ones((128, 128), np.float32), 0).T).copy()
    shead = np.zeros((16, 16 * 128), np.float32)
    for h in range(16):
        shead[h, h * 128:(h + 1) * 128] = 1.0
    ucol = np.zeros((64, 16 * 16), np.float32)
    for h in range(16):
        ucol[:, h * 16 + h] = 1.0
    cpool = np.tile(np.array(CONST_VALS, np.float32)[None, :], (128, 1))
    return dict(c_trild=np.ascontiguousarray(tril),
                c_sheadd=shead, c_ucold=ucol, c_pool=cpool)


def fused_in_maps(inputs):
    x = np.asarray(inputs["L0_attn_x"])
    consts = host_consts()
    ident = np.eye(128, dtype=np.float32)
    sel8 = np.zeros((8, 8 * 128), np.float32)
    for h in range(8):
        sel8[h, h * 128:(h + 1) * 128] = 1.0
    sel16 = np.zeros((16, 16 * 128), np.float32)
    for h in range(16):
        sel16[h, h * 128:(h + 1) * 128] = 1.0
    b1 = np.asarray(inputs["L0_ffn_b1"])
    b1r = np.ascontiguousarray(
        b1.reshape(8, 8, 128).transpose(2, 0, 1).reshape(128, 64))
    bq = np.asarray(inputs["L1_attn_bq"]).reshape(8, 128).T
    bk = np.asarray(inputs["L1_attn_bk"]).reshape(8, 128).T
    bo = np.asarray(inputs["L1_attn_bo"]).reshape(8, 128).T
    ls = np.asarray(inputs["L1_attn_layerscale"]).reshape(8, 128).T
    gwt = np.ascontiguousarray(np.asarray(inputs["L1_ffn_gate_w"]).T)
    common = {
        "c_pool": consts["c_pool"], "c_trild": consts["c_trild"],
        "c_sheadd": consts["c_sheadd"], "c_ucold": consts["c_ucold"],
        "c_sel8d": sel8, "c_sel16d": sel16, "c_identfd": ident,
        "w_q": np.asarray(inputs["L0_attn_Wq"]),
        "w_k": np.asarray(inputs["L0_attn_Wk"]),
        "w_v": np.asarray(inputs["L0_attn_Wv"]),
        "w_o": np.asarray(inputs["L0_attn_Wo"]),
        "w_f": np.asarray(inputs["L0_attn_W_feat"]),
        "f_wgd": np.asarray(inputs["L0_ffn_Wg"]),
        "f_bgd": np.asarray(inputs["L0_ffn_bg"]).reshape(1, 8),
        "f_w1d": np.asarray(inputs["L0_ffn_W1"]),
        "f_b1d": b1r,
        "f_w2d": np.asarray(inputs["L0_ffn_W2"]),
        "f_b2d": np.asarray(inputs["L0_ffn_b2"]),
        "gw_qd": np.asarray(inputs["L1_attn_Wq"]),
        "gw_kd": np.asarray(inputs["L1_attn_Wk"]),
        "gw_vd": np.asarray(inputs["L1_attn_Wv"]),
        "gw_od": np.asarray(inputs["L1_attn_Wo"]),
        "g_bqkd": np.ascontiguousarray(np.concatenate([bq, bk], axis=1)),
        "g_bvrd": np.asarray(inputs["L1_attn_bv"]).reshape(1, D),
        "g_bolsd": np.ascontiguousarray(np.concatenate([ls, bo], axis=1)),
        "m_gwtd": gwt,
        "m_egd": np.asarray(inputs["L1_ffn_expert_gate"]),
        "m_eud": np.asarray(inputs["L1_ffn_expert_up"]),
        "m_edd": np.asarray(inputs["L1_ffn_expert_down"]),
        "m_sgd": np.asarray(inputs["L1_ffn_shared_gate"]),
        "m_sud": np.asarray(inputs["L1_ffn_shared_up"]),
        "m_sdd": np.asarray(inputs["L1_ffn_shared_down"]),
    }
    in_maps = []
    for core in range(8):
        b, half = core // 2, core % 2
        m = dict(common)
        m["x_own"] = np.ascontiguousarray(x[b, half * TOWN:(half + 1) * TOWN])
        m["x_pre"] = np.ascontiguousarray(x[b, 0:TOWN])
        m["c_flagd"] = np.full((128, 1), float(half), np.float32)
        in_maps.append(m)
    return in_maps


def kernel(**inputs):
    in_maps = fused_in_maps(inputs)
    nc = build_fused()
    res = run_bass_kernel_spmd(nc, in_maps, core_ids=list(range(8)))
    out = np.zeros((4, 8192, D), np.float32)
    for core in range(8):
        b, half = core // 2, core % 2
        out[b, half * TOWN:(half + 1) * TOWN] = res.results[core]["out"]
    return out
